# version 12
# baseline (speedup 1.0000x reference)
import contextlib
import numpy as np
import ml_dtypes
import concourse.bass as bass
import concourse.mybir as mybir
from concourse.bass_utils import run_bass_kernel_spmd

F32 = mybir.dt.float32
BF16 = mybir.dt.bfloat16
I32 = mybir.dt.int32
AF = mybir.ActivationFunctionType
OP = mybir.AluOpType
AX = mybir.AxisListType

NCORES = 8
B_CORE = 2
T = 2048
D = 1024
DI = 2048
DS = 16
FF = 3072
TTS = 512
NTILE = B_CORE * (T // TTS)
QMAX_A = 32767.0
QMAX_W = 7.0
WBLK = 32
MAGIC = 12582912.0

SC_C = (0.1 + 0.001) / 2.0
SC_R = (0.1 - 0.001) / 2.0
SC_IN = 0.5 / (SC_R + 1e-8)
SC_OUT = SC_R / 0.5
SCA_IN = 0.5 / (5.0 + 1e-8)
SCA_OUT = 5.0 / 0.5
SCB_IN = 0.5 / (8.0 + 1e-8)
SCB_OUT = 8.0 / 0.5


class Ctx:

    ENGS = ("sync", "tensor", "vector", "scalar", "gpsimd")

    def __init__(self, nc):
        self.nc = nc
        self.q = {e: [] for e in self.ENGS}
        self.sem_handles = {}
        self.cnt = {}
        self.last_inc = {}

    def add_sems(self, names, stack):
        for n in names:
            self.sem_handles[n] = stack.enter_context(self.nc.semaphore(n))
            self.cnt[n] = 0

    def op(self, eng, fn, inc=None, wait=None):
        waits = list(wait or [])
        tgt = None
        if inc is not None:
            self.cnt[inc[0]] += inc[1]
            tgt = self.cnt[inc[0]]
        handles = self.sem_handles

        def replay(e, waits=waits, fn=fn, inc=inc):
            for (ws, wv) in waits:
                e.wait_ge(handles[ws], wv)
            r = fn(e)
            if inc is not None:
                r.then_inc(handles[inc[0]], inc[1])
            return r

        self.q[eng].append(replay)
        self.last_inc[eng] = (inc[0], tgt) if inc is not None else None
        return tgt

    def wait(self, eng, sem, val=None):
        v = self.cnt[sem] if val is None else val
        self.q[eng].append(lambda e, s=sem, v=v: e.wait_ge(self.sem_handles[s], v))
        return v

    def drain(self, eng):
        if not self.q[eng]:
            return
        last = self.last_inc.get(eng)
        if last is None:
            sem = "dr_" + eng
            self.cnt[sem] = self.cnt.get(sem, 0) + 1
            v = self.cnt[sem]
            prev = self.q[eng].pop()
            handles = self.sem_handles

            def wrapped(e, prev=prev, sem=sem, handles=handles):
                r = prev(e)
                r.then_inc(handles[sem], 1)
                return r

            self.q[eng].append(wrapped)
            last = (sem, v)
            self.last_inc[eng] = last
        s, v = last
        self.q[eng].append(lambda e, s=s, v=v: e.wait_ge(self.sem_handles[s], v))

    def val(self, sem):
        return self.cnt[sem]

    def build(self, block):
        for eng in self.ENGS:
            ops = self.q[eng]
            if not ops:
                continue

            def body(e, ops=ops):
                for f in ops:
                    f(e)

            getattr(block, eng)(body)


def ap_(t, off, dims):
    return bass.AP(t.tensor if hasattr(t, "tensor") else t, off, dims)


def emit_ln(ctx, xt, sqt, psS, mrep, vrep, allones, xnt, j_chunks, tile_f,
            ld_sem, ld_val, epscol=None):
    ctx.op("scalar", lambda e: e.activation(sqt[:], xt[:], AF.Square),
           wait=[(ld_sem, ld_val), ("st_ve", ctx.val("st_ve"))], inc=("st_sc", 1))
    for j in range(j_chunks):
        ctx.op("tensor",
               lambda e, j=j: e.matmul(psS[:, 0:tile_f], allones[:],
                                       xt[:, j * tile_f:(j + 1) * tile_f],
                                       start=(j == 0), stop=(j == j_chunks - 1)),
               wait=([(ld_sem, ld_val), ("st_ve", ctx.val("st_ve"))] if j == 0 else None))
    for j in range(j_chunks):
        ctx.op("tensor",
               lambda e, j=j: e.matmul(psS[:, tile_f:2 * tile_f], allones[:],
                                       sqt[:, j * tile_f:(j + 1) * tile_f],
                                       start=(j == 0), stop=(j == j_chunks - 1)),
               wait=([("st_sc", ctx.val("st_sc"))] if j == 0 else None),
               inc=(("st_te", 1) if j == j_chunks - 1 else None))
    te_done = ctx.val("st_te")
    Df = float(j_chunks * 128)
    ctx.op("scalar", lambda e: e.activation(mrep[:], psS[:, 0:tile_f], AF.Copy,
                                            scale=1.0 / Df),
           wait=[("st_te", te_done)])
    ctx.op("scalar", lambda e: e.activation(vrep[:], psS[:, tile_f:2 * tile_f], AF.Copy,
                                            scale=1.0 / Df), inc=("st_sc", 1))
    ctx.op("vector", lambda e: e.tensor_tensor(psS[:, 0:tile_f], mrep[:], mrep[:], OP.mult),
           wait=[("st_sc", ctx.val("st_sc"))])
    ctx.drain("vector")
    ctx.op("vector", lambda e: e.tensor_tensor(vrep[:], vrep[:], psS[:, 0:tile_f],
                                               OP.subtract), inc=("st_ve", 1))
    ctx.op("scalar", lambda e: e.activation(vrep[:], vrep[:], AF.Sqrt,
                                            bias=epscol[:]),
           wait=[("st_ve", ctx.val("st_ve"))], inc=("st_sc", 1))
    ctx.op("vector", lambda e: e.reciprocal(vrep[:], vrep[:]),
           wait=[("st_sc", ctx.val("st_sc"))])
    ctx.drain("vector")
    n_p = 128
    mb = ap_(mrep, 0, [[tile_f, n_p], [0, j_chunks], [1, tile_f]])
    vb = ap_(vrep, 0, [[tile_f, n_p], [0, j_chunks], [1, tile_f]])
    xv = ap_(xt, 0, [[j_chunks * tile_f, n_p], [tile_f, j_chunks], [1, tile_f]])
    ov = ap_(xnt, 0, [[j_chunks * tile_f, n_p], [tile_f, j_chunks], [1, tile_f]])
    ctx.op("vector", lambda e: e.tensor_tensor(ov, xv, mb, OP.subtract))
    ctx.drain("vector")
    ctx.op("vector", lambda e: e.tensor_tensor(ov, ov, vb, OP.mult), inc=("st_ve", 1))
    ctx.drain("vector")
    return ctx.val("st_ve")


def emit_colmax_sb(ctx, pcol, psT, prow, idt):
    ctx.op("tensor", lambda e: e.transpose(psT[:], pcol[:], idt[:]),
           wait=[("st_ve", ctx.val("st_ve"))], inc=("st_te", 1))
    ctx.op("vector", lambda e: e.reduce_max(prow[:, 0:1], psT[:], AX.X),
           wait=[("st_te", ctx.val("st_te"))], inc=("st_ve", 1))
    return ctx.val("st_ve")


def emit_ccmax(ctx, src_sb, ccin, ccout, ve_val):
    lv = ctx.op("gpsimd", lambda e: e.dma_start(out=ccin[:, :], in_=src_sb),
                inc=("ldcc", 16), wait=[("st_ve", ve_val)])
    ctx.wait("gpsimd", "ldcc", lv)
    ctx.op("gpsimd", lambda e: e.collective_compute(
        "AllReduce", OP.max, replica_groups=[list(range(NCORES))],
        ins=[ccin.ap().opt()], outs=[ccout.ap().opt()]), inc=("cc", 1))
    return ctx.val("cc")


def emit_scale_col(ctx, allp_dram, nvals, stride, off, qmax, rowbuf, onesrow,
                   psCol, ccol, scol, extra_mult=1.0, ld_eng="sync",
                   ld_wait=None):
    lv = ctx.op(ld_eng, lambda e: e.dma_start(
        out=rowbuf[:, 0:nvals], in_=ap_(allp_dram, off, [[1, 1], [stride, nvals]]),
        allow_slow_non_contiguous=True), inc=("ldsc", 16), wait=ld_wait)
    ctx.op("vector", lambda e: e.reduce_max(rowbuf[:, nvals:nvals + 1],
                                            rowbuf[:, 0:nvals], AX.X),
           wait=[("ldsc", lv)])
    ctx.drain("vector")
    ctx.op("vector", lambda e: e.reciprocal(rowbuf[:, nvals + 1:nvals + 2],
                                            rowbuf[:, nvals:nvals + 1]))
    ctx.drain("vector")
    ctx.op("vector", lambda e: e.tensor_scalar(rowbuf[:, nvals + 1:nvals + 2],
                                               rowbuf[:, nvals + 1:nvals + 2],
                                               qmax, None, OP.mult))
    ctx.op("vector", lambda e: e.tensor_scalar(rowbuf[:, nvals:nvals + 1],
                                               rowbuf[:, nvals:nvals + 1],
                                               extra_mult / qmax, None, OP.mult),
           inc=("st_ve", 1))
    ctx.drain("vector")
    pitch = psCol.shape[-1]
    pc2 = ap_(psCol, 0, [[pitch, 128], [1, 2]])
    pc_s = ap_(psCol, 0, [[pitch, 128], [1, 1]])
    pc_c = ap_(psCol, 1, [[pitch, 128], [1, 1]])
    ctx.op("tensor", lambda e: e.matmul(pc2, onesrow[:],
                                        rowbuf[:, nvals:nvals + 2], start=True, stop=True),
           wait=[("st_ve", ctx.val("st_ve"))], inc=("st_te", 1))
    ctx.op("scalar", lambda e: e.activation(scol[:], pc_s, AF.Copy),
           wait=[("st_te", ctx.val("st_te"))])
    ctx.op("scalar", lambda e: e.activation(ccol[:], pc_c, AF.Copy),
           inc=("st_sc", 1))
    ctx.wait("vector", "st_sc", ctx.val("st_sc"))
    ctx.wait("sync", "st_sc", ctx.val("st_sc"))


def ph_P0(ctx, nc, st, x, ident, ccin, ccout):
    sb = lambda n, s, d=F32: st.enter_context(nc.sbuf_tensor("p0" + n, s, d))
    ps = lambda n, s: st.enter_context(nc.psum_tensor("p0" + n, s, F32))
    ctx.tile_mark = {}
    xt = [sb(f"xt{i}", [128, 8 * TTS]) for i in range(2)]
    sqt = sb("sqt", [128, 8 * TTS])
    xnt = sb("xnt", [128, 8 * TTS])
    mrep = sb("mrep", [128, TTS])
    vrep = sb("vrep", [128, TTS])
    allones = sb("allones", [128, 128])
    idt = sb("idt", [128, 128])
    pcol = sb("pcol", [128, 1])
    epscol = sb("epscol", [128, 1])
    prow = sb("prow", [1, 128])
    psS = ps("psS", [128, 2 * TTS])
    psT = ps("psT", [1, 128])

    ctx.op("vector", lambda e: e.memset(allones[:], 1.0))
    ctx.op("vector", lambda e: e.memset(epscol[:], 1e-5))
    ctx.op("vector", lambda e: e.memset(prow[:], 0.0))
    ctx.op("vector", lambda e: e.memset(pcol[:], 0.0), inc=("st_ve", 1))
    ctx.op("sync", lambda e: e.dma_start(out=idt[:], in_=ident), inc=("ldc", 16))

    tile_done = {}
    for it in range(NTILE):
        b, tt = divmod(it, T // TTS)
        buf = xt[it % 2]
        lsem = "ldx0" if it % 2 == 0 else "ldx1"
        src = ap_(x, b * D * T + tt * TTS, [[T, 128], [128 * T, 8], [1, TTS]])
        ldwait = [("st_ve", tile_done[it - 2])] if it >= 2 else None
        lv = ctx.op("sync", lambda e, s=src, bu=buf: e.dma_start(out=bu[:], in_=s),
                    inc=(lsem, 16), wait=ldwait)
        emit_ln(ctx, buf, sqt, psS, mrep, vrep, allones, xnt, 8, TTS, lsem, lv, epscol)
        ctx.op("vector", lambda e: e.reduce_max(sqt[:, 0:1], xnt[:], AX.X,
                                                apply_absolute_value=True))
        ctx.drain("vector")
        ctx.op("vector", lambda e: e.tensor_tensor(pcol[:], pcol[:], sqt[:, 0:1],
                                                   OP.max), inc=("st_ve", 1))
        ctx.drain("vector")
        tile_done[it] = ctx.val("st_ve")
    ctx.wait("tensor", "ldc", ctx.val("ldc"))
    ve = emit_colmax_sb(ctx, pcol, psT, prow, idt)
    return emit_ccmax(ctx, prow[:, 0:2], ccin, ccout, ve)


def emit_dequant(ctx, wsrc, n_ot, I, wt, bq, binv, i32w, i32b, zt, wqt, identb, psTr,
                 wqT_sb):
    nblk = I // WBLK
    nkc = I // 128
    pw = wt[0].shape[-1]
    pq = bq.shape[-1]
    pi = i32w.shape[-1]
    for ot in range(n_ot):
        buf = wt[ot % 2]
        wq_par = wqt[ot % 2]
        lsem = "ldw0" if ot % 2 == 0 else "ldw1"
        ldw = [("st_ve", ctx.tile_mark.get(("dqv_last", ot % 2), 0))]
        lv = ctx.op("sync", lambda e, o=ot, b=buf: e.dma_start(
            out=b[:, 0:I], in_=ap_(wsrc, o * 128 * I, [[I, 128], [1, I]])),
            inc=(lsem, 16), wait=ldw)
        ctx.op("vector", lambda e, b=buf: e.reduce_max(
            bq[:, 0:nblk], ap_(b, 0, [[pw, 128], [WBLK, nblk], [1, WBLK]]), AX.X,
            apply_absolute_value=True),
            wait=[(lsem, lv)])
        ctx.drain("vector")
        ctx.op("vector", lambda e: e.tensor_scalar(bq[:, 0:nblk], bq[:, 0:nblk],
                                                   1.0 / QMAX_W, 1e-12,
                                                   OP.mult, OP.max))
        ctx.drain("vector")
        ctx.op("vector", lambda e: e.reciprocal(binv[:, 0:nblk], bq[:, 0:nblk]))
        ctx.drain("vector")
        bqb = ap_(bq, 0, [[pq, 128], [1, nblk], [0, WBLK]])
        bib = ap_(binv, 0, [[pq, 128], [1, nblk], [0, WBLK]])
        wv = ap_(buf, 0, [[pw, 128], [WBLK, nblk], [1, WBLK]])
        pz = zt.shape[-1]
        zv = zt[:, 0:I]
        iv = i32w[:, 0:I]
        av = i32b[:, 0:I]
        qv = ap_(wq_par, 0, [[pi, 128], [WBLK, nblk], [1, WBLK]])
        zblk = ap_(zt, 0, [[pz, 128], [WBLK, nblk], [1, WBLK]])
        ctx.op("vector", lambda e, zblk=zblk, wv=wv, bib=bib: e.tensor_tensor(
            zblk, wv, bib, OP.mult), inc=("st_ve", 1))
        ctx.tile_mark[("dqv_last", ot % 2)] = ctx.val("st_ve")
        ctx.drain("vector")
        ctx.op("vector", lambda e, zv=zv: e.tensor_scalar(zv, zv, MAGIC, None, OP.add))
        ctx.drain("vector")
        ctx.op("vector", lambda e, zv=zv: e.tensor_scalar(zv, zv, MAGIC, None,
                                                          OP.subtract))
        ctx.drain("vector")
        ctx.op("vector", lambda e, iv=iv, zv=zv: e.tensor_scalar(iv, zv, 0.0, None,
                                                                 OP.add))
        ctx.drain("vector")
        ctx.op("vector", lambda e, zv=zv, iv=iv: e.tensor_tensor(zv, zv, iv,
                                                                 OP.subtract))
        ctx.drain("vector")
        ctx.op("vector", lambda e, av=av, zv=zv: e.tensor_scalar(av, zv, 2.0, None,
                                                                 OP.mult))
        ctx.drain("vector")
        ctx.op("vector", lambda e, zv=zv, iv=iv, av=av: e.tensor_tensor(zv, iv, av,
                                                                        OP.add))
        ctx.drain("vector")
        ctx.op("vector", lambda e, qv=qv, zblk=zblk, bqb=bqb: e.tensor_tensor(
            qv, zblk, bqb, OP.mult), inc=("st_ve", 1),
            wait=[("st_te", ctx.tile_mark.get(("dq_te", ot % 2), 0))])
        wq_single = wqt[0] is wqt[1]
        ctx.drain("vector")
        vv = ctx.val("st_ve")
        gbase = ctx.tile_mark.get("dq_g", 0)
        for kc in range(nkc):
            g = gbase + kc
            tw = [("st_ve", vv)] if kc == 0 else []
            em = ctx.tile_mark.get(("dqe", g - 2))
            if em:
                tw.append(("st_sc", em))
            ctx.op("tensor", lambda e, kc=kc, g=g, wq_par=wq_par: e.transpose(
                psTr[g % 2][:], wq_par[:, kc * 128:(kc + 1) * 128], identb[:]),
                wait=(tw or None), inc=("st_te", 1))
            tv = ctx.val("st_te")
            ctx.op("scalar", lambda e, kc=kc, o=ot, g=g: e.activation(
                wqT_sb[:, (kc * n_ot + o) * 128:(kc * n_ot + o) * 128 + 128],
                psTr[g % 2][:], AF.Copy), wait=[("st_te", tv)], inc=("st_sc", 1))
            ctx.tile_mark[("dqe", g)] = ctx.val("st_sc")
        ctx.tile_mark["dq_g"] = gbase + nkc
        ctx.tile_mark[("dq_te", ot % 2)] = ctx.val("st_te")
        if wq_single:
            ctx.tile_mark[("dq_te", (ot + 1) % 2)] = ctx.val("st_te")


def ph_upgate(ctx, nc, st, tag, x, ident, identb, w_up, w_gate, up_o, gate_o,
              cc_ld, ccin, ccout, wq_cache=None):
    ctx.tile_mark = {}
    sb = lambda n, s, d=F32: st.enter_context(nc.sbuf_tensor(tag + n, s, d))
    ps = lambda n, s, d=F32: st.enter_context(nc.psum_tensor(tag + n, s, d))
    wqtu_sb = sb("wqtu_sb", [128, 8 * 24 * 128], BF16)
    wqtg_sb = sb("wqtg_sb", [128, 8 * 24 * 128], BF16)
    xt = [sb(f"xt{i}", [128, 8 * TTS]) for i in range(2)]
    xnt = sb("xnt", [128, 8 * TTS])
    sqt = xnt
    xqt = [sb(f"xqt{i}", [128, 8 * TTS], BF16) for i in range(2)]
    mrep = sb("mrep", [128, TTS])
    vrep = sb("vrep", [128, TTS])
    allones = sb("allones", [128, 128])
    onesrow = sb("onesrow", [1, 128])
    idt = sb("idt", [128, 128])
    idtb = sb("idtb", [128, 128], BF16)
    epscol = sb("epscol", [128, 1])
    c0col = sb("c0col", [128, 1])
    s0col = sb("s0col", [128, 1])
    rowbuf = sb("rowbuf", [1, 16])
    stgU = [sb(f"stgU{i}", [128, TTS]) for i in range(2)]
    stgG = [sb(f"stgG{i}", [128, TTS]) for i in range(2)]
    pupc = sb("pupc", [128, 1])
    pgc = sb("pgc", [128, 1])
    tmpc = sb("tmpc", [128, 2])
    prow2 = sb("prow2", [1, 2])
    psMisc = ps("psMisc", [128, 130])
    psTrSt = contextlib.ExitStack()
    psTr = [psTrSt.enter_context(nc.psum_tensor(tag + f"psTr{i}", [128, 128],
                                                BF16)) for i in range(2)]

    if True:
        ctx.op("vector", lambda e: e.memset(allones[:], 1.0))
        ctx.op("vector", lambda e: e.memset(onesrow[:], 1.0))
        ctx.op("vector", lambda e: e.memset(epscol[:], 1e-5))
        ctx.op("vector", lambda e: e.memset(pupc[:], 0.0))
        ctx.op("vector", lambda e: e.memset(pgc[:], 0.0), inc=("st_ve", 1))
        ctx.op("sync", lambda e: e.dma_start(out=idt[:], in_=ident), inc=("ldc", 16))
        ctx.op("sync", lambda e: e.dma_start(out=idtb[:], in_=identb), inc=("ldc", 16))

        ctx.wait("tensor", "ldc", ctx.val("ldc"))

        if wq_cache is None or wq_cache[2] == "store":
            wtb = [sb(f"wtb{i}", [128, D]) for i in range(2)]
            bq = sb("bq", [128, D // WBLK])
            binv = sb("binv", [128, D // WBLK])
            i32w = st.enter_context(nc.sbuf_tensor(tag + "i32w", [128, D], I32))
            i32b = st.enter_context(nc.sbuf_tensor(tag + "i32b", [128, D], I32))
            ztw = sb("ztw", [128, D])
            wqt_s = [sb(f"wqt_s{i}", [128, D], BF16) for i in range(2)]
            emit_dequant(ctx, w_up, 24, D, wtb, bq, binv, i32w, i32b, ztw,
                         wqt_s, idtb, psTr, wqtu_sb)
            emit_dequant(ctx, w_gate, 24, D, wtb, bq, binv, i32w, i32b, ztw,
                         wqt_s, idtb, psTr, wqtg_sb)
            wq_ready = ctx.val("st_sc")
            wq_wait = [("st_sc", wq_ready)]
            if wq_cache is not None:
                ctx.op("sync", lambda e: e.dma_start(
                    out=wq_cache[0], in_=wqtu_sb[:]),
                    wait=[("st_sc", wq_ready)], inc=("stoW", 16))
                ctx.op("sync", lambda e: e.dma_start(
                    out=wq_cache[1], in_=wqtg_sb[:]), inc=("stoW", 16))
        else:
            lvu = ctx.op("sync", lambda e: e.dma_start(
                out=wqtu_sb[:], in_=wq_cache[0]), inc=("ldw0", 16))
            lvg = ctx.op("gpsimd", lambda e: e.dma_start(
                out=wqtg_sb[:], in_=wq_cache[1]), inc=("ldgw", 16))
            wq_ready = 0
            wq_wait = [("ldw0", lvu), ("ldgw", lvg)]
        psTrSt.close()
        psS = ps("psS", [128, 2 * TTS])
        psU = [ps(f"psU{i}", [128, TTS]) for i in range(2)]
        psG = [ps(f"psG{i}", [128, TTS]) for i in range(2)]
        if wq_ready:
            ctx.wait("tensor", "st_sc", wq_ready)
            ctx.wait("vector", "st_sc", wq_ready)
        emit_scale_col(ctx, cc_ld[0], 1, 1, cc_ld[2], QMAX_A, rowbuf, onesrow,
                       psMisc, c0col, s0col, ld_eng="gpsimd",
                       ld_wait=[("cc", cc_ld[1])])

        evac_mark = {}
        dma_mark = {}
        for it in range(NTILE):
            b, tt = divmod(it, T // TTS)
            buf = xt[it % 2]
            xq = xqt[it % 2]
            lsem = "ldx0" if it % 2 == 0 else "ldx1"
            src = ap_(x, b * D * T + tt * TTS, [[T, 128], [128 * T, 8], [1, TTS]])
            ldwait = [("st_ve", ctx.tile_mark.get(("xq", it - 2), 0))]
            lv = ctx.op("sync", lambda e, s=src, bu=buf: e.dma_start(out=bu[:], in_=s),
                        inc=(lsem, 16), wait=ldwait)
            emit_ln(ctx, buf, sqt, psS, mrep, vrep, allones, xnt, 8, TTS, lsem, lv, epscol)
            ctx.op("vector", lambda e: e.tensor_scalar(xnt[:], xnt[:], c0col[:, 0:1],
                                                       MAGIC, OP.mult, OP.add))
            ctx.drain("vector")
            ctx.op("vector", lambda e, xq=xq: e.tensor_scalar(xq[:], xnt[:], MAGIC,
                                                              None, OP.subtract),
                   inc=("st_ve", 1))
            ctx.drain("vector")
            ctx.tile_mark[("xq", it)] = ctx.val("st_ve")
            xq_v = ctx.val("st_ve")

            for oc in range(24):
                g = it * 24 + oc
                for which, psP, stg, out_d, pc in (
                        (0, psU[oc % 2], stgU[oc % 2], up_o, pupc),
                        (1, psG[oc % 2], stgG[oc % 2], gate_o, pgc)):
                    w_sb = wqtg_sb if which else wqtu_sb
                    mmw = [("st_ve", xq_v)] + wq_wait
                    em = evac_mark.get((which, g - 2))
                    if em:
                        mmw.append(("st_sc", em))
                    for kc in range(8):
                        ctx.op("tensor", lambda e, kc=kc, oc=oc, psP=psP, xq=xq, w_sb=w_sb:
                               e.matmul(psP[:], w_sb[:, (kc * 24 + oc) * 128:
                                                     (kc * 24 + oc) * 128 + 128],
                                        xq[:, kc * TTS:(kc + 1) * TTS],
                                        start=(kc == 0), stop=(kc == 7)),
                               wait=(mmw if kc == 0 else None),
                               inc=(("st_te", 1) if kc == 7 else None))
                    tev = ctx.val("st_te")
                    ssem = ("stoG" if which else "stoU") + str(oc % 2)
                    scw = [("st_te", tev)]
                    dm = dma_mark.get((which, g - 2))
                    if dm:
                        scw.append((ssem, dm))
                    vm = ctx.tile_mark.get(("vem", which, g - 2))
                    if vm:
                        scw.append(("st_ve", vm))
                    ctx.op("scalar", lambda e, psP=psP, stg=stg: e.activation(
                        stg[:], psP[:], AF.Copy, scale=s0col[:, 0:1]),
                        wait=scw, inc=("st_sc", 1))
                    evac_mark[(which, g)] = ctx.val("st_sc")
                    scv = ctx.val("st_sc")
                    ctx.op("vector", lambda e, stg=stg, which=which: e.reduce_max(
                        tmpc[:, which:which + 1], stg[:], AX.X,
                        apply_absolute_value=True), wait=[("st_sc", scv)])
                    ctx.drain("vector")
                    ctx.op("vector", lambda e, pc=pc, which=which: e.tensor_tensor(
                        pc[:], pc[:], tmpc[:, which:which + 1], OP.max),
                        inc=("st_ve", 1))
                    ctx.drain("vector")
                    ctx.tile_mark[("vem", which, g)] = ctx.val("st_ve")
                    dst = ap_(out_d, b * FF * T + oc * 128 * T + tt * TTS,
                              [[T, 128], [1, TTS]])
                    ctx.op("sync", lambda e, stg=stg, dst=dst: e.dma_start(
                        out=dst, in_=stg[:]), wait=[("st_sc", scv)], inc=(ssem, 16))
                    dma_mark[(which, g)] = ctx.val(ssem)

        ctx.op("tensor", lambda e: e.transpose(psMisc[0:1, 0:128], pupc[:], idt[:]),
               wait=[("st_ve", ctx.val("st_ve"))], inc=("st_te", 1))
        ctx.op("vector", lambda e: e.reduce_max(prow2[:, 0:1], psMisc[0:1, 0:128], AX.X),
               wait=[("st_te", ctx.val("st_te"))], inc=("st_ve", 1))
        ctx.op("tensor", lambda e: e.transpose(psMisc[0:1, 0:128], pgc[:], idt[:]),
               wait=[("st_ve", ctx.val("st_ve"))], inc=("st_te", 1))
        ctx.op("vector", lambda e: e.reduce_max(prow2[:, 1:2], psMisc[0:1, 0:128], AX.X),
               wait=[("st_te", ctx.val("st_te"))], inc=("st_ve", 1))
        for s in ("stoU0", "stoU1", "stoG0", "stoG1"):
            ctx.wait("sync", s, ctx.val(s))
        return emit_ccmax(ctx, prow2[:, 0:2], ccin, ccout, ctx.val("st_ve"))


def ph_gate(ctx, nc, st, tag, gate, ident, pre3_o, cc_ld, ccin, ccout):
    ctx.tile_mark = {}
    sb = lambda n, s, d=F32: st.enter_context(nc.sbuf_tensor(tag + n, s, d))
    ps = lambda n, s, d=F32: st.enter_context(nc.psum_tensor(tag + n, s, d))
    if True:
        gt = [sb(f"gt{i}", [128, T]) for i in range(2)]
        sg = [sb(f"sg{i}", [128, T]) for i in range(2)]
        onesrow = sb("onesrow", [1, 128])
        idt = sb("idt", [128, 128])
        c2col = sb("c2col", [128, 1])
        s2col = sb("s2col", [128, 1])
        s2bcol = sb("s2bcol", [128, 1])
        rowbuf = sb("rowbuf", [1, 32])
        pcol = sb("pcol", [128, 1])
        tmpc = sb("tmpc", [128, 1])
        prow = sb("prow", [1, 128])
        psMisc = ps("psMisc", [128, 130])

        ctx.op("vector", lambda e: e.memset(onesrow[:], 1.0))
        ctx.op("vector", lambda e: e.memset(prow[:], 0.0))
        ctx.op("vector", lambda e: e.memset(pcol[:], 0.0), inc=("st_ve", 1))
        ctx.op("sync", lambda e: e.dma_start(out=idt[:], in_=ident), inc=("ldc", 16))
        emit_scale_col(ctx, cc_ld[0], 1, 1, cc_ld[2], QMAX_A, rowbuf, onesrow,
                       psMisc, c2col, s2col, ld_eng="gpsimd",
                       ld_wait=[("cc", cc_ld[1])])
        ctx.drain("scalar")
        ctx.op("scalar", lambda e: e.activation(s2bcol[:], s2col[:], AF.Copy,
                                                scale=1.0 / 65536.0), inc=("st_sc", 1))
        ctx.wait("vector", "st_sc", ctx.val("st_sc"))
        ctx.wait("tensor", "ldc", ctx.val("ldc"))

        ntile = B_CORE * (FF // 128)
        for it in range(ntile):
            b, oc = divmod(it, FF // 128)
            buf = gt[it % 2]
            sgb = sg[it % 2]
            lsem = "ldx0" if it % 2 == 0 else "ldx1"
            ssem = "sto0" if it % 2 == 0 else "sto1"
            src = ap_(gate, b * FF * T + oc * 128 * T, [[T, 128], [1, T]])
            lw = [("st_ve", ctx.tile_mark.get(("done", it - 2), 0))]
            lv = ctx.op("sync", lambda e, s=src, bu=buf: e.dma_start(out=bu[:], in_=s),
                        inc=(lsem, 16), wait=lw)
            ctx.op("vector", lambda e, bu=buf: e.tensor_scalar(
                bu[:], bu[:], c2col[:, 0:1], MAGIC, OP.mult, OP.add),
                wait=[(lsem, lv)])
            ctx.drain("vector")
            ctx.op("vector", lambda e, bu=buf: e.tensor_scalar(
                bu[:], bu[:], MAGIC, None, OP.subtract), inc=("st_ve", 1))
            ctx.drain("vector")
            gqv = ctx.val("st_ve")
            smw = [("st_ve", gqv)]
            dm = ctx.tile_mark.get(("dma", it - 2))
            if dm:
                smw.append((ssem, dm))
            vm = ctx.tile_mark.get(("vem", it - 2))
            if vm:
                smw.append(("st_ve", vm))
            ctx.op("scalar", lambda e, bu=buf, sgb=sgb: e.activation(
                sgb[:], bu[:], AF.Sigmoid, scale=s2col[:, 0:1]),
                wait=smw, inc=("st_sc", 1))
            ctx.drain("scalar")
            ctx.op("scalar", lambda e, sgb=sgb: e.activation(
                sgb[:], sgb[:], AF.Copy, scale=65536.0, bias=MAGIC),
                inc=("st_sc", 1))
            ctx.op("vector", lambda e, sgb=sgb: e.tensor_scalar(
                sgb[:], sgb[:], MAGIC, 65535.0, OP.subtract, OP.min),
                wait=[("st_sc", ctx.val("st_sc"))])
            ctx.drain("vector")
            ctx.op("vector", lambda e, sgb=sgb, bu=buf: e.scalar_tensor_tensor(
                sgb[:], sgb[:], s2bcol[:, 0:1], bu[:], OP.mult, OP.mult),
                inc=("st_ve", 1))
            ctx.drain("vector")
            scv = ctx.val("st_ve")
            ctx.tile_mark[("done", it)] = scv
            ctx.op("vector", lambda e, sgb=sgb: e.reduce_max(
                tmpc[:], sgb[:], AX.X, apply_absolute_value=True))
            ctx.drain("vector")
            ctx.op("vector", lambda e: e.tensor_tensor(pcol[:], pcol[:], tmpc[:],
                                                       OP.max), inc=("st_ve", 1))
            ctx.drain("vector")
            ctx.tile_mark[("vem", it)] = ctx.val("st_ve")
            dst = ap_(pre3_o, b * FF * T + oc * 128 * T, [[T, 128], [1, T]])
            ctx.op("sync", lambda e, sgb=sgb, dst=dst: e.dma_start(
                out=dst, in_=sgb[:]), wait=[("st_ve", scv)], inc=(ssem, 16))
            ctx.tile_mark[("dma", it)] = ctx.val(ssem)

        ve = emit_colmax_sb(ctx, pcol, ap_(psMisc, 0, [[130, 1], [1, 128]]), prow,
                            idt)
        for s in ("sto0", "sto1"):
            ctx.wait("sync", s, ctx.val(s))
        return emit_ccmax(ctx, prow[:, 0:2], ccin, ccout, ve)


def ph_prod(ctx, nc, st, tag, pre3, up, ident, pre4_o, cc3, cc1, ccin, ccout):
    ctx.tile_mark = {}
    sb = lambda n, s, d=F32: st.enter_context(nc.sbuf_tensor(tag + n, s, d))
    ps = lambda n, s, d=F32: st.enter_context(nc.psum_tensor(tag + n, s, d))
    if True:
        g2t = [sb(f"g2t{i}", [128, T]) for i in range(2)]
        upt = [sb(f"upt{i}", [128, T]) for i in range(2)]
        onesrow = sb("onesrow", [1, 128])
        idt = sb("idt", [128, 128])
        c3col = sb("c3col", [128, 1])
        s3col = sb("s3col", [128, 1])
        c1col = sb("c1col", [128, 1])
        s1col = sb("s1col", [128, 1])
        rowbuf = sb("rowbuf", [1, 32])
        pcol = sb("pcol", [128, 1])
        tmpc = sb("tmpc", [128, 1])
        prow = sb("prow", [1, 128])
        psMisc = ps("psMisc", [128, 130])

        ctx.op("vector", lambda e: e.memset(onesrow[:], 1.0))
        ctx.op("vector", lambda e: e.memset(prow[:], 0.0))
        ctx.op("vector", lambda e: e.memset(pcol[:], 0.0), inc=("st_ve", 1))
        ctx.op("sync", lambda e: e.dma_start(out=idt[:], in_=ident), inc=("ldc", 16))
        emit_scale_col(ctx, cc3[0], 1, 1, cc3[2], QMAX_A, rowbuf, onesrow,
                       psMisc, c3col, s3col, ld_eng="gpsimd",
                       ld_wait=[("cc", cc3[1])])
        emit_scale_col(ctx, cc1[0], 1, 1, cc1[2], QMAX_A, rowbuf, onesrow,
                       psMisc, c1col, s1col, ld_eng="gpsimd",
                       ld_wait=[("cc", cc1[1])])
        s31col = sb("s31col", [128, 1])
        ctx.op("vector", lambda e: e.tensor_tensor(
            s31col[:], s3col[:], s1col[:], OP.mult), inc=("st_ve", 1))
        ctx.drain("vector")
        ctx.wait("vector", "st_sc", ctx.val("st_sc"))
        ctx.wait("scalar", "st_ve", ctx.val("st_ve"))
        ctx.wait("tensor", "ldc", ctx.val("ldc"))

        ntile = B_CORE * (FF // 128)
        for it in range(ntile):
            b, oc = divmod(it, FF // 128)
            bg = g2t[it % 2]
            bu = upt[it % 2]
            l1 = "ldx0" if it % 2 == 0 else "ldx1"
            l2 = "ldu0" if it % 2 == 0 else "ldu1"
            ssem = "sto0" if it % 2 == 0 else "sto1"
            off = b * FF * T + oc * 128 * T
            lw = [("st_ve", ctx.tile_mark.get(("done", it - 2), 0))]
            dmp = ctx.tile_mark.get(("dma", it - 2))
            if dmp:
                lw.append((ssem, dmp))
            lv1 = ctx.op("sync", lambda e, bg=bg, off=off: e.dma_start(
                out=bg[:], in_=ap_(pre3, off, [[T, 128], [1, T]])),
                inc=(l1, 16), wait=lw)
            lv2 = ctx.op("sync", lambda e, bu=bu, off=off: e.dma_start(
                out=bu[:], in_=ap_(up, off, [[T, 128], [1, T]])),
                inc=(l2, 16))
            ctx.op("vector", lambda e, bg=bg: e.tensor_scalar(
                bg[:], bg[:], c3col[:, 0:1], MAGIC, OP.mult, OP.add),
                wait=[(l1, lv1)])
            ctx.drain("vector")
            ctx.op("vector", lambda e, bg=bg: e.tensor_scalar(
                bg[:], bg[:], MAGIC, None, OP.subtract))
            ctx.drain("vector")
            scw = [(l2, lv2), ("st_sc", ctx.tile_mark.get(("sdone", it - 2), 0))]
            ctx.op("scalar", lambda e, bu=bu: e.activation(
                bu[:], bu[:], AF.Copy, scale=c1col[:, 0:1], bias=MAGIC),
                wait=scw)
            ctx.drain("scalar")
            ctx.op("scalar", lambda e, bu=bu: e.activation(
                bu[:], bu[:], AF.Copy, bias=-MAGIC), inc=("st_sc", 1))
            ctx.drain("scalar")
            ctx.tile_mark[("sdone", it)] = ctx.val("st_sc")
            ctx.op("vector", lambda e, bg=bg, bu=bu: e.scalar_tensor_tensor(
                bg[:], bg[:], s31col[:, 0:1], bu[:], OP.mult, OP.mult),
                wait=[("st_sc", ctx.val("st_sc"))], inc=("st_ve", 1))
            ctx.drain("vector")
            scv = ctx.val("st_ve")
            ctx.tile_mark[("done", it)] = scv
            ctx.op("vector", lambda e, bg=bg: e.reduce_max(
                tmpc[:], bg[:], AX.X, apply_absolute_value=True))
            ctx.drain("vector")
            ctx.op("vector", lambda e: e.tensor_tensor(pcol[:], pcol[:], tmpc[:],
                                                       OP.max), inc=("st_ve", 1))
            ctx.drain("vector")
            dst = ap_(pre4_o, off, [[T, 128], [1, T]])
            dmw = [("st_ve", scv)]
            dm = ctx.tile_mark.get(("dma", it - 2))
            if dm:
                dmw.append((ssem, dm))
            ctx.op("sync", lambda e, bg=bg, dst=dst: e.dma_start(
                out=dst, in_=bg[:]), wait=dmw, inc=(ssem, 16))
            ctx.tile_mark[("dma", it)] = ctx.val(ssem)

        ve = emit_colmax_sb(ctx, pcol, ap_(psMisc, 0, [[130, 1], [1, 128]]), prow,
                            idt)
        for s in ("sto0", "sto1"):
            ctx.wait("sync", s, ctx.val(s))
        return emit_ccmax(ctx, prow[:, 0:2], ccin, ccout, ve)


def emit_ffn_tail(ctx, nc, st, tag, pre4, xres, cc4, w_down, identb_t,
                  out_writer, wd_cache=None):
    sb = lambda n, s, d=F32: st.enter_context(nc.sbuf_tensor(tag + n, s, d))
    wdt_sb = sb("wdt_sb", [128, 24 * 8 * 128], BF16)
    wdb = [sb(f"wdb{i}", [128, FF]) for i in range(2)]
    bqd = sb("bqd", [128, FF // WBLK])
    binvd = sb("binvd", [128, FF // WBLK])
    i32d = st.enter_context(nc.sbuf_tensor(tag + "i32d", [128, FF], I32))
    i32d2 = st.enter_context(nc.sbuf_tensor(tag + "i32d2", [128, FF], I32))
    ztd = sb("ztd", [128, FF])
    wqt_d1 = sb("wqt_d", [128, FF], BF16)
    wqt_d = [wqt_d1, wqt_d1]
    c4col = sb("c4col", [128, 1])
    s4col = sb("s4col", [128, 1])
    rowbuf4 = sb("rowbuf4", [1, 32])
    obuf = [st.enter_context(nc.sbuf_tensor(tag + f"obuf{i}", [128, 24 * TTS],
                                            BF16))
            for i in range(2)]
    p4t = [sb(f"p4t{i}", [128, TTS]) for i in range(2)]
    xrt = [sb(f"xrt{i}", [128, 8 * TTS]) for i in range(2)]
    stgD = [sb(f"stgD{i}", [128, TTS]) for i in range(2)]
    onesrow4 = sb("onesrow4", [1, 128])
    psMisc4 = st.enter_context(nc.psum_tensor(tag + "psMisc4", [128, 2], F32))
    psTr2St = contextlib.ExitStack()
    psTr2 = [psTr2St.enter_context(nc.psum_tensor(tag + f"psTr2{i}", [128, 128],
                                                  BF16)) for i in range(2)]
    ctx.op("vector", lambda e: e.memset(onesrow4[:], 1.0), inc=("st_ve", 1))
    ctx.wait("tensor", "ldc", ctx.val("ldc"))
    if wd_cache is None or wd_cache[1] == "store":
        emit_dequant(ctx, w_down, 8, FF, wdb, bqd, binvd, i32d, i32d2, ztd,
                     wqt_d, identb_t, psTr2, wdt_sb)
        wd_ready = ctx.val("st_sc")
        wd_wait = [("st_sc", wd_ready)]
        if wd_cache is not None:
            ctx.op("sync", lambda e: e.dma_start(
                out=wd_cache[0], in_=wdt_sb[:]),
                wait=[("st_sc", wd_ready)], inc=("stoW", 16))
    else:
        lvd = ctx.op("sync", lambda e: e.dma_start(
            out=wdt_sb[:], in_=wd_cache[0]), inc=("ldw0", 16))
        wd_wait = [("ldw0", lvd)]
    psTr2St.close()
    emit_scale_col(ctx, cc4[0], 1, 1, cc4[2], QMAX_A, rowbuf4, onesrow4,
                   psMisc4, c4col, s4col, extra_mult=0.5, ld_eng="gpsimd",
                   ld_wait=[("cc", cc4[1])])
    psD = [st.enter_context(nc.psum_tensor(tag + f"psD{i}", [128, TTS], F32))
           for i in range(2)]

    for it in range(NTILE):
        b, tt = divmod(it, T // TTS)
        ob = obuf[it % 2]
        xb = xrt[it % 2]
        rsem = "ldr0" if it % 2 == 0 else "ldr1"
        rw = [("st_sc", ctx.tile_mark.get(("ft_xv", it - 2), 0))]
        rv = ctx.op("sync", lambda e, b=b, tt=tt, xb=xb: e.dma_start(
            out=xb[:], in_=ap_(xres, b * D * T + tt * TTS,
                               [[T, 128], [128 * T, 8], [1, TTS]])),
            inc=(rsem, 16), wait=rw)
        for kc in range(24):
            pb = p4t[kc % 2]
            psem = "ldf0" if kc % 2 == 0 else "ldf1"
            lw = [("st_ve", ctx.tile_mark.get(("ft_rnd", it, kc - 2),
                   ctx.tile_mark.get(("ft_rnd", it - 1, 22 + kc), 0)))]
            lv = ctx.op("sync", lambda e, b=b, tt=tt, kc=kc, pb=pb: e.dma_start(
                out=pb[:], in_=ap_(pre4, b * FF * T + kc * 128 * T + tt * TTS,
                                    [[T, 128], [1, TTS]])),
                inc=(psem, 16), wait=lw)
            vw = [(psem, lv)]
            if kc == 0:
                vw.append(("st_te", ctx.tile_mark.get(("ft_te", it - 2), 0)))
            ctx.op("vector", lambda e, pb=pb: e.tensor_scalar(
                pb[:], pb[:], c4col[:, 0:1], MAGIC, OP.mult, OP.add),
                wait=vw)
            ctx.drain("vector")
            ctx.op("vector", lambda e, ob=ob, kc=kc, pb=pb: e.tensor_scalar(
                ob[:, kc * TTS:(kc + 1) * TTS], pb[:], MAGIC, None, OP.subtract),
                inc=("st_ve", 1))
            ctx.drain("vector")
            ctx.tile_mark[("ft_rnd", it, kc)] = ctx.val("st_ve")
        ov = ctx.val("st_ve")
        for mc in range(8):
            psP = psD[mc % 2]
            mmw = [("st_ve", ov)] + wd_wait
            em = ctx.tile_mark.get(("ft_em", it, mc - 2))
            if em is None and it > 0:
                em = ctx.tile_mark.get(("ft_em", it - 1, 6 + mc))
            if em:
                mmw.append(("st_sc", em))
            for kc in range(24):
                ctx.op("tensor", lambda e, kc=kc, mc=mc, psP=psP, ob=ob: e.matmul(
                    psP[:], wdt_sb[:, (kc * 8 + mc) * 128:(kc * 8 + mc) * 128 + 128],
                    ob[:, kc * TTS:(kc + 1) * TTS],
                    start=(kc == 0), stop=(kc == 23)),
                    wait=(mmw if kc == 0 else None),
                    inc=(("st_te", 1) if kc == 23 else None))
            tev = ctx.val("st_te")
            stg = stgD[mc % 2]
            scw = [("st_te", tev), (rsem, rv)]
            dm = ctx.tile_mark.get(("ft_dm", it, mc - 2))
            if dm is None and it > 0:
                dm = ctx.tile_mark.get(("ft_dm", it - 1, 6 + mc))
            if dm:
                scw.append(dm)
            ctx.op("vector", lambda e, psP=psP, stg=stg, xb=xb, mc=mc:
                   e.scalar_tensor_tensor(stg[:], psP[:], s4col[:, 0:1],
                                          xb[:, mc * TTS:(mc + 1) * TTS],
                                          OP.mult, OP.add),
                   wait=scw, inc=("st_sc", 1))
            ctx.drain("vector")
            ctx.tile_mark[("ft_em", it, mc)] = ctx.val("st_sc")
            scv = ctx.val("st_sc")
            out_writer(ctx, it, b, tt, mc, stg, scv)
        ctx.tile_mark[("ft_xv", it)] = ctx.val("st_sc")
        ctx.tile_mark[("ft_te", it)] = ctx.val("st_te")


def ph_tail(ctx, nc, st, pre4, xres, cc4, w_down, identb, out, wd_cache):
    ctx.tile_mark = {}
    idtb = st.enter_context(nc.sbuf_tensor("t2idtb", [128, 128], BF16))
    ctx.op("sync", lambda e: e.dma_start(out=idtb[:], in_=identb),
           inc=("ldc", 16))

    def writer(ctx, it, b, tt, mc, stg, scv):
        ssem = "sto0" if (it * 8 + mc) % 2 == 0 else "sto1"
        dst = ap_(out, b * D * T + mc * 128 * T + tt * TTS, [[T, 128], [1, TTS]])
        ctx.op("sync", lambda e, stg=stg, dst=dst: e.dma_start(
            out=dst, in_=stg[:]), wait=[("st_sc", scv)], inc=(ssem, 16))
        ctx.tile_mark[("ft_dm", it, mc)] = (ssem, ctx.val(ssem))

    emit_ffn_tail(ctx, nc, st, "t2", pre4, xres, cc4, w_down, idtb, writer,
                  wd_cache=wd_cache)
    for s in ("sto0", "sto1"):
        ctx.wait("sync", s, ctx.val(s))


def ph_P4(ctx, nc, st, x0, pre4, cc4, w_down, ident, identb, w_inT, w_dtT,
          w_bcT, w_outT, cdiag, a_jln, dcol, cbcol, bdtcol, nwcol, rscal,
          x2_o, x1_s, h2_s, mg_s, xc_s, dt_s, bcps_d, ccin, ccout, wd_cache):
    ctx.tile_mark = {}
    sb = lambda n, s, d=F32: st.enter_context(nc.sbuf_tensor(n, s, d))
    ps = lambda n, s, d=F32: st.enter_context(nc.psum_tensor(n, s, d))
    if True:
        idt = sb("idt", [128, 128])
        idtb = sb("idtb", [128, 128], BF16)
        allones = sb("allones", [128, 128])
        onesrow = sb("onesrow", [1, 128])
        epscol = sb("epscol", [128, 1])
        eps6col = sb("eps6col", [128, 1])
        dcol_sb = sb("dcol_sb", [128, 16])
        cbcol_sb = sb("cbcol_sb", [128, 16])
        bdtcol_sb = sb("bdtcol_sb", [128, 16])
        nwcol_sb = sb("nwcol_sb", [128, 8])
        atan_sb = sb("atan_sb", [128, 16 * DS])
        sctanb = sb("sctanb", [128, 1])
        rscol = sb("rscol", [128, 1])
        rowmisc = sb("rowmisc", [1, 8])
        pcol = sb("pcol", [128, 1])
        prow = sb("prow", [1, 128])
        psMisc = ps("psMisc", [128, 130])

        ctx.op("vector", lambda e: e.memset(allones[:], 1.0))
        ctx.op("vector", lambda e: e.memset(onesrow[:], 1.0))
        ctx.op("vector", lambda e: e.memset(epscol[:], 1e-5))
        ctx.op("vector", lambda e: e.memset(eps6col[:], 1e-6))
        ctx.op("vector", lambda e: e.memset(sctanb[:], -SC_C * SC_IN))
        ctx.op("vector", lambda e: e.memset(prow[:], 0.0))
        ctx.op("vector", lambda e: e.memset(pcol[:], 0.0), inc=("st_ve", 1))
        for (dst, srcd) in ((idt, ident), (idtb, identb), (dcol_sb, dcol),
                            (cbcol_sb, cbcol), (bdtcol_sb, bdtcol),
                            (nwcol_sb, nwcol)):
            ctx.op("sync", lambda e, dst=dst, srcd=srcd: e.dma_start(
                out=dst[:], in_=srcd), inc=("ldc", 16))
        ctx.op("sync", lambda e: e.dma_start(out=atan_sb[:], in_=a_jln),
               inc=("ldc", 16))
        ctx.op("sync", lambda e: e.dma_start(out=rowmisc[:, 0:1], in_=rscal),
               inc=("ldc", 16))
        lcv = ctx.val("ldc")
        ctx.op("scalar", lambda e: e.activation(atan_sb[:], atan_sb[:], AF.Tanh,
                                                scale=SCA_IN), wait=[("ldc", lcv)])
        ctx.drain("scalar")
        ctx.op("scalar", lambda e: e.activation(atan_sb[:], atan_sb[:], AF.Exp,
                                                scale=SCA_OUT), inc=("st_sc", 1))
        ctx.drain("scalar")
        ctx.op("vector", lambda e: e.tensor_scalar(atan_sb[:], atan_sb[:],
                                                   -1.0 / 16.0, None, OP.mult),
               wait=[("st_sc", ctx.val("st_sc"))], inc=("st_ve", 1))
        ctx.drain("vector")
        a_ready = ctx.val("st_ve")
        ctx.op("scalar", lambda e: e.activation(rowmisc[:, 1:2], rowmisc[:, 0:1],
                                                AF.Sigmoid),
               wait=[("ldc", ctx.val("ldc"))])
        ctx.drain("scalar")
        ctx.op("scalar", lambda e: e.activation(rowmisc[:, 1:2], rowmisc[:, 1:2],
                                                AF.Copy, scale=0.5), inc=("st_sc", 1))
        ctx.op("tensor", lambda e: e.matmul(
            ap_(psMisc, 128, [[130, 128], [1, 1]]), onesrow[:], rowmisc[:, 1:2],
            start=True, stop=True),
            wait=[("st_sc", ctx.val("st_sc")), ("ldc", ctx.val("ldc")),
                  ("st_ve", 1)],
            inc=("st_te", 1))
        ctx.op("scalar", lambda e: e.activation(rscol[:],
                                                ap_(psMisc, 128, [[130, 128], [1, 1]]),
                                                AF.Copy),
               wait=[("st_te", ctx.val("st_te"))], inc=("st_sc", 1))

        phA = contextlib.ExitStack()

        def writer(ctx, it, b, tt, mc, stg, scv):
            ssem = "stoX0" if (it * 8 + mc) % 2 == 0 else "stoX1"
            dst = ap_(x1_s, b * D * T + mc * 128 * T + tt * TTS, [[T, 128], [1, TTS]])
            ctx.op("sync", lambda e, stg=stg, dst=dst: e.dma_start(
                out=dst, in_=stg[:]), wait=[("st_sc", scv)], inc=(ssem, 16))
            ctx.tile_mark[("ft_dm", it, mc)] = (ssem, ctx.val(ssem))

        emit_ffn_tail(ctx, nc, phA, "t1", pre4, x0, cc4, w_down, idtb, writer,
                      wd_cache=wd_cache)
        ctx.wait("sync", "stoX0", ctx.val("stoX0"))
        ctx.wait("sync", "stoX1", ctx.val("stoX1"))
        phA_sc = ctx.val("st_sc")
        phA_te = ctx.val("st_te")
        phA.close()

        phB = contextlib.ExitStack()
        sbB = lambda n, s, d=F32: phB.enter_context(nc.sbuf_tensor(n, s, d))
        w_in_sb = sbB("w_in_sb", [128, 8 * 32 * 128], BF16)
        cdiag_sb = sbB("cdiag_sb", [128, 16 * 4 * 128], BF16)
        x1t = [sbB(f"x1t{i}", [128, 8 * TTS]) for i in range(2)]
        h2t = sbB("h2t", [128, 8 * TTS])
        xnb = sbB("xnb", [128, 8 * TTS], BF16)
        xbr = [sbB(f"xbr{i}", [128, 16 * (TTS + 4)], BF16) for i in range(2)]
        mrep = sbB("mrepB", [128, TTS])
        vrep = sbB("vrepB", [128, TTS])
        stgB = [sbB(f"stgB{i}", [128, TTS]) for i in range(2)]
        psSB = phB.enter_context(nc.psum_tensor("psSB", [128, 2 * TTS], F32))
        psI = [phB.enter_context(nc.psum_tensor(f"psI{i}", [128, TTS], F32))
               for i in range(2)]

        ctx.wait("gpsimd", "st_te", phA_te)
        ctx.wait("gpsimd", "st_sc", phA_sc)
        ctx.wait("gpsimd", "stoX0", ctx.val("stoX0"))
        ctx.wait("gpsimd", "stoX1", ctx.val("stoX1"))
        ctx.op("gpsimd", lambda e: e.dma_start(out=w_in_sb[:], in_=w_inT),
               inc=("ldgw", 16))
        ctx.op("gpsimd", lambda e: e.dma_start(out=cdiag_sb[:], in_=cdiag),
               inc=("ldgw", 16))
        wB_ready = ctx.val("ldgw")
        ctx.wait("tensor", "ldgw", wB_ready)
        ctx.wait("vector", "st_sc", phA_sc)
        ctx.wait("scalar", "st_te", phA_te)
        ctx.wait("tensor", "st_sc", phA_sc)

        for it in range(NTILE):
            b, tt = divmod(it, T // TTS)
            xb = x1t[it % 2]
            xr = xbr[it % 2]
            lsem = "ldx0" if it % 2 == 0 else "ldx1"
            lw = [("st_sc", ctx.tile_mark.get(("phb_done", it - 2), 0)),
                  ("stoX0", ctx.val("stoX0")), ("stoX1", ctx.val("stoX1"))]
            lv = ctx.op("sync", lambda e, b=b, tt=tt, xb=xb: e.dma_start(
                out=xb[:], in_=ap_(x1_s, b * D * T + tt * TTS,
                                   [[T, 128], [128 * T, 8], [1, TTS]])),
                inc=(lsem, 16), wait=lw)
            for smn in ("stoH",):
                pv = ctx.tile_mark.get(("phb_" + smn, it - 1))
                if pv:
                    ctx.wait("scalar", smn, pv)
            emit_ln(ctx, xb, h2t, psSB, mrep, vrep, allones, h2t, 8, TTS,
                    lsem, lv, epscol)
            h2v = ctx.val("st_ve")
            ctx.op("sync", lambda e, b=b, tt=tt: e.dma_start(
                out=ap_(h2_s, b * D * T + tt * TTS, [[T, 128], [128 * T, 8], [1, TTS]]),
                in_=h2t[:]), wait=[("st_ve", h2v)], inc=("stoH", 16))
            ctx.tile_mark[("phb_stoH", it)] = ctx.val("stoH")
            ctx.op("scalar", lambda e, xb=xb: e.activation(xb[:], h2t[:], AF.Square),
                   wait=[("st_ve", h2v)], inc=("st_sc", 1))
            for j in range(8):
                ctx.op("tensor", lambda e, j=j, xb=xb: e.matmul(
                    psSB[:, 0:TTS], allones[:], xb[:, j * TTS:(j + 1) * TTS],
                    start=(j == 0), stop=(j == 7)),
                    wait=([("st_sc", ctx.val("st_sc"))] if j == 0 else None),
                    inc=(("st_te", 1) if j == 7 else None))
            ctx.op("scalar", lambda e: e.activation(vrep[:], psSB[:, 0:TTS], AF.Copy,
                                                    scale=1.0 / D),
                   wait=[("st_te", ctx.val("st_te"))])
            ctx.drain("scalar")
            ctx.op("scalar", lambda e: e.activation(vrep[:], vrep[:], AF.Sqrt,
                                                    bias=eps6col[:]), inc=("st_sc", 1))
            ctx.op("vector", lambda e: e.reciprocal(vrep[:], vrep[:]),
                   wait=[("st_sc", ctx.val("st_sc"))])
            ctx.drain("vector")
            nwb = ap_(nwcol_sb, 0, [[8, 128], [1, 8], [0, TTS]])
            vb = ap_(vrep, 0, [[TTS, 128], [0, 8], [1, TTS]])
            ctx.op("vector", lambda e, nwb=nwb: e.tensor_tensor(h2t[:], h2t[:], nwb,
                                                                OP.mult),
                   wait=[("stoH", ctx.tile_mark.get(("phb_stoH", it), 0))])
            ctx.drain("vector")
            ctx.op("vector", lambda e, vb=vb: e.tensor_tensor(h2t[:], h2t[:], vb,
                                                              OP.mult))
            ctx.drain("vector")
            ctx.op("vector", lambda e: e.tensor_copy(xnb[:], h2t[:]), inc=("st_ve", 1))
            ctx.drain("vector")
            xnv = ctx.val("st_ve")

            for mc in range(32):
                psP = psI[mc % 2]
                mmw = [("st_ve", xnv)]
                em = ctx.tile_mark.get(("phb_em", it, mc - 2))
                if em is None and it > 0:
                    em = ctx.tile_mark.get(("phb_em", it - 1, 30 + mc))
                if em:
                    mmw.append(("st_sc", em))
                for kc in range(8):
                    ctx.op("tensor", lambda e, kc=kc, mc=mc, psP=psP: e.matmul(
                        psP[:], w_in_sb[:, (kc * 32 + mc) * 128:(kc * 32 + mc) * 128 + 128],
                        xnb[:, kc * TTS:(kc + 1) * TTS],
                        start=(kc == 0), stop=(kc == 7)),
                        wait=(mmw if kc == 0 else None),
                        inc=(("st_te", 1) if kc == 7 else None))
                tev = ctx.val("st_te")
                if mc < 16:
                    xw = [("st_te", tev)]
                    cm = ctx.tile_mark.get(("phb_conv_te", it - 2))
                    if cm:
                        xw.append(("st_te", cm))
                    ctx.op("scalar", lambda e, psP=psP, xr=xr, mc=mc: e.activation(
                        xr[:, mc * (TTS + 4) + 4:mc * (TTS + 4) + 4 + TTS], psP[:],
                        AF.Copy), wait=xw, inc=("st_sc", 1))
                else:
                    stg = stgB[mc % 2]
                    scw = [("st_te", tev)]
                    dm = ctx.tile_mark.get(("stg_last", mc % 2))
                    if dm:
                        scw.append(dm)
                    ctx.op("scalar", lambda e, psP=psP, stg=stg: e.activation(
                        stg[:], psP[:], AF.Copy), wait=scw, inc=("st_sc", 1))
                    gsem = "stoG0" if mc % 2 == 0 else "stoG1"
                    ctx.op("sync", lambda e, stg=stg, b=b, tt=tt, mc=mc: e.dma_start(
                        out=ap_(mg_s, b * DI * T + (mc - 16) * 128 * T + tt * TTS,
                                [[T, 128], [1, TTS]]), in_=stg[:]),
                        wait=[("st_sc", ctx.val("st_sc"))], inc=(gsem, 16))
                    ctx.tile_mark[("stg_last", mc % 2)] = (gsem, ctx.val(gsem))
                ctx.tile_mark[("phb_em", it, mc)] = ctx.val("st_sc")
            xbr_v = ctx.val("st_sc")
            if tt == 0:
                hal = ap_(xr, 0, [[16 * (TTS + 4), 128], [TTS + 4, 16], [1, 4]])
                ctx.op("vector", lambda e, hal=hal: e.memset(hal, 0.0),
                       wait=[("st_sc", xbr_v)], inc=("st_ve", 1))
            else:
                xrp = xbr[(it - 1) % 2]
                dst = ap_(xr, 0, [[16 * (TTS + 4), 128], [TTS + 4, 16], [1, 4]])
                srch = ap_(xrp, TTS, [[16 * (TTS + 4), 128], [TTS + 4, 16], [1, 4]])
                ctx.op("vector", lambda e, dst=dst, srch=srch: e.tensor_copy(dst, srch),
                       wait=[("st_sc", xbr_v)], inc=("st_ve", 1))
            ctx.drain("vector")
            halo_v = ctx.val("st_ve")

            for j in range(16):
                psP = psI[j % 2]
                mmw = [("st_ve", halo_v), ("st_sc", ctx.val("st_sc"))]
                for k in range(4):
                    ctx.op("tensor", lambda e, j=j, k=k, psP=psP, xr=xr: e.matmul(
                        psP[:], cdiag_sb[:, (j * 4 + k) * 128:(j * 4 + k) * 128 + 128],
                        xr[:, j * (TTS + 4) + k + 1:j * (TTS + 4) + k + 1 + TTS],
                        start=(k == 0), stop=(k == 3)),
                        wait=(mmw if k == 0 else None),
                        inc=(("st_te", 1) if k == 3 else None))
                tev = ctx.val("st_te")
                stg = stgB[j % 2]
                scw = [("st_te", tev)]
                cdm = ctx.tile_mark.get(("stg_last", j % 2))
                if cdm:
                    scw.append(cdm)
                ctx.op("scalar", lambda e, psP=psP, j=j, stg=stg: e.activation(
                    stg[:], psP[:], AF.Silu,
                    bias=cbcol_sb[:, j:j + 1]), wait=scw, inc=("st_sc", 1))
                csem = "stoC0" if j % 2 == 0 else "stoC1"
                ctx.op("sync", lambda e, b=b, tt=tt, j=j, stg=stg: e.dma_start(
                    out=ap_(xc_s, b * DI * T + j * 128 * T + tt * TTS,
                            [[T, 128], [1, TTS]]), in_=stg[:]),
                    wait=[("st_sc", ctx.val("st_sc"))], inc=(csem, 16))
                ctx.tile_mark[("stg_last", j % 2)] = (csem, ctx.val(csem))
            ctx.tile_mark[("phb_conv_te", it)] = ctx.val("st_te")
            ctx.tile_mark[("phb_stoC", it)] = (ctx.val("stoC0"), ctx.val("stoC1"))
            ctx.tile_mark[("phb_done", it)] = ctx.val("st_sc")


        phB1_sc = ctx.val("st_sc")
        phB1_ve = ctx.val("st_ve")
        phB1_te = ctx.val("st_te")
        phB.close()

        phB2 = contextlib.ExitStack()
        sbB2 = lambda n, s, d=F32: phB2.enter_context(nc.sbuf_tensor(n, s, d))
        w_dt_sb = sbB2("w_dt_sb", [128, 16 * 16 * 128], BF16)
        w_bc_sb = sbB2("w_bc_sb", [128, 16 * 32], BF16)
        bcstg = [sbB2(f"bcstg{i}", [32, TTS]) for i in range(2)]
        xct = [sbB2(f"xct{i}", [128, 16 * TTS]) for i in range(2)]
        xcb = sbB2("xcb", [128, 16 * TTS], BF16)
        dtt = sbB2("dtt", [128, 16 * TTS])
        psI2 = [phB2.enter_context(nc.psum_tensor(f"psI2{i}", [128, TTS], F32))
                for i in range(2)]
        psBC = phB2.enter_context(nc.psum_tensor("psBC", [32, TTS], F32))
        ctx.wait("gpsimd", "st_sc", phB1_sc)
        ctx.wait("gpsimd", "st_ve", phB1_ve)
        ctx.op("gpsimd", lambda e: e.dma_start(out=w_dt_sb[:], in_=w_dtT),
               inc=("ldgw", 16))
        ctx.op("gpsimd", lambda e: e.dma_start(out=w_bc_sb[:], in_=w_bcT),
               inc=("ldgw", 16))
        wB2_ready = ctx.val("ldgw")
        ctx.wait("tensor", "ldgw", wB2_ready)
        ctx.wait("tensor", "st_sc", phB1_sc)
        ctx.wait("vector", "st_sc", phB1_sc)
        ctx.wait("scalar", "st_ve", phB1_ve)
        for eng in ("scalar", "vector", "tensor"):
            for smn in ("stoC0", "stoC1", "stoG0", "stoG1", "stoH"):
                ctx.wait(eng, smn, ctx.val(smn))

        for it in range(NTILE):
            b, tt = divmod(it, T // TTS)
            xc = xct[it % 2]
            lsem = "lds0" if it % 2 == 0 else "lds1"
            lw = [("st_ve", ctx.tile_mark.get(("b2_done", it - 2), 0))]
            c01 = ctx.tile_mark.get(("phb_stoC", it))
            if c01:
                lw += [("stoC0", c01[0]), ("stoC1", c01[1])]
            lv = ctx.op("sync", lambda e, b=b, tt=tt, xc=xc: e.dma_start(
                out=xc[:], in_=ap_(xc_s, b * DI * T + tt * TTS,
                                   [[T, 128], [128 * T, 16], [1, TTS]])),
                inc=(lsem, 16), wait=lw)
            pdm = ctx.tile_mark.get(("phb_stoD", it - 1))
            vw = [(lsem, lv), ("st_sc", ctx.tile_mark.get(("b2_bc", it - 1), 0))]
            if pdm:
                vw.append(("stoD", pdm))
            ctx.op("vector", lambda e, xc=xc: e.tensor_copy(xcb[:], xc[:]),
                   wait=vw, inc=("st_ve", 1))
            ctx.drain("vector")
            xcb_v = ctx.val("st_ve")

            for mc in range(16):
                psP = psI2[mc % 2]
                mmw = [("st_ve", xcb_v)]
                em = ctx.tile_mark.get(("b2_em", it, mc - 2))
                if em is None and it > 0:
                    em = ctx.tile_mark.get(("b2_em", it - 1, mc + 14))
                if em:
                    mmw.append(("st_sc", em))
                for kc in range(16):
                    ctx.op("tensor", lambda e, kc=kc, mc=mc, psP=psP: e.matmul(
                        psP[:], w_dt_sb[:, (kc * 16 + mc) * 128:(kc * 16 + mc) * 128 + 128],
                        xcb[:, kc * TTS:(kc + 1) * TTS],
                        start=(kc == 0), stop=(kc == 15)),
                        wait=(mmw if kc == 0 else None),
                        inc=(("st_te", 1) if kc == 15 else None))
                tev = ctx.val("st_te")
                ctx.op("scalar", lambda e, psP=psP, mc=mc: e.activation(
                    dtt[:, mc * TTS:(mc + 1) * TTS], psP[:], AF.Exp,
                    bias=bdtcol_sb[:, mc:mc + 1]), wait=[("st_te", tev)],
                    inc=("st_sc", 1))
                ctx.tile_mark[("b2_em", it, mc)] = ctx.val("st_sc")
            ctx.drain("scalar")
            ctx.op("scalar", lambda e: e.activation(dtt[:], dtt[:], AF.Ln, bias=1.0))
            ctx.drain("scalar")
            ctx.op("scalar", lambda e: e.activation(dtt[:], dtt[:], AF.Tanh,
                                                    bias=sctanb[:], scale=SC_IN),
                   inc=("st_sc", 1))
            ctx.drain("scalar")
            ctx.op("vector", lambda e: e.tensor_scalar(dtt[:], dtt[:], SC_OUT, SC_C,
                                                       OP.mult, OP.add),
                   wait=[("st_sc", ctx.val("st_sc"))], inc=("st_ve", 1))
            ctx.drain("vector")
            dtv = ctx.val("st_ve")
            ctx.op("sync", lambda e, b=b, tt=tt: e.dma_start(
                out=ap_(dt_s, b * DI * T + tt * TTS, [[T, 128], [128 * T, 16], [1, TTS]]),
                in_=dtt[:]), wait=[("st_ve", dtv)], inc=("stoD", 16))
            ctx.tile_mark[("phb_stoD", it)] = ctx.val("stoD")

            mmw = [("st_ve", xcb_v), ("st_sc", ctx.tile_mark.get(("b2_bc", it - 1), 0))]
            for kc in range(16):
                ctx.op("tensor", lambda e, kc=kc: e.matmul(
                    psBC[:], w_bc_sb[:, kc * 32:(kc + 1) * 32],
                    xcb[:, kc * TTS:(kc + 1) * TTS],
                    start=(kc == 0), stop=(kc == 15)),
                    wait=(mmw if kc == 0 else None),
                    inc=(("st_te", 1) if kc == 15 else None))
            bcs = bcstg[it % 2]
            bsem = "stoB0" if it % 2 == 0 else "stoB1"
            bw = [("st_te", ctx.val("st_te"))]
            bdm = ctx.tile_mark.get(("b2_bcs", it - 2))
            if bdm:
                bw.append((bsem, bdm))
            ctx.op("scalar", lambda e, bcs=bcs: e.activation(
                bcs[:], psBC[:], AF.Copy), wait=bw, inc=("st_sc", 1))
            ctx.tile_mark[("b2_bc", it)] = ctx.val("st_sc")
            ctx.op("sync", lambda e, b=b, tt=tt, bcs=bcs: e.dma_start(
                out=ap_(bcps_d, b * T + tt * TTS, [[B_CORE * T, 32], [1, TTS]]),
                in_=bcs[:]),
                wait=[("st_sc", ctx.val("st_sc"))], inc=(bsem, 16))
            ctx.tile_mark[("b2_bcs", it)] = ctx.val(bsem)
            ctx.tile_mark[("b2_done", it)] = ctx.val("st_ve")

        phB_sc = ctx.val("st_sc")
        phB_ve = ctx.val("st_ve")
        phB_te = ctx.val("st_te")
        phB2.close()
        for s in ("stoH", "stoC0", "stoC1", "stoD", "stoG0", "stoG1"):
            ctx.wait("sync", s, ctx.val(s))
        emit_scan(ctx, nc, st, locals())
        ve = emit_colmax_sb(ctx, pcol, ap_(psMisc, 0, [[130, 1], [1, 128]]),
                            prow, idt)
        for s in ("sto0", "sto1"):
            ctx.wait("sync", s, ctx.val(s))
        return emit_ccmax(ctx, prow[:, 0:2], ccin, ccout, ve)


def emit_scan(ctx, nc, st, env):
    e = env
    dt_s, xc_s, mg_s, x1_s, h2_s, x2_o = (e["dt_s"], e["xc_s"], e["mg_s"],
                                          e["x1_s"], e["h2_s"], e["x2_o"])
    w_outT, bcps_d = e["w_outT"], e["bcps_d"]
    atan_sb, dcol_sb, allones = e["atan_sb"], e["dcol_sb"], e["allones"]
    rscol, psMisc, pcol, epscol = e["rscol"], e["psMisc"], e["pcol"], e["epscol"]

    SEG = 10
    NSEG = 205
    SPB = 4
    NBLK = 52
    SL = 512 * SEG

    def slen(s):
        return 8 if s == NSEG - 1 else 10

    def bwid(i):
        return 8 if i == NBLK - 1 else 40

    sbS = lambda n, s, d=F32: st.enter_context(nc.sbuf_tensor(n, s, d))
    w_out_sb = sbS("w_out_sb", [128, 16 * 8 * 128], BF16)
    dAb = [sbS(f"dAb{i}", [128, SL]) for i in range(2)]
    wb = [sbS(f"wbb{i}", [128, SL]) for i in range(2)]
    hb = sbS("hbb", [128, SL])
    dtc = [sbS(f"dtc{i}", [128, 16 * 2 * 40]) for i in range(2)]
    xcc = [sbS(f"xcc{i}", [128, 16 * 2 * 40]) for i in range(2)]
    yring = sbS("yring", [128, 16 * 2 * 40])
    yscr = sbS("yscr", [128, 2 * 16 * 10])
    brepblk = [sbS(f"brepblk{i}", [128, 2 * 16 * 40]) for i in range(2)]
    crepblk = [sbS(f"crepblk{i}", [128, 2 * 16 * 40]) for i in range(2)]
    dA0 = [sbS(f"dA0_{i}", [128, 512]) for i in range(2)]
    tmpc = sbS("tmpc", [128, 512])
    sqscr = sbS("sqscr", [128, 256])
    sq2 = sbS("sq2", [128, 2])
    nrmt = sbS("nrmt", [128, 2])
    mgb = sbS("mgb", [128, 16 * 80])
    obuf = sbS("obuf", [128, 16 * 80], BF16)
    x1d = [sbS(f"x1d{i}", [128, 80]) for i in range(2)]
    h2d = [sbS(f"h2d{i}", [128, 80]) for i in range(2)]
    x2blk = sbS("x2blk", [128, 8 * 80])
    sqb = sbS("sqb", [128, 8 * 80])
    mrepD = sbS("mrepD", [128, 80])
    vrepD = sbS("vrepD", [128, 80])
    tmax16 = sbS("tmax16", [128, 16])
    tmax = sbS("tmax", [128, 1])
    psD = [st.enter_context(nc.psum_tensor(f"psDo{i}", [128, 512], F32))
           for i in range(2)]
    psS2 = st.enter_context(nc.psum_tensor("psS2", [128, 1024], F32))

    mk = ctx.tile_mark
    phB_sc, phB_ve, phB_te = e["phB_sc"], e["phB_ve"], e["phB_te"]
    ctx.wait("gpsimd", "st_sc", phB_sc)
    ctx.wait("gpsimd", "st_ve", phB_ve)
    for smn in ("stoG0", "stoG1", "stoX0", "stoX1", "stoH"):
        ctx.wait("gpsimd", smn, ctx.val(smn))
    ctx.op("gpsimd", lambda e2: e2.dma_start(out=w_out_sb[:], in_=w_outT),
           inc=("ldgw", 16))
    wo_ready = ctx.val("ldgw")
    ctx.wait("tensor", "st_sc", phB_sc)
    ctx.wait("vector", "st_sc", phB_sc)
    ctx.wait("scalar", "st_ve", phB_ve)
    ctx.wait("scalar", "st_te", phB_te)
    for eng in ("tensor", "vector", "scalar", "gpsimd"):
        ctx.wait(eng, "stoB0", ctx.val("stoB0"))
        ctx.wait(eng, "stoB1", ctx.val("stoB1"))
        ctx.wait(eng, "stoD", ctx.val("stoD"))

    def emit_block_loads(i):
        W = bwid(i)
        tb0 = 40 * i
        lsem = "lds0" if i % 2 == 0 else "lds1"
        lw = []
        if i > 1:
            lw.append(("st_ve", mk[("wtt", SPB * (i - 1) - 1)]))
            lw.append(("st_ve", mk[("ytt", SPB * (i - 1) - 1)]))
            lw.append(mk[("blkload", i - 2)])
        if i > 2:
            lw.append(("st_ve", mk[("dxz", i - 2)]))
        first = [True]

        def w_():
            if first[0]:
                first[0] = False
                return lw
            return None
        dc = dtc[i % 2]
        xc = xcc[i % 2]
        for b in range(2):
            ctx.op("sync", lambda e2, b=b, tb0=tb0, W=W, dc=dc: e2.dma_start(
                out=ap_(dc, b * 40, [[1280, 128], [80, 16], [1, W]]),
                in_=ap_(dt_s, b * DI * T + tb0, [[T, 128], [128 * T, 16], [1, W]])),
                inc=(lsem, 16), wait=w_())
            ctx.op("sync", lambda e2, b=b, tb0=tb0, W=W, xc=xc: e2.dma_start(
                out=ap_(xc, b * 40, [[1280, 128], [80, 16], [1, W]]),
                in_=ap_(xc_s, b * DI * T + tb0, [[T, 128], [128 * T, 16], [1, W]])),
                inc=(lsem, 16))
        br = brepblk[i % 2]
        cr = crepblk[i % 2]
        for n in range(16):
            ctx.op("sync", lambda e2, n=n, tb0=tb0, W=W, br=br: e2.dma_start(
                out=ap_(br, n * 40, [[1280, 128], [640, 2], [1, W]]),
                in_=ap_(bcps_d, n * (B_CORE * T) + tb0, [[0, 128], [T, 2], [1, W]])),
                inc=(lsem, 16))
            ctx.op("sync", lambda e2, n=n, tb0=tb0, W=W, cr=cr: e2.dma_start(
                out=ap_(cr, n * 40, [[1280, 128], [640, 2], [1, W]]),
                in_=ap_(bcps_d, (16 + n) * (B_CORE * T) + tb0,
                        [[0, 128], [T, 2], [1, W]])),
                inc=(lsem, 16))
        mk[("blkload", i)] = (lsem, ctx.val(lsem))

    def emit_prep(s):
        i = s // SPB
        so = (s - SPB * i) * SEG
        L = slen(s)
        pb = s % 2
        lsem, lval = mk[("blkload", i)]
        da = dAb[pb]
        wv = wb[pb]
        qw = [(lsem, lval)]
        if s >= 2:
            qw.append(("st_sc", mk[("exp0", s - 2)]))
        for b in range(2):
            ctx.op("vector", lambda e2, b=b, so=so, L=L, da=da: e2.tensor_tensor(
                ap_(da, b * 2560, [[SL, 128], [160, 16], [10, 16], [1, L]]),
                ap_(dtc[i % 2], b * 40 + so,
                    [[1280, 128], [80, 16], [0, 16], [1, L]]),
                ap_(atan_sb, 0, [[256, 128], [16, 16], [1, 16], [0, L]]),
                OP.mult), wait=(qw if b == 0 else None),
                inc=(("st_ve", 1) if b == 1 else None))
        ctx.drain("vector")
        mk[("qtt", s)] = ctx.val("st_ve")
        ctx.op("vector", lambda e2, so=so, L=L, i=i: e2.tensor_tensor(
            ap_(dtc[i % 2], so, [[1280, 128], [80, 16], [40, 2], [1, L]]),
            ap_(dtc[i % 2], so, [[1280, 128], [80, 16], [40, 2], [1, L]]),
            ap_(xcc[i % 2], so, [[1280, 128], [80, 16], [40, 2], [1, L]]),
            OP.mult), inc=("st_ve", 1))
        ctx.drain("vector")
        for b in range(2):
            ctx.op("vector", lambda e2, b=b, so=so, L=L, wv=wv, i=i:
                   e2.tensor_tensor(
                ap_(wv, b * 2560, [[SL, 128], [160, 16], [10, 16], [1, L]]),
                ap_(dtc[i % 2], b * 40 + so,
                    [[1280, 128], [80, 16], [0, 16], [1, L]]),
                ap_(brepblk[i % 2], b * 640 + so,
                    [[1280, 128], [0, 16], [40, 16], [1, L]]),
                OP.mult), inc=(("st_ve", 1) if b == 1 else None))
        ctx.drain("vector")
        mk[("wtt", s)] = ctx.val("st_ve")
        ctx.op("scalar", lambda e2, da=da: e2.activation(da[:], da[:], AF.Tanh),
               wait=[("st_ve", mk[("qtt", s)])])
        ctx.drain("scalar")
        d0w = []
        if s >= 1:
            d0w.append(("st_ve", mk[("chain_tmp", s - 1)]))
        ctx.op("scalar", lambda e2, da=da, pb=pb: e2.activation(
            dA0[pb][:], ap_(da, 0, [[SL, 128], [10, 512]]), AF.Exp, scale=16.0),
            wait=d0w)
        ctx.drain("scalar")
        ctx.op("scalar", lambda e2, da=da, L=L: e2.activation(
            ap_(da, 1, [[SL, 128], [10, 512], [1, L - 1]]),
            ap_(da, 1, [[SL, 128], [10, 512], [1, L - 1]]), AF.Exp, scale=16.0),
            inc=("st_sc", 1))
        ctx.drain("scalar")
        mk[("exp0", s)] = ctx.val("st_sc")
        ctx.op("vector", lambda e2, da=da: e2.memset(
            ap_(da, 0, [[SL, 128], [10, 512]]), 0.0),
            wait=[("st_sc", mk[("exp0", s)])], inc=("st_ve", 1))
        ctx.drain("vector")
        mk[("mset", s)] = ctx.val("st_ve")

    def emit_dxz(i):
        W = bwid(i)
        lsem, lval = mk[("blkload", i)]
        ctx.op("vector", lambda e2, W=W, i=i: e2.tensor_tensor(
            ap_(yring, 0, [[1280, 128], [80, 16], [40, 2], [1, W]]),
            ap_(xcc[i % 2], 0, [[1280, 128], [80, 16], [40, 2], [1, W]]),
            ap_(dcol_sb, 0, [[16, 128], [1, 16], [0, 2], [0, W]]),
            OP.mult), wait=[(lsem, lval)], inc=("st_ve", 1))
        ctx.drain("vector")
        mk[("dxz", i)] = ctx.val("st_ve")

    def emit_y(s):
        i = s // SPB
        so = (s - SPB * i) * SEG
        L = slen(s)
        hv = hb
        lsem, lval = mk[("blkload", i)]
        for b in range(2):
            ctx.op("vector", lambda e2, b=b, so=so, L=L, hv=hv, i=i:
                   e2.tensor_tensor(
                ap_(hv, b * 2560, [[SL, 128], [160, 16], [10, 16], [1, L]]),
                ap_(hv, b * 2560, [[SL, 128], [160, 16], [10, 16], [1, L]]),
                ap_(crepblk[i % 2], b * 640 + so,
                    [[1280, 128], [0, 16], [40, 16], [1, L]]),
                OP.mult), wait=([(lsem, lval)] if b == 0 else None),
                inc=(("st_ve", 1) if b == 1 else None))
        ctx.drain("vector")
        mk[("ytt", s)] = ctx.val("st_ve")
        for b in range(2):
            ctx.op("vector", lambda e2, b=b, L=L, hv=hv: e2.reduce_sum(
                ap_(yscr, b * 160, [[320, 128], [10, 16], [1, L]]),
                ap_(hv, b * 2560, [[SL, 128], [160, 16], [1, L], [10, 16]]),
                AX.X))
        ctx.drain("vector")
        ctx.op("vector", lambda e2, so=so, L=L: e2.tensor_tensor(
            ap_(yring, so, [[1280, 128], [40, 2], [80, 16], [1, L]]),
            ap_(yring, so, [[1280, 128], [40, 2], [80, 16], [1, L]]),
            ap_(yscr, 0, [[320, 128], [160, 2], [10, 16], [1, L]]),
            OP.add), inc=("st_ve", 1))
        ctx.drain("vector")
        mk[("yadd", s)] = ctx.val("st_ve")

    def emit_phd(i):
        W = bwid(i)
        tb0 = 40 * i
        gw = [("st_ve", mk.get(("pd_gst",), 0)), mk.get(("pd_ldg",), ("ldg0", 0))]
        for kc in range(16):
            for b in range(2):
                ctx.op("gpsimd", lambda e2, b=b, kc=kc, tb0=tb0, W=W:
                       e2.dma_start(
                    out=ap_(mgb, kc * 80 + b * 40, [[1280, 128], [1, W]]),
                    in_=ap_(mg_s, b * DI * T + kc * 128 * T + tb0,
                            [[T, 128], [1, W]])),
                    inc=("ldg0", 16),
                    wait=(gw if kc == 0 and b == 0 else None))
        lvg = ctx.val("ldg0")
        mk[("pd_ldg",)] = ("ldg0", lvg)
        ctx.op("scalar", lambda e2: e2.activation(mgb[:], mgb[:], AF.Silu),
               wait=[("ldg0", lvg), ("st_ve", mk.get(("pd_gst",), 0))],
               inc=("st_sc", 1))
        ctx.drain("scalar")
        silv = ctx.val("st_sc")
        ctx.op("vector", lambda e2: e2.tensor_tensor(
            mgb[:], mgb[:], yring[:], OP.mult), wait=[("st_sc", silv)])
        ctx.drain("vector")
        ctx.op("vector", lambda e2: e2.tensor_copy(obuf[:], mgb[:]),
               inc=("st_ve", 1))
        ctx.drain("vector")
        mk[("pd_gst",)] = ctx.val("st_ve")
        mk[("phdy", i)] = ctx.val("st_ve")
        if i + 1 < NBLK:
            emit_dxz(i + 1)
        for mc in range(8):
            msem = "ldm0" if mc % 2 == 0 else "ldm1"
            x1t = x1d[mc % 2]
            h2t = h2d[mc % 2]
            mw = [("st_ve", mk.get(("pd_res", mc % 2), 0))]
            for b in range(2):
                ctx.op("gpsimd", lambda e2, b=b, mc=mc, tb0=tb0, W=W, x1t=x1t:
                       e2.dma_start(
                    out=ap_(x1t, b * 40, [[80, 128], [1, W]]),
                    in_=ap_(x1_s, b * D * T + mc * 128 * T + tb0,
                            [[T, 128], [1, W]])),
                    inc=(msem, 16), wait=(mw if b == 0 else None))
                ctx.op("gpsimd", lambda e2, b=b, mc=mc, tb0=tb0, W=W, h2t=h2t:
                       e2.dma_start(
                    out=ap_(h2t, b * 40, [[80, 128], [1, W]]),
                    in_=ap_(h2_s, b * D * T + mc * 128 * T + tb0,
                            [[T, 128], [1, W]])),
                    inc=(msem, 16))
            lvm = ctx.val(msem)
            psP = psD[mc % 2]
            mmw = [("st_ve", mk[("phdy", i)]), ("ldgw", wo_ready),
                   ("st_ve", mk.get(("pd_res", mc % 2), 0))]
            if W == 40:
                for kc in range(16):
                    ctx.op("tensor", lambda e2, kc=kc, mc=mc, psP=psP, W=W:
                           e2.matmul(
                        ap_(psP, 0, [[512, 128], [W, 2], [1, W]]),
                        w_out_sb[:, (kc * 8 + mc) * 128:(kc * 8 + mc) * 128 + 128],
                        ap_(obuf, kc * 80, [[1280, 128], [40, 2], [1, W]]),
                        start=(kc == 0), stop=(kc == 15)),
                        wait=(mmw if kc == 0 else None),
                        inc=(("st_te", 1) if kc == 15 else None))
            else:
                for kc in range(16):
                    for b in range(2):
                        ctx.op("tensor", lambda e2, kc=kc, mc=mc, psP=psP,
                               W=W, b=b: e2.matmul(
                            ap_(psP, b * W, [[512, 128], [1, W]]),
                            w_out_sb[:, (kc * 8 + mc) * 128:
                                     (kc * 8 + mc) * 128 + 128],
                            ap_(obuf, kc * 80 + b * 40, [[1280, 128], [1, W]]),
                            start=(kc == 0 and b == 0),
                            stop=(kc == 15 and b == 1)),
                            wait=(mmw if kc == 0 and b == 0 else None),
                            inc=(("st_te", 1) if kc == 15 and b == 1 else None))
            tev = ctx.val("st_te")
            ssem = "sto0" if mc % 2 == 0 else "sto1"
            stw = [("st_te", tev), (msem, lvm)]
            dm = mk.get(("pd_sto", mc % 2))
            if dm:
                stw.append((ssem, dm))
            ctx.op("vector", lambda e2, psP=psP, mc=mc, h2t=h2t, W=W:
                   e2.scalar_tensor_tensor(
                ap_(x2blk, mc * 80, [[640, 128], [40, 2], [1, W]]),
                ap_(psP, 0, [[512, 128], [W, 2], [1, W]]), rscol[:, 0:1],
                ap_(h2t, 0, [[80, 128], [40, 2], [1, W]]),
                OP.mult, OP.add),
                wait=stw)
            ctx.drain("vector")
            ctx.op("vector", lambda e2, mc=mc, x1t=x1t, W=W: e2.tensor_tensor(
                ap_(x2blk, mc * 80, [[640, 128], [40, 2], [1, W]]),
                ap_(x2blk, mc * 80, [[640, 128], [40, 2], [1, W]]),
                ap_(x1t, 0, [[80, 128], [40, 2], [1, W]]),
                OP.add), inc=("st_ve", 1))
            ctx.drain("vector")
            mk[("pd_res", mc % 2)] = ctx.val("st_ve")
            scv = ctx.val("st_ve")
            ssem = "sto0" if mc % 2 == 0 else "sto1"
            for b in range(2):
                sw2 = [("st_ve", scv)]
                dm = mk.get(("pd_sto", mc % 2))
                if dm and b == 0:
                    sw2.append((ssem, dm))
                ctx.op("sync", lambda e2, b=b, mc=mc, tb0=tb0, W=W: e2.dma_start(
                    out=ap_(x2_o, b * D * T + mc * 128 * T + tb0,
                            [[T, 128], [1, W]]),
                    in_=ap_(x2blk, mc * 80 + b * 40, [[640, 128], [1, W]])),
                    wait=sw2, inc=(ssem, 16))
            mk[("pd_sto", mc % 2)] = ctx.val(ssem)
        x2v = ctx.val("st_ve")
        sqw = [("st_ve", x2v), ("st_ve", mk.get(("pd_lnred",), 0)),
               ("st_sc", ctx.val("st_sc"))]
        ctx.op("scalar", lambda e2: e2.activation(sqb[:], x2blk[:], AF.Square),
               wait=sqw, inc=("st_sc", 1))
        ctx.drain("scalar")
        sqv = ctx.val("st_sc")
        smw = [("st_ve", x2v), ("st_sc", mk.get(("pd_lncp",), 0)),
               ("st_ve", mk.get(("pd_lnm2",), 0))]
        if W == 40:
            for mc in range(8):
                ctx.op("tensor", lambda e2, mc=mc, W=W: e2.matmul(
                    ap_(psS2, 0, [[1024, 128], [40, 2], [1, W]]), allones[:],
                    ap_(x2blk, mc * 80, [[640, 128], [40, 2], [1, W]]),
                    start=(mc == 0), stop=(mc == 7)),
                    wait=(smw if mc == 0 else None))
            for mc in range(8):
                ctx.op("tensor", lambda e2, mc=mc, W=W: e2.matmul(
                    ap_(psS2, 512, [[1024, 128], [40, 2], [1, W]]), allones[:],
                    ap_(sqb, mc * 80, [[640, 128], [40, 2], [1, W]]),
                    start=(mc == 0), stop=(mc == 7)),
                    wait=([("st_sc", sqv)] if mc == 0 else None),
                    inc=(("st_te", 1) if mc == 7 else None))
        else:
            for mc in range(8):
                for b in range(2):
                    ctx.op("tensor", lambda e2, mc=mc, W=W, b=b: e2.matmul(
                        ap_(psS2, b * 40, [[1024, 128], [1, W]]), allones[:],
                        ap_(x2blk, mc * 80 + b * 40, [[640, 128], [1, W]]),
                        start=(mc == 0 and b == 0),
                        stop=(mc == 7 and b == 1)),
                        wait=(smw if mc == 0 and b == 0 else None))
            for mc in range(8):
                for b in range(2):
                    ctx.op("tensor", lambda e2, mc=mc, W=W, b=b: e2.matmul(
                        ap_(psS2, 512 + b * 40, [[1024, 128], [1, W]]),
                        allones[:],
                        ap_(sqb, mc * 80 + b * 40, [[640, 128], [1, W]]),
                        start=(mc == 0 and b == 0),
                        stop=(mc == 7 and b == 1)),
                        wait=([("st_sc", sqv)] if mc == 0 and b == 0 else None),
                        inc=(("st_te", 1) if mc == 7 and b == 1 else None))
        tev = ctx.val("st_te")
        ctx.op("scalar", lambda e2, W=W: e2.activation(
            ap_(mrepD, 0, [[80, 128], [40, 2], [1, W]]),
            ap_(psS2, 0, [[1024, 128], [40, 2], [1, W]]), AF.Copy, scale=1.0 / D),
            wait=[("st_te", tev)])
        ctx.drain("scalar")
        ctx.op("scalar", lambda e2, W=W: e2.activation(
            ap_(vrepD, 0, [[80, 128], [40, 2], [1, W]]),
            ap_(psS2, 512, [[1024, 128], [40, 2], [1, W]]), AF.Copy,
            scale=1.0 / D), inc=("st_sc", 1))
        ctx.drain("scalar")
        mk[("pd_lncp",)] = ctx.val("st_sc")
        ctx.op("vector", lambda e2, W=W: e2.tensor_tensor(
            ap_(psS2, 0, [[1024, 128], [40, 2], [1, W]]),
            ap_(mrepD, 0, [[80, 128], [40, 2], [1, W]]),
            ap_(mrepD, 0, [[80, 128], [40, 2], [1, W]]), OP.mult),
            wait=[("st_sc", mk[("pd_lncp",)])])
        ctx.drain("vector")
        ctx.op("vector", lambda e2, W=W: e2.tensor_tensor(
            ap_(vrepD, 0, [[80, 128], [40, 2], [1, W]]),
            ap_(vrepD, 0, [[80, 128], [40, 2], [1, W]]),
            ap_(psS2, 0, [[1024, 128], [40, 2], [1, W]]), OP.subtract),
            inc=("st_ve", 1))
        ctx.drain("vector")
        mk[("pd_lnm2",)] = ctx.val("st_ve")
        ctx.op("scalar", lambda e2, W=W: e2.activation(
            ap_(vrepD, 0, [[80, 128], [40, 2], [1, W]]),
            ap_(vrepD, 0, [[80, 128], [40, 2], [1, W]]), AF.Ln,
            bias=epscol[:]), wait=[("st_ve", mk[("pd_lnm2",)])])
        ctx.drain("scalar")
        ctx.op("scalar", lambda e2, W=W: e2.activation(
            ap_(vrepD, 0, [[80, 128], [40, 2], [1, W]]),
            ap_(vrepD, 0, [[80, 128], [40, 2], [1, W]]), AF.Exp, scale=-0.5),
            inc=("st_sc", 1))
        ctx.drain("scalar")
        rsv = ctx.val("st_sc")
        ctx.op("vector", lambda e2, W=W: e2.tensor_tensor(
            ap_(sqb, 0, [[640, 128], [80, 8], [40, 2], [1, W]]),
            ap_(x2blk, 0, [[640, 128], [80, 8], [40, 2], [1, W]]),
            ap_(mrepD, 0, [[80, 128], [0, 8], [40, 2], [1, W]]), OP.subtract),
            wait=[("st_sc", rsv)])
        ctx.drain("vector")
        ctx.op("vector", lambda e2, W=W: e2.tensor_tensor(
            ap_(sqb, 0, [[640, 128], [80, 8], [40, 2], [1, W]]),
            ap_(sqb, 0, [[640, 128], [80, 8], [40, 2], [1, W]]),
            ap_(vrepD, 0, [[80, 128], [0, 8], [40, 2], [1, W]]), OP.mult))
        ctx.drain("vector")
        ctx.op("vector", lambda e2, W=W: e2.reduce_max(
            tmax16[:], ap_(sqb, 0, [[640, 128], [80, 8], [40, 2], [1, W]]),
            AX.X, apply_absolute_value=True))
        ctx.drain("vector")
        ctx.op("vector", lambda e2: e2.reduce_max(
            tmax[:], tmax16[:], AX.X, apply_absolute_value=True))
        ctx.drain("vector")
        ctx.op("vector", lambda e2: e2.tensor_tensor(
            pcol[:], pcol[:], tmax[:], OP.max), inc=("st_ve", 1))
        ctx.drain("vector")
        mk[("pd_lnred",)] = ctx.val("st_ve")

    emit_block_loads(0)
    emit_block_loads(1)
    emit_prep(0)
    emit_dxz(0)
    for s in range(NSEG):
        pb = s % 2
        if s == 0:
            ctx.op("vector", lambda e2: e2.tensor_copy(
                tmpc[:], ap_(wb[0], 0, [[SL, 128], [10, 512]])),
                inc=("st_ve", 1))
            ctx.drain("vector")
        else:
            cw = [("st_sc", mk.get(("sq", s - 1), 0)),
                  ("st_sc", mk[("exp0", s)])]
            ctx.op("vector", lambda e2, pb=pb: e2.tensor_tensor(
                tmpc[:], dA0[pb][:],
                ap_(hb, 9, [[SL, 128], [10, 512]]), OP.mult),
                wait=cw)
            ctx.drain("vector")
            ctx.op("vector", lambda e2, pb=pb: e2.tensor_tensor(
                tmpc[:], tmpc[:], ap_(wb[pb], 0, [[SL, 128], [10, 512]]),
                OP.add), inc=("st_ve", 1))
            ctx.drain("vector")
        mk[("chain_tmp", s)] = ctx.val("st_ve")
        sqw = [("st_ve", mk[("chain_tmp", s)]),
               ("st_te", mk.get(("clipmm", s - 1), 0))]
        for b in range(2):
            ctx.op("scalar", lambda e2, b=b: e2.activation(
                sqscr[:], tmpc[:, b * 256:(b + 1) * 256], AF.Square,
                accum_out=sq2[:, b:b + 1]),
                wait=(sqw if b == 0 else None),
                inc=(("st_sc", 1) if b == 1 else None))
            ctx.drain("scalar")
        mk[("sq", s)] = ctx.val("st_sc")
        mmw = [("st_sc", mk[("sq", s)]), ("st_sc", mk.get(("sqrt", s - 1), 0))]
        ctx.op("tensor", lambda e2: e2.matmul(
            ap_(psMisc, 128, [[130, 128], [1, 2]]), allones[:], sq2[:],
            start=True, stop=True), wait=mmw, inc=("st_te", 1))
        mk[("clipmm", s)] = ctx.val("st_te")
        ctx.op("scalar", lambda e2: e2.activation(
            nrmt[:], ap_(psMisc, 128, [[130, 128], [1, 2]]), AF.Ln),
            wait=[("st_te", mk[("clipmm", s)]),
                  ("st_ve", mk.get(("hclip", s - 1), 0))])
        ctx.drain("scalar")
        ctx.op("scalar", lambda e2: e2.activation(
            nrmt[:], nrmt[:], AF.Exp, scale=0.5), inc=("st_sc", 1))
        ctx.drain("scalar")
        mk[("sqrt", s)] = ctx.val("st_sc")
        if s + 1 < NSEG:
            emit_prep(s + 1)
        if s >= 1:
            emit_y(s - 1)
        if (s + 1) % SPB == 1 and s + 1 > 1 and (s + 1) // SPB + 1 < NBLK:
            emit_block_loads((s + 1) // SPB + 1)
        ctx.op("vector", lambda e2: e2.tensor_scalar(
            nrmt[:], nrmt[:], 1e-8, None, OP.add),
            wait=[("st_sc", mk[("sqrt", s)])])
        ctx.drain("vector")
        ctx.op("vector", lambda e2: e2.reciprocal(nrmt[:], nrmt[:]))
        ctx.drain("vector")
        ctx.op("vector", lambda e2: e2.tensor_scalar(
            nrmt[:], nrmt[:], 20.0, 1.0, OP.mult, OP.min))
        ctx.drain("vector")
        ctx.op("vector", lambda e2, pb=pb: e2.tensor_tensor(
            ap_(wb[pb], 0, [[SL, 128], [2560, 2], [10, 256]]),
            ap_(tmpc, 0, [[512, 128], [256, 2], [1, 256]]),
            ap_(nrmt, 0, [[2, 128], [1, 2], [0, 256]]), OP.mult),
            inc=("st_ve", 1))
        ctx.drain("vector")
        mk[("hclip", s)] = ctx.val("st_ve")
        ctx.op("vector", lambda e2, pb=pb: e2.tensor_tensor_scan(
            hb[:], dAb[pb][:], wb[pb][:], 0.0, OP.mult, OP.add),
            wait=[("st_sc", mk[("exp0", s)])], inc=("st_ve", 1))
        ctx.drain("vector")
        mk[("scan", s)] = ctx.val("st_ve")
        if s >= SPB and (s - 1) % SPB == SPB - 1:
            emit_phd((s - 1) // SPB)
    emit_y(NSEG - 1)
    emit_phd(NBLK - 1)


_IDENT = np.eye(128, dtype=np.float32)


def _selbc():
    out = np.zeros((32, 32 * 128), np.float32)
    for r in range(32):
        out[r, r * 128:(r + 1) * 128] = 1.0
    return out


_SELBC = _selbc()


def _conv_diag(conv_w):
    out = np.zeros((128, 16 * 4 * 128), np.float32)
    cw = np.asarray(conv_w)
    for j in range(16):
        for k in range(4):
            blk = np.diag(cw[j * 128:(j + 1) * 128, 0, k].astype(np.float32))
            out[:, (j * 4 + k) * 128:(j * 4 + k) * 128 + 128] = blk
    return out


def _tile_kmajor(wT, nkc, nmc, mwid=128):
    K, M = wT.shape
    assert K == nkc * 128 and M == nmc * mwid
    return np.ascontiguousarray(
        wT.reshape(nkc, 128, nmc, mwid).transpose(1, 0, 2, 3).reshape(128, -1))


ALL_SEMS = [
    "ldc", "ldsc", "ldcc", "cc", "ldx0", "ldx1", "ldu0", "ldu1", "ldw0",
    "ldw1", "ldf0", "ldf1", "ldr0", "ldr1", "lds0", "lds1", "ldm0", "ldm1",
    "ldg0", "ldg1", "ldgw", "ldsel", "st_te", "st_ve", "st_sc", "stoU0",
    "stoU1", "stoG0", "stoG1", "stoC0", "stoC1", "stoD", "stoB0", "stoB1",
    "stoX0", "stoX1", "stoH", "sto0", "sto1", "stoW",
    "dr_sync", "dr_tensor", "dr_vector", "dr_scalar", "dr_gpsimd"]


def full_barrier(ctx, exclude=("cc",)):
    for eng in Ctx.ENGS:
        for s, v in list(ctx.cnt.items()):
            if v and s not in exclude:
                ctx.wait(eng, s, v)


def build_MEGA():
    nc = bass.Bass("TRN2", target_bir_lowering=False,
                   detect_race_conditions=False)
    x = nc.dram_tensor("x", [B_CORE, D, T], F32, kind="ExternalInput").ap()
    ident = nc.dram_tensor("ident", [128, 128], F32, kind="ExternalInput").ap()
    identb = nc.dram_tensor("identb", [128, 128], BF16, kind="ExternalInput").ap()
    w_up = nc.dram_tensor("w_up", [FF, D], F32, kind="ExternalInput").ap()
    w_gate = nc.dram_tensor("w_gate", [FF, D], F32, kind="ExternalInput").ap()
    w_down = nc.dram_tensor("w_down", [D, FF], F32, kind="ExternalInput").ap()
    w_inT = nc.dram_tensor("w_inT", [128, 8 * 32 * 128], F32, kind="ExternalInput").ap()
    w_dtT = nc.dram_tensor("w_dtT", [128, 16 * 16 * 128], F32, kind="ExternalInput").ap()
    w_bcT = nc.dram_tensor("w_bcT", [128, 16 * 32], F32, kind="ExternalInput").ap()
    w_outT = nc.dram_tensor("w_outT", [128, 16 * 8 * 128], F32, kind="ExternalInput").ap()
    cdiag = nc.dram_tensor("cdiag", [128, 16 * 4 * 128], F32, kind="ExternalInput").ap()
    a_jln = nc.dram_tensor("a_jln", [128, 16 * DS], F32, kind="ExternalInput").ap()
    dcol = nc.dram_tensor("dcol", [128, 16], F32, kind="ExternalInput").ap()
    cbcol = nc.dram_tensor("cbcol", [128, 16], F32, kind="ExternalInput").ap()
    bdtcol = nc.dram_tensor("bdtcol", [128, 16], F32, kind="ExternalInput").ap()
    nwcol = nc.dram_tensor("nwcol", [128, 8], F32, kind="ExternalInput").ap()
    rscal = nc.dram_tensor("rscal", [1, 1], F32, kind="ExternalInput").ap()
    out = nc.dram_tensor("out", [B_CORE, D, T], F32, kind="ExternalOutput").ap()
    up_raw = nc.dram_tensor("up_raw", [B_CORE, FF, T], F32).ap()
    gate_raw = nc.dram_tensor("gate_raw", [B_CORE, FF, T], F32).ap()
    pre3_raw = nc.dram_tensor("pre3_raw", [B_CORE, FF, T], F32).ap()
    pre4_raw = nc.dram_tensor("pre4_raw", [B_CORE, FF, T], F32).ap()
    x2_s = nc.dram_tensor("x2_s", [B_CORE, D, T], F32).ap()
    x1_s = nc.dram_tensor("x1_s", [B_CORE, D, T], F32).ap()
    h2_s = nc.dram_tensor("h2_s", [B_CORE, D, T], F32).ap()
    mg_s = nc.dram_tensor("mg_s", [B_CORE, DI, T], F32).ap()
    xc_s = nc.dram_tensor("xc_s", [B_CORE, DI, T], F32).ap()
    dt_s = nc.dram_tensor("dt_s", [B_CORE, DI, T], F32).ap()
    bcps_d = nc.dram_tensor("bcps_d", [32, B_CORE * T], F32).ap()
    NCC = 8
    ccin = [nc.dram_tensor(f"ccin{k}", [1, 2], F32) for k in range(NCC)]
    ccout = [nc.dram_tensor(f"ccout{k}", [1, 2], F32) for k in range(NCC)]
    wqu_c = nc.dram_tensor("wqu_c", [128, 8 * 24 * 128], BF16).ap()
    wqg_c = nc.dram_tensor("wqg_c", [128, 8 * 24 * 128], BF16).ap()
    wdt_c = nc.dram_tensor("wdt_c", [128, 24 * 8 * 128], BF16).ap()

    with contextlib.ExitStack() as root:
        ctx = Ctx(nc)
        ctx.add_sems(ALL_SEMS, root)
        blk = root.enter_context(nc.Block())

        def ph(fn, *args):
            with contextlib.ExitStack() as st:
                r = fn(ctx, nc, st, *args)
            full_barrier(ctx)
            return r

        cc0 = ph(ph_P0, x, ident, ccin[0], ccout[0])
        cc1 = ph(ph_upgate, "f1", x, ident, identb, w_up, w_gate, up_raw,
                 gate_raw, (ccout[0].ap(), cc0, 0), ccin[1], ccout[1],
                 (wqu_c, wqg_c, "store"))
        cc2 = ph(ph_gate, "g1", gate_raw, ident, pre3_raw,
                 (ccout[1].ap(), cc1, 1), ccin[2], ccout[2])
        cc3 = ph(ph_prod, "o1", pre3_raw, up_raw, ident, pre4_raw,
                 (ccout[2].ap(), cc2, 0), (ccout[1].ap(), cc1, 0),
                 ccin[3], ccout[3])
        cc4 = ph(ph_P4, x, pre4_raw, (ccout[3].ap(), cc3, 0), w_down, ident,
                 identb, w_inT, w_dtT, w_bcT, w_outT, cdiag, a_jln, dcol,
                 cbcol, bdtcol, nwcol, rscal, x2_s, x1_s, h2_s, mg_s, xc_s,
                 dt_s, bcps_d, ccin[4], ccout[4], (wdt_c, "store"))
        cc5 = ph(ph_upgate, "f2", x2_s, ident, identb, w_up, w_gate, up_raw,
                 gate_raw, (ccout[4].ap(), cc4, 0), ccin[5], ccout[5],
                 (wqu_c, wqg_c, "load"))
        cc6 = ph(ph_gate, "g2", gate_raw, ident, pre3_raw,
                 (ccout[5].ap(), cc5, 1), ccin[6], ccout[6])
        cc7 = ph(ph_prod, "o2", pre3_raw, up_raw, ident, pre4_raw,
                 (ccout[6].ap(), cc6, 0), (ccout[5].ap(), cc5, 0),
                 ccin[7], ccout[7])
        ph(ph_tail, pre4_raw, x2_s, (ccout[7].ap(), cc7, 0), w_down, identb,
           out, (wdt_c, "load"))
        ctx.build(blk)
    return nc


_PROG = []


def _prog():
    if not _PROG:
        _PROG.append(build_MEGA())
    return _PROG[0]


def _prep_core_inputs(inputs):
    x = np.asarray(inputs["L0_attn_x"], dtype=np.float32)
    per_core = [np.ascontiguousarray(x[2 * c:2 * c + 2].transpose(0, 2, 1))
                for c in range(NCORES)]
    A_log = np.asarray(inputs["L0_attn_A_log"], dtype=np.float32)
    shared = {
        "ident": _IDENT,
        "identb": _IDENT.astype(ml_dtypes.bfloat16),
        "w_up": np.ascontiguousarray(inputs["L0_ffn_w_up"], dtype=np.float32),
        "w_gate": np.ascontiguousarray(inputs["L0_ffn_w_gate"], dtype=np.float32),
        "w_down": np.ascontiguousarray(inputs["L0_ffn_w_down"], dtype=np.float32),
        "w_inT": _tile_kmajor(np.ascontiguousarray(
            np.asarray(inputs["L0_attn_W_in"], dtype=np.float32).T), 8, 32),
        "w_dtT": _tile_kmajor(np.ascontiguousarray(
            np.asarray(inputs["L0_attn_W_dt"], dtype=np.float32).T), 16, 16),
        "w_bcT": _tile_kmajor(np.ascontiguousarray(np.concatenate(
            [np.asarray(inputs["L0_attn_W_B"], dtype=np.float32),
             np.asarray(inputs["L0_attn_W_C"], dtype=np.float32)], 0).T), 16, 1, 32),
        "w_outT": _tile_kmajor(np.ascontiguousarray(
            np.asarray(inputs["L0_attn_W_out"], dtype=np.float32).T), 16, 8),
        "cdiag": _conv_diag(np.asarray(inputs["L0_attn_conv_w"], dtype=np.float32)),
        "a_jln": np.ascontiguousarray(
            A_log.reshape(16, 128, 16).transpose(1, 0, 2).reshape(128, 256)),
        "dcol": np.ascontiguousarray(
            np.asarray(inputs["L0_attn_D_param"], np.float32).reshape(16, 128).T),
        "cbcol": np.ascontiguousarray(
            np.asarray(inputs["L0_attn_conv_b"], np.float32).reshape(16, 128).T),
        "bdtcol": np.ascontiguousarray(
            np.asarray(inputs["L0_attn_b_dt"], np.float32).reshape(16, 128).T),
        "nwcol": np.ascontiguousarray(
            np.asarray(inputs["L0_attn_norm_w"], np.float32).reshape(8, 128).T),
        "rscal": np.asarray(inputs["L0_attn_residual_scale"],
                            np.float32).reshape(1, 1),
    }
    return per_core, shared


def kernel(**inputs):
    xs, sh = _prep_core_inputs(inputs)
    nc = _prog()
    res = run_bass_kernel_spmd(
        nc, [{"x": xs[c], **sh} for c in range(NCORES)],
        core_ids=list(range(NCORES)))
    out = np.concatenate(
        [np.asarray(res.results[c]["out"]).transpose(0, 2, 1)
         for c in range(NCORES)], axis=0)
    return np.ascontiguousarray(out.astype(np.float32))


if __name__ == "__main__":
    pass


# version 13
# speedup vs baseline: 1.0311x; 1.0311x over previous
import contextlib
import numpy as np
import ml_dtypes
import concourse.bass as bass
import concourse.mybir as mybir
from concourse.bass_utils import run_bass_kernel_spmd

F32 = mybir.dt.float32
BF16 = mybir.dt.bfloat16
I32 = mybir.dt.int32
AF = mybir.ActivationFunctionType
OP = mybir.AluOpType
AX = mybir.AxisListType

NCORES = 8
B_CORE = 2
T = 2048
D = 1024
DI = 2048
DS = 16
FF = 3072
TTS = 512
NTILE = B_CORE * (T // TTS)
QMAX_A = 32767.0
QMAX_W = 7.0
WBLK = 32
MAGIC = 12582912.0

SC_C = (0.1 + 0.001) / 2.0
SC_R = (0.1 - 0.001) / 2.0
SC_IN = 0.5 / (SC_R + 1e-8)
SC_OUT = SC_R / 0.5
SCA_IN = 0.5 / (5.0 + 1e-8)
SCA_OUT = 5.0 / 0.5
SCB_IN = 0.5 / (8.0 + 1e-8)
SCB_OUT = 8.0 / 0.5


class Ctx:

    ENGS = ("sync", "tensor", "vector", "scalar", "gpsimd")

    def __init__(self, nc):
        self.nc = nc
        self.q = {e: [] for e in self.ENGS}
        self.sem_handles = {}
        self.cnt = {}
        self.last_inc = {}

    def add_sems(self, names, stack):
        for n in names:
            self.sem_handles[n] = stack.enter_context(self.nc.semaphore(n))
            self.cnt[n] = 0

    def op(self, eng, fn, inc=None, wait=None):
        waits = list(wait or [])
        tgt = None
        if inc is not None:
            self.cnt[inc[0]] += inc[1]
            tgt = self.cnt[inc[0]]
        handles = self.sem_handles

        def replay(e, waits=waits, fn=fn, inc=inc):
            for (ws, wv) in waits:
                e.wait_ge(handles[ws], wv)
            r = fn(e)
            if inc is not None:
                r.then_inc(handles[inc[0]], inc[1])
            return r

        self.q[eng].append(replay)
        self.last_inc[eng] = (inc[0], tgt) if inc is not None else None
        return tgt

    def wait(self, eng, sem, val=None):
        v = self.cnt[sem] if val is None else val
        self.q[eng].append(lambda e, s=sem, v=v: e.wait_ge(self.sem_handles[s], v))
        return v

    def drain(self, eng):
        if not self.q[eng]:
            return
        last = self.last_inc.get(eng)
        if last is None:
            sem = "dr_" + eng
            self.cnt[sem] = self.cnt.get(sem, 0) + 1
            v = self.cnt[sem]
            prev = self.q[eng].pop()
            handles = self.sem_handles

            def wrapped(e, prev=prev, sem=sem, handles=handles):
                r = prev(e)
                r.then_inc(handles[sem], 1)
                return r

            self.q[eng].append(wrapped)
            last = (sem, v)
            self.last_inc[eng] = last
        s, v = last
        self.q[eng].append(lambda e, s=s, v=v: e.wait_ge(self.sem_handles[s], v))

    def val(self, sem):
        return self.cnt[sem]

    def build(self, block):
        for eng in self.ENGS:
            ops = self.q[eng]
            if not ops:
                continue

            def body(e, ops=ops):
                for f in ops:
                    f(e)

            getattr(block, eng)(body)


def ap_(t, off, dims):
    return bass.AP(t.tensor if hasattr(t, "tensor") else t, off, dims)


def emit_ln(ctx, xt, sqt, psS, mrep, vrep, allones, xnt, j_chunks, tile_f,
            ld_sem, ld_val, epscol=None):
    ctx.op("scalar", lambda e: e.activation(sqt[:], xt[:], AF.Square),
           wait=[(ld_sem, ld_val), ("st_ve", ctx.val("st_ve"))], inc=("st_sc", 1))
    for j in range(j_chunks):
        ctx.op("tensor",
               lambda e, j=j: e.matmul(psS[:, 0:tile_f], allones[:],
                                       xt[:, j * tile_f:(j + 1) * tile_f],
                                       start=(j == 0), stop=(j == j_chunks - 1)),
               wait=([(ld_sem, ld_val), ("st_ve", ctx.val("st_ve"))] if j == 0 else None))
    for j in range(j_chunks):
        ctx.op("tensor",
               lambda e, j=j: e.matmul(psS[:, tile_f:2 * tile_f], allones[:],
                                       sqt[:, j * tile_f:(j + 1) * tile_f],
                                       start=(j == 0), stop=(j == j_chunks - 1)),
               wait=([("st_sc", ctx.val("st_sc"))] if j == 0 else None),
               inc=(("st_te", 1) if j == j_chunks - 1 else None))
    te_done = ctx.val("st_te")
    Df = float(j_chunks * 128)
    ctx.op("scalar", lambda e: e.activation(mrep[:], psS[:, 0:tile_f], AF.Copy,
                                            scale=1.0 / Df),
           wait=[("st_te", te_done)])
    ctx.op("scalar", lambda e: e.activation(vrep[:], psS[:, tile_f:2 * tile_f], AF.Copy,
                                            scale=1.0 / Df), inc=("st_sc", 1))
    ctx.op("vector", lambda e: e.tensor_tensor(psS[:, 0:tile_f], mrep[:], mrep[:], OP.mult),
           wait=[("st_sc", ctx.val("st_sc"))])
    ctx.drain("vector")
    ctx.op("vector", lambda e: e.tensor_tensor(vrep[:], vrep[:], psS[:, 0:tile_f],
                                               OP.subtract), inc=("st_ve", 1))
    ctx.op("scalar", lambda e: e.activation(vrep[:], vrep[:], AF.Sqrt,
                                            bias=epscol[:]),
           wait=[("st_ve", ctx.val("st_ve"))], inc=("st_sc", 1))
    ctx.op("vector", lambda e: e.reciprocal(vrep[:], vrep[:]),
           wait=[("st_sc", ctx.val("st_sc"))])
    ctx.drain("vector")
    n_p = 128
    mb = ap_(mrep, 0, [[tile_f, n_p], [0, j_chunks], [1, tile_f]])
    vb = ap_(vrep, 0, [[tile_f, n_p], [0, j_chunks], [1, tile_f]])
    xv = ap_(xt, 0, [[j_chunks * tile_f, n_p], [tile_f, j_chunks], [1, tile_f]])
    ov = ap_(xnt, 0, [[j_chunks * tile_f, n_p], [tile_f, j_chunks], [1, tile_f]])
    ctx.op("vector", lambda e: e.tensor_tensor(ov, xv, mb, OP.subtract))
    ctx.drain("vector")
    ctx.op("vector", lambda e: e.tensor_tensor(ov, ov, vb, OP.mult), inc=("st_ve", 1))
    ctx.drain("vector")
    return ctx.val("st_ve")


def emit_colmax_sb(ctx, pcol, psT, prow, idt):
    ctx.op("tensor", lambda e: e.transpose(psT[:], pcol[:], idt[:]),
           wait=[("st_ve", ctx.val("st_ve"))], inc=("st_te", 1))
    ctx.op("vector", lambda e: e.reduce_max(prow[:, 0:1], psT[:], AX.X),
           wait=[("st_te", ctx.val("st_te"))], inc=("st_ve", 1))
    return ctx.val("st_ve")


def emit_ccmax(ctx, src_sb, ccin, ccout, ve_val):
    lv = ctx.op("gpsimd", lambda e: e.dma_start(out=ccin[:, :], in_=src_sb),
                inc=("ldcc", 16), wait=[("st_ve", ve_val)])
    ctx.wait("gpsimd", "ldcc", lv)
    ctx.op("gpsimd", lambda e: e.collective_compute(
        "AllReduce", OP.max, replica_groups=[list(range(NCORES))],
        ins=[ccin.ap().opt()], outs=[ccout.ap().opt()]), inc=("cc", 1))
    return ctx.val("cc")


def emit_scale_col(ctx, allp_dram, nvals, stride, off, qmax, rowbuf, onesrow,
                   psCol, ccol, scol, extra_mult=1.0, ld_eng="sync",
                   ld_wait=None):
    lv = ctx.op(ld_eng, lambda e: e.dma_start(
        out=rowbuf[:, 0:nvals], in_=ap_(allp_dram, off, [[1, 1], [stride, nvals]]),
        allow_slow_non_contiguous=True), inc=("ldsc", 16), wait=ld_wait)
    ctx.op("vector", lambda e: e.reduce_max(rowbuf[:, nvals:nvals + 1],
                                            rowbuf[:, 0:nvals], AX.X),
           wait=[("ldsc", lv)])
    ctx.drain("vector")
    ctx.op("vector", lambda e: e.reciprocal(rowbuf[:, nvals + 1:nvals + 2],
                                            rowbuf[:, nvals:nvals + 1]))
    ctx.drain("vector")
    ctx.op("vector", lambda e: e.tensor_scalar(rowbuf[:, nvals + 1:nvals + 2],
                                               rowbuf[:, nvals + 1:nvals + 2],
                                               qmax, None, OP.mult))
    ctx.op("vector", lambda e: e.tensor_scalar(rowbuf[:, nvals:nvals + 1],
                                               rowbuf[:, nvals:nvals + 1],
                                               extra_mult / qmax, None, OP.mult),
           inc=("st_ve", 1))
    ctx.drain("vector")
    pitch = psCol.shape[-1]
    pc2 = ap_(psCol, 0, [[pitch, 128], [1, 2]])
    pc_s = ap_(psCol, 0, [[pitch, 128], [1, 1]])
    pc_c = ap_(psCol, 1, [[pitch, 128], [1, 1]])
    ctx.op("tensor", lambda e: e.matmul(pc2, onesrow[:],
                                        rowbuf[:, nvals:nvals + 2], start=True, stop=True),
           wait=[("st_ve", ctx.val("st_ve"))], inc=("st_te", 1))
    ctx.op("scalar", lambda e: e.activation(scol[:], pc_s, AF.Copy),
           wait=[("st_te", ctx.val("st_te"))])
    ctx.op("scalar", lambda e: e.activation(ccol[:], pc_c, AF.Copy),
           inc=("st_sc", 1))
    ctx.wait("vector", "st_sc", ctx.val("st_sc"))
    ctx.wait("sync", "st_sc", ctx.val("st_sc"))


def ph_P0(ctx, nc, st, x, ident, ccin, ccout):
    sb = lambda n, s, d=F32: st.enter_context(nc.sbuf_tensor("p0" + n, s, d))
    ps = lambda n, s: st.enter_context(nc.psum_tensor("p0" + n, s, F32))
    ctx.tile_mark = {}
    xt = [sb(f"xt{i}", [128, 8 * TTS]) for i in range(2)]
    sqt = sb("sqt", [128, 8 * TTS])
    xnt = sb("xnt", [128, 8 * TTS])
    mrep = sb("mrep", [128, TTS])
    vrep = sb("vrep", [128, TTS])
    allones = sb("allones", [128, 128])
    idt = sb("idt", [128, 128])
    pcol = sb("pcol", [128, 1])
    epscol = sb("epscol", [128, 1])
    prow = sb("prow", [1, 128])
    psS = ps("psS", [128, 2 * TTS])
    psT = ps("psT", [1, 128])

    ctx.op("vector", lambda e: e.memset(allones[:], 1.0))
    ctx.op("vector", lambda e: e.memset(epscol[:], 1e-5))
    ctx.op("vector", lambda e: e.memset(prow[:], 0.0))
    ctx.op("vector", lambda e: e.memset(pcol[:], 0.0), inc=("st_ve", 1))
    ctx.op("sync", lambda e: e.dma_start(out=idt[:], in_=ident), inc=("ldc", 16))

    tile_done = {}
    for it in range(NTILE):
        b, tt = divmod(it, T // TTS)
        buf = xt[it % 2]
        lsem = "ldx0" if it % 2 == 0 else "ldx1"
        src = ap_(x, b * D * T + tt * TTS, [[T, 128], [128 * T, 8], [1, TTS]])
        ldwait = [("st_ve", tile_done[it - 2])] if it >= 2 else None
        lv = ctx.op("sync", lambda e, s=src, bu=buf: e.dma_start(out=bu[:], in_=s),
                    inc=(lsem, 16), wait=ldwait)
        emit_ln(ctx, buf, sqt, psS, mrep, vrep, allones, xnt, 8, TTS, lsem, lv, epscol)
        ctx.op("vector", lambda e: e.reduce_max(sqt[:, 0:1], xnt[:], AX.X,
                                                apply_absolute_value=True))
        ctx.drain("vector")
        ctx.op("vector", lambda e: e.tensor_tensor(pcol[:], pcol[:], sqt[:, 0:1],
                                                   OP.max), inc=("st_ve", 1))
        ctx.drain("vector")
        tile_done[it] = ctx.val("st_ve")
    ctx.wait("tensor", "ldc", ctx.val("ldc"))
    ve = emit_colmax_sb(ctx, pcol, psT, prow, idt)
    return emit_ccmax(ctx, prow[:, 0:2], ccin, ccout, ve)


def emit_dequant(ctx, wsrc, n_ot, I, wt, bq, binv, i32w, i32b, zt, wqt, identb, psTr,
                 wqT_sb):
    nblk = I // WBLK
    nkc = I // 128
    pw = wt[0].shape[-1]
    pq = bq.shape[-1]
    pi = i32w.shape[-1]
    for ot in range(n_ot):
        buf = wt[ot % 2]
        wq_par = wqt[ot % 2]
        lsem = "ldw0" if ot % 2 == 0 else "ldw1"
        ldw = [("st_ve", ctx.tile_mark.get(("dqv_last", ot % 2), 0))]
        lv = ctx.op("sync", lambda e, o=ot, b=buf: e.dma_start(
            out=b[:, 0:I], in_=ap_(wsrc, o * 128 * I, [[I, 128], [1, I]])),
            inc=(lsem, 16), wait=ldw)
        ctx.op("vector", lambda e, b=buf: e.reduce_max(
            bq[:, 0:nblk], ap_(b, 0, [[pw, 128], [WBLK, nblk], [1, WBLK]]), AX.X,
            apply_absolute_value=True),
            wait=[(lsem, lv)])
        ctx.drain("vector")
        ctx.op("vector", lambda e: e.tensor_scalar(bq[:, 0:nblk], bq[:, 0:nblk],
                                                   1.0 / QMAX_W, 1e-12,
                                                   OP.mult, OP.max))
        ctx.drain("vector")
        ctx.op("vector", lambda e: e.reciprocal(binv[:, 0:nblk], bq[:, 0:nblk]))
        ctx.drain("vector")
        bqb = ap_(bq, 0, [[pq, 128], [1, nblk], [0, WBLK]])
        bib = ap_(binv, 0, [[pq, 128], [1, nblk], [0, WBLK]])
        wv = ap_(buf, 0, [[pw, 128], [WBLK, nblk], [1, WBLK]])
        pz = zt.shape[-1]
        zv = zt[:, 0:I]
        iv = i32w[:, 0:I]
        av = i32b[:, 0:I]
        qv = ap_(wq_par, 0, [[pi, 128], [WBLK, nblk], [1, WBLK]])
        zblk = ap_(zt, 0, [[pz, 128], [WBLK, nblk], [1, WBLK]])
        ctx.op("vector", lambda e, zblk=zblk, wv=wv, bib=bib: e.tensor_tensor(
            zblk, wv, bib, OP.mult), inc=("st_ve", 1))
        ctx.tile_mark[("dqv_last", ot % 2)] = ctx.val("st_ve")
        ctx.drain("vector")
        ctx.op("vector", lambda e, zv=zv: e.tensor_scalar(zv, zv, MAGIC, None, OP.add))
        ctx.drain("vector")
        ctx.op("vector", lambda e, zv=zv: e.tensor_scalar(zv, zv, MAGIC, None,
                                                          OP.subtract))
        ctx.drain("vector")
        ctx.op("vector", lambda e, iv=iv, zv=zv: e.tensor_scalar(iv, zv, 0.0, None,
                                                                 OP.add))
        ctx.drain("vector")
        ctx.op("vector", lambda e, zv=zv, iv=iv: e.tensor_tensor(zv, zv, iv,
                                                                 OP.subtract))
        ctx.drain("vector")
        ctx.op("vector", lambda e, av=av, zv=zv: e.tensor_scalar(av, zv, 2.0, None,
                                                                 OP.mult))
        ctx.drain("vector")
        ctx.op("vector", lambda e, zv=zv, iv=iv, av=av: e.tensor_tensor(zv, iv, av,
                                                                        OP.add))
        ctx.drain("vector")
        ctx.op("vector", lambda e, qv=qv, zblk=zblk, bqb=bqb: e.tensor_tensor(
            qv, zblk, bqb, OP.mult), inc=("st_ve", 1),
            wait=[("st_te", ctx.tile_mark.get(("dq_te", ot % 2), 0))])
        wq_single = wqt[0] is wqt[1]
        ctx.drain("vector")
        vv = ctx.val("st_ve")
        gbase = ctx.tile_mark.get("dq_g", 0)
        for kc in range(nkc):
            g = gbase + kc
            tw = [("st_ve", vv)] if kc == 0 else []
            em = ctx.tile_mark.get(("dqe", g - 2))
            if em:
                tw.append(("st_sc", em))
            ctx.op("tensor", lambda e, kc=kc, g=g, wq_par=wq_par: e.transpose(
                psTr[g % 2][:], wq_par[:, kc * 128:(kc + 1) * 128], identb[:]),
                wait=(tw or None), inc=("st_te", 1))
            tv = ctx.val("st_te")
            ctx.op("scalar", lambda e, kc=kc, o=ot, g=g: e.activation(
                wqT_sb[:, (kc * n_ot + o) * 128:(kc * n_ot + o) * 128 + 128],
                psTr[g % 2][:], AF.Copy), wait=[("st_te", tv)], inc=("st_sc", 1))
            ctx.tile_mark[("dqe", g)] = ctx.val("st_sc")
        ctx.tile_mark["dq_g"] = gbase + nkc
        ctx.tile_mark[("dq_te", ot % 2)] = ctx.val("st_te")
        if wq_single:
            ctx.tile_mark[("dq_te", (ot + 1) % 2)] = ctx.val("st_te")


def ph_upgate(ctx, nc, st, tag, x, ident, identb, w_up, w_gate, up_o, gate_o,
              cc_ld, ccin, ccout, wq_cache=None):
    ctx.tile_mark = {}
    sb = lambda n, s, d=F32: st.enter_context(nc.sbuf_tensor(tag + n, s, d))
    ps = lambda n, s, d=F32: st.enter_context(nc.psum_tensor(tag + n, s, d))
    wqtu_sb = sb("wqtu_sb", [128, 8 * 24 * 128], BF16)
    wqtg_sb = sb("wqtg_sb", [128, 8 * 24 * 128], BF16)
    xt = [sb(f"xt{i}", [128, 8 * TTS]) for i in range(2)]
    xnt = sb("xnt", [128, 8 * TTS])
    sqt = xnt
    xqt = [sb(f"xqt{i}", [128, 8 * TTS], BF16) for i in range(2)]
    mrep = sb("mrep", [128, TTS])
    vrep = sb("vrep", [128, TTS])
    allones = sb("allones", [128, 128])
    onesrow = sb("onesrow", [1, 128])
    idt = sb("idt", [128, 128])
    idtb = sb("idtb", [128, 128], BF16)
    epscol = sb("epscol", [128, 1])
    c0col = sb("c0col", [128, 1])
    s0col = sb("s0col", [128, 1])
    rowbuf = sb("rowbuf", [1, 16])
    stgU = [sb(f"stgU{i}", [128, TTS]) for i in range(2)]
    stgG = [sb(f"stgG{i}", [128, TTS]) for i in range(2)]
    pupc = sb("pupc", [128, 1])
    pgc = sb("pgc", [128, 1])
    tmpc = sb("tmpc", [128, 2])
    prow2 = sb("prow2", [1, 2])
    psMisc = ps("psMisc", [128, 130])
    psTrSt = contextlib.ExitStack()
    psTr = [psTrSt.enter_context(nc.psum_tensor(tag + f"psTr{i}", [128, 128],
                                                BF16)) for i in range(2)]

    if True:
        ctx.op("vector", lambda e: e.memset(allones[:], 1.0))
        ctx.op("vector", lambda e: e.memset(onesrow[:], 1.0))
        ctx.op("vector", lambda e: e.memset(epscol[:], 1e-5))
        ctx.op("vector", lambda e: e.memset(pupc[:], 0.0))
        ctx.op("vector", lambda e: e.memset(pgc[:], 0.0), inc=("st_ve", 1))
        ctx.op("sync", lambda e: e.dma_start(out=idt[:], in_=ident), inc=("ldc", 16))
        ctx.op("sync", lambda e: e.dma_start(out=idtb[:], in_=identb), inc=("ldc", 16))

        ctx.wait("tensor", "ldc", ctx.val("ldc"))

        if wq_cache is None or wq_cache[2] == "store":
            wtb = [sb(f"wtb{i}", [128, D]) for i in range(2)]
            bq = sb("bq", [128, D // WBLK])
            binv = sb("binv", [128, D // WBLK])
            i32w = st.enter_context(nc.sbuf_tensor(tag + "i32w", [128, D], I32))
            i32b = st.enter_context(nc.sbuf_tensor(tag + "i32b", [128, D], I32))
            ztw = sb("ztw", [128, D])
            wqt_s = [sb(f"wqt_s{i}", [128, D], BF16) for i in range(2)]
            emit_dequant(ctx, w_up, 24, D, wtb, bq, binv, i32w, i32b, ztw,
                         wqt_s, idtb, psTr, wqtu_sb)
            emit_dequant(ctx, w_gate, 24, D, wtb, bq, binv, i32w, i32b, ztw,
                         wqt_s, idtb, psTr, wqtg_sb)
            wq_ready = ctx.val("st_sc")
            wq_wait = [("st_sc", wq_ready)]
            if wq_cache is not None:
                ctx.op("sync", lambda e: e.dma_start(
                    out=wq_cache[0], in_=wqtu_sb[:]),
                    wait=[("st_sc", wq_ready)], inc=("stoW", 16))
                ctx.op("sync", lambda e: e.dma_start(
                    out=wq_cache[1], in_=wqtg_sb[:]), inc=("stoW", 16))
        else:
            lvu = ctx.op("sync", lambda e: e.dma_start(
                out=wqtu_sb[:], in_=wq_cache[0]), inc=("ldw0", 16))
            lvg = ctx.op("gpsimd", lambda e: e.dma_start(
                out=wqtg_sb[:], in_=wq_cache[1]), inc=("ldgw", 16))
            wq_ready = 0
            wq_wait = [("ldw0", lvu), ("ldgw", lvg)]
        psTrSt.close()
        psS = ps("psS", [128, 2 * TTS])
        psU = [ps(f"psU{i}", [128, TTS]) for i in range(2)]
        psG = [ps(f"psG{i}", [128, TTS]) for i in range(2)]
        if wq_ready:
            ctx.wait("tensor", "st_sc", wq_ready)
            ctx.wait("vector", "st_sc", wq_ready)
        emit_scale_col(ctx, cc_ld[0], 1, 1, cc_ld[2], QMAX_A, rowbuf, onesrow,
                       psMisc, c0col, s0col, ld_eng="gpsimd",
                       ld_wait=[("cc", cc_ld[1])])

        evac_mark = {}
        dma_mark = {}
        for it in range(NTILE):
            b, tt = divmod(it, T // TTS)
            buf = xt[it % 2]
            xq = xqt[it % 2]
            lsem = "ldx0" if it % 2 == 0 else "ldx1"
            src = ap_(x, b * D * T + tt * TTS, [[T, 128], [128 * T, 8], [1, TTS]])
            ldwait = [("st_ve", ctx.tile_mark.get(("xq", it - 2), 0))]
            lv = ctx.op("sync", lambda e, s=src, bu=buf: e.dma_start(out=bu[:], in_=s),
                        inc=(lsem, 16), wait=ldwait)
            emit_ln(ctx, buf, sqt, psS, mrep, vrep, allones, xnt, 8, TTS, lsem, lv, epscol)
            ctx.op("vector", lambda e: e.tensor_scalar(xnt[:], xnt[:], c0col[:, 0:1],
                                                       MAGIC, OP.mult, OP.add))
            ctx.drain("vector")
            ctx.op("vector", lambda e, xq=xq: e.tensor_scalar(xq[:], xnt[:], MAGIC,
                                                              None, OP.subtract),
                   inc=("st_ve", 1))
            ctx.drain("vector")
            ctx.tile_mark[("xq", it)] = ctx.val("st_ve")
            xq_v = ctx.val("st_ve")

            for oc in range(24):
                g = it * 24 + oc
                for which, psP, stg, out_d, pc in (
                        (0, psU[oc % 2], stgU[oc % 2], up_o, pupc),
                        (1, psG[oc % 2], stgG[oc % 2], gate_o, pgc)):
                    w_sb = wqtg_sb if which else wqtu_sb
                    mmw = [("st_ve", xq_v)] + wq_wait
                    em = evac_mark.get((which, g - 2))
                    if em:
                        mmw.append(("st_sc", em))
                    for kc in range(8):
                        ctx.op("tensor", lambda e, kc=kc, oc=oc, psP=psP, xq=xq, w_sb=w_sb:
                               e.matmul(psP[:], w_sb[:, (kc * 24 + oc) * 128:
                                                     (kc * 24 + oc) * 128 + 128],
                                        xq[:, kc * TTS:(kc + 1) * TTS],
                                        start=(kc == 0), stop=(kc == 7)),
                               wait=(mmw if kc == 0 else None),
                               inc=(("st_te", 1) if kc == 7 else None))
                    tev = ctx.val("st_te")
                    ssem = ("stoG" if which else "stoU") + str(oc % 2)
                    scw = [("st_te", tev)]
                    dm = dma_mark.get((which, g - 2))
                    if dm:
                        scw.append((ssem, dm))
                    vm = ctx.tile_mark.get(("vem", which, g - 2))
                    if vm:
                        scw.append(("st_ve", vm))
                    ctx.op("scalar", lambda e, psP=psP, stg=stg: e.activation(
                        stg[:], psP[:], AF.Copy, scale=s0col[:, 0:1]),
                        wait=scw, inc=("st_sc", 1))
                    evac_mark[(which, g)] = ctx.val("st_sc")
                    scv = ctx.val("st_sc")
                    ctx.op("vector", lambda e, stg=stg, which=which: e.reduce_max(
                        tmpc[:, which:which + 1], stg[:], AX.X,
                        apply_absolute_value=True), wait=[("st_sc", scv)])
                    ctx.drain("vector")
                    ctx.op("vector", lambda e, pc=pc, which=which: e.tensor_tensor(
                        pc[:], pc[:], tmpc[:, which:which + 1], OP.max),
                        inc=("st_ve", 1))
                    ctx.drain("vector")
                    ctx.tile_mark[("vem", which, g)] = ctx.val("st_ve")
                    dst = ap_(out_d, b * FF * T + oc * 128 * T + tt * TTS,
                              [[T, 128], [1, TTS]])
                    ctx.op("sync", lambda e, stg=stg, dst=dst: e.dma_start(
                        out=dst, in_=stg[:]), wait=[("st_sc", scv)], inc=(ssem, 16))
                    dma_mark[(which, g)] = ctx.val(ssem)

        ctx.op("tensor", lambda e: e.transpose(psMisc[0:1, 0:128], pupc[:], idt[:]),
               wait=[("st_ve", ctx.val("st_ve"))], inc=("st_te", 1))
        ctx.op("vector", lambda e: e.reduce_max(prow2[:, 0:1], psMisc[0:1, 0:128], AX.X),
               wait=[("st_te", ctx.val("st_te"))], inc=("st_ve", 1))
        ctx.op("tensor", lambda e: e.transpose(psMisc[0:1, 0:128], pgc[:], idt[:]),
               wait=[("st_ve", ctx.val("st_ve"))], inc=("st_te", 1))
        ctx.op("vector", lambda e: e.reduce_max(prow2[:, 1:2], psMisc[0:1, 0:128], AX.X),
               wait=[("st_te", ctx.val("st_te"))], inc=("st_ve", 1))
        for s in ("stoU0", "stoU1", "stoG0", "stoG1"):
            ctx.wait("sync", s, ctx.val(s))
        return emit_ccmax(ctx, prow2[:, 0:2], ccin, ccout, ctx.val("st_ve"))


def ph_gate(ctx, nc, st, tag, gate, ident, pre3_o, cc_ld, ccin, ccout):
    ctx.tile_mark = {}
    sb = lambda n, s, d=F32: st.enter_context(nc.sbuf_tensor(tag + n, s, d))
    ps = lambda n, s, d=F32: st.enter_context(nc.psum_tensor(tag + n, s, d))
    if True:
        gt = [sb(f"gt{i}", [128, T]) for i in range(2)]
        sg = [sb(f"sg{i}", [128, T]) for i in range(2)]
        onesrow = sb("onesrow", [1, 128])
        idt = sb("idt", [128, 128])
        c2col = sb("c2col", [128, 1])
        s2col = sb("s2col", [128, 1])
        s2bcol = sb("s2bcol", [128, 1])
        rowbuf = sb("rowbuf", [1, 32])
        pcol = sb("pcol", [128, 1])
        tmpc = sb("tmpc", [128, 1])
        prow = sb("prow", [1, 128])
        psMisc = ps("psMisc", [128, 130])

        ctx.op("vector", lambda e: e.memset(onesrow[:], 1.0))
        ctx.op("vector", lambda e: e.memset(prow[:], 0.0))
        ctx.op("vector", lambda e: e.memset(pcol[:], 0.0), inc=("st_ve", 1))
        ctx.op("sync", lambda e: e.dma_start(out=idt[:], in_=ident), inc=("ldc", 16))
        emit_scale_col(ctx, cc_ld[0], 1, 1, cc_ld[2], QMAX_A, rowbuf, onesrow,
                       psMisc, c2col, s2col, ld_eng="gpsimd",
                       ld_wait=[("cc", cc_ld[1])])
        ctx.drain("scalar")
        ctx.op("scalar", lambda e: e.activation(s2bcol[:], s2col[:], AF.Copy,
                                                scale=1.0 / 65536.0), inc=("st_sc", 1))
        ctx.wait("vector", "st_sc", ctx.val("st_sc"))
        ctx.wait("tensor", "ldc", ctx.val("ldc"))

        ntile = B_CORE * (FF // 128)
        for it in range(ntile):
            b, oc = divmod(it, FF // 128)
            buf = gt[it % 2]
            sgb = sg[it % 2]
            lsem = "ldx0" if it % 2 == 0 else "ldx1"
            ssem = "sto0" if it % 2 == 0 else "sto1"
            src = ap_(gate, b * FF * T + oc * 128 * T, [[T, 128], [1, T]])
            lw = [("st_ve", ctx.tile_mark.get(("done", it - 2), 0))]
            lv = ctx.op("sync", lambda e, s=src, bu=buf: e.dma_start(out=bu[:], in_=s),
                        inc=(lsem, 16), wait=lw)
            ctx.op("vector", lambda e, bu=buf: e.tensor_scalar(
                bu[:], bu[:], c2col[:, 0:1], MAGIC, OP.mult, OP.add),
                wait=[(lsem, lv)])
            ctx.drain("vector")
            ctx.op("vector", lambda e, bu=buf: e.tensor_scalar(
                bu[:], bu[:], MAGIC, None, OP.subtract), inc=("st_ve", 1))
            ctx.drain("vector")
            gqv = ctx.val("st_ve")
            smw = [("st_ve", gqv)]
            dm = ctx.tile_mark.get(("dma", it - 2))
            if dm:
                smw.append((ssem, dm))
            vm = ctx.tile_mark.get(("vem", it - 2))
            if vm:
                smw.append(("st_ve", vm))
            ctx.op("scalar", lambda e, bu=buf, sgb=sgb: e.activation(
                sgb[:], bu[:], AF.Sigmoid, scale=s2col[:, 0:1]),
                wait=smw, inc=("st_sc", 1))
            ctx.drain("scalar")
            ctx.op("scalar", lambda e, sgb=sgb: e.activation(
                sgb[:], sgb[:], AF.Copy, scale=65536.0, bias=MAGIC),
                inc=("st_sc", 1))
            ctx.op("vector", lambda e, sgb=sgb: e.tensor_scalar(
                sgb[:], sgb[:], MAGIC, 65535.0, OP.subtract, OP.min),
                wait=[("st_sc", ctx.val("st_sc"))])
            ctx.drain("vector")
            ctx.op("vector", lambda e, sgb=sgb, bu=buf: e.scalar_tensor_tensor(
                sgb[:], sgb[:], s2bcol[:, 0:1], bu[:], OP.mult, OP.mult),
                inc=("st_ve", 1))
            ctx.drain("vector")
            scv = ctx.val("st_ve")
            ctx.tile_mark[("done", it)] = scv
            ctx.op("vector", lambda e, sgb=sgb: e.reduce_max(
                tmpc[:], sgb[:], AX.X, apply_absolute_value=True))
            ctx.drain("vector")
            ctx.op("vector", lambda e: e.tensor_tensor(pcol[:], pcol[:], tmpc[:],
                                                       OP.max), inc=("st_ve", 1))
            ctx.drain("vector")
            ctx.tile_mark[("vem", it)] = ctx.val("st_ve")
            dst = ap_(pre3_o, b * FF * T + oc * 128 * T, [[T, 128], [1, T]])
            ctx.op("sync", lambda e, sgb=sgb, dst=dst: e.dma_start(
                out=dst, in_=sgb[:]), wait=[("st_ve", scv)], inc=(ssem, 16))
            ctx.tile_mark[("dma", it)] = ctx.val(ssem)

        ve = emit_colmax_sb(ctx, pcol, ap_(psMisc, 0, [[130, 1], [1, 128]]), prow,
                            idt)
        for s in ("sto0", "sto1"):
            ctx.wait("sync", s, ctx.val(s))
        return emit_ccmax(ctx, prow[:, 0:2], ccin, ccout, ve)


def ph_prod(ctx, nc, st, tag, pre3, up, ident, pre4_o, cc3, cc1, ccin, ccout):
    ctx.tile_mark = {}
    sb = lambda n, s, d=F32: st.enter_context(nc.sbuf_tensor(tag + n, s, d))
    ps = lambda n, s, d=F32: st.enter_context(nc.psum_tensor(tag + n, s, d))
    if True:
        g2t = [sb(f"g2t{i}", [128, T]) for i in range(2)]
        upt = [sb(f"upt{i}", [128, T]) for i in range(2)]
        onesrow = sb("onesrow", [1, 128])
        idt = sb("idt", [128, 128])
        c3col = sb("c3col", [128, 1])
        s3col = sb("s3col", [128, 1])
        c1col = sb("c1col", [128, 1])
        s1col = sb("s1col", [128, 1])
        rowbuf = sb("rowbuf", [1, 32])
        pcol = sb("pcol", [128, 1])
        tmpc = sb("tmpc", [128, 1])
        prow = sb("prow", [1, 128])
        psMisc = ps("psMisc", [128, 130])

        ctx.op("vector", lambda e: e.memset(onesrow[:], 1.0))
        ctx.op("vector", lambda e: e.memset(prow[:], 0.0))
        ctx.op("vector", lambda e: e.memset(pcol[:], 0.0), inc=("st_ve", 1))
        ctx.op("sync", lambda e: e.dma_start(out=idt[:], in_=ident), inc=("ldc", 16))
        emit_scale_col(ctx, cc3[0], 1, 1, cc3[2], QMAX_A, rowbuf, onesrow,
                       psMisc, c3col, s3col, ld_eng="gpsimd",
                       ld_wait=[("cc", cc3[1])])
        emit_scale_col(ctx, cc1[0], 1, 1, cc1[2], QMAX_A, rowbuf, onesrow,
                       psMisc, c1col, s1col, ld_eng="gpsimd",
                       ld_wait=[("cc", cc1[1])])
        s31col = sb("s31col", [128, 1])
        ctx.op("vector", lambda e: e.tensor_tensor(
            s31col[:], s3col[:], s1col[:], OP.mult), inc=("st_ve", 1))
        ctx.drain("vector")
        ctx.wait("vector", "st_sc", ctx.val("st_sc"))
        ctx.wait("scalar", "st_ve", ctx.val("st_ve"))
        ctx.wait("tensor", "ldc", ctx.val("ldc"))

        ntile = B_CORE * (FF // 128)
        for it in range(ntile):
            b, oc = divmod(it, FF // 128)
            bg = g2t[it % 2]
            bu = upt[it % 2]
            l1 = "ldx0" if it % 2 == 0 else "ldx1"
            l2 = "ldu0" if it % 2 == 0 else "ldu1"
            ssem = "sto0" if it % 2 == 0 else "sto1"
            off = b * FF * T + oc * 128 * T
            lw = [("st_ve", ctx.tile_mark.get(("done", it - 2), 0))]
            dmp = ctx.tile_mark.get(("dma", it - 2))
            if dmp:
                lw.append((ssem, dmp))
            lv1 = ctx.op("sync", lambda e, bg=bg, off=off: e.dma_start(
                out=bg[:], in_=ap_(pre3, off, [[T, 128], [1, T]])),
                inc=(l1, 16), wait=lw)
            lv2 = ctx.op("sync", lambda e, bu=bu, off=off: e.dma_start(
                out=bu[:], in_=ap_(up, off, [[T, 128], [1, T]])),
                inc=(l2, 16))
            ctx.op("vector", lambda e, bg=bg: e.tensor_scalar(
                bg[:], bg[:], c3col[:, 0:1], MAGIC, OP.mult, OP.add),
                wait=[(l1, lv1)])
            ctx.drain("vector")
            ctx.op("vector", lambda e, bg=bg: e.tensor_scalar(
                bg[:], bg[:], MAGIC, None, OP.subtract))
            ctx.drain("vector")
            scw = [(l2, lv2), ("st_sc", ctx.tile_mark.get(("sdone", it - 2), 0))]
            ctx.op("scalar", lambda e, bu=bu: e.activation(
                bu[:], bu[:], AF.Copy, scale=c1col[:, 0:1], bias=MAGIC),
                wait=scw)
            ctx.drain("scalar")
            ctx.op("scalar", lambda e, bu=bu: e.activation(
                bu[:], bu[:], AF.Copy, bias=-MAGIC), inc=("st_sc", 1))
            ctx.drain("scalar")
            ctx.tile_mark[("sdone", it)] = ctx.val("st_sc")
            ctx.op("vector", lambda e, bg=bg, bu=bu: e.scalar_tensor_tensor(
                bg[:], bg[:], s31col[:, 0:1], bu[:], OP.mult, OP.mult),
                wait=[("st_sc", ctx.val("st_sc"))], inc=("st_ve", 1))
            ctx.drain("vector")
            scv = ctx.val("st_ve")
            ctx.tile_mark[("done", it)] = scv
            ctx.op("vector", lambda e, bg=bg: e.reduce_max(
                tmpc[:], bg[:], AX.X, apply_absolute_value=True))
            ctx.drain("vector")
            ctx.op("vector", lambda e: e.tensor_tensor(pcol[:], pcol[:], tmpc[:],
                                                       OP.max), inc=("st_ve", 1))
            ctx.drain("vector")
            dst = ap_(pre4_o, off, [[T, 128], [1, T]])
            dmw = [("st_ve", scv)]
            dm = ctx.tile_mark.get(("dma", it - 2))
            if dm:
                dmw.append((ssem, dm))
            ctx.op("sync", lambda e, bg=bg, dst=dst: e.dma_start(
                out=dst, in_=bg[:]), wait=dmw, inc=(ssem, 16))
            ctx.tile_mark[("dma", it)] = ctx.val(ssem)

        ve = emit_colmax_sb(ctx, pcol, ap_(psMisc, 0, [[130, 1], [1, 128]]), prow,
                            idt)
        for s in ("sto0", "sto1"):
            ctx.wait("sync", s, ctx.val(s))
        return emit_ccmax(ctx, prow[:, 0:2], ccin, ccout, ve)


def emit_ffn_tail(ctx, nc, st, tag, pre4, xres, cc4, w_down, identb_t,
                  out_writer, wd_cache=None):
    sb = lambda n, s, d=F32: st.enter_context(nc.sbuf_tensor(tag + n, s, d))
    wdt_sb = sb("wdt_sb", [128, 24 * 8 * 128], BF16)
    wdb = [sb(f"wdb{i}", [128, FF]) for i in range(2)]
    bqd = sb("bqd", [128, FF // WBLK])
    binvd = sb("binvd", [128, FF // WBLK])
    i32d = st.enter_context(nc.sbuf_tensor(tag + "i32d", [128, FF], I32))
    i32d2 = st.enter_context(nc.sbuf_tensor(tag + "i32d2", [128, FF], I32))
    ztd = sb("ztd", [128, FF])
    wqt_d1 = sb("wqt_d", [128, FF], BF16)
    wqt_d = [wqt_d1, wqt_d1]
    c4col = sb("c4col", [128, 1])
    s4col = sb("s4col", [128, 1])
    rowbuf4 = sb("rowbuf4", [1, 32])
    obuf = [st.enter_context(nc.sbuf_tensor(tag + f"obuf{i}", [128, 24 * TTS],
                                            BF16))
            for i in range(2)]
    p4t = [sb(f"p4t{i}", [128, TTS]) for i in range(2)]
    xrt = [sb(f"xrt{i}", [128, 8 * TTS]) for i in range(2)]
    stgD = [sb(f"stgD{i}", [128, TTS]) for i in range(2)]
    onesrow4 = sb("onesrow4", [1, 128])
    psMisc4 = st.enter_context(nc.psum_tensor(tag + "psMisc4", [128, 2], F32))
    psTr2St = contextlib.ExitStack()
    psTr2 = [psTr2St.enter_context(nc.psum_tensor(tag + f"psTr2{i}", [128, 128],
                                                  BF16)) for i in range(2)]
    ctx.op("vector", lambda e: e.memset(onesrow4[:], 1.0), inc=("st_ve", 1))
    ctx.wait("tensor", "ldc", ctx.val("ldc"))
    if wd_cache is None or wd_cache[1] == "store":
        emit_dequant(ctx, w_down, 8, FF, wdb, bqd, binvd, i32d, i32d2, ztd,
                     wqt_d, identb_t, psTr2, wdt_sb)
        wd_ready = ctx.val("st_sc")
        wd_wait = [("st_sc", wd_ready)]
        if wd_cache is not None:
            ctx.op("sync", lambda e: e.dma_start(
                out=wd_cache[0], in_=wdt_sb[:]),
                wait=[("st_sc", wd_ready)], inc=("stoW", 16))
    else:
        lvd = ctx.op("sync", lambda e: e.dma_start(
            out=wdt_sb[:], in_=wd_cache[0]), inc=("ldw0", 16))
        wd_wait = [("ldw0", lvd)]
    psTr2St.close()
    emit_scale_col(ctx, cc4[0], 1, 1, cc4[2], QMAX_A, rowbuf4, onesrow4,
                   psMisc4, c4col, s4col, extra_mult=0.5, ld_eng="gpsimd",
                   ld_wait=[("cc", cc4[1])])
    psD = [st.enter_context(nc.psum_tensor(tag + f"psD{i}", [128, TTS], F32))
           for i in range(2)]

    for it in range(NTILE):
        b, tt = divmod(it, T // TTS)
        ob = obuf[it % 2]
        xb = xrt[it % 2]
        rsem = "ldr0" if it % 2 == 0 else "ldr1"
        rw = [("st_sc", ctx.tile_mark.get(("ft_xv", it - 2), 0))]
        rv = ctx.op("sync", lambda e, b=b, tt=tt, xb=xb: e.dma_start(
            out=xb[:], in_=ap_(xres, b * D * T + tt * TTS,
                               [[T, 128], [128 * T, 8], [1, TTS]])),
            inc=(rsem, 16), wait=rw)
        for kc in range(24):
            pb = p4t[kc % 2]
            psem = "ldf0" if kc % 2 == 0 else "ldf1"
            lw = [("st_ve", ctx.tile_mark.get(("ft_rnd", it, kc - 2),
                   ctx.tile_mark.get(("ft_rnd", it - 1, 22 + kc), 0)))]
            lv = ctx.op("sync", lambda e, b=b, tt=tt, kc=kc, pb=pb: e.dma_start(
                out=pb[:], in_=ap_(pre4, b * FF * T + kc * 128 * T + tt * TTS,
                                    [[T, 128], [1, TTS]])),
                inc=(psem, 16), wait=lw)
            vw = [(psem, lv)]
            if kc == 0:
                vw.append(("st_te", ctx.tile_mark.get(("ft_te", it - 2), 0)))
            ctx.op("vector", lambda e, pb=pb: e.tensor_scalar(
                pb[:], pb[:], c4col[:, 0:1], MAGIC, OP.mult, OP.add),
                wait=vw)
            ctx.drain("vector")
            ctx.op("vector", lambda e, ob=ob, kc=kc, pb=pb: e.tensor_scalar(
                ob[:, kc * TTS:(kc + 1) * TTS], pb[:], MAGIC, None, OP.subtract),
                inc=("st_ve", 1))
            ctx.drain("vector")
            ctx.tile_mark[("ft_rnd", it, kc)] = ctx.val("st_ve")
        ov = ctx.val("st_ve")
        for mc in range(8):
            psP = psD[mc % 2]
            mmw = [("st_ve", ov)] + wd_wait
            em = ctx.tile_mark.get(("ft_em", it, mc - 2))
            if em is None and it > 0:
                em = ctx.tile_mark.get(("ft_em", it - 1, 6 + mc))
            if em:
                mmw.append(("st_sc", em))
            for kc in range(24):
                ctx.op("tensor", lambda e, kc=kc, mc=mc, psP=psP, ob=ob: e.matmul(
                    psP[:], wdt_sb[:, (kc * 8 + mc) * 128:(kc * 8 + mc) * 128 + 128],
                    ob[:, kc * TTS:(kc + 1) * TTS],
                    start=(kc == 0), stop=(kc == 23)),
                    wait=(mmw if kc == 0 else None),
                    inc=(("st_te", 1) if kc == 23 else None))
            tev = ctx.val("st_te")
            stg = stgD[mc % 2]
            scw = [("st_te", tev), (rsem, rv)]
            dm = ctx.tile_mark.get(("ft_dm", it, mc - 2))
            if dm is None and it > 0:
                dm = ctx.tile_mark.get(("ft_dm", it - 1, 6 + mc))
            if dm:
                scw.append(dm)
            ctx.op("vector", lambda e, psP=psP, stg=stg, xb=xb, mc=mc:
                   e.scalar_tensor_tensor(stg[:], psP[:], s4col[:, 0:1],
                                          xb[:, mc * TTS:(mc + 1) * TTS],
                                          OP.mult, OP.add),
                   wait=scw, inc=("st_sc", 1))
            ctx.drain("vector")
            ctx.tile_mark[("ft_em", it, mc)] = ctx.val("st_sc")
            scv = ctx.val("st_sc")
            out_writer(ctx, it, b, tt, mc, stg, scv)
        ctx.tile_mark[("ft_xv", it)] = ctx.val("st_sc")
        ctx.tile_mark[("ft_te", it)] = ctx.val("st_te")


def ph_tail(ctx, nc, st, pre4, xres, cc4, w_down, identb, out, wd_cache):
    ctx.tile_mark = {}
    idtb = st.enter_context(nc.sbuf_tensor("t2idtb", [128, 128], BF16))
    ctx.op("sync", lambda e: e.dma_start(out=idtb[:], in_=identb),
           inc=("ldc", 16))

    def writer(ctx, it, b, tt, mc, stg, scv):
        ssem = "sto0" if (it * 8 + mc) % 2 == 0 else "sto1"
        dst = ap_(out, b * D * T + mc * 128 * T + tt * TTS, [[T, 128], [1, TTS]])
        ctx.op("sync", lambda e, stg=stg, dst=dst: e.dma_start(
            out=dst, in_=stg[:]), wait=[("st_sc", scv)], inc=(ssem, 16))
        ctx.tile_mark[("ft_dm", it, mc)] = (ssem, ctx.val(ssem))

    emit_ffn_tail(ctx, nc, st, "t2", pre4, xres, cc4, w_down, idtb, writer,
                  wd_cache=wd_cache)
    for s in ("sto0", "sto1"):
        ctx.wait("sync", s, ctx.val(s))


def ph_P4(ctx, nc, st, x0, pre4, cc4, w_down, ident, identb, w_inT, w_dtT,
          w_bcT, w_outT, cdiag, a_jln, dcol, cbcol, bdtcol, nwcol, rscal,
          x2_o, x1_s, h2_s, mg_s, xc_s, dt_s, bcps_d, ccin, ccout, wd_cache):
    ctx.tile_mark = {}
    sb = lambda n, s, d=F32: st.enter_context(nc.sbuf_tensor(n, s, d))
    ps = lambda n, s, d=F32: st.enter_context(nc.psum_tensor(n, s, d))
    if True:
        idt = sb("idt", [128, 128])
        idtb = sb("idtb", [128, 128], BF16)
        allones = sb("allones", [128, 128])
        onesrow = sb("onesrow", [1, 128])
        epscol = sb("epscol", [128, 1])
        eps6col = sb("eps6col", [128, 1])
        dcol_sb = sb("dcol_sb", [128, 16])
        cbcol_sb = sb("cbcol_sb", [128, 16])
        bdtcol_sb = sb("bdtcol_sb", [128, 16])
        nwcol_sb = sb("nwcol_sb", [128, 8])
        atan_sb = sb("atan_sb", [128, 16 * DS])
        sctanb = sb("sctanb", [128, 1])
        rscol = sb("rscol", [128, 1])
        rowmisc = sb("rowmisc", [1, 8])
        pcol = sb("pcol", [128, 1])
        prow = sb("prow", [1, 128])
        psMisc = ps("psMisc", [128, 130])

        ctx.op("vector", lambda e: e.memset(allones[:], 1.0))
        ctx.op("vector", lambda e: e.memset(onesrow[:], 1.0))
        ctx.op("vector", lambda e: e.memset(epscol[:], 1e-5))
        ctx.op("vector", lambda e: e.memset(eps6col[:], 1e-6))
        ctx.op("vector", lambda e: e.memset(sctanb[:], -SC_C * SC_IN))
        ctx.op("vector", lambda e: e.memset(prow[:], 0.0))
        ctx.op("vector", lambda e: e.memset(pcol[:], 0.0), inc=("st_ve", 1))
        for (dst, srcd) in ((idt, ident), (idtb, identb), (dcol_sb, dcol),
                            (cbcol_sb, cbcol), (bdtcol_sb, bdtcol),
                            (nwcol_sb, nwcol)):
            ctx.op("sync", lambda e, dst=dst, srcd=srcd: e.dma_start(
                out=dst[:], in_=srcd), inc=("ldc", 16))
        ctx.op("sync", lambda e: e.dma_start(out=atan_sb[:], in_=a_jln),
               inc=("ldc", 16))
        ctx.op("sync", lambda e: e.dma_start(out=rowmisc[:, 0:1], in_=rscal),
               inc=("ldc", 16))
        lcv = ctx.val("ldc")
        ctx.op("scalar", lambda e: e.activation(atan_sb[:], atan_sb[:], AF.Tanh,
                                                scale=SCA_IN), wait=[("ldc", lcv)])
        ctx.drain("scalar")
        ctx.op("scalar", lambda e: e.activation(atan_sb[:], atan_sb[:], AF.Exp,
                                                scale=SCA_OUT), inc=("st_sc", 1))
        ctx.drain("scalar")
        ctx.op("vector", lambda e: e.tensor_scalar(atan_sb[:], atan_sb[:],
                                                   -1.0 / 16.0, None, OP.mult),
               wait=[("st_sc", ctx.val("st_sc"))], inc=("st_ve", 1))
        ctx.drain("vector")
        a_ready = ctx.val("st_ve")
        ctx.op("scalar", lambda e: e.activation(rowmisc[:, 1:2], rowmisc[:, 0:1],
                                                AF.Sigmoid),
               wait=[("ldc", ctx.val("ldc"))])
        ctx.drain("scalar")
        ctx.op("scalar", lambda e: e.activation(rowmisc[:, 1:2], rowmisc[:, 1:2],
                                                AF.Copy, scale=0.5), inc=("st_sc", 1))
        ctx.op("tensor", lambda e: e.matmul(
            ap_(psMisc, 128, [[130, 128], [1, 1]]), onesrow[:], rowmisc[:, 1:2],
            start=True, stop=True),
            wait=[("st_sc", ctx.val("st_sc")), ("ldc", ctx.val("ldc")),
                  ("st_ve", 1)],
            inc=("st_te", 1))
        ctx.op("scalar", lambda e: e.activation(rscol[:],
                                                ap_(psMisc, 128, [[130, 128], [1, 1]]),
                                                AF.Copy),
               wait=[("st_te", ctx.val("st_te"))], inc=("st_sc", 1))

        phA = contextlib.ExitStack()

        def writer(ctx, it, b, tt, mc, stg, scv):
            ssem = "stoX0" if (it * 8 + mc) % 2 == 0 else "stoX1"
            dst = ap_(x1_s, b * D * T + mc * 128 * T + tt * TTS, [[T, 128], [1, TTS]])
            ctx.op("sync", lambda e, stg=stg, dst=dst: e.dma_start(
                out=dst, in_=stg[:]), wait=[("st_sc", scv)], inc=(ssem, 16))
            ctx.tile_mark[("ft_dm", it, mc)] = (ssem, ctx.val(ssem))

        emit_ffn_tail(ctx, nc, phA, "t1", pre4, x0, cc4, w_down, idtb, writer,
                      wd_cache=wd_cache)
        ctx.wait("sync", "stoX0", ctx.val("stoX0"))
        ctx.wait("sync", "stoX1", ctx.val("stoX1"))
        phA_sc = ctx.val("st_sc")
        phA_te = ctx.val("st_te")
        phA.close()

        phB = contextlib.ExitStack()
        sbB = lambda n, s, d=F32: phB.enter_context(nc.sbuf_tensor(n, s, d))
        w_in_sb = sbB("w_in_sb", [128, 8 * 32 * 128], BF16)
        cdiag_sb = sbB("cdiag_sb", [128, 16 * 4 * 128], BF16)
        x1t = [sbB(f"x1t{i}", [128, 8 * TTS]) for i in range(2)]
        h2t = sbB("h2t", [128, 8 * TTS])
        xnb = sbB("xnb", [128, 8 * TTS], BF16)
        xbr = [sbB(f"xbr{i}", [128, 16 * (TTS + 4)], BF16) for i in range(2)]
        mrep = sbB("mrepB", [128, TTS])
        vrep = sbB("vrepB", [128, TTS])
        stgB = [sbB(f"stgB{i}", [128, TTS]) for i in range(2)]
        psSB = phB.enter_context(nc.psum_tensor("psSB", [128, 2 * TTS], F32))
        psI = [phB.enter_context(nc.psum_tensor(f"psI{i}", [128, TTS], F32))
               for i in range(2)]

        ctx.wait("gpsimd", "st_te", phA_te)
        ctx.wait("gpsimd", "st_sc", phA_sc)
        ctx.wait("gpsimd", "stoX0", ctx.val("stoX0"))
        ctx.wait("gpsimd", "stoX1", ctx.val("stoX1"))
        ctx.op("gpsimd", lambda e: e.dma_start(out=w_in_sb[:], in_=w_inT),
               inc=("ldgw", 16))
        ctx.op("gpsimd", lambda e: e.dma_start(out=cdiag_sb[:], in_=cdiag),
               inc=("ldgw", 16))
        wB_ready = ctx.val("ldgw")
        ctx.wait("tensor", "ldgw", wB_ready)
        ctx.wait("vector", "st_sc", phA_sc)
        ctx.wait("scalar", "st_te", phA_te)
        ctx.wait("tensor", "st_sc", phA_sc)

        for it in range(NTILE):
            b, tt = divmod(it, T // TTS)
            xb = x1t[it % 2]
            xr = xbr[it % 2]
            lsem = "ldx0" if it % 2 == 0 else "ldx1"
            lw = [("st_sc", ctx.tile_mark.get(("phb_done", it - 2), 0)),
                  ("stoX0", ctx.val("stoX0")), ("stoX1", ctx.val("stoX1"))]
            lv = ctx.op("sync", lambda e, b=b, tt=tt, xb=xb: e.dma_start(
                out=xb[:], in_=ap_(x1_s, b * D * T + tt * TTS,
                                   [[T, 128], [128 * T, 8], [1, TTS]])),
                inc=(lsem, 16), wait=lw)
            for smn in ("stoH",):
                pv = ctx.tile_mark.get(("phb_" + smn, it - 1))
                if pv:
                    ctx.wait("scalar", smn, pv)
            emit_ln(ctx, xb, h2t, psSB, mrep, vrep, allones, h2t, 8, TTS,
                    lsem, lv, epscol)
            h2v = ctx.val("st_ve")
            ctx.op("sync", lambda e, b=b, tt=tt: e.dma_start(
                out=ap_(h2_s, b * D * T + tt * TTS, [[T, 128], [128 * T, 8], [1, TTS]]),
                in_=h2t[:]), wait=[("st_ve", h2v)], inc=("stoH", 16))
            ctx.tile_mark[("phb_stoH", it)] = ctx.val("stoH")
            ctx.op("scalar", lambda e, xb=xb: e.activation(xb[:], h2t[:], AF.Square),
                   wait=[("st_ve", h2v)], inc=("st_sc", 1))
            for j in range(8):
                ctx.op("tensor", lambda e, j=j, xb=xb: e.matmul(
                    psSB[:, 0:TTS], allones[:], xb[:, j * TTS:(j + 1) * TTS],
                    start=(j == 0), stop=(j == 7)),
                    wait=([("st_sc", ctx.val("st_sc"))] if j == 0 else None),
                    inc=(("st_te", 1) if j == 7 else None))
            ctx.op("scalar", lambda e: e.activation(vrep[:], psSB[:, 0:TTS], AF.Copy,
                                                    scale=1.0 / D),
                   wait=[("st_te", ctx.val("st_te"))])
            ctx.drain("scalar")
            ctx.op("scalar", lambda e: e.activation(vrep[:], vrep[:], AF.Sqrt,
                                                    bias=eps6col[:]), inc=("st_sc", 1))
            ctx.op("vector", lambda e: e.reciprocal(vrep[:], vrep[:]),
                   wait=[("st_sc", ctx.val("st_sc"))])
            ctx.drain("vector")
            nwb = ap_(nwcol_sb, 0, [[8, 128], [1, 8], [0, TTS]])
            vb = ap_(vrep, 0, [[TTS, 128], [0, 8], [1, TTS]])
            ctx.op("vector", lambda e, nwb=nwb: e.tensor_tensor(h2t[:], h2t[:], nwb,
                                                                OP.mult),
                   wait=[("stoH", ctx.tile_mark.get(("phb_stoH", it), 0))])
            ctx.drain("vector")
            ctx.op("vector", lambda e, vb=vb: e.tensor_tensor(h2t[:], h2t[:], vb,
                                                              OP.mult))
            ctx.drain("vector")
            ctx.op("vector", lambda e: e.tensor_copy(xnb[:], h2t[:]), inc=("st_ve", 1))
            ctx.drain("vector")
            xnv = ctx.val("st_ve")

            for mc in range(32):
                psP = psI[mc % 2]
                mmw = [("st_ve", xnv)]
                em = ctx.tile_mark.get(("phb_em", it, mc - 2))
                if em is None and it > 0:
                    em = ctx.tile_mark.get(("phb_em", it - 1, 30 + mc))
                if em:
                    mmw.append(("st_sc", em))
                for kc in range(8):
                    ctx.op("tensor", lambda e, kc=kc, mc=mc, psP=psP: e.matmul(
                        psP[:], w_in_sb[:, (kc * 32 + mc) * 128:(kc * 32 + mc) * 128 + 128],
                        xnb[:, kc * TTS:(kc + 1) * TTS],
                        start=(kc == 0), stop=(kc == 7)),
                        wait=(mmw if kc == 0 else None),
                        inc=(("st_te", 1) if kc == 7 else None))
                tev = ctx.val("st_te")
                if mc < 16:
                    xw = [("st_te", tev)]
                    cm = ctx.tile_mark.get(("phb_conv_te", it - 2))
                    if cm:
                        xw.append(("st_te", cm))
                    ctx.op("scalar", lambda e, psP=psP, xr=xr, mc=mc: e.activation(
                        xr[:, mc * (TTS + 4) + 4:mc * (TTS + 4) + 4 + TTS], psP[:],
                        AF.Copy), wait=xw, inc=("st_sc", 1))
                else:
                    stg = stgB[mc % 2]
                    scw = [("st_te", tev)]
                    dm = ctx.tile_mark.get(("stg_last", mc % 2))
                    if dm:
                        scw.append(dm)
                    ctx.op("scalar", lambda e, psP=psP, stg=stg: e.activation(
                        stg[:], psP[:], AF.Copy), wait=scw, inc=("st_sc", 1))
                    gsem = "stoG0" if mc % 2 == 0 else "stoG1"
                    ctx.op("sync", lambda e, stg=stg, b=b, tt=tt, mc=mc: e.dma_start(
                        out=ap_(mg_s, b * DI * T + (mc - 16) * 128 * T + tt * TTS,
                                [[T, 128], [1, TTS]]), in_=stg[:]),
                        wait=[("st_sc", ctx.val("st_sc"))], inc=(gsem, 16))
                    ctx.tile_mark[("stg_last", mc % 2)] = (gsem, ctx.val(gsem))
                ctx.tile_mark[("phb_em", it, mc)] = ctx.val("st_sc")
            xbr_v = ctx.val("st_sc")
            if tt == 0:
                hal = ap_(xr, 0, [[16 * (TTS + 4), 128], [TTS + 4, 16], [1, 4]])
                ctx.op("vector", lambda e, hal=hal: e.memset(hal, 0.0),
                       wait=[("st_sc", xbr_v)], inc=("st_ve", 1))
            else:
                xrp = xbr[(it - 1) % 2]
                dst = ap_(xr, 0, [[16 * (TTS + 4), 128], [TTS + 4, 16], [1, 4]])
                srch = ap_(xrp, TTS, [[16 * (TTS + 4), 128], [TTS + 4, 16], [1, 4]])
                ctx.op("vector", lambda e, dst=dst, srch=srch: e.tensor_copy(dst, srch),
                       wait=[("st_sc", xbr_v)], inc=("st_ve", 1))
            ctx.drain("vector")
            halo_v = ctx.val("st_ve")

            for j in range(16):
                psP = psI[j % 2]
                mmw = [("st_ve", halo_v), ("st_sc", ctx.val("st_sc"))]
                for k in range(4):
                    ctx.op("tensor", lambda e, j=j, k=k, psP=psP, xr=xr: e.matmul(
                        psP[:], cdiag_sb[:, (j * 4 + k) * 128:(j * 4 + k) * 128 + 128],
                        xr[:, j * (TTS + 4) + k + 1:j * (TTS + 4) + k + 1 + TTS],
                        start=(k == 0), stop=(k == 3)),
                        wait=(mmw if k == 0 else None),
                        inc=(("st_te", 1) if k == 3 else None))
                tev = ctx.val("st_te")
                stg = stgB[j % 2]
                scw = [("st_te", tev)]
                cdm = ctx.tile_mark.get(("stg_last", j % 2))
                if cdm:
                    scw.append(cdm)
                ctx.op("scalar", lambda e, psP=psP, j=j, stg=stg: e.activation(
                    stg[:], psP[:], AF.Silu,
                    bias=cbcol_sb[:, j:j + 1]), wait=scw, inc=("st_sc", 1))
                csem = "stoC0" if j % 2 == 0 else "stoC1"
                ctx.op("sync", lambda e, b=b, tt=tt, j=j, stg=stg: e.dma_start(
                    out=ap_(xc_s, b * DI * T + j * 128 * T + tt * TTS,
                            [[T, 128], [1, TTS]]), in_=stg[:]),
                    wait=[("st_sc", ctx.val("st_sc"))], inc=(csem, 16))
                ctx.tile_mark[("stg_last", j % 2)] = (csem, ctx.val(csem))
            ctx.tile_mark[("phb_conv_te", it)] = ctx.val("st_te")
            ctx.tile_mark[("phb_stoC", it)] = (ctx.val("stoC0"), ctx.val("stoC1"))
            ctx.tile_mark[("phb_done", it)] = ctx.val("st_sc")


        phB1_sc = ctx.val("st_sc")
        phB1_ve = ctx.val("st_ve")
        phB1_te = ctx.val("st_te")
        phB.close()

        phB2 = contextlib.ExitStack()
        sbB2 = lambda n, s, d=F32: phB2.enter_context(nc.sbuf_tensor(n, s, d))
        w_dt_sb = sbB2("w_dt_sb", [128, 16 * 16 * 128], BF16)
        w_bc_sb = sbB2("w_bc_sb", [128, 16 * 32], BF16)
        bcstg = [sbB2(f"bcstg{i}", [32, TTS]) for i in range(2)]
        xct = [sbB2(f"xct{i}", [128, 16 * TTS]) for i in range(2)]
        xcb = sbB2("xcb", [128, 16 * TTS], BF16)
        dtt = sbB2("dtt", [128, 16 * TTS])
        psI2 = [phB2.enter_context(nc.psum_tensor(f"psI2{i}", [128, TTS], F32))
                for i in range(2)]
        psBC = phB2.enter_context(nc.psum_tensor("psBC", [32, TTS], F32))
        ctx.wait("gpsimd", "st_sc", phB1_sc)
        ctx.wait("gpsimd", "st_ve", phB1_ve)
        ctx.op("gpsimd", lambda e: e.dma_start(out=w_dt_sb[:], in_=w_dtT),
               inc=("ldgw", 16))
        ctx.op("gpsimd", lambda e: e.dma_start(out=w_bc_sb[:], in_=w_bcT),
               inc=("ldgw", 16))
        wB2_ready = ctx.val("ldgw")
        ctx.wait("tensor", "ldgw", wB2_ready)
        ctx.wait("tensor", "st_sc", phB1_sc)
        ctx.wait("vector", "st_sc", phB1_sc)
        ctx.wait("scalar", "st_ve", phB1_ve)
        for eng in ("scalar", "vector", "tensor"):
            for smn in ("stoC0", "stoC1", "stoG0", "stoG1", "stoH"):
                ctx.wait(eng, smn, ctx.val(smn))

        for it in range(NTILE):
            b, tt = divmod(it, T // TTS)
            xc = xct[it % 2]
            lsem = "lds0" if it % 2 == 0 else "lds1"
            lw = [("st_ve", ctx.tile_mark.get(("b2_done", it - 2), 0))]
            c01 = ctx.tile_mark.get(("phb_stoC", it))
            if c01:
                lw += [("stoC0", c01[0]), ("stoC1", c01[1])]
            lv = ctx.op("sync", lambda e, b=b, tt=tt, xc=xc: e.dma_start(
                out=xc[:], in_=ap_(xc_s, b * DI * T + tt * TTS,
                                   [[T, 128], [128 * T, 16], [1, TTS]])),
                inc=(lsem, 16), wait=lw)
            pdm = ctx.tile_mark.get(("phb_stoD", it - 1))
            vw = [(lsem, lv), ("st_sc", ctx.tile_mark.get(("b2_bc", it - 1), 0))]
            if pdm:
                vw.append(("stoD", pdm))
            ctx.op("vector", lambda e, xc=xc: e.tensor_copy(xcb[:], xc[:]),
                   wait=vw, inc=("st_ve", 1))
            ctx.drain("vector")
            xcb_v = ctx.val("st_ve")

            for mc in range(16):
                psP = psI2[mc % 2]
                mmw = [("st_ve", xcb_v)]
                em = ctx.tile_mark.get(("b2_em", it, mc - 2))
                if em is None and it > 0:
                    em = ctx.tile_mark.get(("b2_em", it - 1, mc + 14))
                if em:
                    mmw.append(("st_sc", em))
                for kc in range(16):
                    ctx.op("tensor", lambda e, kc=kc, mc=mc, psP=psP: e.matmul(
                        psP[:], w_dt_sb[:, (kc * 16 + mc) * 128:(kc * 16 + mc) * 128 + 128],
                        xcb[:, kc * TTS:(kc + 1) * TTS],
                        start=(kc == 0), stop=(kc == 15)),
                        wait=(mmw if kc == 0 else None),
                        inc=(("st_te", 1) if kc == 15 else None))
                tev = ctx.val("st_te")
                ctx.op("scalar", lambda e, psP=psP, mc=mc: e.activation(
                    dtt[:, mc * TTS:(mc + 1) * TTS], psP[:], AF.Exp,
                    bias=bdtcol_sb[:, mc:mc + 1]), wait=[("st_te", tev)],
                    inc=("st_sc", 1))
                ctx.tile_mark[("b2_em", it, mc)] = ctx.val("st_sc")
            ctx.drain("scalar")
            ctx.op("scalar", lambda e: e.activation(dtt[:], dtt[:], AF.Ln, bias=1.0))
            ctx.drain("scalar")
            ctx.op("scalar", lambda e: e.activation(dtt[:], dtt[:], AF.Tanh,
                                                    bias=sctanb[:], scale=SC_IN),
                   inc=("st_sc", 1))
            ctx.drain("scalar")
            ctx.op("vector", lambda e: e.tensor_scalar(dtt[:], dtt[:], SC_OUT, SC_C,
                                                       OP.mult, OP.add),
                   wait=[("st_sc", ctx.val("st_sc"))], inc=("st_ve", 1))
            ctx.drain("vector")
            dtv = ctx.val("st_ve")
            ctx.op("sync", lambda e, b=b, tt=tt: e.dma_start(
                out=ap_(dt_s, b * DI * T + tt * TTS, [[T, 128], [128 * T, 16], [1, TTS]]),
                in_=dtt[:]), wait=[("st_ve", dtv)], inc=("stoD", 16))
            ctx.tile_mark[("phb_stoD", it)] = ctx.val("stoD")

            mmw = [("st_ve", xcb_v), ("st_sc", ctx.tile_mark.get(("b2_bc", it - 1), 0))]
            for kc in range(16):
                ctx.op("tensor", lambda e, kc=kc: e.matmul(
                    psBC[:], w_bc_sb[:, kc * 32:(kc + 1) * 32],
                    xcb[:, kc * TTS:(kc + 1) * TTS],
                    start=(kc == 0), stop=(kc == 15)),
                    wait=(mmw if kc == 0 else None),
                    inc=(("st_te", 1) if kc == 15 else None))
            bcs = bcstg[it % 2]
            bsem = "stoB0" if it % 2 == 0 else "stoB1"
            bw = [("st_te", ctx.val("st_te"))]
            bdm = ctx.tile_mark.get(("b2_bcs", it - 2))
            if bdm:
                bw.append((bsem, bdm))
            ctx.op("scalar", lambda e, bcs=bcs: e.activation(
                bcs[:], psBC[:], AF.Copy), wait=bw, inc=("st_sc", 1))
            ctx.tile_mark[("b2_bc", it)] = ctx.val("st_sc")
            ctx.op("sync", lambda e, b=b, tt=tt, bcs=bcs: e.dma_start(
                out=ap_(bcps_d, b * T + tt * TTS, [[B_CORE * T, 32], [1, TTS]]),
                in_=bcs[:]),
                wait=[("st_sc", ctx.val("st_sc"))], inc=(bsem, 16))
            ctx.tile_mark[("b2_bcs", it)] = ctx.val(bsem)
            ctx.tile_mark[("b2_done", it)] = ctx.val("st_ve")

        phB_sc = ctx.val("st_sc")
        phB_ve = ctx.val("st_ve")
        phB_te = ctx.val("st_te")
        phB2.close()
        for s in ("stoH", "stoC0", "stoC1", "stoD", "stoG0", "stoG1"):
            ctx.wait("sync", s, ctx.val(s))
        emit_scan(ctx, nc, st, locals())
        ve = emit_colmax_sb(ctx, pcol, ap_(psMisc, 0, [[130, 1], [1, 128]]),
                            prow, idt)
        for s in ("sto0", "sto1"):
            ctx.wait("sync", s, ctx.val(s))
        return emit_ccmax(ctx, prow[:, 0:2], ccin, ccout, ve)


def emit_scan(ctx, nc, st, env):
    e = env
    dt_s, xc_s, mg_s, x1_s, h2_s, x2_o = (e["dt_s"], e["xc_s"], e["mg_s"],
                                          e["x1_s"], e["h2_s"], e["x2_o"])
    w_outT, bcps_d = e["w_outT"], e["bcps_d"]
    atan_sb, dcol_sb, allones = e["atan_sb"], e["dcol_sb"], e["allones"]
    rscol, psMisc, pcol, epscol = e["rscol"], e["psMisc"], e["pcol"], e["epscol"]

    SEG = 10
    NSEG = 205
    SPB = 4
    NBLK = 52
    SL = 512 * SEG

    def slen(s):
        return 8 if s == NSEG - 1 else 10

    def bwid(i):
        return 8 if i == NBLK - 1 else 40

    sbS = lambda n, s, d=F32: st.enter_context(nc.sbuf_tensor(n, s, d))
    w_out_sb = sbS("w_out_sb", [128, 16 * 8 * 128], BF16)
    dAb = [sbS(f"dAb{i}", [128, SL]) for i in range(2)]
    wb = [sbS(f"wbb{i}", [128, SL]) for i in range(2)]
    hb = sbS("hbb", [128, SL])
    dtc = [sbS(f"dtc{i}", [128, 16 * 2 * 40]) for i in range(2)]
    xcc = [sbS(f"xcc{i}", [128, 16 * 2 * 40]) for i in range(2)]
    yring = sbS("yring", [128, 16 * 2 * 40])
    yscr = sbS("yscr", [128, 2 * 16 * 10])
    brepblk = [sbS(f"brepblk{i}", [128, 2 * 16 * 40]) for i in range(2)]
    crepblk = [sbS(f"crepblk{i}", [128, 2 * 16 * 40]) for i in range(2)]
    dA0 = [sbS(f"dA0_{i}", [128, 512]) for i in range(2)]
    tmpc = sbS("tmpc", [128, 512])
    sqscr = sbS("sqscr", [128, 256])
    sq2 = sbS("sq2", [128, 2])
    nrmt = sbS("nrmt", [128, 2])
    mgb = sbS("mgb", [128, 16 * 80])
    obuf = sbS("obuf", [128, 16 * 80], BF16)
    x1d = [sbS(f"x1d{i}", [128, 80]) for i in range(2)]
    h2d = [sbS(f"h2d{i}", [128, 80]) for i in range(2)]
    x2blk = sbS("x2blk", [128, 8 * 80])
    sqb = sbS("sqb", [128, 8 * 80])
    mrepD = sbS("mrepD", [128, 80])
    vrepD = sbS("vrepD", [128, 80])
    tmax16 = sbS("tmax16", [128, 16])
    tmax = sbS("tmax", [128, 1])
    psD = [st.enter_context(nc.psum_tensor(f"psDo{i}", [128, 512], F32))
           for i in range(2)]
    psS2 = st.enter_context(nc.psum_tensor("psS2", [128, 1024], F32))

    mk = ctx.tile_mark
    phB_sc, phB_ve, phB_te = e["phB_sc"], e["phB_ve"], e["phB_te"]
    ctx.wait("gpsimd", "st_sc", phB_sc)
    ctx.wait("gpsimd", "st_ve", phB_ve)
    for smn in ("stoG0", "stoG1", "stoX0", "stoX1", "stoH"):
        ctx.wait("gpsimd", smn, ctx.val(smn))
    ctx.op("gpsimd", lambda e2: e2.dma_start(out=w_out_sb[:], in_=w_outT),
           inc=("ldgw", 16))
    wo_ready = ctx.val("ldgw")
    ctx.wait("tensor", "st_sc", phB_sc)
    ctx.wait("vector", "st_sc", phB_sc)
    ctx.wait("scalar", "st_ve", phB_ve)
    ctx.wait("scalar", "st_te", phB_te)
    for eng in ("tensor", "vector", "scalar", "gpsimd"):
        ctx.wait(eng, "stoB0", ctx.val("stoB0"))
        ctx.wait(eng, "stoB1", ctx.val("stoB1"))
        ctx.wait(eng, "stoD", ctx.val("stoD"))

    def emit_block_loads(i):
        W = bwid(i)
        tb0 = 40 * i
        lsem = "lds0" if i % 2 == 0 else "lds1"
        lw = []
        if i > 1:
            lw.append(("st_ve", mk[("wtt", SPB * (i - 1) - 1)]))
            lw.append(("st_ve", mk[("ytt", SPB * (i - 1) - 1)]))
            lw.append(mk[("blkload", i - 2)])
        if i > 2:
            lw.append(("st_ve", mk[("dxz", i - 2)]))
        first = [True]

        def w_():
            if first[0]:
                first[0] = False
                return lw
            return None
        dc = dtc[i % 2]
        xc = xcc[i % 2]
        for b in range(2):
            ctx.op("sync", lambda e2, b=b, tb0=tb0, W=W, dc=dc: e2.dma_start(
                out=ap_(dc, b * 40, [[1280, 128], [80, 16], [1, W]]),
                in_=ap_(dt_s, b * DI * T + tb0, [[T, 128], [128 * T, 16], [1, W]])),
                inc=(lsem, 16), wait=w_())
            ctx.op("sync", lambda e2, b=b, tb0=tb0, W=W, xc=xc: e2.dma_start(
                out=ap_(xc, b * 40, [[1280, 128], [80, 16], [1, W]]),
                in_=ap_(xc_s, b * DI * T + tb0, [[T, 128], [128 * T, 16], [1, W]])),
                inc=(lsem, 16))
        br = brepblk[i % 2]
        cr = crepblk[i % 2]
        for n in range(16):
            ctx.op("sync", lambda e2, n=n, tb0=tb0, W=W, br=br: e2.dma_start(
                out=ap_(br, n * 40, [[1280, 128], [640, 2], [1, W]]),
                in_=ap_(bcps_d, n * (B_CORE * T) + tb0, [[0, 128], [T, 2], [1, W]])),
                inc=(lsem, 16))
            ctx.op("sync", lambda e2, n=n, tb0=tb0, W=W, cr=cr: e2.dma_start(
                out=ap_(cr, n * 40, [[1280, 128], [640, 2], [1, W]]),
                in_=ap_(bcps_d, (16 + n) * (B_CORE * T) + tb0,
                        [[0, 128], [T, 2], [1, W]])),
                inc=(lsem, 16))
        mk[("blkload", i)] = (lsem, ctx.val(lsem))

    def emit_prep(s):
        i = s // SPB
        so = (s - SPB * i) * SEG
        L = slen(s)
        pb = s % 2
        lsem, lval = mk[("blkload", i)]
        da = dAb[pb]
        wv = wb[pb]
        qw = [(lsem, lval)]
        if s >= 2:
            qw.append(("st_sc", mk[("exp0", s - 2)]))
        for b in range(2):
            ctx.op("vector", lambda e2, b=b, so=so, L=L, da=da: e2.tensor_tensor(
                ap_(da, b * 2560, [[SL, 128], [160, 16], [10, 16], [1, L]]),
                ap_(dtc[i % 2], b * 40 + so,
                    [[1280, 128], [80, 16], [0, 16], [1, L]]),
                ap_(atan_sb, 0, [[256, 128], [16, 16], [1, 16], [0, L]]),
                OP.mult), wait=(qw if b == 0 else None),
                inc=(("st_ve", 1) if b == 1 else None))
        ctx.drain("vector")
        mk[("qtt", s)] = ctx.val("st_ve")
        ctx.op("vector", lambda e2, so=so, L=L, i=i: e2.tensor_tensor(
            ap_(dtc[i % 2], so, [[1280, 128], [80, 16], [40, 2], [1, L]]),
            ap_(dtc[i % 2], so, [[1280, 128], [80, 16], [40, 2], [1, L]]),
            ap_(xcc[i % 2], so, [[1280, 128], [80, 16], [40, 2], [1, L]]),
            OP.mult), inc=("st_ve", 1))
        ctx.drain("vector")
        for b in range(2):
            ctx.op("vector", lambda e2, b=b, so=so, L=L, wv=wv, i=i:
                   e2.tensor_tensor(
                ap_(wv, b * 2560, [[SL, 128], [160, 16], [10, 16], [1, L]]),
                ap_(dtc[i % 2], b * 40 + so,
                    [[1280, 128], [80, 16], [0, 16], [1, L]]),
                ap_(brepblk[i % 2], b * 640 + so,
                    [[1280, 128], [0, 16], [40, 16], [1, L]]),
                OP.mult), inc=(("st_ve", 1) if b == 1 else None))
        ctx.drain("vector")
        mk[("wtt", s)] = ctx.val("st_ve")
        ctx.op("scalar", lambda e2, da=da: e2.activation(da[:], da[:], AF.Tanh),
               wait=[("st_ve", mk[("qtt", s)])])
        ctx.drain("scalar")
        d0w = []
        if s >= 1:
            d0w.append(("st_ve", mk[("chain_tmp", s - 1)]))
        ctx.op("scalar", lambda e2, da=da, pb=pb: e2.activation(
            dA0[pb][:], ap_(da, 0, [[SL, 128], [10, 512]]), AF.Exp, scale=16.0),
            wait=d0w)
        ctx.drain("scalar")
        ctx.op("scalar", lambda e2, da=da, L=L: e2.activation(
            ap_(da, 1, [[SL, 128], [10, 512], [1, L - 1]]),
            ap_(da, 1, [[SL, 128], [10, 512], [1, L - 1]]), AF.Exp, scale=16.0),
            inc=("st_sc", 1))
        ctx.drain("scalar")
        mk[("exp0", s)] = ctx.val("st_sc")

    def emit_mset(s):
        da = dAb[s % 2]
        ctx.op("vector", lambda e2, da=da: e2.memset(
            ap_(da, 0, [[SL, 128], [10, 512]]), 0.0),
            wait=[("st_sc", mk[("exp0", s)])], inc=("st_ve", 1))
        ctx.drain("vector")
        mk[("mset", s)] = ctx.val("st_ve")

    def emit_dxz(i):
        W = bwid(i)
        lsem, lval = mk[("blkload", i)]
        ctx.op("vector", lambda e2, W=W, i=i: e2.tensor_tensor(
            ap_(yring, 0, [[1280, 128], [80, 16], [40, 2], [1, W]]),
            ap_(xcc[i % 2], 0, [[1280, 128], [80, 16], [40, 2], [1, W]]),
            ap_(dcol_sb, 0, [[16, 128], [1, 16], [0, 2], [0, W]]),
            OP.mult), wait=[(lsem, lval)], inc=("st_ve", 1))
        ctx.drain("vector")
        mk[("dxz", i)] = ctx.val("st_ve")

    def emit_y(s):
        i = s // SPB
        so = (s - SPB * i) * SEG
        L = slen(s)
        hv = hb
        lsem, lval = mk[("blkload", i)]
        for b in range(2):
            ctx.op("vector", lambda e2, b=b, so=so, L=L, hv=hv, i=i:
                   e2.tensor_tensor(
                ap_(hv, b * 2560, [[SL, 128], [160, 16], [10, 16], [1, L]]),
                ap_(hv, b * 2560, [[SL, 128], [160, 16], [10, 16], [1, L]]),
                ap_(crepblk[i % 2], b * 640 + so,
                    [[1280, 128], [0, 16], [40, 16], [1, L]]),
                OP.mult), wait=([(lsem, lval)] if b == 0 else None),
                inc=(("st_ve", 1) if b == 1 else None))
        ctx.drain("vector")
        mk[("ytt", s)] = ctx.val("st_ve")
        for b in range(2):
            ctx.op("vector", lambda e2, b=b, L=L, hv=hv: e2.reduce_sum(
                ap_(yscr, b * 160, [[320, 128], [10, 16], [1, L]]),
                ap_(hv, b * 2560, [[SL, 128], [160, 16], [1, L], [10, 16]]),
                AX.X))
        ctx.drain("vector")
        ctx.op("vector", lambda e2, so=so, L=L: e2.tensor_tensor(
            ap_(yring, so, [[1280, 128], [40, 2], [80, 16], [1, L]]),
            ap_(yring, so, [[1280, 128], [40, 2], [80, 16], [1, L]]),
            ap_(yscr, 0, [[320, 128], [160, 2], [10, 16], [1, L]]),
            OP.add), inc=("st_ve", 1))
        ctx.drain("vector")
        mk[("yadd", s)] = ctx.val("st_ve")

    def emit_phd(i):
        W = bwid(i)
        tb0 = 40 * i
        gw = [("st_ve", mk.get(("pd_gst",), 0)), mk.get(("pd_ldg",), ("ldg0", 0))]
        for kc in range(16):
            for b in range(2):
                ctx.op("gpsimd", lambda e2, b=b, kc=kc, tb0=tb0, W=W:
                       e2.dma_start(
                    out=ap_(mgb, kc * 80 + b * 40, [[1280, 128], [1, W]]),
                    in_=ap_(mg_s, b * DI * T + kc * 128 * T + tb0,
                            [[T, 128], [1, W]])),
                    inc=("ldg0", 16),
                    wait=(gw if kc == 0 and b == 0 else None))
        lvg = ctx.val("ldg0")
        mk[("pd_ldg",)] = ("ldg0", lvg)
        ctx.op("scalar", lambda e2: e2.activation(mgb[:], mgb[:], AF.Silu),
               wait=[("ldg0", lvg), ("st_ve", mk.get(("pd_gst",), 0))],
               inc=("st_sc", 1))
        ctx.drain("scalar")
        silv = ctx.val("st_sc")
        ctx.op("vector", lambda e2: e2.tensor_tensor(
            mgb[:], mgb[:], yring[:], OP.mult), wait=[("st_sc", silv)])
        ctx.drain("vector")
        ctx.op("vector", lambda e2: e2.tensor_copy(obuf[:], mgb[:]),
               inc=("st_ve", 1))
        ctx.drain("vector")
        mk[("pd_gst",)] = ctx.val("st_ve")
        mk[("phdy", i)] = ctx.val("st_ve")
        if i + 1 < NBLK:
            emit_dxz(i + 1)
        for mc in range(8):
            msem = "ldm0" if mc % 2 == 0 else "ldm1"
            x1t = x1d[mc % 2]
            h2t = h2d[mc % 2]
            mw = [("st_ve", mk.get(("pd_res", mc % 2), 0))]
            for b in range(2):
                ctx.op("gpsimd", lambda e2, b=b, mc=mc, tb0=tb0, W=W, x1t=x1t:
                       e2.dma_start(
                    out=ap_(x1t, b * 40, [[80, 128], [1, W]]),
                    in_=ap_(x1_s, b * D * T + mc * 128 * T + tb0,
                            [[T, 128], [1, W]])),
                    inc=(msem, 16), wait=(mw if b == 0 else None))
                ctx.op("gpsimd", lambda e2, b=b, mc=mc, tb0=tb0, W=W, h2t=h2t:
                       e2.dma_start(
                    out=ap_(h2t, b * 40, [[80, 128], [1, W]]),
                    in_=ap_(h2_s, b * D * T + mc * 128 * T + tb0,
                            [[T, 128], [1, W]])),
                    inc=(msem, 16))
            lvm = ctx.val(msem)
            psP = psD[mc % 2]
            mmw = [("st_ve", mk[("phdy", i)]), ("ldgw", wo_ready),
                   ("st_ve", mk.get(("pd_res", mc % 2), 0))]
            if W == 40:
                for kc in range(16):
                    ctx.op("tensor", lambda e2, kc=kc, mc=mc, psP=psP, W=W:
                           e2.matmul(
                        ap_(psP, 0, [[512, 128], [W, 2], [1, W]]),
                        w_out_sb[:, (kc * 8 + mc) * 128:(kc * 8 + mc) * 128 + 128],
                        ap_(obuf, kc * 80, [[1280, 128], [40, 2], [1, W]]),
                        start=(kc == 0), stop=(kc == 15)),
                        wait=(mmw if kc == 0 else None),
                        inc=(("st_te", 1) if kc == 15 else None))
            else:
                for kc in range(16):
                    for b in range(2):
                        ctx.op("tensor", lambda e2, kc=kc, mc=mc, psP=psP,
                               W=W, b=b: e2.matmul(
                            ap_(psP, b * W, [[512, 128], [1, W]]),
                            w_out_sb[:, (kc * 8 + mc) * 128:
                                     (kc * 8 + mc) * 128 + 128],
                            ap_(obuf, kc * 80 + b * 40, [[1280, 128], [1, W]]),
                            start=(kc == 0 and b == 0),
                            stop=(kc == 15 and b == 1)),
                            wait=(mmw if kc == 0 and b == 0 else None),
                            inc=(("st_te", 1) if kc == 15 and b == 1 else None))
            tev = ctx.val("st_te")
            ssem = "sto0" if mc % 2 == 0 else "sto1"
            stw = [("st_te", tev), (msem, lvm)]
            dm = mk.get(("pd_sto", mc % 2))
            if dm:
                stw.append((ssem, dm))
            ctx.op("vector", lambda e2, psP=psP, mc=mc, h2t=h2t, W=W:
                   e2.scalar_tensor_tensor(
                ap_(x2blk, mc * 80, [[640, 128], [40, 2], [1, W]]),
                ap_(psP, 0, [[512, 128], [W, 2], [1, W]]), rscol[:, 0:1],
                ap_(h2t, 0, [[80, 128], [40, 2], [1, W]]),
                OP.mult, OP.add),
                wait=stw)
            ctx.drain("vector")
            ctx.op("vector", lambda e2, mc=mc, x1t=x1t, W=W: e2.tensor_tensor(
                ap_(x2blk, mc * 80, [[640, 128], [40, 2], [1, W]]),
                ap_(x2blk, mc * 80, [[640, 128], [40, 2], [1, W]]),
                ap_(x1t, 0, [[80, 128], [40, 2], [1, W]]),
                OP.add), inc=("st_ve", 1))
            ctx.drain("vector")
            mk[("pd_res", mc % 2)] = ctx.val("st_ve")
            scv = ctx.val("st_ve")
            ssem = "sto0" if mc % 2 == 0 else "sto1"
            for b in range(2):
                sw2 = [("st_ve", scv)]
                dm = mk.get(("pd_sto", mc % 2))
                if dm and b == 0:
                    sw2.append((ssem, dm))
                ctx.op("sync", lambda e2, b=b, mc=mc, tb0=tb0, W=W: e2.dma_start(
                    out=ap_(x2_o, b * D * T + mc * 128 * T + tb0,
                            [[T, 128], [1, W]]),
                    in_=ap_(x2blk, mc * 80 + b * 40, [[640, 128], [1, W]])),
                    wait=sw2, inc=(ssem, 16))
            mk[("pd_sto", mc % 2)] = ctx.val(ssem)
        x2v = ctx.val("st_ve")
        sqw = [("st_ve", x2v), ("st_ve", mk.get(("pd_lnred",), 0)),
               ("st_sc", ctx.val("st_sc"))]
        ctx.op("scalar", lambda e2: e2.activation(sqb[:], x2blk[:], AF.Square),
               wait=sqw, inc=("st_sc", 1))
        ctx.drain("scalar")
        sqv = ctx.val("st_sc")
        smw = [("st_ve", x2v), ("st_sc", mk.get(("pd_lncp",), 0)),
               ("st_ve", mk.get(("pd_lnm2",), 0))]
        if W == 40:
            for mc in range(8):
                ctx.op("tensor", lambda e2, mc=mc, W=W: e2.matmul(
                    ap_(psS2, 0, [[1024, 128], [40, 2], [1, W]]), allones[:],
                    ap_(x2blk, mc * 80, [[640, 128], [40, 2], [1, W]]),
                    start=(mc == 0), stop=(mc == 7)),
                    wait=(smw if mc == 0 else None))
            for mc in range(8):
                ctx.op("tensor", lambda e2, mc=mc, W=W: e2.matmul(
                    ap_(psS2, 512, [[1024, 128], [40, 2], [1, W]]), allones[:],
                    ap_(sqb, mc * 80, [[640, 128], [40, 2], [1, W]]),
                    start=(mc == 0), stop=(mc == 7)),
                    wait=([("st_sc", sqv)] if mc == 0 else None),
                    inc=(("st_te", 1) if mc == 7 else None))
        else:
            for mc in range(8):
                for b in range(2):
                    ctx.op("tensor", lambda e2, mc=mc, W=W, b=b: e2.matmul(
                        ap_(psS2, b * 40, [[1024, 128], [1, W]]), allones[:],
                        ap_(x2blk, mc * 80 + b * 40, [[640, 128], [1, W]]),
                        start=(mc == 0 and b == 0),
                        stop=(mc == 7 and b == 1)),
                        wait=(smw if mc == 0 and b == 0 else None))
            for mc in range(8):
                for b in range(2):
                    ctx.op("tensor", lambda e2, mc=mc, W=W, b=b: e2.matmul(
                        ap_(psS2, 512 + b * 40, [[1024, 128], [1, W]]),
                        allones[:],
                        ap_(sqb, mc * 80 + b * 40, [[640, 128], [1, W]]),
                        start=(mc == 0 and b == 0),
                        stop=(mc == 7 and b == 1)),
                        wait=([("st_sc", sqv)] if mc == 0 and b == 0 else None),
                        inc=(("st_te", 1) if mc == 7 and b == 1 else None))
        tev = ctx.val("st_te")
        ctx.op("scalar", lambda e2, W=W: e2.activation(
            ap_(mrepD, 0, [[80, 128], [40, 2], [1, W]]),
            ap_(psS2, 0, [[1024, 128], [40, 2], [1, W]]), AF.Copy, scale=1.0 / D),
            wait=[("st_te", tev)])
        ctx.drain("scalar")
        ctx.op("scalar", lambda e2, W=W: e2.activation(
            ap_(vrepD, 0, [[80, 128], [40, 2], [1, W]]),
            ap_(psS2, 512, [[1024, 128], [40, 2], [1, W]]), AF.Copy,
            scale=1.0 / D), inc=("st_sc", 1))
        ctx.drain("scalar")
        mk[("pd_lncp",)] = ctx.val("st_sc")
        ctx.op("vector", lambda e2, W=W: e2.tensor_tensor(
            ap_(psS2, 0, [[1024, 128], [40, 2], [1, W]]),
            ap_(mrepD, 0, [[80, 128], [40, 2], [1, W]]),
            ap_(mrepD, 0, [[80, 128], [40, 2], [1, W]]), OP.mult),
            wait=[("st_sc", mk[("pd_lncp",)])])
        ctx.drain("vector")
        ctx.op("vector", lambda e2, W=W: e2.tensor_tensor(
            ap_(vrepD, 0, [[80, 128], [40, 2], [1, W]]),
            ap_(vrepD, 0, [[80, 128], [40, 2], [1, W]]),
            ap_(psS2, 0, [[1024, 128], [40, 2], [1, W]]), OP.subtract),
            inc=("st_ve", 1))
        ctx.drain("vector")
        mk[("pd_lnm2",)] = ctx.val("st_ve")
        ctx.op("scalar", lambda e2, W=W: e2.activation(
            ap_(vrepD, 0, [[80, 128], [40, 2], [1, W]]),
            ap_(vrepD, 0, [[80, 128], [40, 2], [1, W]]), AF.Ln,
            bias=epscol[:]), wait=[("st_ve", mk[("pd_lnm2",)])])
        ctx.drain("scalar")
        ctx.op("scalar", lambda e2, W=W: e2.activation(
            ap_(vrepD, 0, [[80, 128], [40, 2], [1, W]]),
            ap_(vrepD, 0, [[80, 128], [40, 2], [1, W]]), AF.Exp, scale=-0.5),
            inc=("st_sc", 1))
        ctx.drain("scalar")
        rsv = ctx.val("st_sc")
        ctx.op("vector", lambda e2, W=W: e2.tensor_tensor(
            ap_(sqb, 0, [[640, 128], [80, 8], [40, 2], [1, W]]),
            ap_(x2blk, 0, [[640, 128], [80, 8], [40, 2], [1, W]]),
            ap_(mrepD, 0, [[80, 128], [0, 8], [40, 2], [1, W]]), OP.subtract),
            wait=[("st_sc", rsv)])
        ctx.drain("vector")
        ctx.op("vector", lambda e2, W=W: e2.tensor_tensor(
            ap_(sqb, 0, [[640, 128], [80, 8], [40, 2], [1, W]]),
            ap_(sqb, 0, [[640, 128], [80, 8], [40, 2], [1, W]]),
            ap_(vrepD, 0, [[80, 128], [0, 8], [40, 2], [1, W]]), OP.mult))
        ctx.drain("vector")
        ctx.op("vector", lambda e2, W=W: e2.reduce_max(
            tmax16[:], ap_(sqb, 0, [[640, 128], [80, 8], [40, 2], [1, W]]),
            AX.X, apply_absolute_value=True))
        ctx.drain("vector")
        ctx.op("vector", lambda e2: e2.reduce_max(
            tmax[:], tmax16[:], AX.X, apply_absolute_value=True))
        ctx.drain("vector")
        ctx.op("vector", lambda e2: e2.tensor_tensor(
            pcol[:], pcol[:], tmax[:], OP.max), inc=("st_ve", 1))
        ctx.drain("vector")
        mk[("pd_lnred",)] = ctx.val("st_ve")

    emit_block_loads(0)
    emit_block_loads(1)
    emit_prep(0)
    emit_dxz(0)
    for s in range(NSEG):
        pb = s % 2
        if s == 0:
            ctx.op("vector", lambda e2: e2.tensor_copy(
                tmpc[:], ap_(wb[0], 0, [[SL, 128], [10, 512]])),
                inc=("st_ve", 1))
            ctx.drain("vector")
        else:
            cw = [("st_sc", mk.get(("sq", s - 1), 0)),
                  ("st_sc", mk[("exp0", s)])]
            ctx.op("vector", lambda e2, pb=pb: e2.tensor_tensor(
                tmpc[:], dA0[pb][:],
                ap_(hb, 9, [[SL, 128], [10, 512]]), OP.mult),
                wait=cw)
            ctx.drain("vector")
            ctx.op("vector", lambda e2, pb=pb: e2.tensor_tensor(
                tmpc[:], tmpc[:], ap_(wb[pb], 0, [[SL, 128], [10, 512]]),
                OP.add), inc=("st_ve", 1))
            ctx.drain("vector")
        mk[("chain_tmp", s)] = ctx.val("st_ve")
        sqw = [("st_ve", mk[("chain_tmp", s)]),
               ("st_te", mk.get(("clipmm", s - 1), 0))]
        for b in range(2):
            ctx.op("scalar", lambda e2, b=b: e2.activation(
                sqscr[:], tmpc[:, b * 256:(b + 1) * 256], AF.Square,
                accum_out=sq2[:, b:b + 1]),
                wait=(sqw if b == 0 else None),
                inc=(("st_sc", 1) if b == 1 else None))
            ctx.drain("scalar")
        mk[("sq", s)] = ctx.val("st_sc")
        mmw = [("st_sc", mk[("sq", s)]), ("st_sc", mk.get(("sqrt", s - 1), 0))]
        ctx.op("tensor", lambda e2: e2.matmul(
            ap_(psMisc, 128, [[130, 128], [1, 2]]), allones[:], sq2[:],
            start=True, stop=True), wait=mmw, inc=("st_te", 1))
        mk[("clipmm", s)] = ctx.val("st_te")
        ctx.op("scalar", lambda e2: e2.activation(
            nrmt[:], ap_(psMisc, 128, [[130, 128], [1, 2]]), AF.Ln),
            wait=[("st_te", mk[("clipmm", s)]),
                  ("st_ve", mk.get(("hclip", s - 1), 0))])
        ctx.drain("scalar")
        ctx.op("scalar", lambda e2: e2.activation(
            nrmt[:], nrmt[:], AF.Exp, scale=0.5), inc=("st_sc", 1))
        ctx.drain("scalar")
        mk[("sqrt", s)] = ctx.val("st_sc")
        if s + 1 < NSEG:
            emit_prep(s + 1)
        if s >= 1:
            emit_y(s - 1)
        if (s + 1) % SPB == 1 and s + 1 > 1 and (s + 1) // SPB + 1 < NBLK:
            emit_block_loads((s + 1) // SPB + 1)
        ctx.op("vector", lambda e2: e2.tensor_scalar(
            nrmt[:], nrmt[:], 1e-8, None, OP.add),
            wait=[("st_sc", mk[("sqrt", s)])])
        ctx.drain("vector")
        ctx.op("vector", lambda e2: e2.reciprocal(nrmt[:], nrmt[:]))
        ctx.drain("vector")
        ctx.op("vector", lambda e2: e2.tensor_scalar(
            nrmt[:], nrmt[:], 20.0, 1.0, OP.mult, OP.min))
        ctx.drain("vector")
        ctx.op("vector", lambda e2, pb=pb: e2.tensor_tensor(
            ap_(wb[pb], 0, [[SL, 128], [2560, 2], [10, 256]]),
            ap_(tmpc, 0, [[512, 128], [256, 2], [1, 256]]),
            ap_(nrmt, 0, [[2, 128], [1, 2], [0, 256]]), OP.mult),
            inc=("st_ve", 1))
        ctx.drain("vector")
        mk[("hclip", s)] = ctx.val("st_ve")
        emit_mset(s)
        ctx.op("vector", lambda e2, pb=pb: e2.tensor_tensor_scan(
            hb[:], dAb[pb][:], wb[pb][:], 0.0, OP.mult, OP.add),
            wait=[("st_sc", mk[("exp0", s)])], inc=("st_ve", 1))
        ctx.drain("vector")
        mk[("scan", s)] = ctx.val("st_ve")
        if s >= SPB and (s - 1) % SPB == SPB - 1:
            emit_phd((s - 1) // SPB)
    emit_y(NSEG - 1)
    emit_phd(NBLK - 1)


_IDENT = np.eye(128, dtype=np.float32)


def _selbc():
    out = np.zeros((32, 32 * 128), np.float32)
    for r in range(32):
        out[r, r * 128:(r + 1) * 128] = 1.0
    return out


_SELBC = _selbc()


def _conv_diag(conv_w):
    out = np.zeros((128, 16 * 4 * 128), np.float32)
    cw = np.asarray(conv_w)
    for j in range(16):
        for k in range(4):
            blk = np.diag(cw[j * 128:(j + 1) * 128, 0, k].astype(np.float32))
            out[:, (j * 4 + k) * 128:(j * 4 + k) * 128 + 128] = blk
    return out


def _tile_kmajor(wT, nkc, nmc, mwid=128):
    K, M = wT.shape
    assert K == nkc * 128 and M == nmc * mwid
    return np.ascontiguousarray(
        wT.reshape(nkc, 128, nmc, mwid).transpose(1, 0, 2, 3).reshape(128, -1))


ALL_SEMS = [
    "ldc", "ldsc", "ldcc", "cc", "ldx0", "ldx1", "ldu0", "ldu1", "ldw0",
    "ldw1", "ldf0", "ldf1", "ldr0", "ldr1", "lds0", "lds1", "ldm0", "ldm1",
    "ldg0", "ldg1", "ldgw", "ldsel", "st_te", "st_ve", "st_sc", "stoU0",
    "stoU1", "stoG0", "stoG1", "stoC0", "stoC1", "stoD", "stoB0", "stoB1",
    "stoX0", "stoX1", "stoH", "sto0", "sto1", "stoW",
    "dr_sync", "dr_tensor", "dr_vector", "dr_scalar", "dr_gpsimd"]


def full_barrier(ctx, exclude=("cc",)):
    for eng in Ctx.ENGS:
        for s, v in list(ctx.cnt.items()):
            if v and s not in exclude:
                ctx.wait(eng, s, v)


def build_MEGA():
    nc = bass.Bass("TRN2", target_bir_lowering=False,
                   detect_race_conditions=False)
    x = nc.dram_tensor("x", [B_CORE, D, T], F32, kind="ExternalInput").ap()
    ident = nc.dram_tensor("ident", [128, 128], F32, kind="ExternalInput").ap()
    identb = nc.dram_tensor("identb", [128, 128], BF16, kind="ExternalInput").ap()
    w_up = nc.dram_tensor("w_up", [FF, D], F32, kind="ExternalInput").ap()
    w_gate = nc.dram_tensor("w_gate", [FF, D], F32, kind="ExternalInput").ap()
    w_down = nc.dram_tensor("w_down", [D, FF], F32, kind="ExternalInput").ap()
    w_inT = nc.dram_tensor("w_inT", [128, 8 * 32 * 128], F32, kind="ExternalInput").ap()
    w_dtT = nc.dram_tensor("w_dtT", [128, 16 * 16 * 128], F32, kind="ExternalInput").ap()
    w_bcT = nc.dram_tensor("w_bcT", [128, 16 * 32], F32, kind="ExternalInput").ap()
    w_outT = nc.dram_tensor("w_outT", [128, 16 * 8 * 128], F32, kind="ExternalInput").ap()
    cdiag = nc.dram_tensor("cdiag", [128, 16 * 4 * 128], F32, kind="ExternalInput").ap()
    a_jln = nc.dram_tensor("a_jln", [128, 16 * DS], F32, kind="ExternalInput").ap()
    dcol = nc.dram_tensor("dcol", [128, 16], F32, kind="ExternalInput").ap()
    cbcol = nc.dram_tensor("cbcol", [128, 16], F32, kind="ExternalInput").ap()
    bdtcol = nc.dram_tensor("bdtcol", [128, 16], F32, kind="ExternalInput").ap()
    nwcol = nc.dram_tensor("nwcol", [128, 8], F32, kind="ExternalInput").ap()
    rscal = nc.dram_tensor("rscal", [1, 1], F32, kind="ExternalInput").ap()
    out = nc.dram_tensor("out", [B_CORE, D, T], F32, kind="ExternalOutput").ap()
    up_raw = nc.dram_tensor("up_raw", [B_CORE, FF, T], F32).ap()
    gate_raw = nc.dram_tensor("gate_raw", [B_CORE, FF, T], F32).ap()
    pre3_raw = nc.dram_tensor("pre3_raw", [B_CORE, FF, T], F32).ap()
    pre4_raw = nc.dram_tensor("pre4_raw", [B_CORE, FF, T], F32).ap()
    x2_s = nc.dram_tensor("x2_s", [B_CORE, D, T], F32).ap()
    x1_s = nc.dram_tensor("x1_s", [B_CORE, D, T], F32).ap()
    h2_s = nc.dram_tensor("h2_s", [B_CORE, D, T], F32).ap()
    mg_s = nc.dram_tensor("mg_s", [B_CORE, DI, T], F32).ap()
    xc_s = nc.dram_tensor("xc_s", [B_CORE, DI, T], F32).ap()
    dt_s = nc.dram_tensor("dt_s", [B_CORE, DI, T], F32).ap()
    bcps_d = nc.dram_tensor("bcps_d", [32, B_CORE * T], F32).ap()
    NCC = 8
    ccin = [nc.dram_tensor(f"ccin{k}", [1, 2], F32) for k in range(NCC)]
    ccout = [nc.dram_tensor(f"ccout{k}", [1, 2], F32) for k in range(NCC)]
    wqu_c = nc.dram_tensor("wqu_c", [128, 8 * 24 * 128], BF16).ap()
    wqg_c = nc.dram_tensor("wqg_c", [128, 8 * 24 * 128], BF16).ap()
    wdt_c = nc.dram_tensor("wdt_c", [128, 24 * 8 * 128], BF16).ap()

    with contextlib.ExitStack() as root:
        ctx = Ctx(nc)
        ctx.add_sems(ALL_SEMS, root)
        blk = root.enter_context(nc.Block())

        def ph(fn, *args):
            with contextlib.ExitStack() as st:
                r = fn(ctx, nc, st, *args)
            full_barrier(ctx)
            return r

        cc0 = ph(ph_P0, x, ident, ccin[0], ccout[0])
        cc1 = ph(ph_upgate, "f1", x, ident, identb, w_up, w_gate, up_raw,
                 gate_raw, (ccout[0].ap(), cc0, 0), ccin[1], ccout[1],
                 (wqu_c, wqg_c, "store"))
        cc2 = ph(ph_gate, "g1", gate_raw, ident, pre3_raw,
                 (ccout[1].ap(), cc1, 1), ccin[2], ccout[2])
        cc3 = ph(ph_prod, "o1", pre3_raw, up_raw, ident, pre4_raw,
                 (ccout[2].ap(), cc2, 0), (ccout[1].ap(), cc1, 0),
                 ccin[3], ccout[3])
        cc4 = ph(ph_P4, x, pre4_raw, (ccout[3].ap(), cc3, 0), w_down, ident,
                 identb, w_inT, w_dtT, w_bcT, w_outT, cdiag, a_jln, dcol,
                 cbcol, bdtcol, nwcol, rscal, x2_s, x1_s, h2_s, mg_s, xc_s,
                 dt_s, bcps_d, ccin[4], ccout[4], (wdt_c, "store"))
        cc5 = ph(ph_upgate, "f2", x2_s, ident, identb, w_up, w_gate, up_raw,
                 gate_raw, (ccout[4].ap(), cc4, 0), ccin[5], ccout[5],
                 (wqu_c, wqg_c, "load"))
        cc6 = ph(ph_gate, "g2", gate_raw, ident, pre3_raw,
                 (ccout[5].ap(), cc5, 1), ccin[6], ccout[6])
        cc7 = ph(ph_prod, "o2", pre3_raw, up_raw, ident, pre4_raw,
                 (ccout[6].ap(), cc6, 0), (ccout[5].ap(), cc5, 0),
                 ccin[7], ccout[7])
        ph(ph_tail, pre4_raw, x2_s, (ccout[7].ap(), cc7, 0), w_down, identb,
           out, (wdt_c, "load"))
        ctx.build(blk)
    return nc


_PROG = []


def _prog():
    if not _PROG:
        _PROG.append(build_MEGA())
    return _PROG[0]


def _prep_core_inputs(inputs):
    x = np.asarray(inputs["L0_attn_x"], dtype=np.float32)
    per_core = [np.ascontiguousarray(x[2 * c:2 * c + 2].transpose(0, 2, 1))
                for c in range(NCORES)]
    A_log = np.asarray(inputs["L0_attn_A_log"], dtype=np.float32)
    shared = {
        "ident": _IDENT,
        "identb": _IDENT.astype(ml_dtypes.bfloat16),
        "w_up": np.ascontiguousarray(inputs["L0_ffn_w_up"], dtype=np.float32),
        "w_gate": np.ascontiguousarray(inputs["L0_ffn_w_gate"], dtype=np.float32),
        "w_down": np.ascontiguousarray(inputs["L0_ffn_w_down"], dtype=np.float32),
        "w_inT": _tile_kmajor(np.ascontiguousarray(
            np.asarray(inputs["L0_attn_W_in"], dtype=np.float32).T), 8, 32),
        "w_dtT": _tile_kmajor(np.ascontiguousarray(
            np.asarray(inputs["L0_attn_W_dt"], dtype=np.float32).T), 16, 16),
        "w_bcT": _tile_kmajor(np.ascontiguousarray(np.concatenate(
            [np.asarray(inputs["L0_attn_W_B"], dtype=np.float32),
             np.asarray(inputs["L0_attn_W_C"], dtype=np.float32)], 0).T), 16, 1, 32),
        "w_outT": _tile_kmajor(np.ascontiguousarray(
            np.asarray(inputs["L0_attn_W_out"], dtype=np.float32).T), 16, 8),
        "cdiag": _conv_diag(np.asarray(inputs["L0_attn_conv_w"], dtype=np.float32)),
        "a_jln": np.ascontiguousarray(
            A_log.reshape(16, 128, 16).transpose(1, 0, 2).reshape(128, 256)),
        "dcol": np.ascontiguousarray(
            np.asarray(inputs["L0_attn_D_param"], np.float32).reshape(16, 128).T),
        "cbcol": np.ascontiguousarray(
            np.asarray(inputs["L0_attn_conv_b"], np.float32).reshape(16, 128).T),
        "bdtcol": np.ascontiguousarray(
            np.asarray(inputs["L0_attn_b_dt"], np.float32).reshape(16, 128).T),
        "nwcol": np.ascontiguousarray(
            np.asarray(inputs["L0_attn_norm_w"], np.float32).reshape(8, 128).T),
        "rscal": np.asarray(inputs["L0_attn_residual_scale"],
                            np.float32).reshape(1, 1),
    }
    return per_core, shared


def kernel(**inputs):
    xs, sh = _prep_core_inputs(inputs)
    nc = _prog()
    res = run_bass_kernel_spmd(
        nc, [{"x": xs[c], **sh} for c in range(NCORES)],
        core_ids=list(range(NCORES)))
    out = np.concatenate(
        [np.asarray(res.results[c]["out"]).transpose(0, 2, 1)
         for c in range(NCORES)], axis=0)
    return np.ascontiguousarray(out.astype(np.float32))


if __name__ == "__main__":
    pass


# version 14
# speedup vs baseline: 1.0746x; 1.0422x over previous
import contextlib
import numpy as np
import ml_dtypes
import concourse.bass as bass
import concourse.mybir as mybir
from concourse.bass_utils import run_bass_kernel_spmd

F32 = mybir.dt.float32
BF16 = mybir.dt.bfloat16
I32 = mybir.dt.int32
AF = mybir.ActivationFunctionType
OP = mybir.AluOpType
AX = mybir.AxisListType

NCORES = 8
B_CORE = 2
T = 2048
D = 1024
DI = 2048
DS = 16
FF = 3072
TTS = 512
NTILE = B_CORE * (T // TTS)
QMAX_A = 32767.0
QMAX_W = 7.0
WBLK = 32
MAGIC = 12582912.0

SC_C = (0.1 + 0.001) / 2.0
SC_R = (0.1 - 0.001) / 2.0
SC_IN = 0.5 / (SC_R + 1e-8)
SC_OUT = SC_R / 0.5
SCA_IN = 0.5 / (5.0 + 1e-8)
SCA_OUT = 5.0 / 0.5
SCB_IN = 0.5 / (8.0 + 1e-8)
SCB_OUT = 8.0 / 0.5


class Ctx:

    ENGS = ("sync", "tensor", "vector", "scalar", "gpsimd")

    def __init__(self, nc):
        self.nc = nc
        self.q = {e: [] for e in self.ENGS}
        self.sem_handles = {}
        self.cnt = {}
        self.last_inc = {}

    def add_sems(self, names, stack):
        for n in names:
            self.sem_handles[n] = stack.enter_context(self.nc.semaphore(n))
            self.cnt[n] = 0

    def op(self, eng, fn, inc=None, wait=None):
        waits = list(wait or [])
        tgt = None
        if inc is not None:
            self.cnt[inc[0]] += inc[1]
            tgt = self.cnt[inc[0]]
        handles = self.sem_handles

        def replay(e, waits=waits, fn=fn, inc=inc):
            for (ws, wv) in waits:
                e.wait_ge(handles[ws], wv)
            r = fn(e)
            if inc is not None:
                r.then_inc(handles[inc[0]], inc[1])
            return r

        self.q[eng].append(replay)
        self.last_inc[eng] = (inc[0], tgt) if inc is not None else None
        return tgt

    def wait(self, eng, sem, val=None):
        v = self.cnt[sem] if val is None else val
        self.q[eng].append(lambda e, s=sem, v=v: e.wait_ge(self.sem_handles[s], v))
        return v

    def drain(self, eng):
        if not self.q[eng]:
            return
        last = self.last_inc.get(eng)
        if last is None:
            sem = "dr_" + eng
            self.cnt[sem] = self.cnt.get(sem, 0) + 1
            v = self.cnt[sem]
            prev = self.q[eng].pop()
            handles = self.sem_handles

            def wrapped(e, prev=prev, sem=sem, handles=handles):
                r = prev(e)
                r.then_inc(handles[sem], 1)
                return r

            self.q[eng].append(wrapped)
            last = (sem, v)
            self.last_inc[eng] = last
        s, v = last
        self.q[eng].append(lambda e, s=s, v=v: e.wait_ge(self.sem_handles[s], v))

    def val(self, sem):
        return self.cnt[sem]

    def build(self, block):
        for eng in self.ENGS:
            ops = self.q[eng]
            if not ops:
                continue

            def body(e, ops=ops):
                for f in ops:
                    f(e)

            getattr(block, eng)(body)


def ap_(t, off, dims):
    return bass.AP(t.tensor if hasattr(t, "tensor") else t, off, dims)


def emit_ln(ctx, xt, sqt, psS, mrep, vrep, allones, xnt, j_chunks, tile_f,
            ld_sem, ld_val, epscol=None):
    ctx.op("scalar", lambda e: e.activation(sqt[:], xt[:], AF.Square),
           wait=[(ld_sem, ld_val), ("st_ve", ctx.val("st_ve"))], inc=("st_sc", 1))
    for j in range(j_chunks):
        ctx.op("tensor",
               lambda e, j=j: e.matmul(psS[:, 0:tile_f], allones[:],
                                       xt[:, j * tile_f:(j + 1) * tile_f],
                                       start=(j == 0), stop=(j == j_chunks - 1)),
               wait=([(ld_sem, ld_val), ("st_ve", ctx.val("st_ve"))] if j == 0 else None))
    for j in range(j_chunks):
        ctx.op("tensor",
               lambda e, j=j: e.matmul(psS[:, tile_f:2 * tile_f], allones[:],
                                       sqt[:, j * tile_f:(j + 1) * tile_f],
                                       start=(j == 0), stop=(j == j_chunks - 1)),
               wait=([("st_sc", ctx.val("st_sc"))] if j == 0 else None),
               inc=(("st_te", 1) if j == j_chunks - 1 else None))
    te_done = ctx.val("st_te")
    Df = float(j_chunks * 128)
    ctx.op("scalar", lambda e: e.activation(mrep[:], psS[:, 0:tile_f], AF.Copy,
                                            scale=1.0 / Df),
           wait=[("st_te", te_done)])
    ctx.op("scalar", lambda e: e.activation(vrep[:], psS[:, tile_f:2 * tile_f], AF.Copy,
                                            scale=1.0 / Df), inc=("st_sc", 1))
    ctx.op("vector", lambda e: e.tensor_tensor(psS[:, 0:tile_f], mrep[:], mrep[:], OP.mult),
           wait=[("st_sc", ctx.val("st_sc"))])
    ctx.drain("vector")
    ctx.op("vector", lambda e: e.tensor_tensor(vrep[:], vrep[:], psS[:, 0:tile_f],
                                               OP.subtract), inc=("st_ve", 1))
    ctx.op("scalar", lambda e: e.activation(vrep[:], vrep[:], AF.Sqrt,
                                            bias=epscol[:]),
           wait=[("st_ve", ctx.val("st_ve"))], inc=("st_sc", 1))
    ctx.op("vector", lambda e: e.reciprocal(vrep[:], vrep[:]),
           wait=[("st_sc", ctx.val("st_sc"))])
    ctx.drain("vector")
    n_p = 128
    mb = ap_(mrep, 0, [[tile_f, n_p], [0, j_chunks], [1, tile_f]])
    vb = ap_(vrep, 0, [[tile_f, n_p], [0, j_chunks], [1, tile_f]])
    xv = ap_(xt, 0, [[j_chunks * tile_f, n_p], [tile_f, j_chunks], [1, tile_f]])
    ov = ap_(xnt, 0, [[j_chunks * tile_f, n_p], [tile_f, j_chunks], [1, tile_f]])
    ctx.op("vector", lambda e: e.tensor_tensor(ov, xv, mb, OP.subtract))
    ctx.drain("vector")
    ctx.op("vector", lambda e: e.tensor_tensor(ov, ov, vb, OP.mult), inc=("st_ve", 1))
    ctx.drain("vector")
    return ctx.val("st_ve")


def emit_colmax_sb(ctx, pcol, psT, prow, idt):
    ctx.op("tensor", lambda e: e.transpose(psT[:], pcol[:], idt[:]),
           wait=[("st_ve", ctx.val("st_ve"))], inc=("st_te", 1))
    ctx.op("vector", lambda e: e.reduce_max(prow[:, 0:1], psT[:], AX.X),
           wait=[("st_te", ctx.val("st_te"))], inc=("st_ve", 1))
    return ctx.val("st_ve")


def emit_ccmax(ctx, src_sb, ccin, ccout, ve_val):
    lv = ctx.op("gpsimd", lambda e: e.dma_start(out=ccin[:, :], in_=src_sb),
                inc=("ldcc", 16), wait=[("st_ve", ve_val)])
    ctx.wait("gpsimd", "ldcc", lv)
    ctx.op("gpsimd", lambda e: e.collective_compute(
        "AllReduce", OP.max, replica_groups=[list(range(NCORES))],
        ins=[ccin.ap().opt()], outs=[ccout.ap().opt()]), inc=("cc", 1))
    return ctx.val("cc")


def emit_scale_col(ctx, allp_dram, nvals, stride, off, qmax, rowbuf, onesrow,
                   psCol, ccol, scol, extra_mult=1.0, ld_eng="sync",
                   ld_wait=None):
    lv = ctx.op(ld_eng, lambda e: e.dma_start(
        out=rowbuf[:, 0:nvals], in_=ap_(allp_dram, off, [[1, 1], [stride, nvals]]),
        allow_slow_non_contiguous=True), inc=("ldsc", 16), wait=ld_wait)
    ctx.op("vector", lambda e: e.reduce_max(rowbuf[:, nvals:nvals + 1],
                                            rowbuf[:, 0:nvals], AX.X),
           wait=[("ldsc", lv)])
    ctx.drain("vector")
    ctx.op("vector", lambda e: e.reciprocal(rowbuf[:, nvals + 1:nvals + 2],
                                            rowbuf[:, nvals:nvals + 1]))
    ctx.drain("vector")
    ctx.op("vector", lambda e: e.tensor_scalar(rowbuf[:, nvals + 1:nvals + 2],
                                               rowbuf[:, nvals + 1:nvals + 2],
                                               qmax, None, OP.mult))
    ctx.op("vector", lambda e: e.tensor_scalar(rowbuf[:, nvals:nvals + 1],
                                               rowbuf[:, nvals:nvals + 1],
                                               extra_mult / qmax, None, OP.mult),
           inc=("st_ve", 1))
    ctx.drain("vector")
    pitch = psCol.shape[-1]
    pc2 = ap_(psCol, 0, [[pitch, 128], [1, 2]])
    pc_s = ap_(psCol, 0, [[pitch, 128], [1, 1]])
    pc_c = ap_(psCol, 1, [[pitch, 128], [1, 1]])
    ctx.op("tensor", lambda e: e.matmul(pc2, onesrow[:],
                                        rowbuf[:, nvals:nvals + 2], start=True, stop=True),
           wait=[("st_ve", ctx.val("st_ve"))], inc=("st_te", 1))
    ctx.op("scalar", lambda e: e.activation(scol[:], pc_s, AF.Copy),
           wait=[("st_te", ctx.val("st_te"))])
    ctx.op("scalar", lambda e: e.activation(ccol[:], pc_c, AF.Copy),
           inc=("st_sc", 1))
    ctx.wait("vector", "st_sc", ctx.val("st_sc"))
    ctx.wait("sync", "st_sc", ctx.val("st_sc"))


def ph_P0(ctx, nc, st, x, ident, ccin, ccout):
    sb = lambda n, s, d=F32: st.enter_context(nc.sbuf_tensor("p0" + n, s, d))
    ps = lambda n, s: st.enter_context(nc.psum_tensor("p0" + n, s, F32))
    ctx.tile_mark = {}
    xt = [sb(f"xt{i}", [128, 8 * TTS]) for i in range(2)]
    sqt = sb("sqt", [128, 8 * TTS])
    xnt = sb("xnt", [128, 8 * TTS])
    mrep = sb("mrep", [128, TTS])
    vrep = sb("vrep", [128, TTS])
    allones = sb("allones", [128, 128])
    idt = sb("idt", [128, 128])
    pcol = sb("pcol", [128, 1])
    epscol = sb("epscol", [128, 1])
    prow = sb("prow", [1, 128])
    psS = ps("psS", [128, 2 * TTS])
    psT = ps("psT", [1, 128])

    ctx.op("vector", lambda e: e.memset(allones[:], 1.0))
    ctx.op("vector", lambda e: e.memset(epscol[:], 1e-5))
    ctx.op("vector", lambda e: e.memset(prow[:], 0.0))
    ctx.op("vector", lambda e: e.memset(pcol[:], 0.0), inc=("st_ve", 1))
    ctx.op("sync", lambda e: e.dma_start(out=idt[:], in_=ident), inc=("ldc", 16))

    tile_done = {}
    for it in range(NTILE):
        b, tt = divmod(it, T // TTS)
        buf = xt[it % 2]
        lsem = "ldx0" if it % 2 == 0 else "ldx1"
        src = ap_(x, b * D * T + tt * TTS, [[T, 128], [128 * T, 8], [1, TTS]])
        ldwait = [("st_ve", tile_done[it - 2])] if it >= 2 else None
        lv = ctx.op("sync", lambda e, s=src, bu=buf: e.dma_start(out=bu[:], in_=s),
                    inc=(lsem, 16), wait=ldwait)
        emit_ln(ctx, buf, sqt, psS, mrep, vrep, allones, xnt, 8, TTS, lsem, lv, epscol)
        ctx.op("vector", lambda e: e.reduce_max(sqt[:, 0:1], xnt[:], AX.X,
                                                apply_absolute_value=True))
        ctx.drain("vector")
        ctx.op("vector", lambda e: e.tensor_tensor(pcol[:], pcol[:], sqt[:, 0:1],
                                                   OP.max), inc=("st_ve", 1))
        ctx.drain("vector")
        tile_done[it] = ctx.val("st_ve")
    ctx.wait("tensor", "ldc", ctx.val("ldc"))
    ve = emit_colmax_sb(ctx, pcol, psT, prow, idt)
    return emit_ccmax(ctx, prow[:, 0:2], ccin, ccout, ve)


def emit_dequant(ctx, wsrc, n_ot, I, wt, bq, binv, i32w, i32b, zt, wqt, identb, psTr,
                 wqT_sb):
    nblk = I // WBLK
    nkc = I // 128
    pw = wt[0].shape[-1]
    pq = bq.shape[-1]
    pi = i32w.shape[-1]
    for ot in range(n_ot):
        buf = wt[ot % 2]
        wq_par = wqt[ot % 2]
        lsem = "ldw0" if ot % 2 == 0 else "ldw1"
        ldw = [("st_ve", ctx.tile_mark.get(("dqv_last", ot % 2), 0))]
        lv = ctx.op("sync", lambda e, o=ot, b=buf: e.dma_start(
            out=b[:, 0:I], in_=ap_(wsrc, o * 128 * I, [[I, 128], [1, I]])),
            inc=(lsem, 16), wait=ldw)
        ctx.op("vector", lambda e, b=buf: e.reduce_max(
            bq[:, 0:nblk], ap_(b, 0, [[pw, 128], [WBLK, nblk], [1, WBLK]]), AX.X,
            apply_absolute_value=True),
            wait=[(lsem, lv)])
        ctx.drain("vector")
        ctx.op("vector", lambda e: e.tensor_scalar(bq[:, 0:nblk], bq[:, 0:nblk],
                                                   1.0 / QMAX_W, 1e-12,
                                                   OP.mult, OP.max))
        ctx.drain("vector")
        ctx.op("vector", lambda e: e.reciprocal(binv[:, 0:nblk], bq[:, 0:nblk]))
        ctx.drain("vector")
        bqb = ap_(bq, 0, [[pq, 128], [1, nblk], [0, WBLK]])
        bib = ap_(binv, 0, [[pq, 128], [1, nblk], [0, WBLK]])
        wv = ap_(buf, 0, [[pw, 128], [WBLK, nblk], [1, WBLK]])
        pz = zt.shape[-1]
        zv = zt[:, 0:I]
        iv = i32w[:, 0:I]
        av = i32b[:, 0:I]
        qv = ap_(wq_par, 0, [[pi, 128], [WBLK, nblk], [1, WBLK]])
        zblk = ap_(zt, 0, [[pz, 128], [WBLK, nblk], [1, WBLK]])
        ctx.op("vector", lambda e, zblk=zblk, wv=wv, bib=bib: e.tensor_tensor(
            zblk, wv, bib, OP.mult), inc=("st_ve", 1))
        ctx.tile_mark[("dqv_last", ot % 2)] = ctx.val("st_ve")
        ctx.drain("vector")
        ctx.op("vector", lambda e, zv=zv: e.tensor_scalar(zv, zv, MAGIC, None, OP.add))
        ctx.drain("vector")
        ctx.op("vector", lambda e, zv=zv: e.tensor_scalar(zv, zv, MAGIC, None,
                                                          OP.subtract))
        ctx.drain("vector")
        ctx.op("vector", lambda e, iv=iv, zv=zv: e.tensor_scalar(iv, zv, 0.0, None,
                                                                 OP.add))
        ctx.drain("vector")
        ctx.op("vector", lambda e, zv=zv, iv=iv: e.tensor_tensor(zv, zv, iv,
                                                                 OP.subtract))
        ctx.drain("vector")
        ctx.op("vector", lambda e, av=av, zv=zv: e.tensor_scalar(av, zv, 2.0, None,
                                                                 OP.mult))
        ctx.drain("vector")
        ctx.op("vector", lambda e, zv=zv, iv=iv, av=av: e.tensor_tensor(zv, iv, av,
                                                                        OP.add))
        ctx.drain("vector")
        ctx.op("vector", lambda e, qv=qv, zblk=zblk, bqb=bqb: e.tensor_tensor(
            qv, zblk, bqb, OP.mult), inc=("st_ve", 1),
            wait=[("st_te", ctx.tile_mark.get(("dq_te", ot % 2), 0))])
        wq_single = wqt[0] is wqt[1]
        ctx.drain("vector")
        vv = ctx.val("st_ve")
        gbase = ctx.tile_mark.get("dq_g", 0)
        for kc in range(nkc):
            g = gbase + kc
            tw = [("st_ve", vv)] if kc == 0 else []
            em = ctx.tile_mark.get(("dqe", g - 2))
            if em:
                tw.append(("st_sc", em))
            ctx.op("tensor", lambda e, kc=kc, g=g, wq_par=wq_par: e.transpose(
                psTr[g % 2][:], wq_par[:, kc * 128:(kc + 1) * 128], identb[:]),
                wait=(tw or None), inc=("st_te", 1))
            tv = ctx.val("st_te")
            ctx.op("scalar", lambda e, kc=kc, o=ot, g=g: e.activation(
                wqT_sb[:, (kc * n_ot + o) * 128:(kc * n_ot + o) * 128 + 128],
                psTr[g % 2][:], AF.Copy), wait=[("st_te", tv)], inc=("st_sc", 1))
            ctx.tile_mark[("dqe", g)] = ctx.val("st_sc")
        ctx.tile_mark["dq_g"] = gbase + nkc
        ctx.tile_mark[("dq_te", ot % 2)] = ctx.val("st_te")
        if wq_single:
            ctx.tile_mark[("dq_te", (ot + 1) % 2)] = ctx.val("st_te")


def ph_upgate(ctx, nc, st, tag, x, ident, identb, w_up, w_gate, up_o, gate_o,
              cc_ld, ccin, ccout, wq_cache=None):
    ctx.tile_mark = {}
    sb = lambda n, s, d=F32: st.enter_context(nc.sbuf_tensor(tag + n, s, d))
    ps = lambda n, s, d=F32: st.enter_context(nc.psum_tensor(tag + n, s, d))
    wqtu_sb = sb("wqtu_sb", [128, 8 * 24 * 128], BF16)
    wqtg_sb = sb("wqtg_sb", [128, 8 * 24 * 128], BF16)
    xt = [sb(f"xt{i}", [128, 8 * TTS]) for i in range(2)]
    xnt = sb("xnt", [128, 8 * TTS])
    sqt = xnt
    xqt = [sb(f"xqt{i}", [128, 8 * TTS], BF16) for i in range(2)]
    mrep = sb("mrep", [128, TTS])
    vrep = sb("vrep", [128, TTS])
    allones = sb("allones", [128, 128])
    onesrow = sb("onesrow", [1, 128])
    idt = sb("idt", [128, 128])
    idtb = sb("idtb", [128, 128], BF16)
    epscol = sb("epscol", [128, 1])
    c0col = sb("c0col", [128, 1])
    s0col = sb("s0col", [128, 1])
    rowbuf = sb("rowbuf", [1, 16])
    stgU = [sb(f"stgU{i}", [128, TTS]) for i in range(2)]
    stgG = [sb(f"stgG{i}", [128, TTS]) for i in range(2)]
    pupc = sb("pupc", [128, 1])
    pgc = sb("pgc", [128, 1])
    tmpc = sb("tmpc", [128, 2])
    prow2 = sb("prow2", [1, 2])
    psMisc = ps("psMisc", [128, 130])
    psTrSt = contextlib.ExitStack()
    psTr = [psTrSt.enter_context(nc.psum_tensor(tag + f"psTr{i}", [128, 128],
                                                BF16)) for i in range(2)]

    if True:
        ctx.op("vector", lambda e: e.memset(allones[:], 1.0))
        ctx.op("vector", lambda e: e.memset(onesrow[:], 1.0))
        ctx.op("vector", lambda e: e.memset(epscol[:], 1e-5))
        ctx.op("vector", lambda e: e.memset(pupc[:], 0.0))
        ctx.op("vector", lambda e: e.memset(pgc[:], 0.0), inc=("st_ve", 1))
        ctx.op("sync", lambda e: e.dma_start(out=idt[:], in_=ident), inc=("ldc", 16))
        ctx.op("sync", lambda e: e.dma_start(out=idtb[:], in_=identb), inc=("ldc", 16))

        ctx.wait("tensor", "ldc", ctx.val("ldc"))

        if wq_cache is None or wq_cache[2] == "store":
            wtb = [sb(f"wtb{i}", [128, D]) for i in range(2)]
            bq = sb("bq", [128, D // WBLK])
            binv = sb("binv", [128, D // WBLK])
            i32w = st.enter_context(nc.sbuf_tensor(tag + "i32w", [128, D], I32))
            i32b = st.enter_context(nc.sbuf_tensor(tag + "i32b", [128, D], I32))
            ztw = sb("ztw", [128, D])
            wqt_s = [sb(f"wqt_s{i}", [128, D], BF16) for i in range(2)]
            emit_dequant(ctx, w_up, 24, D, wtb, bq, binv, i32w, i32b, ztw,
                         wqt_s, idtb, psTr, wqtu_sb)
            emit_dequant(ctx, w_gate, 24, D, wtb, bq, binv, i32w, i32b, ztw,
                         wqt_s, idtb, psTr, wqtg_sb)
            wq_ready = ctx.val("st_sc")
            wq_wait = [("st_sc", wq_ready)]
            if wq_cache is not None:
                ctx.op("sync", lambda e: e.dma_start(
                    out=wq_cache[0], in_=wqtu_sb[:]),
                    wait=[("st_sc", wq_ready)], inc=("stoW", 16))
                ctx.op("sync", lambda e: e.dma_start(
                    out=wq_cache[1], in_=wqtg_sb[:]), inc=("stoW", 16))
        else:
            lvu = ctx.op("sync", lambda e: e.dma_start(
                out=wqtu_sb[:], in_=wq_cache[0]), inc=("ldw0", 16))
            lvg = ctx.op("gpsimd", lambda e: e.dma_start(
                out=wqtg_sb[:], in_=wq_cache[1]), inc=("ldgw", 16))
            wq_ready = 0
            wq_wait = [("ldw0", lvu), ("ldgw", lvg)]
        psTrSt.close()
        psS = ps("psS", [128, 2 * TTS])
        psU = [ps(f"psU{i}", [128, TTS]) for i in range(2)]
        psG = [ps(f"psG{i}", [128, TTS]) for i in range(2)]
        if wq_ready:
            ctx.wait("tensor", "st_sc", wq_ready)
            ctx.wait("vector", "st_sc", wq_ready)
        emit_scale_col(ctx, cc_ld[0], 1, 1, cc_ld[2], QMAX_A, rowbuf, onesrow,
                       psMisc, c0col, s0col, ld_eng="gpsimd",
                       ld_wait=[("cc", cc_ld[1])])

        evac_mark = {}
        dma_mark = {}
        for it in range(NTILE):
            b, tt = divmod(it, T // TTS)
            buf = xt[it % 2]
            xq = xqt[it % 2]
            lsem = "ldx0" if it % 2 == 0 else "ldx1"
            src = ap_(x, b * D * T + tt * TTS, [[T, 128], [128 * T, 8], [1, TTS]])
            ldwait = [("st_ve", ctx.tile_mark.get(("xq", it - 2), 0))]
            lv = ctx.op("sync", lambda e, s=src, bu=buf: e.dma_start(out=bu[:], in_=s),
                        inc=(lsem, 16), wait=ldwait)
            emit_ln(ctx, buf, sqt, psS, mrep, vrep, allones, xnt, 8, TTS, lsem, lv, epscol)
            ctx.op("vector", lambda e: e.tensor_scalar(xnt[:], xnt[:], c0col[:, 0:1],
                                                       MAGIC, OP.mult, OP.add))
            ctx.drain("vector")
            ctx.op("vector", lambda e, xq=xq: e.tensor_scalar(xq[:], xnt[:], MAGIC,
                                                              None, OP.subtract),
                   inc=("st_ve", 1))
            ctx.drain("vector")
            ctx.tile_mark[("xq", it)] = ctx.val("st_ve")
            xq_v = ctx.val("st_ve")

            for oc in range(24):
                g = it * 24 + oc
                for which, psP, stg, out_d, pc in (
                        (0, psU[oc % 2], stgU[oc % 2], up_o, pupc),
                        (1, psG[oc % 2], stgG[oc % 2], gate_o, pgc)):
                    w_sb = wqtg_sb if which else wqtu_sb
                    mmw = [("st_ve", xq_v)] + wq_wait
                    em = evac_mark.get((which, g - 2))
                    if em:
                        mmw.append(("st_sc", em))
                    for kc in range(8):
                        ctx.op("tensor", lambda e, kc=kc, oc=oc, psP=psP, xq=xq, w_sb=w_sb:
                               e.matmul(psP[:], w_sb[:, (kc * 24 + oc) * 128:
                                                     (kc * 24 + oc) * 128 + 128],
                                        xq[:, kc * TTS:(kc + 1) * TTS],
                                        start=(kc == 0), stop=(kc == 7)),
                               wait=(mmw if kc == 0 else None),
                               inc=(("st_te", 1) if kc == 7 else None))
                    tev = ctx.val("st_te")
                    ssem = ("stoG" if which else "stoU") + str(oc % 2)
                    scw = [("st_te", tev)]
                    dm = dma_mark.get((which, g - 2))
                    if dm:
                        scw.append((ssem, dm))
                    vm = ctx.tile_mark.get(("vem", which, g - 2))
                    if vm:
                        scw.append(("st_ve", vm))
                    ctx.op("scalar", lambda e, psP=psP, stg=stg: e.activation(
                        stg[:], psP[:], AF.Copy, scale=s0col[:, 0:1]),
                        wait=scw, inc=("st_sc", 1))
                    evac_mark[(which, g)] = ctx.val("st_sc")
                    scv = ctx.val("st_sc")
                    ctx.op("vector", lambda e, stg=stg, which=which: e.reduce_max(
                        tmpc[:, which:which + 1], stg[:], AX.X,
                        apply_absolute_value=True), wait=[("st_sc", scv)])
                    ctx.drain("vector")
                    ctx.op("vector", lambda e, pc=pc, which=which: e.tensor_tensor(
                        pc[:], pc[:], tmpc[:, which:which + 1], OP.max),
                        inc=("st_ve", 1))
                    ctx.drain("vector")
                    ctx.tile_mark[("vem", which, g)] = ctx.val("st_ve")
                    dst = ap_(out_d, b * FF * T + oc * 128 * T + tt * TTS,
                              [[T, 128], [1, TTS]])
                    ctx.op("sync", lambda e, stg=stg, dst=dst: e.dma_start(
                        out=dst, in_=stg[:]), wait=[("st_sc", scv)], inc=(ssem, 16))
                    dma_mark[(which, g)] = ctx.val(ssem)

        ctx.op("tensor", lambda e: e.transpose(psMisc[0:1, 0:128], pupc[:], idt[:]),
               wait=[("st_ve", ctx.val("st_ve"))], inc=("st_te", 1))
        ctx.op("vector", lambda e: e.reduce_max(prow2[:, 0:1], psMisc[0:1, 0:128], AX.X),
               wait=[("st_te", ctx.val("st_te"))], inc=("st_ve", 1))
        ctx.op("tensor", lambda e: e.transpose(psMisc[0:1, 0:128], pgc[:], idt[:]),
               wait=[("st_ve", ctx.val("st_ve"))], inc=("st_te", 1))
        ctx.op("vector", lambda e: e.reduce_max(prow2[:, 1:2], psMisc[0:1, 0:128], AX.X),
               wait=[("st_te", ctx.val("st_te"))], inc=("st_ve", 1))
        for s in ("stoU0", "stoU1", "stoG0", "stoG1"):
            ctx.wait("sync", s, ctx.val(s))
        return emit_ccmax(ctx, prow2[:, 0:2], ccin, ccout, ctx.val("st_ve"))


def ph_gate(ctx, nc, st, tag, gate, ident, pre3_o, cc_ld, ccin, ccout):
    ctx.tile_mark = {}
    sb = lambda n, s, d=F32: st.enter_context(nc.sbuf_tensor(tag + n, s, d))
    ps = lambda n, s, d=F32: st.enter_context(nc.psum_tensor(tag + n, s, d))
    if True:
        gt = [sb(f"gt{i}", [128, T]) for i in range(4)]
        sg = [sb(f"sg{i}", [128, T]) for i in range(4)]
        onesrow = sb("onesrow", [1, 128])
        idt = sb("idt", [128, 128])
        c2col = sb("c2col", [128, 1])
        s2col = sb("s2col", [128, 1])
        s2bcol = sb("s2bcol", [128, 1])
        rowbuf = sb("rowbuf", [1, 32])
        pcol = sb("pcol", [128, 1])
        tmpc = sb("tmpc", [128, 1])
        prow = sb("prow", [1, 128])
        psMisc = ps("psMisc", [128, 130])

        ctx.op("vector", lambda e: e.memset(onesrow[:], 1.0))
        ctx.op("vector", lambda e: e.memset(prow[:], 0.0))
        ctx.op("vector", lambda e: e.memset(pcol[:], 0.0), inc=("st_ve", 1))
        ctx.op("sync", lambda e: e.dma_start(out=idt[:], in_=ident), inc=("ldc", 16))
        emit_scale_col(ctx, cc_ld[0], 1, 1, cc_ld[2], QMAX_A, rowbuf, onesrow,
                       psMisc, c2col, s2col, ld_eng="gpsimd",
                       ld_wait=[("cc", cc_ld[1])])
        ctx.drain("scalar")
        ctx.op("scalar", lambda e: e.activation(s2bcol[:], s2col[:], AF.Copy,
                                                scale=1.0 / 65536.0), inc=("st_sc", 1))
        ctx.wait("vector", "st_sc", ctx.val("st_sc"))
        ctx.wait("tensor", "ldc", ctx.val("ldc"))

        ntile = B_CORE * (FF // 128)
        LSEMS = ("ldx0", "ldx1", "ldu0", "ldu1")

        def stageA(it):
            b, oc = divmod(it, FF // 128)
            buf = gt[it % 4]
            lsem = LSEMS[it % 4]
            src = ap_(gate, b * FF * T + oc * 128 * T, [[T, 128], [1, T]])
            lw = [("st_ve", ctx.tile_mark.get(("done", it - 4), 0))]
            lv = ctx.op("sync", lambda e, s=src, bu=buf: e.dma_start(out=bu[:], in_=s),
                        inc=(lsem, 16), wait=lw)
            ctx.op("vector", lambda e, bu=buf: e.tensor_scalar(
                bu[:], bu[:], c2col[:, 0:1], MAGIC, OP.mult, OP.add),
                wait=[(lsem, lv)])
            ctx.drain("vector")
            ctx.op("vector", lambda e, bu=buf: e.tensor_scalar(
                bu[:], bu[:], MAGIC, None, OP.subtract), inc=("st_ve", 1))
            ctx.drain("vector")
            ctx.tile_mark[("gq", it)] = ctx.val("st_ve")

        def stageB(it):
            buf = gt[it % 4]
            sgb = sg[it % 4]
            smw = [("st_ve", ctx.tile_mark[("gq", it)]),
                   ("st_ve", ctx.tile_mark.get(("done", it - 4), 0))]
            dm = ctx.tile_mark.get(("dma", it - 4))
            if dm:
                smw.append(dm)
            ctx.op("scalar", lambda e, bu=buf, sgb=sgb: e.activation(
                sgb[:], bu[:], AF.Sigmoid, scale=s2col[:, 0:1]),
                wait=smw)
            ctx.drain("scalar")
            ctx.op("scalar", lambda e, sgb=sgb: e.activation(
                sgb[:], sgb[:], AF.Copy, scale=65536.0, bias=MAGIC),
                inc=("st_sc", 1))
            ctx.tile_mark[("sig", it)] = ctx.val("st_sc")

        def stageC(it):
            b, oc = divmod(it, FF // 128)
            buf = gt[it % 4]
            sgb = sg[it % 4]
            ssem = "sto0" if it % 2 == 0 else "sto1"
            ctx.op("vector", lambda e, sgb=sgb: e.tensor_scalar(
                sgb[:], sgb[:], MAGIC, 65535.0, OP.subtract, OP.min),
                wait=[("st_sc", ctx.tile_mark[("sig", it)])])
            ctx.drain("vector")
            ctx.op("vector", lambda e, sgb=sgb, bu=buf: e.scalar_tensor_tensor(
                sgb[:], sgb[:], s2bcol[:, 0:1], bu[:], OP.mult, OP.mult),
                inc=("st_ve", 1))
            ctx.drain("vector")
            scv = ctx.val("st_ve")
            ctx.op("vector", lambda e, sgb=sgb: e.reduce_max(
                tmpc[:], sgb[:], AX.X, apply_absolute_value=True))
            ctx.drain("vector")
            ctx.op("vector", lambda e: e.tensor_tensor(pcol[:], pcol[:], tmpc[:],
                                                       OP.max), inc=("st_ve", 1))
            ctx.drain("vector")
            ctx.tile_mark[("done", it)] = ctx.val("st_ve")
            dst = ap_(pre3_o, b * FF * T + oc * 128 * T, [[T, 128], [1, T]])
            ctx.op("sync", lambda e, sgb=sgb, dst=dst: e.dma_start(
                out=dst, in_=sgb[:]), wait=[("st_ve", scv)], inc=(ssem, 16))
            ctx.tile_mark[("dma", it)] = (ssem, ctx.val(ssem))

        for it in range(ntile):
            stageA(it)
            stageB(it)
            if it >= 1:
                stageC(it - 1)
        stageC(ntile - 1)

        ve = emit_colmax_sb(ctx, pcol, ap_(psMisc, 0, [[130, 1], [1, 128]]), prow,
                            idt)
        for s in ("sto0", "sto1"):
            ctx.wait("sync", s, ctx.val(s))
        return emit_ccmax(ctx, prow[:, 0:2], ccin, ccout, ve)


def ph_prod(ctx, nc, st, tag, pre3, up, ident, pre4_o, cc3, cc1, ccin, ccout):
    ctx.tile_mark = {}
    sb = lambda n, s, d=F32: st.enter_context(nc.sbuf_tensor(tag + n, s, d))
    ps = lambda n, s, d=F32: st.enter_context(nc.psum_tensor(tag + n, s, d))
    if True:
        g2t = [sb(f"g2t{i}", [128, T]) for i in range(4)]
        upt = [sb(f"upt{i}", [128, T]) for i in range(4)]
        onesrow = sb("onesrow", [1, 128])
        idt = sb("idt", [128, 128])
        c3col = sb("c3col", [128, 1])
        s3col = sb("s3col", [128, 1])
        c1col = sb("c1col", [128, 1])
        s1col = sb("s1col", [128, 1])
        rowbuf = sb("rowbuf", [1, 32])
        pcol = sb("pcol", [128, 1])
        tmpc = sb("tmpc", [128, 1])
        prow = sb("prow", [1, 128])
        psMisc = ps("psMisc", [128, 130])

        ctx.op("vector", lambda e: e.memset(onesrow[:], 1.0))
        ctx.op("vector", lambda e: e.memset(prow[:], 0.0))
        ctx.op("vector", lambda e: e.memset(pcol[:], 0.0), inc=("st_ve", 1))
        ctx.op("sync", lambda e: e.dma_start(out=idt[:], in_=ident), inc=("ldc", 16))
        emit_scale_col(ctx, cc3[0], 1, 1, cc3[2], QMAX_A, rowbuf, onesrow,
                       psMisc, c3col, s3col, ld_eng="gpsimd",
                       ld_wait=[("cc", cc3[1])])
        emit_scale_col(ctx, cc1[0], 1, 1, cc1[2], QMAX_A, rowbuf, onesrow,
                       psMisc, c1col, s1col, ld_eng="gpsimd",
                       ld_wait=[("cc", cc1[1])])
        s31col = sb("s31col", [128, 1])
        ctx.op("vector", lambda e: e.tensor_tensor(
            s31col[:], s3col[:], s1col[:], OP.mult), inc=("st_ve", 1))
        ctx.drain("vector")
        ctx.wait("vector", "st_sc", ctx.val("st_sc"))
        ctx.wait("scalar", "st_ve", ctx.val("st_ve"))
        ctx.wait("tensor", "ldc", ctx.val("ldc"))

        ntile = B_CORE * (FF // 128)
        LSEMS = ("ldx0", "ldx1", "ldu0", "ldu1")
        LSEMS2 = ("ldw0", "ldw1", "ldm0", "ldm1")

        def stageA(it):
            b, oc = divmod(it, FF // 128)
            bg = g2t[it % 4]
            bu = upt[it % 4]
            off = b * FF * T + oc * 128 * T
            lw = [("st_ve", ctx.tile_mark.get(("done", it - 4), 0))]
            dmp = ctx.tile_mark.get(("dma", it - 4))
            if dmp:
                lw.append(dmp)
            lv1 = ctx.op("sync", lambda e, bg=bg, off=off: e.dma_start(
                out=bg[:], in_=ap_(pre3, off, [[T, 128], [1, T]])),
                inc=(LSEMS[it % 4], 16), wait=lw)
            lv2 = ctx.op("sync", lambda e, bu=bu, off=off: e.dma_start(
                out=bu[:], in_=ap_(up, off, [[T, 128], [1, T]])),
                inc=(LSEMS2[it % 4], 16))
            ctx.tile_mark[("ld2", it)] = (LSEMS2[it % 4], lv2)
            ctx.op("vector", lambda e, bg=bg: e.tensor_scalar(
                bg[:], bg[:], c3col[:, 0:1], MAGIC, OP.mult, OP.add),
                wait=[(LSEMS[it % 4], lv1)])
            ctx.drain("vector")
            ctx.op("vector", lambda e, bg=bg: e.tensor_scalar(
                bg[:], bg[:], MAGIC, None, OP.subtract), inc=("st_ve", 1))
            ctx.drain("vector")
            ctx.tile_mark[("gq3", it)] = ctx.val("st_ve")

        def stageB(it):
            bu = upt[it % 4]
            scw = [ctx.tile_mark[("ld2", it)],
                   ("st_ve", ctx.tile_mark.get(("done", it - 4), 0))]
            ctx.op("scalar", lambda e, bu=bu: e.activation(
                bu[:], bu[:], AF.Copy, scale=c1col[:, 0:1], bias=MAGIC),
                wait=scw)
            ctx.drain("scalar")
            ctx.op("scalar", lambda e, bu=bu: e.activation(
                bu[:], bu[:], AF.Copy, bias=-MAGIC), inc=("st_sc", 1))
            ctx.drain("scalar")
            ctx.tile_mark[("sdone", it)] = ctx.val("st_sc")

        def stageC(it):
            b, oc = divmod(it, FF // 128)
            bg = g2t[it % 4]
            bu = upt[it % 4]
            off = b * FF * T + oc * 128 * T
            ssem = "sto0" if it % 2 == 0 else "sto1"
            ctx.op("vector", lambda e, bg=bg, bu=bu: e.scalar_tensor_tensor(
                bg[:], bg[:], s31col[:, 0:1], bu[:], OP.mult, OP.mult),
                wait=[("st_sc", ctx.tile_mark[("sdone", it)]),
                      ("st_ve", ctx.tile_mark[("gq3", it)])], inc=("st_ve", 1))
            ctx.drain("vector")
            scv = ctx.val("st_ve")
            ctx.op("vector", lambda e, bg=bg: e.reduce_max(
                tmpc[:], bg[:], AX.X, apply_absolute_value=True))
            ctx.drain("vector")
            ctx.op("vector", lambda e: e.tensor_tensor(pcol[:], pcol[:], tmpc[:],
                                                       OP.max), inc=("st_ve", 1))
            ctx.drain("vector")
            ctx.tile_mark[("done", it)] = ctx.val("st_ve")
            dst = ap_(pre4_o, off, [[T, 128], [1, T]])
            ctx.op("sync", lambda e, bg=bg, dst=dst: e.dma_start(
                out=dst, in_=bg[:]), wait=[("st_ve", scv)], inc=(ssem, 16))
            ctx.tile_mark[("dma", it)] = (ssem, ctx.val(ssem))

        for it in range(ntile):
            stageA(it)
            stageB(it)
            if it >= 1:
                stageC(it - 1)
        stageC(ntile - 1)

        ve = emit_colmax_sb(ctx, pcol, ap_(psMisc, 0, [[130, 1], [1, 128]]), prow,
                            idt)
        for s in ("sto0", "sto1"):
            ctx.wait("sync", s, ctx.val(s))
        return emit_ccmax(ctx, prow[:, 0:2], ccin, ccout, ve)


def emit_ffn_tail(ctx, nc, st, tag, pre4, xres, cc4, w_down, identb_t,
                  out_writer, wd_cache=None):
    sb = lambda n, s, d=F32: st.enter_context(nc.sbuf_tensor(tag + n, s, d))
    wdt_sb = sb("wdt_sb", [128, 24 * 8 * 128], BF16)
    wdb = [sb(f"wdb{i}", [128, FF]) for i in range(2)]
    bqd = sb("bqd", [128, FF // WBLK])
    binvd = sb("binvd", [128, FF // WBLK])
    i32d = st.enter_context(nc.sbuf_tensor(tag + "i32d", [128, FF], I32))
    i32d2 = st.enter_context(nc.sbuf_tensor(tag + "i32d2", [128, FF], I32))
    ztd = sb("ztd", [128, FF])
    wqt_d1 = sb("wqt_d", [128, FF], BF16)
    wqt_d = [wqt_d1, wqt_d1]
    c4col = sb("c4col", [128, 1])
    s4col = sb("s4col", [128, 1])
    rowbuf4 = sb("rowbuf4", [1, 32])
    obuf = [st.enter_context(nc.sbuf_tensor(tag + f"obuf{i}", [128, 24 * TTS],
                                            BF16))
            for i in range(2)]
    p4t = [sb(f"p4t{i}", [128, TTS]) for i in range(2)]
    xrt = [sb(f"xrt{i}", [128, 8 * TTS]) for i in range(2)]
    stgD = [sb(f"stgD{i}", [128, TTS]) for i in range(2)]
    onesrow4 = sb("onesrow4", [1, 128])
    psMisc4 = st.enter_context(nc.psum_tensor(tag + "psMisc4", [128, 2], F32))
    psTr2St = contextlib.ExitStack()
    psTr2 = [psTr2St.enter_context(nc.psum_tensor(tag + f"psTr2{i}", [128, 128],
                                                  BF16)) for i in range(2)]
    ctx.op("vector", lambda e: e.memset(onesrow4[:], 1.0), inc=("st_ve", 1))
    ctx.wait("tensor", "ldc", ctx.val("ldc"))
    if wd_cache is None or wd_cache[1] == "store":
        emit_dequant(ctx, w_down, 8, FF, wdb, bqd, binvd, i32d, i32d2, ztd,
                     wqt_d, identb_t, psTr2, wdt_sb)
        wd_ready = ctx.val("st_sc")
        wd_wait = [("st_sc", wd_ready)]
        if wd_cache is not None:
            ctx.op("sync", lambda e: e.dma_start(
                out=wd_cache[0], in_=wdt_sb[:]),
                wait=[("st_sc", wd_ready)], inc=("stoW", 16))
    else:
        lvd = ctx.op("sync", lambda e: e.dma_start(
            out=wdt_sb[:], in_=wd_cache[0]), inc=("ldw0", 16))
        wd_wait = [("ldw0", lvd)]
    psTr2St.close()
    emit_scale_col(ctx, cc4[0], 1, 1, cc4[2], QMAX_A, rowbuf4, onesrow4,
                   psMisc4, c4col, s4col, extra_mult=0.5, ld_eng="gpsimd",
                   ld_wait=[("cc", cc4[1])])
    psD = [st.enter_context(nc.psum_tensor(tag + f"psD{i}", [128, TTS], F32))
           for i in range(2)]

    for it in range(NTILE):
        b, tt = divmod(it, T // TTS)
        ob = obuf[it % 2]
        xb = xrt[it % 2]
        rsem = "ldr0" if it % 2 == 0 else "ldr1"
        rw = [("st_sc", ctx.tile_mark.get(("ft_xv", it - 2), 0))]
        rv = ctx.op("sync", lambda e, b=b, tt=tt, xb=xb: e.dma_start(
            out=xb[:], in_=ap_(xres, b * D * T + tt * TTS,
                               [[T, 128], [128 * T, 8], [1, TTS]])),
            inc=(rsem, 16), wait=rw)
        for kc in range(24):
            pb = p4t[kc % 2]
            psem = "ldf0" if kc % 2 == 0 else "ldf1"
            lw = [("st_ve", ctx.tile_mark.get(("ft_rnd", it, kc - 2),
                   ctx.tile_mark.get(("ft_rnd", it - 1, 22 + kc), 0)))]
            lv = ctx.op("sync", lambda e, b=b, tt=tt, kc=kc, pb=pb: e.dma_start(
                out=pb[:], in_=ap_(pre4, b * FF * T + kc * 128 * T + tt * TTS,
                                    [[T, 128], [1, TTS]])),
                inc=(psem, 16), wait=lw)
            vw = [(psem, lv)]
            if kc == 0:
                vw.append(("st_te", ctx.tile_mark.get(("ft_te", it - 2), 0)))
            ctx.op("vector", lambda e, pb=pb: e.tensor_scalar(
                pb[:], pb[:], c4col[:, 0:1], MAGIC, OP.mult, OP.add),
                wait=vw)
            ctx.drain("vector")
            ctx.op("vector", lambda e, ob=ob, kc=kc, pb=pb: e.tensor_scalar(
                ob[:, kc * TTS:(kc + 1) * TTS], pb[:], MAGIC, None, OP.subtract),
                inc=("st_ve", 1))
            ctx.drain("vector")
            ctx.tile_mark[("ft_rnd", it, kc)] = ctx.val("st_ve")
        ov = ctx.val("st_ve")
        for mc in range(8):
            psP = psD[mc % 2]
            mmw = [("st_ve", ov)] + wd_wait
            em = ctx.tile_mark.get(("ft_em", it, mc - 2))
            if em is None and it > 0:
                em = ctx.tile_mark.get(("ft_em", it - 1, 6 + mc))
            if em:
                mmw.append(("st_sc", em))
            for kc in range(24):
                ctx.op("tensor", lambda e, kc=kc, mc=mc, psP=psP, ob=ob: e.matmul(
                    psP[:], wdt_sb[:, (kc * 8 + mc) * 128:(kc * 8 + mc) * 128 + 128],
                    ob[:, kc * TTS:(kc + 1) * TTS],
                    start=(kc == 0), stop=(kc == 23)),
                    wait=(mmw if kc == 0 else None),
                    inc=(("st_te", 1) if kc == 23 else None))
            tev = ctx.val("st_te")
            stg = stgD[mc % 2]
            scw = [("st_te", tev), (rsem, rv)]
            dm = ctx.tile_mark.get(("ft_dm", it, mc - 2))
            if dm is None and it > 0:
                dm = ctx.tile_mark.get(("ft_dm", it - 1, 6 + mc))
            if dm:
                scw.append(dm)
            ctx.op("vector", lambda e, psP=psP, stg=stg, xb=xb, mc=mc:
                   e.scalar_tensor_tensor(stg[:], psP[:], s4col[:, 0:1],
                                          xb[:, mc * TTS:(mc + 1) * TTS],
                                          OP.mult, OP.add),
                   wait=scw, inc=("st_sc", 1))
            ctx.drain("vector")
            ctx.tile_mark[("ft_em", it, mc)] = ctx.val("st_sc")
            scv = ctx.val("st_sc")
            out_writer(ctx, it, b, tt, mc, stg, scv)
        ctx.tile_mark[("ft_xv", it)] = ctx.val("st_sc")
        ctx.tile_mark[("ft_te", it)] = ctx.val("st_te")


def ph_tail(ctx, nc, st, pre4, xres, cc4, w_down, identb, out, wd_cache):
    ctx.tile_mark = {}
    idtb = st.enter_context(nc.sbuf_tensor("t2idtb", [128, 128], BF16))
    ctx.op("sync", lambda e: e.dma_start(out=idtb[:], in_=identb),
           inc=("ldc", 16))

    def writer(ctx, it, b, tt, mc, stg, scv):
        ssem = "sto0" if (it * 8 + mc) % 2 == 0 else "sto1"
        dst = ap_(out, b * D * T + mc * 128 * T + tt * TTS, [[T, 128], [1, TTS]])
        ctx.op("sync", lambda e, stg=stg, dst=dst: e.dma_start(
            out=dst, in_=stg[:]), wait=[("st_sc", scv)], inc=(ssem, 16))
        ctx.tile_mark[("ft_dm", it, mc)] = (ssem, ctx.val(ssem))

    emit_ffn_tail(ctx, nc, st, "t2", pre4, xres, cc4, w_down, idtb, writer,
                  wd_cache=wd_cache)
    for s in ("sto0", "sto1"):
        ctx.wait("sync", s, ctx.val(s))


def ph_P4(ctx, nc, st, x0, pre4, cc4, w_down, ident, identb, w_inT, w_dtT,
          w_bcT, w_outT, cdiag, a_jln, dcol, cbcol, bdtcol, nwcol, rscal,
          x2_o, x1_s, h2_s, mg_s, xc_s, dt_s, bcps_d, ccin, ccout, wd_cache):
    ctx.tile_mark = {}
    sb = lambda n, s, d=F32: st.enter_context(nc.sbuf_tensor(n, s, d))
    ps = lambda n, s, d=F32: st.enter_context(nc.psum_tensor(n, s, d))
    if True:
        idt = sb("idt", [128, 128])
        idtb = sb("idtb", [128, 128], BF16)
        allones = sb("allones", [128, 128])
        onesrow = sb("onesrow", [1, 128])
        epscol = sb("epscol", [128, 1])
        eps6col = sb("eps6col", [128, 1])
        dcol_sb = sb("dcol_sb", [128, 16])
        cbcol_sb = sb("cbcol_sb", [128, 16])
        bdtcol_sb = sb("bdtcol_sb", [128, 16])
        nwcol_sb = sb("nwcol_sb", [128, 8])
        atan_sb = sb("atan_sb", [128, 16 * DS])
        sctanb = sb("sctanb", [128, 1])
        rscol = sb("rscol", [128, 1])
        rowmisc = sb("rowmisc", [1, 8])
        pcol = sb("pcol", [128, 1])
        prow = sb("prow", [1, 128])
        psMisc = ps("psMisc", [128, 130])

        ctx.op("vector", lambda e: e.memset(allones[:], 1.0))
        ctx.op("vector", lambda e: e.memset(onesrow[:], 1.0))
        ctx.op("vector", lambda e: e.memset(epscol[:], 1e-5))
        ctx.op("vector", lambda e: e.memset(eps6col[:], 1e-6))
        ctx.op("vector", lambda e: e.memset(sctanb[:], -SC_C * SC_IN))
        ctx.op("vector", lambda e: e.memset(prow[:], 0.0))
        ctx.op("vector", lambda e: e.memset(pcol[:], 0.0), inc=("st_ve", 1))
        for (dst, srcd) in ((idt, ident), (idtb, identb), (dcol_sb, dcol),
                            (cbcol_sb, cbcol), (bdtcol_sb, bdtcol),
                            (nwcol_sb, nwcol)):
            ctx.op("sync", lambda e, dst=dst, srcd=srcd: e.dma_start(
                out=dst[:], in_=srcd), inc=("ldc", 16))
        ctx.op("sync", lambda e: e.dma_start(out=atan_sb[:], in_=a_jln),
               inc=("ldc", 16))
        ctx.op("sync", lambda e: e.dma_start(out=rowmisc[:, 0:1], in_=rscal),
               inc=("ldc", 16))
        lcv = ctx.val("ldc")
        ctx.op("scalar", lambda e: e.activation(atan_sb[:], atan_sb[:], AF.Tanh,
                                                scale=SCA_IN), wait=[("ldc", lcv)])
        ctx.drain("scalar")
        ctx.op("scalar", lambda e: e.activation(atan_sb[:], atan_sb[:], AF.Exp,
                                                scale=SCA_OUT), inc=("st_sc", 1))
        ctx.drain("scalar")
        ctx.op("vector", lambda e: e.tensor_scalar(atan_sb[:], atan_sb[:],
                                                   -1.0 / 16.0, None, OP.mult),
               wait=[("st_sc", ctx.val("st_sc"))], inc=("st_ve", 1))
        ctx.drain("vector")
        a_ready = ctx.val("st_ve")
        ctx.op("scalar", lambda e: e.activation(rowmisc[:, 1:2], rowmisc[:, 0:1],
                                                AF.Sigmoid),
               wait=[("ldc", ctx.val("ldc"))])
        ctx.drain("scalar")
        ctx.op("scalar", lambda e: e.activation(rowmisc[:, 1:2], rowmisc[:, 1:2],
                                                AF.Copy, scale=0.5), inc=("st_sc", 1))
        ctx.op("tensor", lambda e: e.matmul(
            ap_(psMisc, 128, [[130, 128], [1, 1]]), onesrow[:], rowmisc[:, 1:2],
            start=True, stop=True),
            wait=[("st_sc", ctx.val("st_sc")), ("ldc", ctx.val("ldc")),
                  ("st_ve", 1)],
            inc=("st_te", 1))
        ctx.op("scalar", lambda e: e.activation(rscol[:],
                                                ap_(psMisc, 128, [[130, 128], [1, 1]]),
                                                AF.Copy),
               wait=[("st_te", ctx.val("st_te"))], inc=("st_sc", 1))

        phA = contextlib.ExitStack()

        def writer(ctx, it, b, tt, mc, stg, scv):
            ssem = "stoX0" if (it * 8 + mc) % 2 == 0 else "stoX1"
            dst = ap_(x1_s, b * D * T + mc * 128 * T + tt * TTS, [[T, 128], [1, TTS]])
            ctx.op("sync", lambda e, stg=stg, dst=dst: e.dma_start(
                out=dst, in_=stg[:]), wait=[("st_sc", scv)], inc=(ssem, 16))
            ctx.tile_mark[("ft_dm", it, mc)] = (ssem, ctx.val(ssem))

        emit_ffn_tail(ctx, nc, phA, "t1", pre4, x0, cc4, w_down, idtb, writer,
                      wd_cache=wd_cache)
        ctx.wait("sync", "stoX0", ctx.val("stoX0"))
        ctx.wait("sync", "stoX1", ctx.val("stoX1"))
        phA_sc = ctx.val("st_sc")
        phA_te = ctx.val("st_te")
        phA.close()

        phB = contextlib.ExitStack()
        sbB = lambda n, s, d=F32: phB.enter_context(nc.sbuf_tensor(n, s, d))
        w_in_sb = sbB("w_in_sb", [128, 8 * 32 * 128], BF16)
        cdiag_sb = sbB("cdiag_sb", [128, 16 * 4 * 128], BF16)
        x1t = [sbB(f"x1t{i}", [128, 8 * TTS]) for i in range(2)]
        h2t = sbB("h2t", [128, 8 * TTS])
        xnb = sbB("xnb", [128, 8 * TTS], BF16)
        xbr = [sbB(f"xbr{i}", [128, 16 * (TTS + 4)], BF16) for i in range(2)]
        mrep = sbB("mrepB", [128, TTS])
        vrep = sbB("vrepB", [128, TTS])
        stgB = [sbB(f"stgB{i}", [128, TTS]) for i in range(2)]
        psSB = phB.enter_context(nc.psum_tensor("psSB", [128, 2 * TTS], F32))
        psI = [phB.enter_context(nc.psum_tensor(f"psI{i}", [128, TTS], F32))
               for i in range(2)]

        ctx.wait("gpsimd", "st_te", phA_te)
        ctx.wait("gpsimd", "st_sc", phA_sc)
        ctx.wait("gpsimd", "stoX0", ctx.val("stoX0"))
        ctx.wait("gpsimd", "stoX1", ctx.val("stoX1"))
        ctx.op("gpsimd", lambda e: e.dma_start(out=w_in_sb[:], in_=w_inT),
               inc=("ldgw", 16))
        ctx.op("gpsimd", lambda e: e.dma_start(out=cdiag_sb[:], in_=cdiag),
               inc=("ldgw", 16))
        wB_ready = ctx.val("ldgw")
        ctx.wait("tensor", "ldgw", wB_ready)
        ctx.wait("vector", "st_sc", phA_sc)
        ctx.wait("scalar", "st_te", phA_te)
        ctx.wait("tensor", "st_sc", phA_sc)

        for it in range(NTILE):
            b, tt = divmod(it, T // TTS)
            xb = x1t[it % 2]
            xr = xbr[it % 2]
            lsem = "ldx0" if it % 2 == 0 else "ldx1"
            lw = [("st_sc", ctx.tile_mark.get(("phb_done", it - 2), 0)),
                  ("stoX0", ctx.val("stoX0")), ("stoX1", ctx.val("stoX1"))]
            lv = ctx.op("sync", lambda e, b=b, tt=tt, xb=xb: e.dma_start(
                out=xb[:], in_=ap_(x1_s, b * D * T + tt * TTS,
                                   [[T, 128], [128 * T, 8], [1, TTS]])),
                inc=(lsem, 16), wait=lw)
            for smn in ("stoH",):
                pv = ctx.tile_mark.get(("phb_" + smn, it - 1))
                if pv:
                    ctx.wait("scalar", smn, pv)
            emit_ln(ctx, xb, h2t, psSB, mrep, vrep, allones, h2t, 8, TTS,
                    lsem, lv, epscol)
            h2v = ctx.val("st_ve")
            ctx.op("sync", lambda e, b=b, tt=tt: e.dma_start(
                out=ap_(h2_s, b * D * T + tt * TTS, [[T, 128], [128 * T, 8], [1, TTS]]),
                in_=h2t[:]), wait=[("st_ve", h2v)], inc=("stoH", 16))
            ctx.tile_mark[("phb_stoH", it)] = ctx.val("stoH")
            ctx.op("scalar", lambda e, xb=xb: e.activation(xb[:], h2t[:], AF.Square),
                   wait=[("st_ve", h2v)], inc=("st_sc", 1))
            for j in range(8):
                ctx.op("tensor", lambda e, j=j, xb=xb: e.matmul(
                    psSB[:, 0:TTS], allones[:], xb[:, j * TTS:(j + 1) * TTS],
                    start=(j == 0), stop=(j == 7)),
                    wait=([("st_sc", ctx.val("st_sc"))] if j == 0 else None),
                    inc=(("st_te", 1) if j == 7 else None))
            ctx.op("scalar", lambda e: e.activation(vrep[:], psSB[:, 0:TTS], AF.Copy,
                                                    scale=1.0 / D),
                   wait=[("st_te", ctx.val("st_te"))])
            ctx.drain("scalar")
            ctx.op("scalar", lambda e: e.activation(vrep[:], vrep[:], AF.Sqrt,
                                                    bias=eps6col[:]), inc=("st_sc", 1))
            ctx.op("vector", lambda e: e.reciprocal(vrep[:], vrep[:]),
                   wait=[("st_sc", ctx.val("st_sc"))])
            ctx.drain("vector")
            nwb = ap_(nwcol_sb, 0, [[8, 128], [1, 8], [0, TTS]])
            vb = ap_(vrep, 0, [[TTS, 128], [0, 8], [1, TTS]])
            ctx.op("vector", lambda e, nwb=nwb: e.tensor_tensor(h2t[:], h2t[:], nwb,
                                                                OP.mult),
                   wait=[("stoH", ctx.tile_mark.get(("phb_stoH", it), 0))])
            ctx.drain("vector")
            ctx.op("vector", lambda e, vb=vb: e.tensor_tensor(h2t[:], h2t[:], vb,
                                                              OP.mult))
            ctx.drain("vector")
            ctx.op("vector", lambda e: e.tensor_copy(xnb[:], h2t[:]), inc=("st_ve", 1))
            ctx.drain("vector")
            xnv = ctx.val("st_ve")

            for mc in range(32):
                psP = psI[mc % 2]
                mmw = [("st_ve", xnv)]
                em = ctx.tile_mark.get(("phb_em", it, mc - 2))
                if em is None and it > 0:
                    em = ctx.tile_mark.get(("phb_em", it - 1, 30 + mc))
                if em:
                    mmw.append(("st_sc", em))
                for kc in range(8):
                    ctx.op("tensor", lambda e, kc=kc, mc=mc, psP=psP: e.matmul(
                        psP[:], w_in_sb[:, (kc * 32 + mc) * 128:(kc * 32 + mc) * 128 + 128],
                        xnb[:, kc * TTS:(kc + 1) * TTS],
                        start=(kc == 0), stop=(kc == 7)),
                        wait=(mmw if kc == 0 else None),
                        inc=(("st_te", 1) if kc == 7 else None))
                tev = ctx.val("st_te")
                if mc < 16:
                    xw = [("st_te", tev)]
                    cm = ctx.tile_mark.get(("phb_conv_te", it - 2))
                    if cm:
                        xw.append(("st_te", cm))
                    ctx.op("scalar", lambda e, psP=psP, xr=xr, mc=mc: e.activation(
                        xr[:, mc * (TTS + 4) + 4:mc * (TTS + 4) + 4 + TTS], psP[:],
                        AF.Copy), wait=xw, inc=("st_sc", 1))
                else:
                    stg = stgB[mc % 2]
                    scw = [("st_te", tev)]
                    dm = ctx.tile_mark.get(("stg_last", mc % 2))
                    if dm:
                        scw.append(dm)
                    ctx.op("scalar", lambda e, psP=psP, stg=stg: e.activation(
                        stg[:], psP[:], AF.Copy), wait=scw, inc=("st_sc", 1))
                    gsem = "stoG0" if mc % 2 == 0 else "stoG1"
                    ctx.op("sync", lambda e, stg=stg, b=b, tt=tt, mc=mc: e.dma_start(
                        out=ap_(mg_s, b * DI * T + (mc - 16) * 128 * T + tt * TTS,
                                [[T, 128], [1, TTS]]), in_=stg[:]),
                        wait=[("st_sc", ctx.val("st_sc"))], inc=(gsem, 16))
                    ctx.tile_mark[("stg_last", mc % 2)] = (gsem, ctx.val(gsem))
                ctx.tile_mark[("phb_em", it, mc)] = ctx.val("st_sc")
            xbr_v = ctx.val("st_sc")
            if tt == 0:
                hal = ap_(xr, 0, [[16 * (TTS + 4), 128], [TTS + 4, 16], [1, 4]])
                ctx.op("vector", lambda e, hal=hal: e.memset(hal, 0.0),
                       wait=[("st_sc", xbr_v)], inc=("st_ve", 1))
            else:
                xrp = xbr[(it - 1) % 2]
                dst = ap_(xr, 0, [[16 * (TTS + 4), 128], [TTS + 4, 16], [1, 4]])
                srch = ap_(xrp, TTS, [[16 * (TTS + 4), 128], [TTS + 4, 16], [1, 4]])
                ctx.op("vector", lambda e, dst=dst, srch=srch: e.tensor_copy(dst, srch),
                       wait=[("st_sc", xbr_v)], inc=("st_ve", 1))
            ctx.drain("vector")
            halo_v = ctx.val("st_ve")

            for j in range(16):
                psP = psI[j % 2]
                mmw = [("st_ve", halo_v), ("st_sc", ctx.val("st_sc"))]
                for k in range(4):
                    ctx.op("tensor", lambda e, j=j, k=k, psP=psP, xr=xr: e.matmul(
                        psP[:], cdiag_sb[:, (j * 4 + k) * 128:(j * 4 + k) * 128 + 128],
                        xr[:, j * (TTS + 4) + k + 1:j * (TTS + 4) + k + 1 + TTS],
                        start=(k == 0), stop=(k == 3)),
                        wait=(mmw if k == 0 else None),
                        inc=(("st_te", 1) if k == 3 else None))
                tev = ctx.val("st_te")
                stg = stgB[j % 2]
                scw = [("st_te", tev)]
                cdm = ctx.tile_mark.get(("stg_last", j % 2))
                if cdm:
                    scw.append(cdm)
                ctx.op("scalar", lambda e, psP=psP, j=j, stg=stg: e.activation(
                    stg[:], psP[:], AF.Silu,
                    bias=cbcol_sb[:, j:j + 1]), wait=scw, inc=("st_sc", 1))
                csem = "stoC0" if j % 2 == 0 else "stoC1"
                ctx.op("sync", lambda e, b=b, tt=tt, j=j, stg=stg: e.dma_start(
                    out=ap_(xc_s, b * DI * T + j * 128 * T + tt * TTS,
                            [[T, 128], [1, TTS]]), in_=stg[:]),
                    wait=[("st_sc", ctx.val("st_sc"))], inc=(csem, 16))
                ctx.tile_mark[("stg_last", j % 2)] = (csem, ctx.val(csem))
            ctx.tile_mark[("phb_conv_te", it)] = ctx.val("st_te")
            ctx.tile_mark[("phb_stoC", it)] = (ctx.val("stoC0"), ctx.val("stoC1"))
            ctx.tile_mark[("phb_done", it)] = ctx.val("st_sc")


        phB1_sc = ctx.val("st_sc")
        phB1_ve = ctx.val("st_ve")
        phB1_te = ctx.val("st_te")
        phB.close()

        phB2 = contextlib.ExitStack()
        sbB2 = lambda n, s, d=F32: phB2.enter_context(nc.sbuf_tensor(n, s, d))
        w_dt_sb = sbB2("w_dt_sb", [128, 16 * 16 * 128], BF16)
        w_bc_sb = sbB2("w_bc_sb", [128, 16 * 32], BF16)
        bcstg = [sbB2(f"bcstg{i}", [32, TTS]) for i in range(2)]
        xct = [sbB2(f"xct{i}", [128, 16 * TTS]) for i in range(2)]
        xcb = sbB2("xcb", [128, 16 * TTS], BF16)
        dtt = sbB2("dtt", [128, 16 * TTS])
        psI2 = [phB2.enter_context(nc.psum_tensor(f"psI2{i}", [128, TTS], F32))
                for i in range(2)]
        psBC = phB2.enter_context(nc.psum_tensor("psBC", [32, TTS], F32))
        ctx.wait("gpsimd", "st_sc", phB1_sc)
        ctx.wait("gpsimd", "st_ve", phB1_ve)
        ctx.op("gpsimd", lambda e: e.dma_start(out=w_dt_sb[:], in_=w_dtT),
               inc=("ldgw", 16))
        ctx.op("gpsimd", lambda e: e.dma_start(out=w_bc_sb[:], in_=w_bcT),
               inc=("ldgw", 16))
        wB2_ready = ctx.val("ldgw")
        ctx.wait("tensor", "ldgw", wB2_ready)
        ctx.wait("tensor", "st_sc", phB1_sc)
        ctx.wait("vector", "st_sc", phB1_sc)
        ctx.wait("scalar", "st_ve", phB1_ve)
        for eng in ("scalar", "vector", "tensor"):
            for smn in ("stoC0", "stoC1", "stoG0", "stoG1", "stoH"):
                ctx.wait(eng, smn, ctx.val(smn))

        for it in range(NTILE):
            b, tt = divmod(it, T // TTS)
            xc = xct[it % 2]
            lsem = "lds0" if it % 2 == 0 else "lds1"
            lw = [("st_ve", ctx.tile_mark.get(("b2_done", it - 2), 0))]
            c01 = ctx.tile_mark.get(("phb_stoC", it))
            if c01:
                lw += [("stoC0", c01[0]), ("stoC1", c01[1])]
            lv = ctx.op("sync", lambda e, b=b, tt=tt, xc=xc: e.dma_start(
                out=xc[:], in_=ap_(xc_s, b * DI * T + tt * TTS,
                                   [[T, 128], [128 * T, 16], [1, TTS]])),
                inc=(lsem, 16), wait=lw)
            pdm = ctx.tile_mark.get(("phb_stoD", it - 1))
            vw = [(lsem, lv), ("st_sc", ctx.tile_mark.get(("b2_bc", it - 1), 0))]
            if pdm:
                vw.append(("stoD", pdm))
            ctx.op("vector", lambda e, xc=xc: e.tensor_copy(xcb[:], xc[:]),
                   wait=vw, inc=("st_ve", 1))
            ctx.drain("vector")
            xcb_v = ctx.val("st_ve")

            for mc in range(16):
                psP = psI2[mc % 2]
                mmw = [("st_ve", xcb_v)]
                em = ctx.tile_mark.get(("b2_em", it, mc - 2))
                if em is None and it > 0:
                    em = ctx.tile_mark.get(("b2_em", it - 1, mc + 14))
                if em:
                    mmw.append(("st_sc", em))
                for kc in range(16):
                    ctx.op("tensor", lambda e, kc=kc, mc=mc, psP=psP: e.matmul(
                        psP[:], w_dt_sb[:, (kc * 16 + mc) * 128:(kc * 16 + mc) * 128 + 128],
                        xcb[:, kc * TTS:(kc + 1) * TTS],
                        start=(kc == 0), stop=(kc == 15)),
                        wait=(mmw if kc == 0 else None),
                        inc=(("st_te", 1) if kc == 15 else None))
                tev = ctx.val("st_te")
                ctx.op("scalar", lambda e, psP=psP, mc=mc: e.activation(
                    dtt[:, mc * TTS:(mc + 1) * TTS], psP[:], AF.Exp,
                    bias=bdtcol_sb[:, mc:mc + 1]), wait=[("st_te", tev)],
                    inc=("st_sc", 1))
                ctx.tile_mark[("b2_em", it, mc)] = ctx.val("st_sc")
            ctx.drain("scalar")
            ctx.op("scalar", lambda e: e.activation(dtt[:], dtt[:], AF.Ln, bias=1.0))
            ctx.drain("scalar")
            ctx.op("scalar", lambda e: e.activation(dtt[:], dtt[:], AF.Tanh,
                                                    bias=sctanb[:], scale=SC_IN),
                   inc=("st_sc", 1))
            ctx.drain("scalar")
            ctx.op("vector", lambda e: e.tensor_scalar(dtt[:], dtt[:], SC_OUT, SC_C,
                                                       OP.mult, OP.add),
                   wait=[("st_sc", ctx.val("st_sc"))], inc=("st_ve", 1))
            ctx.drain("vector")
            dtv = ctx.val("st_ve")
            ctx.op("sync", lambda e, b=b, tt=tt: e.dma_start(
                out=ap_(dt_s, b * DI * T + tt * TTS, [[T, 128], [128 * T, 16], [1, TTS]]),
                in_=dtt[:]), wait=[("st_ve", dtv)], inc=("stoD", 16))
            ctx.tile_mark[("phb_stoD", it)] = ctx.val("stoD")

            mmw = [("st_ve", xcb_v), ("st_sc", ctx.tile_mark.get(("b2_bc", it - 1), 0))]
            for kc in range(16):
                ctx.op("tensor", lambda e, kc=kc: e.matmul(
                    psBC[:], w_bc_sb[:, kc * 32:(kc + 1) * 32],
                    xcb[:, kc * TTS:(kc + 1) * TTS],
                    start=(kc == 0), stop=(kc == 15)),
                    wait=(mmw if kc == 0 else None),
                    inc=(("st_te", 1) if kc == 15 else None))
            bcs = bcstg[it % 2]
            bsem = "stoB0" if it % 2 == 0 else "stoB1"
            bw = [("st_te", ctx.val("st_te"))]
            bdm = ctx.tile_mark.get(("b2_bcs", it - 2))
            if bdm:
                bw.append((bsem, bdm))
            ctx.op("scalar", lambda e, bcs=bcs: e.activation(
                bcs[:], psBC[:], AF.Copy), wait=bw, inc=("st_sc", 1))
            ctx.tile_mark[("b2_bc", it)] = ctx.val("st_sc")
            ctx.op("sync", lambda e, b=b, tt=tt, bcs=bcs: e.dma_start(
                out=ap_(bcps_d, b * T + tt * TTS, [[B_CORE * T, 32], [1, TTS]]),
                in_=bcs[:]),
                wait=[("st_sc", ctx.val("st_sc"))], inc=(bsem, 16))
            ctx.tile_mark[("b2_bcs", it)] = ctx.val(bsem)
            ctx.tile_mark[("b2_done", it)] = ctx.val("st_ve")

        phB_sc = ctx.val("st_sc")
        phB_ve = ctx.val("st_ve")
        phB_te = ctx.val("st_te")
        phB2.close()
        for s in ("stoH", "stoC0", "stoC1", "stoD", "stoG0", "stoG1"):
            ctx.wait("sync", s, ctx.val(s))
        emit_scan(ctx, nc, st, locals())
        ve = emit_colmax_sb(ctx, pcol, ap_(psMisc, 0, [[130, 1], [1, 128]]),
                            prow, idt)
        for s in ("sto0", "sto1"):
            ctx.wait("sync", s, ctx.val(s))
        return emit_ccmax(ctx, prow[:, 0:2], ccin, ccout, ve)


def emit_scan(ctx, nc, st, env):
    e = env
    dt_s, xc_s, mg_s, x1_s, h2_s, x2_o = (e["dt_s"], e["xc_s"], e["mg_s"],
                                          e["x1_s"], e["h2_s"], e["x2_o"])
    w_outT, bcps_d = e["w_outT"], e["bcps_d"]
    atan_sb, dcol_sb, allones = e["atan_sb"], e["dcol_sb"], e["allones"]
    rscol, psMisc, pcol, epscol = e["rscol"], e["psMisc"], e["pcol"], e["epscol"]

    SEG = 10
    NSEG = 205
    SPB = 4
    NBLK = 52
    SL = 512 * SEG

    def slen(s):
        return 8 if s == NSEG - 1 else 10

    def bwid(i):
        return 8 if i == NBLK - 1 else 40

    sbS = lambda n, s, d=F32: st.enter_context(nc.sbuf_tensor(n, s, d))
    w_out_sb = sbS("w_out_sb", [128, 16 * 8 * 128], BF16)
    dAb = [sbS(f"dAb{i}", [128, SL]) for i in range(2)]
    wb = [sbS(f"wbb{i}", [128, SL]) for i in range(2)]
    hb = sbS("hbb", [128, SL])
    dtc = [sbS(f"dtc{i}", [128, 16 * 2 * 40]) for i in range(2)]
    xcc = [sbS(f"xcc{i}", [128, 16 * 2 * 40]) for i in range(2)]
    yring = sbS("yring", [128, 16 * 2 * 40])
    yscr = sbS("yscr", [128, 2 * 16 * 10])
    brepblk = [sbS(f"brepblk{i}", [128, 2 * 16 * 40]) for i in range(2)]
    crepblk = [sbS(f"crepblk{i}", [128, 2 * 16 * 40]) for i in range(2)]
    dA0 = [sbS(f"dA0_{i}", [128, 512]) for i in range(2)]
    tmpc = sbS("tmpc", [128, 512])
    sqscr = sbS("sqscr", [128, 256])
    sq2 = sbS("sq2", [128, 2])
    nrmt = sbS("nrmt", [128, 2])
    mgb = sbS("mgb", [128, 16 * 80])
    obuf = sbS("obuf", [128, 16 * 80], BF16)
    x1d = [sbS(f"x1d{i}", [128, 80]) for i in range(2)]
    h2d = [sbS(f"h2d{i}", [128, 80]) for i in range(2)]
    x2blk = sbS("x2blk", [128, 8 * 80])
    sqb = sbS("sqb", [128, 8 * 80])
    mrepD = sbS("mrepD", [128, 80])
    vrepD = sbS("vrepD", [128, 80])
    tmax16 = sbS("tmax16", [128, 16])
    tmax = sbS("tmax", [128, 1])
    psD = [st.enter_context(nc.psum_tensor(f"psDo{i}", [128, 512], F32))
           for i in range(2)]
    psS2 = st.enter_context(nc.psum_tensor("psS2", [128, 1024], F32))

    mk = ctx.tile_mark
    phB_sc, phB_ve, phB_te = e["phB_sc"], e["phB_ve"], e["phB_te"]
    ctx.wait("gpsimd", "st_sc", phB_sc)
    ctx.wait("gpsimd", "st_ve", phB_ve)
    for smn in ("stoG0", "stoG1", "stoX0", "stoX1", "stoH"):
        ctx.wait("gpsimd", smn, ctx.val(smn))
    ctx.op("gpsimd", lambda e2: e2.dma_start(out=w_out_sb[:], in_=w_outT),
           inc=("ldgw", 16))
    wo_ready = ctx.val("ldgw")
    ctx.wait("tensor", "st_sc", phB_sc)
    ctx.wait("vector", "st_sc", phB_sc)
    ctx.wait("scalar", "st_ve", phB_ve)
    ctx.wait("scalar", "st_te", phB_te)
    for eng in ("tensor", "vector", "scalar", "gpsimd"):
        ctx.wait(eng, "stoB0", ctx.val("stoB0"))
        ctx.wait(eng, "stoB1", ctx.val("stoB1"))
        ctx.wait(eng, "stoD", ctx.val("stoD"))

    def emit_block_loads(i):
        W = bwid(i)
        tb0 = 40 * i
        lsem = "lds0" if i % 2 == 0 else "lds1"
        lw = []
        if i > 1:
            lw.append(("st_ve", mk[("wtt", SPB * (i - 1) - 1)]))
            lw.append(("st_ve", mk[("ytt", SPB * (i - 1) - 1)]))
            lw.append(mk[("blkload", i - 2)])
        if i > 2:
            lw.append(("st_ve", mk[("dxz", i - 2)]))
        first = [True]

        def w_():
            if first[0]:
                first[0] = False
                return lw
            return None
        dc = dtc[i % 2]
        xc = xcc[i % 2]
        for b in range(2):
            ctx.op("sync", lambda e2, b=b, tb0=tb0, W=W, dc=dc: e2.dma_start(
                out=ap_(dc, b * 40, [[1280, 128], [80, 16], [1, W]]),
                in_=ap_(dt_s, b * DI * T + tb0, [[T, 128], [128 * T, 16], [1, W]])),
                inc=(lsem, 16), wait=w_())
            ctx.op("sync", lambda e2, b=b, tb0=tb0, W=W, xc=xc: e2.dma_start(
                out=ap_(xc, b * 40, [[1280, 128], [80, 16], [1, W]]),
                in_=ap_(xc_s, b * DI * T + tb0, [[T, 128], [128 * T, 16], [1, W]])),
                inc=(lsem, 16))
        br = brepblk[i % 2]
        cr = crepblk[i % 2]
        for n in range(16):
            ctx.op("sync", lambda e2, n=n, tb0=tb0, W=W, br=br: e2.dma_start(
                out=ap_(br, n * 40, [[1280, 128], [640, 2], [1, W]]),
                in_=ap_(bcps_d, n * (B_CORE * T) + tb0, [[0, 128], [T, 2], [1, W]])),
                inc=(lsem, 16))
            ctx.op("sync", lambda e2, n=n, tb0=tb0, W=W, cr=cr: e2.dma_start(
                out=ap_(cr, n * 40, [[1280, 128], [640, 2], [1, W]]),
                in_=ap_(bcps_d, (16 + n) * (B_CORE * T) + tb0,
                        [[0, 128], [T, 2], [1, W]])),
                inc=(lsem, 16))
        mk[("blkload", i)] = (lsem, ctx.val(lsem))

    def emit_prep(s):
        i = s // SPB
        so = (s - SPB * i) * SEG
        L = slen(s)
        pb = s % 2
        lsem, lval = mk[("blkload", i)]
        da = dAb[pb]
        wv = wb[pb]
        qw = [(lsem, lval)]
        if s >= 2:
            qw.append(("st_sc", mk[("exp0", s - 2)]))
        for b in range(2):
            ctx.op("vector", lambda e2, b=b, so=so, L=L, da=da: e2.tensor_tensor(
                ap_(da, b * 2560, [[SL, 128], [160, 16], [10, 16], [1, L]]),
                ap_(dtc[i % 2], b * 40 + so,
                    [[1280, 128], [80, 16], [0, 16], [1, L]]),
                ap_(atan_sb, 0, [[256, 128], [16, 16], [1, 16], [0, L]]),
                OP.mult), wait=(qw if b == 0 else None),
                inc=(("st_ve", 1) if b == 1 else None))
        ctx.drain("vector")
        mk[("qtt", s)] = ctx.val("st_ve")
        ctx.op("vector", lambda e2, so=so, L=L, i=i: e2.tensor_tensor(
            ap_(dtc[i % 2], so, [[1280, 128], [80, 16], [40, 2], [1, L]]),
            ap_(dtc[i % 2], so, [[1280, 128], [80, 16], [40, 2], [1, L]]),
            ap_(xcc[i % 2], so, [[1280, 128], [80, 16], [40, 2], [1, L]]),
            OP.mult), inc=("st_ve", 1))
        ctx.drain("vector")
        for b in range(2):
            ctx.op("vector", lambda e2, b=b, so=so, L=L, wv=wv, i=i:
                   e2.tensor_tensor(
                ap_(wv, b * 2560, [[SL, 128], [160, 16], [10, 16], [1, L]]),
                ap_(dtc[i % 2], b * 40 + so,
                    [[1280, 128], [80, 16], [0, 16], [1, L]]),
                ap_(brepblk[i % 2], b * 640 + so,
                    [[1280, 128], [0, 16], [40, 16], [1, L]]),
                OP.mult), inc=(("st_ve", 1) if b == 1 else None))
        ctx.drain("vector")
        mk[("wtt", s)] = ctx.val("st_ve")
        ctx.op("scalar", lambda e2, da=da: e2.activation(da[:], da[:], AF.Tanh),
               wait=[("st_ve", mk[("qtt", s)])])
        ctx.drain("scalar")
        d0w = []
        if s >= 1:
            d0w.append(("st_ve", mk[("chain_tmp", s - 1)]))
        ctx.op("scalar", lambda e2, da=da, pb=pb: e2.activation(
            dA0[pb][:], ap_(da, 0, [[SL, 128], [10, 512]]), AF.Exp, scale=16.0),
            wait=d0w)
        ctx.drain("scalar")
        ctx.op("scalar", lambda e2, da=da, L=L: e2.activation(
            ap_(da, 1, [[SL, 128], [10, 512], [1, L - 1]]),
            ap_(da, 1, [[SL, 128], [10, 512], [1, L - 1]]), AF.Exp, scale=16.0),
            inc=("st_sc", 1))
        ctx.drain("scalar")
        mk[("exp0", s)] = ctx.val("st_sc")

    def emit_mset(s):
        da = dAb[s % 2]
        ctx.op("vector", lambda e2, da=da: e2.memset(
            ap_(da, 0, [[SL, 128], [10, 512]]), 0.0),
            wait=[("st_sc", mk[("exp0", s)])], inc=("st_ve", 1))
        ctx.drain("vector")
        mk[("mset", s)] = ctx.val("st_ve")

    def emit_dxz(i):
        W = bwid(i)
        lsem, lval = mk[("blkload", i)]
        ctx.op("vector", lambda e2, W=W, i=i: e2.tensor_tensor(
            ap_(yring, 0, [[1280, 128], [80, 16], [40, 2], [1, W]]),
            ap_(xcc[i % 2], 0, [[1280, 128], [80, 16], [40, 2], [1, W]]),
            ap_(dcol_sb, 0, [[16, 128], [1, 16], [0, 2], [0, W]]),
            OP.mult), wait=[(lsem, lval)], inc=("st_ve", 1))
        ctx.drain("vector")
        mk[("dxz", i)] = ctx.val("st_ve")

    def emit_y(s):
        i = s // SPB
        so = (s - SPB * i) * SEG
        L = slen(s)
        hv = hb
        lsem, lval = mk[("blkload", i)]
        for b in range(2):
            ctx.op("vector", lambda e2, b=b, so=so, L=L, hv=hv, i=i:
                   e2.tensor_tensor(
                ap_(hv, b * 2560, [[SL, 128], [160, 16], [10, 16], [1, L]]),
                ap_(hv, b * 2560, [[SL, 128], [160, 16], [10, 16], [1, L]]),
                ap_(crepblk[i % 2], b * 640 + so,
                    [[1280, 128], [0, 16], [40, 16], [1, L]]),
                OP.mult), wait=([(lsem, lval)] if b == 0 else None),
                inc=(("st_ve", 1) if b == 1 else None))
        ctx.drain("vector")
        mk[("ytt", s)] = ctx.val("st_ve")
        for b in range(2):
            ctx.op("vector", lambda e2, b=b, L=L, hv=hv: e2.reduce_sum(
                ap_(yscr, b * 160, [[320, 128], [10, 16], [1, L]]),
                ap_(hv, b * 2560, [[SL, 128], [160, 16], [1, L], [10, 16]]),
                AX.X))
        ctx.drain("vector")
        ctx.op("vector", lambda e2, so=so, L=L: e2.tensor_tensor(
            ap_(yring, so, [[1280, 128], [40, 2], [80, 16], [1, L]]),
            ap_(yring, so, [[1280, 128], [40, 2], [80, 16], [1, L]]),
            ap_(yscr, 0, [[320, 128], [160, 2], [10, 16], [1, L]]),
            OP.add), inc=("st_ve", 1))
        ctx.drain("vector")
        mk[("yadd", s)] = ctx.val("st_ve")

    def emit_phd(i):
        W = bwid(i)
        tb0 = 40 * i
        gw = [("st_ve", mk.get(("pd_gst",), 0)), mk.get(("pd_ldg",), ("ldg0", 0))]
        for kc in range(16):
            for b in range(2):
                ctx.op("gpsimd", lambda e2, b=b, kc=kc, tb0=tb0, W=W:
                       e2.dma_start(
                    out=ap_(mgb, kc * 80 + b * 40, [[1280, 128], [1, W]]),
                    in_=ap_(mg_s, b * DI * T + kc * 128 * T + tb0,
                            [[T, 128], [1, W]])),
                    inc=("ldg0", 16),
                    wait=(gw if kc == 0 and b == 0 else None))
        lvg = ctx.val("ldg0")
        mk[("pd_ldg",)] = ("ldg0", lvg)
        ctx.op("scalar", lambda e2: e2.activation(mgb[:], mgb[:], AF.Silu),
               wait=[("ldg0", lvg), ("st_ve", mk.get(("pd_gst",), 0))],
               inc=("st_sc", 1))
        ctx.drain("scalar")
        silv = ctx.val("st_sc")
        ctx.op("vector", lambda e2: e2.tensor_tensor(
            mgb[:], mgb[:], yring[:], OP.mult), wait=[("st_sc", silv)])
        ctx.drain("vector")
        ctx.op("vector", lambda e2: e2.tensor_copy(obuf[:], mgb[:]),
               inc=("st_ve", 1))
        ctx.drain("vector")
        mk[("pd_gst",)] = ctx.val("st_ve")
        mk[("phdy", i)] = ctx.val("st_ve")
        if i + 1 < NBLK:
            emit_dxz(i + 1)
        for mc in range(8):
            msem = "ldm0" if mc % 2 == 0 else "ldm1"
            x1t = x1d[mc % 2]
            h2t = h2d[mc % 2]
            mw = [("st_ve", mk.get(("pd_res", mc % 2), 0))]
            for b in range(2):
                ctx.op("gpsimd", lambda e2, b=b, mc=mc, tb0=tb0, W=W, x1t=x1t:
                       e2.dma_start(
                    out=ap_(x1t, b * 40, [[80, 128], [1, W]]),
                    in_=ap_(x1_s, b * D * T + mc * 128 * T + tb0,
                            [[T, 128], [1, W]])),
                    inc=(msem, 16), wait=(mw if b == 0 else None))
                ctx.op("gpsimd", lambda e2, b=b, mc=mc, tb0=tb0, W=W, h2t=h2t:
                       e2.dma_start(
                    out=ap_(h2t, b * 40, [[80, 128], [1, W]]),
                    in_=ap_(h2_s, b * D * T + mc * 128 * T + tb0,
                            [[T, 128], [1, W]])),
                    inc=(msem, 16))
            lvm = ctx.val(msem)
            psP = psD[mc % 2]
            mmw = [("st_ve", mk[("phdy", i)]), ("ldgw", wo_ready),
                   ("st_ve", mk.get(("pd_res", mc % 2), 0))]
            if W == 40:
                for kc in range(16):
                    ctx.op("tensor", lambda e2, kc=kc, mc=mc, psP=psP, W=W:
                           e2.matmul(
                        ap_(psP, 0, [[512, 128], [W, 2], [1, W]]),
                        w_out_sb[:, (kc * 8 + mc) * 128:(kc * 8 + mc) * 128 + 128],
                        ap_(obuf, kc * 80, [[1280, 128], [40, 2], [1, W]]),
                        start=(kc == 0), stop=(kc == 15)),
                        wait=(mmw if kc == 0 else None),
                        inc=(("st_te", 1) if kc == 15 else None))
            else:
                for kc in range(16):
                    for b in range(2):
                        ctx.op("tensor", lambda e2, kc=kc, mc=mc, psP=psP,
                               W=W, b=b: e2.matmul(
                            ap_(psP, b * W, [[512, 128], [1, W]]),
                            w_out_sb[:, (kc * 8 + mc) * 128:
                                     (kc * 8 + mc) * 128 + 128],
                            ap_(obuf, kc * 80 + b * 40, [[1280, 128], [1, W]]),
                            start=(kc == 0 and b == 0),
                            stop=(kc == 15 and b == 1)),
                            wait=(mmw if kc == 0 and b == 0 else None),
                            inc=(("st_te", 1) if kc == 15 and b == 1 else None))
            tev = ctx.val("st_te")
            ssem = "sto0" if mc % 2 == 0 else "sto1"
            stw = [("st_te", tev), (msem, lvm)]
            dm = mk.get(("pd_sto", mc % 2))
            if dm:
                stw.append((ssem, dm))
            ctx.op("vector", lambda e2, psP=psP, mc=mc, h2t=h2t, W=W:
                   e2.scalar_tensor_tensor(
                ap_(x2blk, mc * 80, [[640, 128], [40, 2], [1, W]]),
                ap_(psP, 0, [[512, 128], [W, 2], [1, W]]), rscol[:, 0:1],
                ap_(h2t, 0, [[80, 128], [40, 2], [1, W]]),
                OP.mult, OP.add),
                wait=stw)
            ctx.drain("vector")
            ctx.op("vector", lambda e2, mc=mc, x1t=x1t, W=W: e2.tensor_tensor(
                ap_(x2blk, mc * 80, [[640, 128], [40, 2], [1, W]]),
                ap_(x2blk, mc * 80, [[640, 128], [40, 2], [1, W]]),
                ap_(x1t, 0, [[80, 128], [40, 2], [1, W]]),
                OP.add), inc=("st_ve", 1))
            ctx.drain("vector")
            mk[("pd_res", mc % 2)] = ctx.val("st_ve")
            scv = ctx.val("st_ve")
            ssem = "sto0" if mc % 2 == 0 else "sto1"
            for b in range(2):
                sw2 = [("st_ve", scv)]
                dm = mk.get(("pd_sto", mc % 2))
                if dm and b == 0:
                    sw2.append((ssem, dm))
                ctx.op("sync", lambda e2, b=b, mc=mc, tb0=tb0, W=W: e2.dma_start(
                    out=ap_(x2_o, b * D * T + mc * 128 * T + tb0,
                            [[T, 128], [1, W]]),
                    in_=ap_(x2blk, mc * 80 + b * 40, [[640, 128], [1, W]])),
                    wait=sw2, inc=(ssem, 16))
            mk[("pd_sto", mc % 2)] = ctx.val(ssem)
        x2v = ctx.val("st_ve")
        sqw = [("st_ve", x2v), ("st_ve", mk.get(("pd_lnred",), 0)),
               ("st_sc", ctx.val("st_sc"))]
        ctx.op("scalar", lambda e2: e2.activation(sqb[:], x2blk[:], AF.Square),
               wait=sqw, inc=("st_sc", 1))
        ctx.drain("scalar")
        sqv = ctx.val("st_sc")
        smw = [("st_ve", x2v), ("st_sc", mk.get(("pd_lncp",), 0)),
               ("st_ve", mk.get(("pd_lnm2",), 0))]
        if W == 40:
            for mc in range(8):
                ctx.op("tensor", lambda e2, mc=mc, W=W: e2.matmul(
                    ap_(psS2, 0, [[1024, 128], [40, 2], [1, W]]), allones[:],
                    ap_(x2blk, mc * 80, [[640, 128], [40, 2], [1, W]]),
                    start=(mc == 0), stop=(mc == 7)),
                    wait=(smw if mc == 0 else None))
            for mc in range(8):
                ctx.op("tensor", lambda e2, mc=mc, W=W: e2.matmul(
                    ap_(psS2, 512, [[1024, 128], [40, 2], [1, W]]), allones[:],
                    ap_(sqb, mc * 80, [[640, 128], [40, 2], [1, W]]),
                    start=(mc == 0), stop=(mc == 7)),
                    wait=([("st_sc", sqv)] if mc == 0 else None),
                    inc=(("st_te", 1) if mc == 7 else None))
        else:
            for mc in range(8):
                for b in range(2):
                    ctx.op("tensor", lambda e2, mc=mc, W=W, b=b: e2.matmul(
                        ap_(psS2, b * 40, [[1024, 128], [1, W]]), allones[:],
                        ap_(x2blk, mc * 80 + b * 40, [[640, 128], [1, W]]),
                        start=(mc == 0 and b == 0),
                        stop=(mc == 7 and b == 1)),
                        wait=(smw if mc == 0 and b == 0 else None))
            for mc in range(8):
                for b in range(2):
                    ctx.op("tensor", lambda e2, mc=mc, W=W, b=b: e2.matmul(
                        ap_(psS2, 512 + b * 40, [[1024, 128], [1, W]]),
                        allones[:],
                        ap_(sqb, mc * 80 + b * 40, [[640, 128], [1, W]]),
                        start=(mc == 0 and b == 0),
                        stop=(mc == 7 and b == 1)),
                        wait=([("st_sc", sqv)] if mc == 0 and b == 0 else None),
                        inc=(("st_te", 1) if mc == 7 and b == 1 else None))
        tev = ctx.val("st_te")
        ctx.op("scalar", lambda e2, W=W: e2.activation(
            ap_(mrepD, 0, [[80, 128], [40, 2], [1, W]]),
            ap_(psS2, 0, [[1024, 128], [40, 2], [1, W]]), AF.Copy, scale=1.0 / D),
            wait=[("st_te", tev)])
        ctx.drain("scalar")
        ctx.op("scalar", lambda e2, W=W: e2.activation(
            ap_(vrepD, 0, [[80, 128], [40, 2], [1, W]]),
            ap_(psS2, 512, [[1024, 128], [40, 2], [1, W]]), AF.Copy,
            scale=1.0 / D), inc=("st_sc", 1))
        ctx.drain("scalar")
        mk[("pd_lncp",)] = ctx.val("st_sc")
        ctx.op("vector", lambda e2, W=W: e2.tensor_tensor(
            ap_(psS2, 0, [[1024, 128], [40, 2], [1, W]]),
            ap_(mrepD, 0, [[80, 128], [40, 2], [1, W]]),
            ap_(mrepD, 0, [[80, 128], [40, 2], [1, W]]), OP.mult),
            wait=[("st_sc", mk[("pd_lncp",)])])
        ctx.drain("vector")
        ctx.op("vector", lambda e2, W=W: e2.tensor_tensor(
            ap_(vrepD, 0, [[80, 128], [40, 2], [1, W]]),
            ap_(vrepD, 0, [[80, 128], [40, 2], [1, W]]),
            ap_(psS2, 0, [[1024, 128], [40, 2], [1, W]]), OP.subtract),
            inc=("st_ve", 1))
        ctx.drain("vector")
        mk[("pd_lnm2",)] = ctx.val("st_ve")
        ctx.op("scalar", lambda e2, W=W: e2.activation(
            ap_(vrepD, 0, [[80, 128], [40, 2], [1, W]]),
            ap_(vrepD, 0, [[80, 128], [40, 2], [1, W]]), AF.Ln,
            bias=epscol[:]), wait=[("st_ve", mk[("pd_lnm2",)])])
        ctx.drain("scalar")
        ctx.op("scalar", lambda e2, W=W: e2.activation(
            ap_(vrepD, 0, [[80, 128], [40, 2], [1, W]]),
            ap_(vrepD, 0, [[80, 128], [40, 2], [1, W]]), AF.Exp, scale=-0.5),
            inc=("st_sc", 1))
        ctx.drain("scalar")
        rsv = ctx.val("st_sc")
        ctx.op("vector", lambda e2, W=W: e2.tensor_tensor(
            ap_(sqb, 0, [[640, 128], [80, 8], [40, 2], [1, W]]),
            ap_(x2blk, 0, [[640, 128], [80, 8], [40, 2], [1, W]]),
            ap_(mrepD, 0, [[80, 128], [0, 8], [40, 2], [1, W]]), OP.subtract),
            wait=[("st_sc", rsv)])
        ctx.drain("vector")
        ctx.op("vector", lambda e2, W=W: e2.tensor_tensor(
            ap_(sqb, 0, [[640, 128], [80, 8], [40, 2], [1, W]]),
            ap_(sqb, 0, [[640, 128], [80, 8], [40, 2], [1, W]]),
            ap_(vrepD, 0, [[80, 128], [0, 8], [40, 2], [1, W]]), OP.mult))
        ctx.drain("vector")
        ctx.op("vector", lambda e2, W=W: e2.reduce_max(
            tmax16[:], ap_(sqb, 0, [[640, 128], [80, 8], [40, 2], [1, W]]),
            AX.X, apply_absolute_value=True))
        ctx.drain("vector")
        ctx.op("vector", lambda e2: e2.reduce_max(
            tmax[:], tmax16[:], AX.X, apply_absolute_value=True))
        ctx.drain("vector")
        ctx.op("vector", lambda e2: e2.tensor_tensor(
            pcol[:], pcol[:], tmax[:], OP.max), inc=("st_ve", 1))
        ctx.drain("vector")
        mk[("pd_lnred",)] = ctx.val("st_ve")

    emit_block_loads(0)
    emit_block_loads(1)
    emit_prep(0)
    emit_dxz(0)
    for s in range(NSEG):
        pb = s % 2
        if s == 0:
            ctx.op("vector", lambda e2: e2.tensor_copy(
                tmpc[:], ap_(wb[0], 0, [[SL, 128], [10, 512]])),
                inc=("st_ve", 1))
            ctx.drain("vector")
        else:
            cw = [("st_sc", mk.get(("sq", s - 1), 0)),
                  ("st_sc", mk[("exp0", s)])]
            ctx.op("vector", lambda e2, pb=pb: e2.tensor_tensor(
                tmpc[:], dA0[pb][:],
                ap_(hb, 9, [[SL, 128], [10, 512]]), OP.mult),
                wait=cw)
            ctx.drain("vector")
            ctx.op("vector", lambda e2, pb=pb: e2.tensor_tensor(
                tmpc[:], tmpc[:], ap_(wb[pb], 0, [[SL, 128], [10, 512]]),
                OP.add), inc=("st_ve", 1))
            ctx.drain("vector")
        mk[("chain_tmp", s)] = ctx.val("st_ve")
        sqw = [("st_ve", mk[("chain_tmp", s)]),
               ("st_te", mk.get(("clipmm", s - 1), 0))]
        for b in range(2):
            ctx.op("scalar", lambda e2, b=b: e2.activation(
                sqscr[:], tmpc[:, b * 256:(b + 1) * 256], AF.Square,
                accum_out=sq2[:, b:b + 1]),
                wait=(sqw if b == 0 else None),
                inc=(("st_sc", 1) if b == 1 else None))
            ctx.drain("scalar")
        mk[("sq", s)] = ctx.val("st_sc")
        mmw = [("st_sc", mk[("sq", s)]), ("st_sc", mk.get(("sqrt", s - 1), 0))]
        ctx.op("tensor", lambda e2: e2.matmul(
            ap_(psMisc, 128, [[130, 128], [1, 2]]), allones[:], sq2[:],
            start=True, stop=True), wait=mmw, inc=("st_te", 1))
        mk[("clipmm", s)] = ctx.val("st_te")
        ctx.op("scalar", lambda e2: e2.activation(
            nrmt[:], ap_(psMisc, 128, [[130, 128], [1, 2]]), AF.Ln),
            wait=[("st_te", mk[("clipmm", s)]),
                  ("st_ve", mk.get(("hclip", s - 1), 0))])
        ctx.drain("scalar")
        ctx.op("scalar", lambda e2: e2.activation(
            nrmt[:], nrmt[:], AF.Exp, scale=0.5), inc=("st_sc", 1))
        ctx.drain("scalar")
        mk[("sqrt", s)] = ctx.val("st_sc")
        if s + 1 < NSEG:
            emit_prep(s + 1)
        if s >= 1:
            emit_y(s - 1)
        if (s + 1) % SPB == 1 and s + 1 > 1 and (s + 1) // SPB + 1 < NBLK:
            emit_block_loads((s + 1) // SPB + 1)
        ctx.op("vector", lambda e2: e2.tensor_scalar(
            nrmt[:], nrmt[:], 1e-8, None, OP.add),
            wait=[("st_sc", mk[("sqrt", s)])])
        ctx.drain("vector")
        ctx.op("vector", lambda e2: e2.reciprocal(nrmt[:], nrmt[:]))
        ctx.drain("vector")
        ctx.op("vector", lambda e2: e2.tensor_scalar(
            nrmt[:], nrmt[:], 20.0, 1.0, OP.mult, OP.min))
        ctx.drain("vector")
        ctx.op("vector", lambda e2, pb=pb: e2.tensor_tensor(
            ap_(wb[pb], 0, [[SL, 128], [2560, 2], [10, 256]]),
            ap_(tmpc, 0, [[512, 128], [256, 2], [1, 256]]),
            ap_(nrmt, 0, [[2, 128], [1, 2], [0, 256]]), OP.mult),
            inc=("st_ve", 1))
        ctx.drain("vector")
        mk[("hclip", s)] = ctx.val("st_ve")
        emit_mset(s)
        ctx.op("vector", lambda e2, pb=pb: e2.tensor_tensor_scan(
            hb[:], dAb[pb][:], wb[pb][:], 0.0, OP.mult, OP.add),
            wait=[("st_sc", mk[("exp0", s)])], inc=("st_ve", 1))
        ctx.drain("vector")
        mk[("scan", s)] = ctx.val("st_ve")
        if s >= SPB and (s - 1) % SPB == SPB - 1:
            emit_phd((s - 1) // SPB)
    emit_y(NSEG - 1)
    emit_phd(NBLK - 1)


_IDENT = np.eye(128, dtype=np.float32)


def _selbc():
    out = np.zeros((32, 32 * 128), np.float32)
    for r in range(32):
        out[r, r * 128:(r + 1) * 128] = 1.0
    return out


_SELBC = _selbc()


def _conv_diag(conv_w):
    out = np.zeros((128, 16 * 4 * 128), np.float32)
    cw = np.asarray(conv_w)
    for j in range(16):
        for k in range(4):
            blk = np.diag(cw[j * 128:(j + 1) * 128, 0, k].astype(np.float32))
            out[:, (j * 4 + k) * 128:(j * 4 + k) * 128 + 128] = blk
    return out


def _tile_kmajor(wT, nkc, nmc, mwid=128):
    K, M = wT.shape
    assert K == nkc * 128 and M == nmc * mwid
    return np.ascontiguousarray(
        wT.reshape(nkc, 128, nmc, mwid).transpose(1, 0, 2, 3).reshape(128, -1))


ALL_SEMS = [
    "ldc", "ldsc", "ldcc", "cc", "ldx0", "ldx1", "ldu0", "ldu1", "ldw0",
    "ldw1", "ldf0", "ldf1", "ldr0", "ldr1", "lds0", "lds1", "ldm0", "ldm1",
    "ldg0", "ldg1", "ldgw", "ldsel", "st_te", "st_ve", "st_sc", "stoU0",
    "stoU1", "stoG0", "stoG1", "stoC0", "stoC1", "stoD", "stoB0", "stoB1",
    "stoX0", "stoX1", "stoH", "sto0", "sto1", "stoW",
    "dr_sync", "dr_tensor", "dr_vector", "dr_scalar", "dr_gpsimd"]


def full_barrier(ctx, exclude=("cc",)):
    for eng in Ctx.ENGS:
        for s, v in list(ctx.cnt.items()):
            if v and s not in exclude:
                ctx.wait(eng, s, v)


def build_MEGA():
    nc = bass.Bass("TRN2", target_bir_lowering=False,
                   detect_race_conditions=False)
    x = nc.dram_tensor("x", [B_CORE, D, T], F32, kind="ExternalInput").ap()
    ident = nc.dram_tensor("ident", [128, 128], F32, kind="ExternalInput").ap()
    identb = nc.dram_tensor("identb", [128, 128], BF16, kind="ExternalInput").ap()
    w_up = nc.dram_tensor("w_up", [FF, D], F32, kind="ExternalInput").ap()
    w_gate = nc.dram_tensor("w_gate", [FF, D], F32, kind="ExternalInput").ap()
    w_down = nc.dram_tensor("w_down", [D, FF], F32, kind="ExternalInput").ap()
    w_inT = nc.dram_tensor("w_inT", [128, 8 * 32 * 128], F32, kind="ExternalInput").ap()
    w_dtT = nc.dram_tensor("w_dtT", [128, 16 * 16 * 128], F32, kind="ExternalInput").ap()
    w_bcT = nc.dram_tensor("w_bcT", [128, 16 * 32], F32, kind="ExternalInput").ap()
    w_outT = nc.dram_tensor("w_outT", [128, 16 * 8 * 128], F32, kind="ExternalInput").ap()
    cdiag = nc.dram_tensor("cdiag", [128, 16 * 4 * 128], F32, kind="ExternalInput").ap()
    a_jln = nc.dram_tensor("a_jln", [128, 16 * DS], F32, kind="ExternalInput").ap()
    dcol = nc.dram_tensor("dcol", [128, 16], F32, kind="ExternalInput").ap()
    cbcol = nc.dram_tensor("cbcol", [128, 16], F32, kind="ExternalInput").ap()
    bdtcol = nc.dram_tensor("bdtcol", [128, 16], F32, kind="ExternalInput").ap()
    nwcol = nc.dram_tensor("nwcol", [128, 8], F32, kind="ExternalInput").ap()
    rscal = nc.dram_tensor("rscal", [1, 1], F32, kind="ExternalInput").ap()
    out = nc.dram_tensor("out", [B_CORE, D, T], F32, kind="ExternalOutput").ap()
    up_raw = nc.dram_tensor("up_raw", [B_CORE, FF, T], F32).ap()
    gate_raw = nc.dram_tensor("gate_raw", [B_CORE, FF, T], F32).ap()
    pre3_raw = nc.dram_tensor("pre3_raw", [B_CORE, FF, T], F32).ap()
    pre4_raw = nc.dram_tensor("pre4_raw", [B_CORE, FF, T], F32).ap()
    x2_s = nc.dram_tensor("x2_s", [B_CORE, D, T], F32).ap()
    x1_s = nc.dram_tensor("x1_s", [B_CORE, D, T], F32).ap()
    h2_s = nc.dram_tensor("h2_s", [B_CORE, D, T], F32).ap()
    mg_s = nc.dram_tensor("mg_s", [B_CORE, DI, T], F32).ap()
    xc_s = nc.dram_tensor("xc_s", [B_CORE, DI, T], F32).ap()
    dt_s = nc.dram_tensor("dt_s", [B_CORE, DI, T], F32).ap()
    bcps_d = nc.dram_tensor("bcps_d", [32, B_CORE * T], F32).ap()
    NCC = 8
    ccin = [nc.dram_tensor(f"ccin{k}", [1, 2], F32) for k in range(NCC)]
    ccout = [nc.dram_tensor(f"ccout{k}", [1, 2], F32) for k in range(NCC)]
    wqu_c = nc.dram_tensor("wqu_c", [128, 8 * 24 * 128], BF16).ap()
    wqg_c = nc.dram_tensor("wqg_c", [128, 8 * 24 * 128], BF16).ap()
    wdt_c = nc.dram_tensor("wdt_c", [128, 24 * 8 * 128], BF16).ap()

    with contextlib.ExitStack() as root:
        ctx = Ctx(nc)
        ctx.add_sems(ALL_SEMS, root)
        blk = root.enter_context(nc.Block())

        def ph(fn, *args):
            with contextlib.ExitStack() as st:
                r = fn(ctx, nc, st, *args)
            full_barrier(ctx)
            return r

        cc0 = ph(ph_P0, x, ident, ccin[0], ccout[0])
        cc1 = ph(ph_upgate, "f1", x, ident, identb, w_up, w_gate, up_raw,
                 gate_raw, (ccout[0].ap(), cc0, 0), ccin[1], ccout[1],
                 (wqu_c, wqg_c, "store"))
        cc2 = ph(ph_gate, "g1", gate_raw, ident, pre3_raw,
                 (ccout[1].ap(), cc1, 1), ccin[2], ccout[2])
        cc3 = ph(ph_prod, "o1", pre3_raw, up_raw, ident, pre4_raw,
                 (ccout[2].ap(), cc2, 0), (ccout[1].ap(), cc1, 0),
                 ccin[3], ccout[3])
        cc4 = ph(ph_P4, x, pre4_raw, (ccout[3].ap(), cc3, 0), w_down, ident,
                 identb, w_inT, w_dtT, w_bcT, w_outT, cdiag, a_jln, dcol,
                 cbcol, bdtcol, nwcol, rscal, x2_s, x1_s, h2_s, mg_s, xc_s,
                 dt_s, bcps_d, ccin[4], ccout[4], (wdt_c, "store"))
        cc5 = ph(ph_upgate, "f2", x2_s, ident, identb, w_up, w_gate, up_raw,
                 gate_raw, (ccout[4].ap(), cc4, 0), ccin[5], ccout[5],
                 (wqu_c, wqg_c, "load"))
        cc6 = ph(ph_gate, "g2", gate_raw, ident, pre3_raw,
                 (ccout[5].ap(), cc5, 1), ccin[6], ccout[6])
        cc7 = ph(ph_prod, "o2", pre3_raw, up_raw, ident, pre4_raw,
                 (ccout[6].ap(), cc6, 0), (ccout[5].ap(), cc5, 0),
                 ccin[7], ccout[7])
        ph(ph_tail, pre4_raw, x2_s, (ccout[7].ap(), cc7, 0), w_down, identb,
           out, (wdt_c, "load"))
        ctx.build(blk)
    return nc


_PROG = []


def _prog():
    if not _PROG:
        _PROG.append(build_MEGA())
    return _PROG[0]


def _prep_core_inputs(inputs):
    x = np.asarray(inputs["L0_attn_x"], dtype=np.float32)
    per_core = [np.ascontiguousarray(x[2 * c:2 * c + 2].transpose(0, 2, 1))
                for c in range(NCORES)]
    A_log = np.asarray(inputs["L0_attn_A_log"], dtype=np.float32)
    shared = {
        "ident": _IDENT,
        "identb": _IDENT.astype(ml_dtypes.bfloat16),
        "w_up": np.ascontiguousarray(inputs["L0_ffn_w_up"], dtype=np.float32),
        "w_gate": np.ascontiguousarray(inputs["L0_ffn_w_gate"], dtype=np.float32),
        "w_down": np.ascontiguousarray(inputs["L0_ffn_w_down"], dtype=np.float32),
        "w_inT": _tile_kmajor(np.ascontiguousarray(
            np.asarray(inputs["L0_attn_W_in"], dtype=np.float32).T), 8, 32),
        "w_dtT": _tile_kmajor(np.ascontiguousarray(
            np.asarray(inputs["L0_attn_W_dt"], dtype=np.float32).T), 16, 16),
        "w_bcT": _tile_kmajor(np.ascontiguousarray(np.concatenate(
            [np.asarray(inputs["L0_attn_W_B"], dtype=np.float32),
             np.asarray(inputs["L0_attn_W_C"], dtype=np.float32)], 0).T), 16, 1, 32),
        "w_outT": _tile_kmajor(np.ascontiguousarray(
            np.asarray(inputs["L0_attn_W_out"], dtype=np.float32).T), 16, 8),
        "cdiag": _conv_diag(np.asarray(inputs["L0_attn_conv_w"], dtype=np.float32)),
        "a_jln": np.ascontiguousarray(
            A_log.reshape(16, 128, 16).transpose(1, 0, 2).reshape(128, 256)),
        "dcol": np.ascontiguousarray(
            np.asarray(inputs["L0_attn_D_param"], np.float32).reshape(16, 128).T),
        "cbcol": np.ascontiguousarray(
            np.asarray(inputs["L0_attn_conv_b"], np.float32).reshape(16, 128).T),
        "bdtcol": np.ascontiguousarray(
            np.asarray(inputs["L0_attn_b_dt"], np.float32).reshape(16, 128).T),
        "nwcol": np.ascontiguousarray(
            np.asarray(inputs["L0_attn_norm_w"], np.float32).reshape(8, 128).T),
        "rscal": np.asarray(inputs["L0_attn_residual_scale"],
                            np.float32).reshape(1, 1),
    }
    return per_core, shared


def kernel(**inputs):
    xs, sh = _prep_core_inputs(inputs)
    nc = _prog()
    res = run_bass_kernel_spmd(
        nc, [{"x": xs[c], **sh} for c in range(NCORES)],
        core_ids=list(range(NCORES)))
    out = np.concatenate(
        [np.asarray(res.results[c]["out"]).transpose(0, 2, 1)
         for c in range(NCORES)], axis=0)
    return np.ascontiguousarray(out.astype(np.float32))


if __name__ == "__main__":
    pass


# version 15
# speedup vs baseline: 1.0898x; 1.0141x over previous
import contextlib
import numpy as np
import ml_dtypes
import concourse.bass as bass
import concourse.mybir as mybir
from concourse.bass_utils import run_bass_kernel_spmd

F32 = mybir.dt.float32
BF16 = mybir.dt.bfloat16
I32 = mybir.dt.int32
AF = mybir.ActivationFunctionType
OP = mybir.AluOpType
AX = mybir.AxisListType

NCORES = 8
B_CORE = 2
T = 2048
D = 1024
DI = 2048
DS = 16
FF = 3072
TTS = 512
NTILE = B_CORE * (T // TTS)
QMAX_A = 32767.0
QMAX_W = 7.0
WBLK = 32
MAGIC = 12582912.0

SC_C = (0.1 + 0.001) / 2.0
SC_R = (0.1 - 0.001) / 2.0
SC_IN = 0.5 / (SC_R + 1e-8)
SC_OUT = SC_R / 0.5
SCA_IN = 0.5 / (5.0 + 1e-8)
SCA_OUT = 5.0 / 0.5
SCB_IN = 0.5 / (8.0 + 1e-8)
SCB_OUT = 8.0 / 0.5


class Ctx:

    ENGS = ("sync", "tensor", "vector", "scalar", "gpsimd")

    def __init__(self, nc):
        self.nc = nc
        self.q = {e: [] for e in self.ENGS}
        self.sem_handles = {}
        self.cnt = {}
        self.last_inc = {}

    def add_sems(self, names, stack):
        for n in names:
            self.sem_handles[n] = stack.enter_context(self.nc.semaphore(n))
            self.cnt[n] = 0

    def op(self, eng, fn, inc=None, wait=None):
        waits = list(wait or [])
        tgt = None
        if inc is not None:
            self.cnt[inc[0]] += inc[1]
            tgt = self.cnt[inc[0]]
        handles = self.sem_handles

        def replay(e, waits=waits, fn=fn, inc=inc):
            for (ws, wv) in waits:
                e.wait_ge(handles[ws], wv)
            r = fn(e)
            if inc is not None:
                r.then_inc(handles[inc[0]], inc[1])
            return r

        self.q[eng].append(replay)
        self.last_inc[eng] = (inc[0], tgt) if inc is not None else None
        return tgt

    def wait(self, eng, sem, val=None):
        v = self.cnt[sem] if val is None else val
        self.q[eng].append(lambda e, s=sem, v=v: e.wait_ge(self.sem_handles[s], v))
        return v

    def drain(self, eng):
        if not self.q[eng]:
            return
        last = self.last_inc.get(eng)
        if last is None:
            sem = "dr_" + eng
            self.cnt[sem] = self.cnt.get(sem, 0) + 1
            v = self.cnt[sem]
            prev = self.q[eng].pop()
            handles = self.sem_handles

            def wrapped(e, prev=prev, sem=sem, handles=handles):
                r = prev(e)
                r.then_inc(handles[sem], 1)
                return r

            self.q[eng].append(wrapped)
            last = (sem, v)
            self.last_inc[eng] = last
        s, v = last
        self.q[eng].append(lambda e, s=s, v=v: e.wait_ge(self.sem_handles[s], v))

    def val(self, sem):
        return self.cnt[sem]

    def build(self, block):
        for eng in self.ENGS:
            ops = self.q[eng]
            if not ops:
                continue

            def body(e, ops=ops):
                for f in ops:
                    f(e)

            getattr(block, eng)(body)


def ap_(t, off, dims):
    return bass.AP(t.tensor if hasattr(t, "tensor") else t, off, dims)


def emit_ln(ctx, xt, sqt, psS, mrep, vrep, allones, xnt, j_chunks, tile_f,
            ld_sem, ld_val, epscol=None):
    ctx.op("scalar", lambda e: e.activation(sqt[:], xt[:], AF.Square),
           wait=[(ld_sem, ld_val), ("st_ve", ctx.val("st_ve"))], inc=("st_sc", 1))
    for j in range(j_chunks):
        ctx.op("tensor",
               lambda e, j=j: e.matmul(psS[:, 0:tile_f], allones[:],
                                       xt[:, j * tile_f:(j + 1) * tile_f],
                                       start=(j == 0), stop=(j == j_chunks - 1)),
               wait=([(ld_sem, ld_val), ("st_ve", ctx.val("st_ve"))] if j == 0 else None))
    for j in range(j_chunks):
        ctx.op("tensor",
               lambda e, j=j: e.matmul(psS[:, tile_f:2 * tile_f], allones[:],
                                       sqt[:, j * tile_f:(j + 1) * tile_f],
                                       start=(j == 0), stop=(j == j_chunks - 1)),
               wait=([("st_sc", ctx.val("st_sc"))] if j == 0 else None),
               inc=(("st_te", 1) if j == j_chunks - 1 else None))
    te_done = ctx.val("st_te")
    Df = float(j_chunks * 128)
    ctx.op("scalar", lambda e: e.activation(mrep[:], psS[:, 0:tile_f], AF.Copy,
                                            scale=1.0 / Df),
           wait=[("st_te", te_done)])
    ctx.op("scalar", lambda e: e.activation(vrep[:], psS[:, tile_f:2 * tile_f], AF.Copy,
                                            scale=1.0 / Df), inc=("st_sc", 1))
    ctx.op("vector", lambda e: e.tensor_tensor(psS[:, 0:tile_f], mrep[:], mrep[:], OP.mult),
           wait=[("st_sc", ctx.val("st_sc"))])
    ctx.drain("vector")
    ctx.op("vector", lambda e: e.tensor_tensor(vrep[:], vrep[:], psS[:, 0:tile_f],
                                               OP.subtract), inc=("st_ve", 1))
    ctx.op("scalar", lambda e: e.activation(vrep[:], vrep[:], AF.Sqrt,
                                            bias=epscol[:]),
           wait=[("st_ve", ctx.val("st_ve"))], inc=("st_sc", 1))
    ctx.op("vector", lambda e: e.reciprocal(vrep[:], vrep[:]),
           wait=[("st_sc", ctx.val("st_sc"))])
    ctx.drain("vector")
    n_p = 128
    mb = ap_(mrep, 0, [[tile_f, n_p], [0, j_chunks], [1, tile_f]])
    vb = ap_(vrep, 0, [[tile_f, n_p], [0, j_chunks], [1, tile_f]])
    xv = ap_(xt, 0, [[j_chunks * tile_f, n_p], [tile_f, j_chunks], [1, tile_f]])
    ov = ap_(xnt, 0, [[j_chunks * tile_f, n_p], [tile_f, j_chunks], [1, tile_f]])
    ctx.op("vector", lambda e: e.tensor_tensor(ov, xv, mb, OP.subtract))
    ctx.drain("vector")
    ctx.op("vector", lambda e: e.tensor_tensor(ov, ov, vb, OP.mult), inc=("st_ve", 1))
    ctx.drain("vector")
    return ctx.val("st_ve")


def emit_colmax_sb(ctx, pcol, psT, prow, idt):
    ctx.op("tensor", lambda e: e.transpose(psT[:], pcol[:], idt[:]),
           wait=[("st_ve", ctx.val("st_ve"))], inc=("st_te", 1))
    ctx.op("vector", lambda e: e.reduce_max(prow[:, 0:1], psT[:], AX.X),
           wait=[("st_te", ctx.val("st_te"))], inc=("st_ve", 1))
    return ctx.val("st_ve")


def emit_ccmax(ctx, src_sb, ccin, ccout, ve_val):
    lv = ctx.op("gpsimd", lambda e: e.dma_start(out=ccin[:, :], in_=src_sb),
                inc=("ldcc", 16), wait=[("st_ve", ve_val)])
    ctx.wait("gpsimd", "ldcc", lv)
    ctx.op("gpsimd", lambda e: e.collective_compute(
        "AllReduce", OP.max, replica_groups=[list(range(NCORES))],
        ins=[ccin.ap().opt()], outs=[ccout.ap().opt()]), inc=("cc", 1))
    return ctx.val("cc")


def emit_scale_col(ctx, allp_dram, nvals, stride, off, qmax, rowbuf, onesrow,
                   psCol, ccol, scol, extra_mult=1.0, ld_eng="sync",
                   ld_wait=None):
    lv = ctx.op(ld_eng, lambda e: e.dma_start(
        out=rowbuf[:, 0:nvals], in_=ap_(allp_dram, off, [[1, 1], [stride, nvals]]),
        allow_slow_non_contiguous=True), inc=("ldsc", 16), wait=ld_wait)
    ctx.op("vector", lambda e: e.reduce_max(rowbuf[:, nvals:nvals + 1],
                                            rowbuf[:, 0:nvals], AX.X),
           wait=[("ldsc", lv)])
    ctx.drain("vector")
    ctx.op("vector", lambda e: e.reciprocal(rowbuf[:, nvals + 1:nvals + 2],
                                            rowbuf[:, nvals:nvals + 1]))
    ctx.drain("vector")
    ctx.op("vector", lambda e: e.tensor_scalar(rowbuf[:, nvals + 1:nvals + 2],
                                               rowbuf[:, nvals + 1:nvals + 2],
                                               qmax, None, OP.mult))
    ctx.op("vector", lambda e: e.tensor_scalar(rowbuf[:, nvals:nvals + 1],
                                               rowbuf[:, nvals:nvals + 1],
                                               extra_mult / qmax, None, OP.mult),
           inc=("st_ve", 1))
    ctx.drain("vector")
    pitch = psCol.shape[-1]
    pc2 = ap_(psCol, 0, [[pitch, 128], [1, 2]])
    pc_s = ap_(psCol, 0, [[pitch, 128], [1, 1]])
    pc_c = ap_(psCol, 1, [[pitch, 128], [1, 1]])
    ctx.op("tensor", lambda e: e.matmul(pc2, onesrow[:],
                                        rowbuf[:, nvals:nvals + 2], start=True, stop=True),
           wait=[("st_ve", ctx.val("st_ve"))], inc=("st_te", 1))
    ctx.op("scalar", lambda e: e.activation(scol[:], pc_s, AF.Copy),
           wait=[("st_te", ctx.val("st_te"))])
    ctx.op("scalar", lambda e: e.activation(ccol[:], pc_c, AF.Copy),
           inc=("st_sc", 1))
    ctx.wait("vector", "st_sc", ctx.val("st_sc"))
    ctx.wait("sync", "st_sc", ctx.val("st_sc"))


def ph_P0(ctx, nc, st, x, ident, ccin, ccout):
    sb = lambda n, s, d=F32: st.enter_context(nc.sbuf_tensor("p0" + n, s, d))
    ps = lambda n, s: st.enter_context(nc.psum_tensor("p0" + n, s, F32))
    ctx.tile_mark = {}
    xt = [sb(f"xt{i}", [128, 8 * TTS]) for i in range(2)]
    sqt = sb("sqt", [128, 8 * TTS])
    xnt = sb("xnt", [128, 8 * TTS])
    mrep = sb("mrep", [128, TTS])
    vrep = sb("vrep", [128, TTS])
    allones = sb("allones", [128, 128])
    idt = sb("idt", [128, 128])
    pcol = sb("pcol", [128, 1])
    epscol = sb("epscol", [128, 1])
    prow = sb("prow", [1, 128])
    psS = ps("psS", [128, 2 * TTS])
    psT = ps("psT", [1, 128])

    ctx.op("vector", lambda e: e.memset(allones[:], 1.0))
    ctx.op("vector", lambda e: e.memset(epscol[:], 1e-5))
    ctx.op("vector", lambda e: e.memset(prow[:], 0.0))
    ctx.op("vector", lambda e: e.memset(pcol[:], 0.0), inc=("st_ve", 1))
    ctx.op("sync", lambda e: e.dma_start(out=idt[:], in_=ident), inc=("ldc", 16))

    tile_done = {}
    for it in range(NTILE):
        b, tt = divmod(it, T // TTS)
        buf = xt[it % 2]
        lsem = "ldx0" if it % 2 == 0 else "ldx1"
        src = ap_(x, b * D * T + tt * TTS, [[T, 128], [128 * T, 8], [1, TTS]])
        ldwait = [("st_ve", tile_done[it - 2])] if it >= 2 else None
        lv = ctx.op("sync", lambda e, s=src, bu=buf: e.dma_start(out=bu[:], in_=s),
                    inc=(lsem, 16), wait=ldwait)
        emit_ln(ctx, buf, sqt, psS, mrep, vrep, allones, xnt, 8, TTS, lsem, lv, epscol)
        ctx.op("vector", lambda e: e.reduce_max(sqt[:, 0:1], xnt[:], AX.X,
                                                apply_absolute_value=True))
        ctx.drain("vector")
        ctx.op("vector", lambda e: e.tensor_tensor(pcol[:], pcol[:], sqt[:, 0:1],
                                                   OP.max), inc=("st_ve", 1))
        ctx.drain("vector")
        tile_done[it] = ctx.val("st_ve")
    ctx.wait("tensor", "ldc", ctx.val("ldc"))
    ve = emit_colmax_sb(ctx, pcol, psT, prow, idt)
    return emit_ccmax(ctx, prow[:, 0:2], ccin, ccout, ve)


def emit_dequant(ctx, wsrc, n_ot, I, wt, bq, binv, i32w, i32b, zt, wqt, identb, psTr,
                 wqT_sb):
    nblk = I // WBLK
    nkc = I // 128
    pw = wt[0].shape[-1]
    pq = bq.shape[-1]
    pi = i32w.shape[-1]
    for ot in range(n_ot):
        buf = wt[ot % 2]
        wq_par = wqt[ot % 2]
        lsem = "ldw0" if ot % 2 == 0 else "ldw1"
        ldw = [("st_ve", ctx.tile_mark.get(("dqv_last", ot % 2), 0))]
        lv = ctx.op("sync", lambda e, o=ot, b=buf: e.dma_start(
            out=b[:, 0:I], in_=ap_(wsrc, o * 128 * I, [[I, 128], [1, I]])),
            inc=(lsem, 16), wait=ldw)
        ctx.op("vector", lambda e, b=buf: e.reduce_max(
            bq[:, 0:nblk], ap_(b, 0, [[pw, 128], [WBLK, nblk], [1, WBLK]]), AX.X,
            apply_absolute_value=True),
            wait=[(lsem, lv)])
        ctx.drain("vector")
        ctx.op("vector", lambda e: e.tensor_scalar(bq[:, 0:nblk], bq[:, 0:nblk],
                                                   1.0 / QMAX_W, 1e-12,
                                                   OP.mult, OP.max))
        ctx.drain("vector")
        ctx.op("vector", lambda e: e.reciprocal(binv[:, 0:nblk], bq[:, 0:nblk]))
        ctx.drain("vector")
        bqb = ap_(bq, 0, [[pq, 128], [1, nblk], [0, WBLK]])
        bib = ap_(binv, 0, [[pq, 128], [1, nblk], [0, WBLK]])
        wv = ap_(buf, 0, [[pw, 128], [WBLK, nblk], [1, WBLK]])
        pz = zt.shape[-1]
        zv = zt[:, 0:I]
        iv = i32w[:, 0:I]
        av = i32b[:, 0:I]
        qv = ap_(wq_par, 0, [[pi, 128], [WBLK, nblk], [1, WBLK]])
        zblk = ap_(zt, 0, [[pz, 128], [WBLK, nblk], [1, WBLK]])
        ctx.op("vector", lambda e, zblk=zblk, wv=wv, bib=bib: e.tensor_tensor(
            zblk, wv, bib, OP.mult), inc=("st_ve", 1))
        ctx.tile_mark[("dqv_last", ot % 2)] = ctx.val("st_ve")
        ctx.drain("vector")
        ctx.op("vector", lambda e, zv=zv: e.tensor_scalar(zv, zv, MAGIC, None, OP.add))
        ctx.drain("vector")
        ctx.op("vector", lambda e, zv=zv: e.tensor_scalar(zv, zv, MAGIC, None,
                                                          OP.subtract))
        ctx.drain("vector")
        ctx.op("vector", lambda e, iv=iv, zv=zv: e.tensor_scalar(iv, zv, 0.0, None,
                                                                 OP.add))
        ctx.drain("vector")
        ctx.op("vector", lambda e, zv=zv, iv=iv: e.tensor_tensor(zv, zv, iv,
                                                                 OP.subtract))
        ctx.drain("vector")
        ctx.op("vector", lambda e, av=av, zv=zv: e.tensor_scalar(av, zv, 2.0, None,
                                                                 OP.mult))
        ctx.drain("vector")
        ctx.op("vector", lambda e, zv=zv, iv=iv, av=av: e.tensor_tensor(zv, iv, av,
                                                                        OP.add))
        ctx.drain("vector")
        ctx.op("vector", lambda e, qv=qv, zblk=zblk, bqb=bqb: e.tensor_tensor(
            qv, zblk, bqb, OP.mult), inc=("st_ve", 1),
            wait=[("st_te", ctx.tile_mark.get(("dq_te", ot % 2), 0))])
        wq_single = wqt[0] is wqt[1]
        ctx.drain("vector")
        vv = ctx.val("st_ve")
        gbase = ctx.tile_mark.get("dq_g", 0)
        for kc in range(nkc):
            g = gbase + kc
            tw = [("st_ve", vv)] if kc == 0 else []
            em = ctx.tile_mark.get(("dqe", g - 2))
            if em:
                tw.append(("st_sc", em))
            ctx.op("tensor", lambda e, kc=kc, g=g, wq_par=wq_par: e.transpose(
                psTr[g % 2][:], wq_par[:, kc * 128:(kc + 1) * 128], identb[:]),
                wait=(tw or None), inc=("st_te", 1))
            tv = ctx.val("st_te")
            ctx.op("scalar", lambda e, kc=kc, o=ot, g=g: e.activation(
                wqT_sb[:, (kc * n_ot + o) * 128:(kc * n_ot + o) * 128 + 128],
                psTr[g % 2][:], AF.Copy), wait=[("st_te", tv)], inc=("st_sc", 1))
            ctx.tile_mark[("dqe", g)] = ctx.val("st_sc")
        ctx.tile_mark["dq_g"] = gbase + nkc
        ctx.tile_mark[("dq_te", ot % 2)] = ctx.val("st_te")
        if wq_single:
            ctx.tile_mark[("dq_te", (ot + 1) % 2)] = ctx.val("st_te")


def ph_upgate(ctx, nc, st, tag, x, ident, identb, w_up, w_gate, up_o, gate_o,
              cc_ld, ccin, ccout, wq_cache=None):
    ctx.tile_mark = {}
    sb = lambda n, s, d=F32: st.enter_context(nc.sbuf_tensor(tag + n, s, d))
    ps = lambda n, s, d=F32: st.enter_context(nc.psum_tensor(tag + n, s, d))
    wqtu_sb = sb("wqtu_sb", [128, 8 * 24 * 128], BF16)
    wqtg_sb = sb("wqtg_sb", [128, 8 * 24 * 128], BF16)
    xt = [sb(f"xt{i}", [128, 8 * TTS]) for i in range(2)]
    xnt = sb("xnt", [128, 8 * TTS])
    sqt = xnt
    xqt = [sb(f"xqt{i}", [128, 8 * TTS], BF16) for i in range(2)]
    mrep = sb("mrep", [128, TTS])
    vrep = sb("vrep", [128, TTS])
    allones = sb("allones", [128, 128])
    onesrow = sb("onesrow", [1, 128])
    idt = sb("idt", [128, 128])
    idtb = sb("idtb", [128, 128], BF16)
    epscol = sb("epscol", [128, 1])
    c0col = sb("c0col", [128, 1])
    s0col = sb("s0col", [128, 1])
    rowbuf = sb("rowbuf", [1, 16])
    stgU = [sb(f"stgU{i}", [128, TTS]) for i in range(2)]
    stgG = [sb(f"stgG{i}", [128, TTS]) for i in range(2)]
    pupc = sb("pupc", [128, 1])
    pgc = sb("pgc", [128, 1])
    tmpc = sb("tmpc", [128, 2])
    prow2 = sb("prow2", [1, 2])
    psMisc = ps("psMisc", [128, 130])
    psTrSt = contextlib.ExitStack()
    psTr = [psTrSt.enter_context(nc.psum_tensor(tag + f"psTr{i}", [128, 128],
                                                BF16)) for i in range(2)]

    if True:
        ctx.op("vector", lambda e: e.memset(allones[:], 1.0))
        ctx.op("vector", lambda e: e.memset(onesrow[:], 1.0))
        ctx.op("vector", lambda e: e.memset(epscol[:], 1e-5))
        ctx.op("vector", lambda e: e.memset(pupc[:], 0.0))
        ctx.op("vector", lambda e: e.memset(pgc[:], 0.0), inc=("st_ve", 1))
        ctx.op("sync", lambda e: e.dma_start(out=idt[:], in_=ident), inc=("ldc", 16))
        ctx.op("sync", lambda e: e.dma_start(out=idtb[:], in_=identb), inc=("ldc", 16))

        ctx.wait("tensor", "ldc", ctx.val("ldc"))

        if wq_cache is None or wq_cache[2] == "store":
            wtb = [sb(f"wtb{i}", [128, D]) for i in range(2)]
            bq = sb("bq", [128, D // WBLK])
            binv = sb("binv", [128, D // WBLK])
            i32w = st.enter_context(nc.sbuf_tensor(tag + "i32w", [128, D], I32))
            i32b = st.enter_context(nc.sbuf_tensor(tag + "i32b", [128, D], I32))
            ztw = sb("ztw", [128, D])
            wqt_s = [sb(f"wqt_s{i}", [128, D], BF16) for i in range(2)]
            emit_dequant(ctx, w_up, 24, D, wtb, bq, binv, i32w, i32b, ztw,
                         wqt_s, idtb, psTr, wqtu_sb)
            u_ready = ctx.val("st_sc")
            emit_dequant(ctx, w_gate, 24, D, wtb, bq, binv, i32w, i32b, ztw,
                         wqt_s, idtb, psTr, wqtg_sb)
            wq_ready = ctx.val("st_sc")
            wq_wait_u = [("st_sc", u_ready)]
            wq_wait_g = [("st_sc", wq_ready)]
            if wq_cache is not None:
                ctx.op("sync", lambda e: e.dma_start(
                    out=wq_cache[0], in_=wqtu_sb[:]),
                    wait=[("st_sc", u_ready)], inc=("stoW", 16))
                ctx.op("sync", lambda e: e.dma_start(
                    out=wq_cache[1], in_=wqtg_sb[:]),
                    wait=[("st_sc", wq_ready)], inc=("stoW", 16))
        else:
            lvu = ctx.op("sync", lambda e: e.dma_start(
                out=wqtu_sb[:], in_=wq_cache[0]), inc=("ldw0", 16))
            lvg = ctx.op("gpsimd", lambda e: e.dma_start(
                out=wqtg_sb[:], in_=wq_cache[1]), inc=("ldgw", 16))
            wq_ready = 0
            wq_wait_u = [("ldw0", lvu)]
            wq_wait_g = [("ldgw", lvg)]
        psTrSt.close()
        psS = ps("psS", [128, 2 * TTS])
        psU = [ps(f"psU{i}", [128, TTS]) for i in range(2)]
        psG = [ps(f"psG{i}", [128, TTS]) for i in range(2)]
        if wq_ready:
            ctx.wait("tensor", "st_sc", wq_ready)
            ctx.wait("vector", "st_sc", wq_ready)
        emit_scale_col(ctx, cc_ld[0], 1, 1, cc_ld[2], QMAX_A, rowbuf, onesrow,
                       psMisc, c0col, s0col, ld_eng="gpsimd",
                       ld_wait=[("cc", cc_ld[1])])

        evac_mark = {}
        dma_mark = {}
        for it in range(NTILE):
            b, tt = divmod(it, T // TTS)
            buf = xt[it % 2]
            xq = xqt[it % 2]
            lsem = "ldx0" if it % 2 == 0 else "ldx1"
            src = ap_(x, b * D * T + tt * TTS, [[T, 128], [128 * T, 8], [1, TTS]])
            ldwait = [("st_ve", ctx.tile_mark.get(("xq", it - 2), 0))]
            lv = ctx.op("sync", lambda e, s=src, bu=buf: e.dma_start(out=bu[:], in_=s),
                        inc=(lsem, 16), wait=ldwait)
            emit_ln(ctx, buf, sqt, psS, mrep, vrep, allones, xnt, 8, TTS, lsem, lv, epscol)
            ctx.op("vector", lambda e: e.tensor_scalar(xnt[:], xnt[:], c0col[:, 0:1],
                                                       MAGIC, OP.mult, OP.add))
            ctx.drain("vector")
            ctx.op("vector", lambda e, xq=xq: e.tensor_scalar(xq[:], xnt[:], MAGIC,
                                                              None, OP.subtract),
                   inc=("st_ve", 1))
            ctx.drain("vector")
            ctx.tile_mark[("xq", it)] = ctx.val("st_ve")
            xq_v = ctx.val("st_ve")

            for oc in range(24):
                g = it * 24 + oc
                for which, psP, stg, out_d, pc in (
                        (0, psU[oc % 2], stgU[oc % 2], up_o, pupc),
                        (1, psG[oc % 2], stgG[oc % 2], gate_o, pgc)):
                    w_sb = wqtg_sb if which else wqtu_sb
                    mmw = [("st_ve", xq_v)] + (wq_wait_g if which else wq_wait_u)
                    em = evac_mark.get((which, g - 2))
                    if em:
                        mmw.append(("st_sc", em))
                    for kc in range(8):
                        ctx.op("tensor", lambda e, kc=kc, oc=oc, psP=psP, xq=xq, w_sb=w_sb:
                               e.matmul(psP[:], w_sb[:, (kc * 24 + oc) * 128:
                                                     (kc * 24 + oc) * 128 + 128],
                                        xq[:, kc * TTS:(kc + 1) * TTS],
                                        start=(kc == 0), stop=(kc == 7)),
                               wait=(mmw if kc == 0 else None),
                               inc=(("st_te", 1) if kc == 7 else None))
                    tev = ctx.val("st_te")
                    ssem = ("stoG" if which else "stoU") + str(oc % 2)
                    scw = [("st_te", tev)]
                    dm = dma_mark.get((which, g - 2))
                    if dm:
                        scw.append((ssem, dm))
                    vm = ctx.tile_mark.get(("vem", which, g - 2))
                    if vm:
                        scw.append(("st_ve", vm))
                    ctx.op("scalar", lambda e, psP=psP, stg=stg: e.activation(
                        stg[:], psP[:], AF.Copy, scale=s0col[:, 0:1]),
                        wait=scw, inc=("st_sc", 1))
                    evac_mark[(which, g)] = ctx.val("st_sc")
                    scv = ctx.val("st_sc")
                    ctx.op("vector", lambda e, stg=stg, which=which: e.reduce_max(
                        tmpc[:, which:which + 1], stg[:], AX.X,
                        apply_absolute_value=True), wait=[("st_sc", scv)])
                    ctx.drain("vector")
                    ctx.op("vector", lambda e, pc=pc, which=which: e.tensor_tensor(
                        pc[:], pc[:], tmpc[:, which:which + 1], OP.max),
                        inc=("st_ve", 1))
                    ctx.drain("vector")
                    ctx.tile_mark[("vem", which, g)] = ctx.val("st_ve")
                    dst = ap_(out_d, b * FF * T + oc * 128 * T + tt * TTS,
                              [[T, 128], [1, TTS]])
                    ctx.op("sync", lambda e, stg=stg, dst=dst: e.dma_start(
                        out=dst, in_=stg[:]), wait=[("st_sc", scv)], inc=(ssem, 16))
                    dma_mark[(which, g)] = ctx.val(ssem)

        ctx.op("tensor", lambda e: e.transpose(psMisc[0:1, 0:128], pupc[:], idt[:]),
               wait=[("st_ve", ctx.val("st_ve"))], inc=("st_te", 1))
        ctx.op("vector", lambda e: e.reduce_max(prow2[:, 0:1], psMisc[0:1, 0:128], AX.X),
               wait=[("st_te", ctx.val("st_te"))], inc=("st_ve", 1))
        ctx.op("tensor", lambda e: e.transpose(psMisc[0:1, 0:128], pgc[:], idt[:]),
               wait=[("st_ve", ctx.val("st_ve"))], inc=("st_te", 1))
        ctx.op("vector", lambda e: e.reduce_max(prow2[:, 1:2], psMisc[0:1, 0:128], AX.X),
               wait=[("st_te", ctx.val("st_te"))], inc=("st_ve", 1))
        for s in ("stoU0", "stoU1", "stoG0", "stoG1"):
            ctx.wait("sync", s, ctx.val(s))
        return emit_ccmax(ctx, prow2[:, 0:2], ccin, ccout, ctx.val("st_ve"))


def ph_gate(ctx, nc, st, tag, gate, ident, pre3_o, cc_ld, ccin, ccout):
    ctx.tile_mark = {}
    sb = lambda n, s, d=F32: st.enter_context(nc.sbuf_tensor(tag + n, s, d))
    ps = lambda n, s, d=F32: st.enter_context(nc.psum_tensor(tag + n, s, d))
    if True:
        gt = [sb(f"gt{i}", [128, T]) for i in range(4)]
        sg = [sb(f"sg{i}", [128, T]) for i in range(4)]
        onesrow = sb("onesrow", [1, 128])
        idt = sb("idt", [128, 128])
        c2col = sb("c2col", [128, 1])
        s2col = sb("s2col", [128, 1])
        s2bcol = sb("s2bcol", [128, 1])
        rowbuf = sb("rowbuf", [1, 32])
        pcol = sb("pcol", [128, 1])
        tmpc = sb("tmpc", [128, 1])
        prow = sb("prow", [1, 128])
        psMisc = ps("psMisc", [128, 130])

        ctx.op("vector", lambda e: e.memset(onesrow[:], 1.0))
        ctx.op("vector", lambda e: e.memset(prow[:], 0.0))
        ctx.op("vector", lambda e: e.memset(pcol[:], 0.0), inc=("st_ve", 1))
        ctx.op("sync", lambda e: e.dma_start(out=idt[:], in_=ident), inc=("ldc", 16))
        emit_scale_col(ctx, cc_ld[0], 1, 1, cc_ld[2], QMAX_A, rowbuf, onesrow,
                       psMisc, c2col, s2col, ld_eng="gpsimd",
                       ld_wait=[("cc", cc_ld[1])])
        ctx.drain("scalar")
        ctx.op("scalar", lambda e: e.activation(s2bcol[:], s2col[:], AF.Copy,
                                                scale=1.0 / 65536.0), inc=("st_sc", 1))
        ctx.wait("vector", "st_sc", ctx.val("st_sc"))
        ctx.wait("tensor", "ldc", ctx.val("ldc"))

        ntile = B_CORE * (FF // 128)
        LSEMS = ("ldx0", "ldx1", "ldu0", "ldu1")

        def stageA(it):
            b, oc = divmod(it, FF // 128)
            buf = gt[it % 4]
            lsem = LSEMS[it % 4]
            src = ap_(gate, b * FF * T + oc * 128 * T, [[T, 128], [1, T]])
            lw = [("st_ve", ctx.tile_mark.get(("done", it - 4), 0))]
            lv = ctx.op("sync", lambda e, s=src, bu=buf: e.dma_start(out=bu[:], in_=s),
                        inc=(lsem, 16), wait=lw)
            ctx.op("vector", lambda e, bu=buf: e.tensor_scalar(
                bu[:], bu[:], c2col[:, 0:1], MAGIC, OP.mult, OP.add),
                wait=[(lsem, lv)])
            ctx.drain("vector")
            ctx.op("vector", lambda e, bu=buf: e.tensor_scalar(
                bu[:], bu[:], MAGIC, None, OP.subtract), inc=("st_ve", 1))
            ctx.drain("vector")
            ctx.tile_mark[("gq", it)] = ctx.val("st_ve")

        def stageB(it):
            buf = gt[it % 4]
            sgb = sg[it % 4]
            smw = [("st_ve", ctx.tile_mark[("gq", it)]),
                   ("st_ve", ctx.tile_mark.get(("done", it - 4), 0))]
            dm = ctx.tile_mark.get(("dma", it - 4))
            if dm:
                smw.append(dm)
            ctx.op("scalar", lambda e, bu=buf, sgb=sgb: e.activation(
                sgb[:], bu[:], AF.Sigmoid, scale=s2col[:, 0:1]),
                wait=smw)
            ctx.drain("scalar")
            ctx.op("scalar", lambda e, sgb=sgb: e.activation(
                sgb[:], sgb[:], AF.Copy, scale=65536.0, bias=MAGIC),
                inc=("st_sc", 1))
            ctx.tile_mark[("sig", it)] = ctx.val("st_sc")

        def stageC(it):
            b, oc = divmod(it, FF // 128)
            buf = gt[it % 4]
            sgb = sg[it % 4]
            ssem = "sto0" if it % 2 == 0 else "sto1"
            ctx.op("vector", lambda e, sgb=sgb: e.tensor_scalar(
                sgb[:], sgb[:], MAGIC, 65535.0, OP.subtract, OP.min),
                wait=[("st_sc", ctx.tile_mark[("sig", it)])])
            ctx.drain("vector")
            ctx.op("vector", lambda e, sgb=sgb, bu=buf: e.scalar_tensor_tensor(
                sgb[:], sgb[:], s2bcol[:, 0:1], bu[:], OP.mult, OP.mult),
                inc=("st_ve", 1))
            ctx.drain("vector")
            scv = ctx.val("st_ve")
            ctx.op("vector", lambda e, sgb=sgb: e.reduce_max(
                tmpc[:], sgb[:], AX.X, apply_absolute_value=True))
            ctx.drain("vector")
            ctx.op("vector", lambda e: e.tensor_tensor(pcol[:], pcol[:], tmpc[:],
                                                       OP.max), inc=("st_ve", 1))
            ctx.drain("vector")
            ctx.tile_mark[("done", it)] = ctx.val("st_ve")
            dst = ap_(pre3_o, b * FF * T + oc * 128 * T, [[T, 128], [1, T]])
            ctx.op("sync", lambda e, sgb=sgb, dst=dst: e.dma_start(
                out=dst, in_=sgb[:]), wait=[("st_ve", scv)], inc=(ssem, 16))
            ctx.tile_mark[("dma", it)] = (ssem, ctx.val(ssem))

        for it in range(ntile):
            stageA(it)
            stageB(it)
            if it >= 1:
                stageC(it - 1)
        stageC(ntile - 1)

        ve = emit_colmax_sb(ctx, pcol, ap_(psMisc, 0, [[130, 1], [1, 128]]), prow,
                            idt)
        for s in ("sto0", "sto1"):
            ctx.wait("sync", s, ctx.val(s))
        return emit_ccmax(ctx, prow[:, 0:2], ccin, ccout, ve)


def ph_prod(ctx, nc, st, tag, pre3, up, ident, pre4_o, cc3, cc1, ccin, ccout):
    ctx.tile_mark = {}
    sb = lambda n, s, d=F32: st.enter_context(nc.sbuf_tensor(tag + n, s, d))
    ps = lambda n, s, d=F32: st.enter_context(nc.psum_tensor(tag + n, s, d))
    if True:
        g2t = [sb(f"g2t{i}", [128, T]) for i in range(4)]
        upt = [sb(f"upt{i}", [128, T]) for i in range(4)]
        onesrow = sb("onesrow", [1, 128])
        idt = sb("idt", [128, 128])
        c3col = sb("c3col", [128, 1])
        s3col = sb("s3col", [128, 1])
        c1col = sb("c1col", [128, 1])
        s1col = sb("s1col", [128, 1])
        rowbuf = sb("rowbuf", [1, 32])
        pcol = sb("pcol", [128, 1])
        tmpc = sb("tmpc", [128, 1])
        prow = sb("prow", [1, 128])
        psMisc = ps("psMisc", [128, 130])

        ctx.op("vector", lambda e: e.memset(onesrow[:], 1.0))
        ctx.op("vector", lambda e: e.memset(prow[:], 0.0))
        ctx.op("vector", lambda e: e.memset(pcol[:], 0.0), inc=("st_ve", 1))
        ctx.op("sync", lambda e: e.dma_start(out=idt[:], in_=ident), inc=("ldc", 16))
        emit_scale_col(ctx, cc3[0], 1, 1, cc3[2], QMAX_A, rowbuf, onesrow,
                       psMisc, c3col, s3col, ld_eng="gpsimd",
                       ld_wait=[("cc", cc3[1])])
        emit_scale_col(ctx, cc1[0], 1, 1, cc1[2], QMAX_A, rowbuf, onesrow,
                       psMisc, c1col, s1col, ld_eng="gpsimd",
                       ld_wait=[("cc", cc1[1])])
        s31col = sb("s31col", [128, 1])
        ctx.op("vector", lambda e: e.tensor_tensor(
            s31col[:], s3col[:], s1col[:], OP.mult), inc=("st_ve", 1))
        ctx.drain("vector")
        ctx.wait("vector", "st_sc", ctx.val("st_sc"))
        ctx.wait("scalar", "st_ve", ctx.val("st_ve"))
        ctx.wait("tensor", "ldc", ctx.val("ldc"))

        ntile = B_CORE * (FF // 128)
        LSEMS = ("ldx0", "ldx1", "ldu0", "ldu1")
        LSEMS2 = ("ldw0", "ldw1", "ldm0", "ldm1")

        def stageA(it):
            b, oc = divmod(it, FF // 128)
            bg = g2t[it % 4]
            bu = upt[it % 4]
            off = b * FF * T + oc * 128 * T
            lw = [("st_ve", ctx.tile_mark.get(("done", it - 4), 0))]
            dmp = ctx.tile_mark.get(("dma", it - 4))
            if dmp:
                lw.append(dmp)
            lv1 = ctx.op("sync", lambda e, bg=bg, off=off: e.dma_start(
                out=bg[:], in_=ap_(pre3, off, [[T, 128], [1, T]])),
                inc=(LSEMS[it % 4], 16), wait=lw)
            lv2 = ctx.op("sync", lambda e, bu=bu, off=off: e.dma_start(
                out=bu[:], in_=ap_(up, off, [[T, 128], [1, T]])),
                inc=(LSEMS2[it % 4], 16))
            ctx.tile_mark[("ld2", it)] = (LSEMS2[it % 4], lv2)
            ctx.op("vector", lambda e, bg=bg: e.tensor_scalar(
                bg[:], bg[:], c3col[:, 0:1], MAGIC, OP.mult, OP.add),
                wait=[(LSEMS[it % 4], lv1)])
            ctx.drain("vector")
            ctx.op("vector", lambda e, bg=bg: e.tensor_scalar(
                bg[:], bg[:], MAGIC, None, OP.subtract), inc=("st_ve", 1))
            ctx.drain("vector")
            ctx.tile_mark[("gq3", it)] = ctx.val("st_ve")

        def stageB(it):
            bu = upt[it % 4]
            scw = [ctx.tile_mark[("ld2", it)],
                   ("st_ve", ctx.tile_mark.get(("done", it - 4), 0))]
            ctx.op("scalar", lambda e, bu=bu: e.activation(
                bu[:], bu[:], AF.Copy, scale=c1col[:, 0:1], bias=MAGIC),
                wait=scw)
            ctx.drain("scalar")
            ctx.op("scalar", lambda e, bu=bu: e.activation(
                bu[:], bu[:], AF.Copy, bias=-MAGIC), inc=("st_sc", 1))
            ctx.drain("scalar")
            ctx.tile_mark[("sdone", it)] = ctx.val("st_sc")

        def stageC(it):
            b, oc = divmod(it, FF // 128)
            bg = g2t[it % 4]
            bu = upt[it % 4]
            off = b * FF * T + oc * 128 * T
            ssem = "sto0" if it % 2 == 0 else "sto1"
            ctx.op("vector", lambda e, bg=bg, bu=bu: e.scalar_tensor_tensor(
                bg[:], bg[:], s31col[:, 0:1], bu[:], OP.mult, OP.mult),
                wait=[("st_sc", ctx.tile_mark[("sdone", it)]),
                      ("st_ve", ctx.tile_mark[("gq3", it)])], inc=("st_ve", 1))
            ctx.drain("vector")
            scv = ctx.val("st_ve")
            ctx.op("vector", lambda e, bg=bg: e.reduce_max(
                tmpc[:], bg[:], AX.X, apply_absolute_value=True))
            ctx.drain("vector")
            ctx.op("vector", lambda e: e.tensor_tensor(pcol[:], pcol[:], tmpc[:],
                                                       OP.max), inc=("st_ve", 1))
            ctx.drain("vector")
            ctx.tile_mark[("done", it)] = ctx.val("st_ve")
            dst = ap_(pre4_o, off, [[T, 128], [1, T]])
            ctx.op("sync", lambda e, bg=bg, dst=dst: e.dma_start(
                out=dst, in_=bg[:]), wait=[("st_ve", scv)], inc=(ssem, 16))
            ctx.tile_mark[("dma", it)] = (ssem, ctx.val(ssem))

        for it in range(ntile):
            stageA(it)
            stageB(it)
            if it >= 1:
                stageC(it - 1)
        stageC(ntile - 1)

        ve = emit_colmax_sb(ctx, pcol, ap_(psMisc, 0, [[130, 1], [1, 128]]), prow,
                            idt)
        for s in ("sto0", "sto1"):
            ctx.wait("sync", s, ctx.val(s))
        return emit_ccmax(ctx, prow[:, 0:2], ccin, ccout, ve)


def emit_ffn_tail(ctx, nc, st, tag, pre4, xres, cc4, w_down, identb_t,
                  out_writer, wd_cache=None):
    sb = lambda n, s, d=F32: st.enter_context(nc.sbuf_tensor(tag + n, s, d))
    wdt_sb = sb("wdt_sb", [128, 24 * 8 * 128], BF16)
    wdb = [sb(f"wdb{i}", [128, FF]) for i in range(2)]
    bqd = sb("bqd", [128, FF // WBLK])
    binvd = sb("binvd", [128, FF // WBLK])
    i32d = st.enter_context(nc.sbuf_tensor(tag + "i32d", [128, FF], I32))
    i32d2 = st.enter_context(nc.sbuf_tensor(tag + "i32d2", [128, FF], I32))
    ztd = sb("ztd", [128, FF])
    wqt_d1 = sb("wqt_d", [128, FF], BF16)
    wqt_d = [wqt_d1, wqt_d1]
    c4col = sb("c4col", [128, 1])
    s4col = sb("s4col", [128, 1])
    rowbuf4 = sb("rowbuf4", [1, 32])
    obuf = [st.enter_context(nc.sbuf_tensor(tag + f"obuf{i}", [128, 24 * TTS],
                                            BF16))
            for i in range(2)]
    p4t = [sb(f"p4t{i}", [128, TTS]) for i in range(2)]
    xrt = [sb(f"xrt{i}", [128, 8 * TTS]) for i in range(2)]
    stgD = [sb(f"stgD{i}", [128, TTS]) for i in range(2)]
    onesrow4 = sb("onesrow4", [1, 128])
    psMisc4 = st.enter_context(nc.psum_tensor(tag + "psMisc4", [128, 2], F32))
    psTr2St = contextlib.ExitStack()
    psTr2 = [psTr2St.enter_context(nc.psum_tensor(tag + f"psTr2{i}", [128, 128],
                                                  BF16)) for i in range(2)]
    ctx.op("vector", lambda e: e.memset(onesrow4[:], 1.0), inc=("st_ve", 1))
    ctx.wait("tensor", "ldc", ctx.val("ldc"))
    if wd_cache is None or wd_cache[1] == "store":
        emit_dequant(ctx, w_down, 8, FF, wdb, bqd, binvd, i32d, i32d2, ztd,
                     wqt_d, identb_t, psTr2, wdt_sb)
        wd_ready = ctx.val("st_sc")
        wd_wait = [("st_sc", wd_ready)]
        if wd_cache is not None:
            ctx.op("sync", lambda e: e.dma_start(
                out=wd_cache[0], in_=wdt_sb[:]),
                wait=[("st_sc", wd_ready)], inc=("stoW", 16))
    else:
        lvd = ctx.op("sync", lambda e: e.dma_start(
            out=wdt_sb[:], in_=wd_cache[0]), inc=("ldw0", 16))
        wd_wait = [("ldw0", lvd)]
    psTr2St.close()
    emit_scale_col(ctx, cc4[0], 1, 1, cc4[2], QMAX_A, rowbuf4, onesrow4,
                   psMisc4, c4col, s4col, extra_mult=0.5, ld_eng="gpsimd",
                   ld_wait=[("cc", cc4[1])])
    psD = [st.enter_context(nc.psum_tensor(tag + f"psD{i}", [128, TTS], F32))
           for i in range(2)]

    def emit_rounds(it, slot):
        b, tt = divmod(it, T // TTS)
        ob = obuf[it % 2]
        if slot == 0:
            xb = xrt[it % 2]
            rsem = "ldr0" if it % 2 == 0 else "ldr1"
            rw = [("st_sc", ctx.tile_mark.get(("ft_xv", it - 2), 0))]
            rv = ctx.op("sync", lambda e, b=b, tt=tt, xb=xb: e.dma_start(
                out=xb[:], in_=ap_(xres, b * D * T + tt * TTS,
                                   [[T, 128], [128 * T, 8], [1, TTS]])),
                inc=(rsem, 16), wait=rw)
            ctx.tile_mark[("ft_rv", it)] = (rsem, rv)
        for kc in range(3 * slot, 3 * slot + 3):
            pb = p4t[kc % 2]
            psem = "ldf0" if kc % 2 == 0 else "ldf1"
            lw = [("st_ve", ctx.tile_mark.get(("ft_rnd", it, kc - 2),
                   ctx.tile_mark.get(("ft_rnd", it - 1, 22 + kc), 0)))]
            lv = ctx.op("sync", lambda e, b=b, tt=tt, kc=kc, pb=pb: e.dma_start(
                out=pb[:], in_=ap_(pre4, b * FF * T + kc * 128 * T + tt * TTS,
                                    [[T, 128], [1, TTS]])),
                inc=(psem, 16), wait=lw)
            vw = [(psem, lv)]
            if kc == 0:
                vw.append(("st_te", ctx.tile_mark.get(("ft_te", it - 2), 0)))
            ctx.op("vector", lambda e, pb=pb: e.tensor_scalar(
                pb[:], pb[:], c4col[:, 0:1], MAGIC, OP.mult, OP.add),
                wait=vw)
            ctx.drain("vector")
            ctx.op("vector", lambda e, ob=ob, kc=kc, pb=pb: e.tensor_scalar(
                ob[:, kc * TTS:(kc + 1) * TTS], pb[:], MAGIC, None, OP.subtract),
                inc=("st_ve", 1))
            ctx.drain("vector")
            ctx.tile_mark[("ft_rnd", it, kc)] = ctx.val("st_ve")
        if slot == 7:
            ctx.tile_mark[("ft_ov", it)] = ctx.val("st_ve")

    for slot in range(8):
        emit_rounds(0, slot)
    for it in range(NTILE):
        b, tt = divmod(it, T // TTS)
        ob = obuf[it % 2]
        xb = xrt[it % 2]
        rsem, rv = ctx.tile_mark[("ft_rv", it)]
        ov = ctx.tile_mark[("ft_ov", it)]
        for mc in range(8):
            if it + 1 < NTILE:
                emit_rounds(it + 1, mc)
            psP = psD[mc % 2]
            mmw = [("st_ve", ov)] + wd_wait
            em = ctx.tile_mark.get(("ft_em", it, mc - 2))
            if em is None and it > 0:
                em = ctx.tile_mark.get(("ft_em", it - 1, 6 + mc))
            if em:
                mmw.append(("st_sc", em))
            for kc in range(24):
                ctx.op("tensor", lambda e, kc=kc, mc=mc, psP=psP, ob=ob: e.matmul(
                    psP[:], wdt_sb[:, (kc * 8 + mc) * 128:(kc * 8 + mc) * 128 + 128],
                    ob[:, kc * TTS:(kc + 1) * TTS],
                    start=(kc == 0), stop=(kc == 23)),
                    wait=(mmw if kc == 0 else None),
                    inc=(("st_te", 1) if kc == 23 else None))
            tev = ctx.val("st_te")
            stg = stgD[mc % 2]
            scw = [("st_te", tev), (rsem, rv)]
            dm = ctx.tile_mark.get(("ft_dm", it, mc - 2))
            if dm is None and it > 0:
                dm = ctx.tile_mark.get(("ft_dm", it - 1, 6 + mc))
            if dm:
                scw.append(dm)
            ctx.op("vector", lambda e, psP=psP, stg=stg, xb=xb, mc=mc:
                   e.scalar_tensor_tensor(stg[:], psP[:], s4col[:, 0:1],
                                          xb[:, mc * TTS:(mc + 1) * TTS],
                                          OP.mult, OP.add),
                   wait=scw, inc=("st_sc", 1))
            ctx.drain("vector")
            ctx.tile_mark[("ft_em", it, mc)] = ctx.val("st_sc")
            scv = ctx.val("st_sc")
            out_writer(ctx, it, b, tt, mc, stg, scv)
        ctx.tile_mark[("ft_xv", it)] = ctx.val("st_sc")
        ctx.tile_mark[("ft_te", it)] = ctx.val("st_te")


def ph_tail(ctx, nc, st, pre4, xres, cc4, w_down, identb, out, wd_cache):
    ctx.tile_mark = {}
    idtb = st.enter_context(nc.sbuf_tensor("t2idtb", [128, 128], BF16))
    ctx.op("sync", lambda e: e.dma_start(out=idtb[:], in_=identb),
           inc=("ldc", 16))

    def writer(ctx, it, b, tt, mc, stg, scv):
        ssem = "sto0" if (it * 8 + mc) % 2 == 0 else "sto1"
        dst = ap_(out, b * D * T + mc * 128 * T + tt * TTS, [[T, 128], [1, TTS]])
        ctx.op("sync", lambda e, stg=stg, dst=dst: e.dma_start(
            out=dst, in_=stg[:]), wait=[("st_sc", scv)], inc=(ssem, 16))
        ctx.tile_mark[("ft_dm", it, mc)] = (ssem, ctx.val(ssem))

    emit_ffn_tail(ctx, nc, st, "t2", pre4, xres, cc4, w_down, idtb, writer,
                  wd_cache=wd_cache)
    for s in ("sto0", "sto1"):
        ctx.wait("sync", s, ctx.val(s))


def ph_P4(ctx, nc, st, x0, pre4, cc4, w_down, ident, identb, w_inT, w_dtT,
          w_bcT, w_outT, cdiag, a_jln, dcol, cbcol, bdtcol, nwcol, rscal,
          x2_o, x1_s, h2_s, mg_s, xc_s, dt_s, bcps_d, ccin, ccout, wd_cache):
    ctx.tile_mark = {}
    sb = lambda n, s, d=F32: st.enter_context(nc.sbuf_tensor(n, s, d))
    ps = lambda n, s, d=F32: st.enter_context(nc.psum_tensor(n, s, d))
    if True:
        idt = sb("idt", [128, 128])
        idtb = sb("idtb", [128, 128], BF16)
        allones = sb("allones", [128, 128])
        onesrow = sb("onesrow", [1, 128])
        epscol = sb("epscol", [128, 1])
        eps6col = sb("eps6col", [128, 1])
        dcol_sb = sb("dcol_sb", [128, 16])
        cbcol_sb = sb("cbcol_sb", [128, 16])
        bdtcol_sb = sb("bdtcol_sb", [128, 16])
        nwcol_sb = sb("nwcol_sb", [128, 8])
        atan_sb = sb("atan_sb", [128, 16 * DS])
        sctanb = sb("sctanb", [128, 1])
        rscol = sb("rscol", [128, 1])
        rowmisc = sb("rowmisc", [1, 8])
        pcol = sb("pcol", [128, 1])
        prow = sb("prow", [1, 128])
        psMisc = ps("psMisc", [128, 130])

        ctx.op("vector", lambda e: e.memset(allones[:], 1.0))
        ctx.op("vector", lambda e: e.memset(onesrow[:], 1.0))
        ctx.op("vector", lambda e: e.memset(epscol[:], 1e-5))
        ctx.op("vector", lambda e: e.memset(eps6col[:], 1e-6))
        ctx.op("vector", lambda e: e.memset(sctanb[:], -SC_C * SC_IN))
        ctx.op("vector", lambda e: e.memset(prow[:], 0.0))
        ctx.op("vector", lambda e: e.memset(pcol[:], 0.0), inc=("st_ve", 1))
        for (dst, srcd) in ((idt, ident), (idtb, identb), (dcol_sb, dcol),
                            (cbcol_sb, cbcol), (bdtcol_sb, bdtcol),
                            (nwcol_sb, nwcol)):
            ctx.op("sync", lambda e, dst=dst, srcd=srcd: e.dma_start(
                out=dst[:], in_=srcd), inc=("ldc", 16))
        ctx.op("sync", lambda e: e.dma_start(out=atan_sb[:], in_=a_jln),
               inc=("ldc", 16))
        ctx.op("sync", lambda e: e.dma_start(out=rowmisc[:, 0:1], in_=rscal),
               inc=("ldc", 16))
        lcv = ctx.val("ldc")
        ctx.op("scalar", lambda e: e.activation(atan_sb[:], atan_sb[:], AF.Tanh,
                                                scale=SCA_IN), wait=[("ldc", lcv)])
        ctx.drain("scalar")
        ctx.op("scalar", lambda e: e.activation(atan_sb[:], atan_sb[:], AF.Exp,
                                                scale=SCA_OUT), inc=("st_sc", 1))
        ctx.drain("scalar")
        ctx.op("vector", lambda e: e.tensor_scalar(atan_sb[:], atan_sb[:],
                                                   -1.0 / 16.0, None, OP.mult),
               wait=[("st_sc", ctx.val("st_sc"))], inc=("st_ve", 1))
        ctx.drain("vector")
        a_ready = ctx.val("st_ve")
        ctx.op("scalar", lambda e: e.activation(rowmisc[:, 1:2], rowmisc[:, 0:1],
                                                AF.Sigmoid),
               wait=[("ldc", ctx.val("ldc"))])
        ctx.drain("scalar")
        ctx.op("scalar", lambda e: e.activation(rowmisc[:, 1:2], rowmisc[:, 1:2],
                                                AF.Copy, scale=0.5), inc=("st_sc", 1))
        ctx.op("tensor", lambda e: e.matmul(
            ap_(psMisc, 128, [[130, 128], [1, 1]]), onesrow[:], rowmisc[:, 1:2],
            start=True, stop=True),
            wait=[("st_sc", ctx.val("st_sc")), ("ldc", ctx.val("ldc")),
                  ("st_ve", 1)],
            inc=("st_te", 1))
        ctx.op("scalar", lambda e: e.activation(rscol[:],
                                                ap_(psMisc, 128, [[130, 128], [1, 1]]),
                                                AF.Copy),
               wait=[("st_te", ctx.val("st_te"))], inc=("st_sc", 1))

        phA = contextlib.ExitStack()

        def writer(ctx, it, b, tt, mc, stg, scv):
            ssem = "stoX0" if (it * 8 + mc) % 2 == 0 else "stoX1"
            dst = ap_(x1_s, b * D * T + mc * 128 * T + tt * TTS, [[T, 128], [1, TTS]])
            ctx.op("sync", lambda e, stg=stg, dst=dst: e.dma_start(
                out=dst, in_=stg[:]), wait=[("st_sc", scv)], inc=(ssem, 16))
            ctx.tile_mark[("ft_dm", it, mc)] = (ssem, ctx.val(ssem))

        emit_ffn_tail(ctx, nc, phA, "t1", pre4, x0, cc4, w_down, idtb, writer,
                      wd_cache=wd_cache)
        ctx.wait("sync", "stoX0", ctx.val("stoX0"))
        ctx.wait("sync", "stoX1", ctx.val("stoX1"))
        phA_sc = ctx.val("st_sc")
        phA_te = ctx.val("st_te")
        phA.close()

        phB = contextlib.ExitStack()
        sbB = lambda n, s, d=F32: phB.enter_context(nc.sbuf_tensor(n, s, d))
        w_in_sb = sbB("w_in_sb", [128, 8 * 32 * 128], BF16)
        cdiag_sb = sbB("cdiag_sb", [128, 16 * 4 * 128], BF16)
        x1t = [sbB(f"x1t{i}", [128, 8 * TTS]) for i in range(2)]
        h2t = sbB("h2t", [128, 8 * TTS])
        xnb = sbB("xnb", [128, 8 * TTS], BF16)
        xbr = [sbB(f"xbr{i}", [128, 16 * (TTS + 4)], BF16) for i in range(2)]
        mrep = sbB("mrepB", [128, TTS])
        vrep = sbB("vrepB", [128, TTS])
        stgB = [sbB(f"stgB{i}", [128, TTS]) for i in range(2)]
        psSB = phB.enter_context(nc.psum_tensor("psSB", [128, 2 * TTS], F32))
        psI = [phB.enter_context(nc.psum_tensor(f"psI{i}", [128, TTS], F32))
               for i in range(2)]

        ctx.wait("gpsimd", "st_te", phA_te)
        ctx.wait("gpsimd", "st_sc", phA_sc)
        ctx.wait("gpsimd", "stoX0", ctx.val("stoX0"))
        ctx.wait("gpsimd", "stoX1", ctx.val("stoX1"))
        ctx.op("gpsimd", lambda e: e.dma_start(out=w_in_sb[:], in_=w_inT),
               inc=("ldgw", 16))
        ctx.op("gpsimd", lambda e: e.dma_start(out=cdiag_sb[:], in_=cdiag),
               inc=("ldgw", 16))
        wB_ready = ctx.val("ldgw")
        ctx.wait("tensor", "ldgw", wB_ready)
        ctx.wait("vector", "st_sc", phA_sc)
        ctx.wait("scalar", "st_te", phA_te)
        ctx.wait("tensor", "st_sc", phA_sc)

        for it in range(NTILE):
            b, tt = divmod(it, T // TTS)
            xb = x1t[it % 2]
            xr = xbr[it % 2]
            lsem = "ldx0" if it % 2 == 0 else "ldx1"
            lw = [("st_sc", ctx.tile_mark.get(("phb_done", it - 2), 0)),
                  ("stoX0", ctx.val("stoX0")), ("stoX1", ctx.val("stoX1"))]
            lv = ctx.op("sync", lambda e, b=b, tt=tt, xb=xb: e.dma_start(
                out=xb[:], in_=ap_(x1_s, b * D * T + tt * TTS,
                                   [[T, 128], [128 * T, 8], [1, TTS]])),
                inc=(lsem, 16), wait=lw)
            for smn in ("stoH",):
                pv = ctx.tile_mark.get(("phb_" + smn, it - 1))
                if pv:
                    ctx.wait("scalar", smn, pv)
            emit_ln(ctx, xb, h2t, psSB, mrep, vrep, allones, h2t, 8, TTS,
                    lsem, lv, epscol)
            h2v = ctx.val("st_ve")
            ctx.op("sync", lambda e, b=b, tt=tt: e.dma_start(
                out=ap_(h2_s, b * D * T + tt * TTS, [[T, 128], [128 * T, 8], [1, TTS]]),
                in_=h2t[:]), wait=[("st_ve", h2v)], inc=("stoH", 16))
            ctx.tile_mark[("phb_stoH", it)] = ctx.val("stoH")
            ctx.op("scalar", lambda e, xb=xb: e.activation(xb[:], h2t[:], AF.Square),
                   wait=[("st_ve", h2v)], inc=("st_sc", 1))
            for j in range(8):
                ctx.op("tensor", lambda e, j=j, xb=xb: e.matmul(
                    psSB[:, 0:TTS], allones[:], xb[:, j * TTS:(j + 1) * TTS],
                    start=(j == 0), stop=(j == 7)),
                    wait=([("st_sc", ctx.val("st_sc"))] if j == 0 else None),
                    inc=(("st_te", 1) if j == 7 else None))
            ctx.op("scalar", lambda e: e.activation(vrep[:], psSB[:, 0:TTS], AF.Copy,
                                                    scale=1.0 / D),
                   wait=[("st_te", ctx.val("st_te"))])
            ctx.drain("scalar")
            ctx.op("scalar", lambda e: e.activation(vrep[:], vrep[:], AF.Sqrt,
                                                    bias=eps6col[:]), inc=("st_sc", 1))
            ctx.op("vector", lambda e: e.reciprocal(vrep[:], vrep[:]),
                   wait=[("st_sc", ctx.val("st_sc"))])
            ctx.drain("vector")
            nwb = ap_(nwcol_sb, 0, [[8, 128], [1, 8], [0, TTS]])
            vb = ap_(vrep, 0, [[TTS, 128], [0, 8], [1, TTS]])
            ctx.op("vector", lambda e, nwb=nwb: e.tensor_tensor(h2t[:], h2t[:], nwb,
                                                                OP.mult),
                   wait=[("stoH", ctx.tile_mark.get(("phb_stoH", it), 0))])
            ctx.drain("vector")
            ctx.op("vector", lambda e, vb=vb: e.tensor_tensor(h2t[:], h2t[:], vb,
                                                              OP.mult))
            ctx.drain("vector")
            ctx.op("vector", lambda e: e.tensor_copy(xnb[:], h2t[:]), inc=("st_ve", 1))
            ctx.drain("vector")
            xnv = ctx.val("st_ve")

            for mc in range(32):
                psP = psI[mc % 2]
                mmw = [("st_ve", xnv)]
                em = ctx.tile_mark.get(("phb_em", it, mc - 2))
                if em is None and it > 0:
                    em = ctx.tile_mark.get(("phb_em", it - 1, 30 + mc))
                if em:
                    mmw.append(("st_sc", em))
                for kc in range(8):
                    ctx.op("tensor", lambda e, kc=kc, mc=mc, psP=psP: e.matmul(
                        psP[:], w_in_sb[:, (kc * 32 + mc) * 128:(kc * 32 + mc) * 128 + 128],
                        xnb[:, kc * TTS:(kc + 1) * TTS],
                        start=(kc == 0), stop=(kc == 7)),
                        wait=(mmw if kc == 0 else None),
                        inc=(("st_te", 1) if kc == 7 else None))
                tev = ctx.val("st_te")
                if mc < 16:
                    xw = [("st_te", tev)]
                    cm = ctx.tile_mark.get(("phb_conv_te", it - 2))
                    if cm:
                        xw.append(("st_te", cm))
                    ctx.op("scalar", lambda e, psP=psP, xr=xr, mc=mc: e.activation(
                        xr[:, mc * (TTS + 4) + 4:mc * (TTS + 4) + 4 + TTS], psP[:],
                        AF.Copy), wait=xw, inc=("st_sc", 1))
                else:
                    stg = stgB[mc % 2]
                    scw = [("st_te", tev)]
                    dm = ctx.tile_mark.get(("stg_last", mc % 2))
                    if dm:
                        scw.append(dm)
                    ctx.op("scalar", lambda e, psP=psP, stg=stg: e.activation(
                        stg[:], psP[:], AF.Copy), wait=scw, inc=("st_sc", 1))
                    gsem = "stoG0" if mc % 2 == 0 else "stoG1"
                    ctx.op("sync", lambda e, stg=stg, b=b, tt=tt, mc=mc: e.dma_start(
                        out=ap_(mg_s, b * DI * T + (mc - 16) * 128 * T + tt * TTS,
                                [[T, 128], [1, TTS]]), in_=stg[:]),
                        wait=[("st_sc", ctx.val("st_sc"))], inc=(gsem, 16))
                    ctx.tile_mark[("stg_last", mc % 2)] = (gsem, ctx.val(gsem))
                ctx.tile_mark[("phb_em", it, mc)] = ctx.val("st_sc")
            xbr_v = ctx.val("st_sc")
            if tt == 0:
                hal = ap_(xr, 0, [[16 * (TTS + 4), 128], [TTS + 4, 16], [1, 4]])
                ctx.op("vector", lambda e, hal=hal: e.memset(hal, 0.0),
                       wait=[("st_sc", xbr_v)], inc=("st_ve", 1))
            else:
                xrp = xbr[(it - 1) % 2]
                dst = ap_(xr, 0, [[16 * (TTS + 4), 128], [TTS + 4, 16], [1, 4]])
                srch = ap_(xrp, TTS, [[16 * (TTS + 4), 128], [TTS + 4, 16], [1, 4]])
                ctx.op("vector", lambda e, dst=dst, srch=srch: e.tensor_copy(dst, srch),
                       wait=[("st_sc", xbr_v)], inc=("st_ve", 1))
            ctx.drain("vector")
            halo_v = ctx.val("st_ve")

            for j in range(16):
                psP = psI[j % 2]
                mmw = [("st_ve", halo_v), ("st_sc", ctx.val("st_sc"))]
                for k in range(4):
                    ctx.op("tensor", lambda e, j=j, k=k, psP=psP, xr=xr: e.matmul(
                        psP[:], cdiag_sb[:, (j * 4 + k) * 128:(j * 4 + k) * 128 + 128],
                        xr[:, j * (TTS + 4) + k + 1:j * (TTS + 4) + k + 1 + TTS],
                        start=(k == 0), stop=(k == 3)),
                        wait=(mmw if k == 0 else None),
                        inc=(("st_te", 1) if k == 3 else None))
                tev = ctx.val("st_te")
                stg = stgB[j % 2]
                scw = [("st_te", tev)]
                cdm = ctx.tile_mark.get(("stg_last", j % 2))
                if cdm:
                    scw.append(cdm)
                ctx.op("scalar", lambda e, psP=psP, j=j, stg=stg: e.activation(
                    stg[:], psP[:], AF.Silu,
                    bias=cbcol_sb[:, j:j + 1]), wait=scw, inc=("st_sc", 1))
                csem = "stoC0" if j % 2 == 0 else "stoC1"
                ctx.op("sync", lambda e, b=b, tt=tt, j=j, stg=stg: e.dma_start(
                    out=ap_(xc_s, b * DI * T + j * 128 * T + tt * TTS,
                            [[T, 128], [1, TTS]]), in_=stg[:]),
                    wait=[("st_sc", ctx.val("st_sc"))], inc=(csem, 16))
                ctx.tile_mark[("stg_last", j % 2)] = (csem, ctx.val(csem))
            ctx.tile_mark[("phb_conv_te", it)] = ctx.val("st_te")
            ctx.tile_mark[("phb_stoC", it)] = (ctx.val("stoC0"), ctx.val("stoC1"))
            ctx.tile_mark[("phb_done", it)] = ctx.val("st_sc")


        phB1_sc = ctx.val("st_sc")
        phB1_ve = ctx.val("st_ve")
        phB1_te = ctx.val("st_te")
        phB.close()

        phB2 = contextlib.ExitStack()
        sbB2 = lambda n, s, d=F32: phB2.enter_context(nc.sbuf_tensor(n, s, d))
        w_dt_sb = sbB2("w_dt_sb", [128, 16 * 16 * 128], BF16)
        w_bc_sb = sbB2("w_bc_sb", [128, 16 * 32], BF16)
        bcstg = [sbB2(f"bcstg{i}", [32, TTS]) for i in range(2)]
        xct = [sbB2(f"xct{i}", [128, 16 * TTS]) for i in range(2)]
        xcb = sbB2("xcb", [128, 16 * TTS], BF16)
        dtt = sbB2("dtt", [128, 16 * TTS])
        psI2 = [phB2.enter_context(nc.psum_tensor(f"psI2{i}", [128, TTS], F32))
                for i in range(2)]
        psBC = phB2.enter_context(nc.psum_tensor("psBC", [32, TTS], F32))
        ctx.wait("gpsimd", "st_sc", phB1_sc)
        ctx.wait("gpsimd", "st_ve", phB1_ve)
        ctx.op("gpsimd", lambda e: e.dma_start(out=w_dt_sb[:], in_=w_dtT),
               inc=("ldgw", 16))
        ctx.op("gpsimd", lambda e: e.dma_start(out=w_bc_sb[:], in_=w_bcT),
               inc=("ldgw", 16))
        wB2_ready = ctx.val("ldgw")
        ctx.wait("tensor", "ldgw", wB2_ready)
        ctx.wait("tensor", "st_sc", phB1_sc)
        ctx.wait("vector", "st_sc", phB1_sc)
        ctx.wait("scalar", "st_ve", phB1_ve)
        for eng in ("scalar", "vector", "tensor"):
            for smn in ("stoC0", "stoC1", "stoG0", "stoG1", "stoH"):
                ctx.wait(eng, smn, ctx.val(smn))

        for it in range(NTILE):
            b, tt = divmod(it, T // TTS)
            xc = xct[it % 2]
            lsem = "lds0" if it % 2 == 0 else "lds1"
            lw = [("st_ve", ctx.tile_mark.get(("b2_done", it - 2), 0))]
            c01 = ctx.tile_mark.get(("phb_stoC", it))
            if c01:
                lw += [("stoC0", c01[0]), ("stoC1", c01[1])]
            lv = ctx.op("sync", lambda e, b=b, tt=tt, xc=xc: e.dma_start(
                out=xc[:], in_=ap_(xc_s, b * DI * T + tt * TTS,
                                   [[T, 128], [128 * T, 16], [1, TTS]])),
                inc=(lsem, 16), wait=lw)
            pdm = ctx.tile_mark.get(("phb_stoD", it - 1))
            vw = [(lsem, lv), ("st_sc", ctx.tile_mark.get(("b2_bc", it - 1), 0))]
            if pdm:
                vw.append(("stoD", pdm))
            ctx.op("vector", lambda e, xc=xc: e.tensor_copy(xcb[:], xc[:]),
                   wait=vw, inc=("st_ve", 1))
            ctx.drain("vector")
            xcb_v = ctx.val("st_ve")

            for mc in range(16):
                psP = psI2[mc % 2]
                mmw = [("st_ve", xcb_v)]
                em = ctx.tile_mark.get(("b2_em", it, mc - 2))
                if em is None and it > 0:
                    em = ctx.tile_mark.get(("b2_em", it - 1, mc + 14))
                if em:
                    mmw.append(("st_sc", em))
                for kc in range(16):
                    ctx.op("tensor", lambda e, kc=kc, mc=mc, psP=psP: e.matmul(
                        psP[:], w_dt_sb[:, (kc * 16 + mc) * 128:(kc * 16 + mc) * 128 + 128],
                        xcb[:, kc * TTS:(kc + 1) * TTS],
                        start=(kc == 0), stop=(kc == 15)),
                        wait=(mmw if kc == 0 else None),
                        inc=(("st_te", 1) if kc == 15 else None))
                tev = ctx.val("st_te")
                ctx.op("scalar", lambda e, psP=psP, mc=mc: e.activation(
                    dtt[:, mc * TTS:(mc + 1) * TTS], psP[:], AF.Exp,
                    bias=bdtcol_sb[:, mc:mc + 1]), wait=[("st_te", tev)],
                    inc=("st_sc", 1))
                ctx.tile_mark[("b2_em", it, mc)] = ctx.val("st_sc")
            ctx.drain("scalar")
            ctx.op("scalar", lambda e: e.activation(dtt[:], dtt[:], AF.Ln, bias=1.0))
            ctx.drain("scalar")
            ctx.op("scalar", lambda e: e.activation(dtt[:], dtt[:], AF.Tanh,
                                                    bias=sctanb[:], scale=SC_IN),
                   inc=("st_sc", 1))
            ctx.drain("scalar")
            ctx.op("vector", lambda e: e.tensor_scalar(dtt[:], dtt[:], SC_OUT, SC_C,
                                                       OP.mult, OP.add),
                   wait=[("st_sc", ctx.val("st_sc"))], inc=("st_ve", 1))
            ctx.drain("vector")
            dtv = ctx.val("st_ve")
            ctx.op("sync", lambda e, b=b, tt=tt: e.dma_start(
                out=ap_(dt_s, b * DI * T + tt * TTS, [[T, 128], [128 * T, 16], [1, TTS]]),
                in_=dtt[:]), wait=[("st_ve", dtv)], inc=("stoD", 16))
            ctx.tile_mark[("phb_stoD", it)] = ctx.val("stoD")

            mmw = [("st_ve", xcb_v), ("st_sc", ctx.tile_mark.get(("b2_bc", it - 1), 0))]
            for kc in range(16):
                ctx.op("tensor", lambda e, kc=kc: e.matmul(
                    psBC[:], w_bc_sb[:, kc * 32:(kc + 1) * 32],
                    xcb[:, kc * TTS:(kc + 1) * TTS],
                    start=(kc == 0), stop=(kc == 15)),
                    wait=(mmw if kc == 0 else None),
                    inc=(("st_te", 1) if kc == 15 else None))
            bcs = bcstg[it % 2]
            bsem = "stoB0" if it % 2 == 0 else "stoB1"
            bw = [("st_te", ctx.val("st_te"))]
            bdm = ctx.tile_mark.get(("b2_bcs", it - 2))
            if bdm:
                bw.append((bsem, bdm))
            ctx.op("scalar", lambda e, bcs=bcs: e.activation(
                bcs[:], psBC[:], AF.Copy), wait=bw, inc=("st_sc", 1))
            ctx.tile_mark[("b2_bc", it)] = ctx.val("st_sc")
            ctx.op("sync", lambda e, b=b, tt=tt, bcs=bcs: e.dma_start(
                out=ap_(bcps_d, b * T + tt * TTS, [[B_CORE * T, 32], [1, TTS]]),
                in_=bcs[:]),
                wait=[("st_sc", ctx.val("st_sc"))], inc=(bsem, 16))
            ctx.tile_mark[("b2_bcs", it)] = ctx.val(bsem)
            ctx.tile_mark[("b2_done", it)] = ctx.val("st_ve")

        phB_sc = ctx.val("st_sc")
        phB_ve = ctx.val("st_ve")
        phB_te = ctx.val("st_te")
        phB2.close()
        for s in ("stoH", "stoC0", "stoC1", "stoD", "stoG0", "stoG1"):
            ctx.wait("sync", s, ctx.val(s))
        emit_scan(ctx, nc, st, locals())
        ve = emit_colmax_sb(ctx, pcol, ap_(psMisc, 0, [[130, 1], [1, 128]]),
                            prow, idt)
        for s in ("sto0", "sto1"):
            ctx.wait("sync", s, ctx.val(s))
        return emit_ccmax(ctx, prow[:, 0:2], ccin, ccout, ve)


def emit_scan(ctx, nc, st, env):
    e = env
    dt_s, xc_s, mg_s, x1_s, h2_s, x2_o = (e["dt_s"], e["xc_s"], e["mg_s"],
                                          e["x1_s"], e["h2_s"], e["x2_o"])
    w_outT, bcps_d = e["w_outT"], e["bcps_d"]
    atan_sb, dcol_sb, allones = e["atan_sb"], e["dcol_sb"], e["allones"]
    rscol, psMisc, pcol, epscol = e["rscol"], e["psMisc"], e["pcol"], e["epscol"]

    SEG = 10
    NSEG = 205
    SPB = 4
    NBLK = 52
    SL = 512 * SEG

    def slen(s):
        return 8 if s == NSEG - 1 else 10

    def bwid(i):
        return 8 if i == NBLK - 1 else 40

    sbS = lambda n, s, d=F32: st.enter_context(nc.sbuf_tensor(n, s, d))
    w_out_sb = sbS("w_out_sb", [128, 16 * 8 * 128], BF16)
    dAb = [sbS(f"dAb{i}", [128, SL]) for i in range(2)]
    wb = [sbS(f"wbb{i}", [128, SL]) for i in range(2)]
    hb = sbS("hbb", [128, SL])
    dtc = [sbS(f"dtc{i}", [128, 16 * 2 * 40]) for i in range(2)]
    xcc = [sbS(f"xcc{i}", [128, 16 * 2 * 40]) for i in range(2)]
    yring = sbS("yring", [128, 16 * 2 * 40])
    yscr = sbS("yscr", [128, 2 * 16 * 10])
    brepblk = [sbS(f"brepblk{i}", [128, 2 * 16 * 40]) for i in range(2)]
    crepblk = [sbS(f"crepblk{i}", [128, 2 * 16 * 40]) for i in range(2)]
    dA0 = [sbS(f"dA0_{i}", [128, 512]) for i in range(2)]
    tmpc = sbS("tmpc", [128, 512])
    sqscr = sbS("sqscr", [128, 256])
    sq2 = sbS("sq2", [128, 2])
    nrmt = sbS("nrmt", [128, 2])
    mgb = sbS("mgb", [128, 16 * 80])
    obuf = sbS("obuf", [128, 16 * 80], BF16)
    x1d = [sbS(f"x1d{i}", [128, 80]) for i in range(2)]
    h2d = [sbS(f"h2d{i}", [128, 80]) for i in range(2)]
    x2blk = sbS("x2blk", [128, 8 * 80])
    sqb = sbS("sqb", [128, 8 * 80])
    mrepD = sbS("mrepD", [128, 80])
    vrepD = sbS("vrepD", [128, 80])
    tmax16 = sbS("tmax16", [128, 16])
    tmax = sbS("tmax", [128, 1])
    psD = [st.enter_context(nc.psum_tensor(f"psDo{i}", [128, 512], F32))
           for i in range(2)]
    psS2 = st.enter_context(nc.psum_tensor("psS2", [128, 1024], F32))

    mk = ctx.tile_mark
    phB_sc, phB_ve, phB_te = e["phB_sc"], e["phB_ve"], e["phB_te"]
    ctx.wait("gpsimd", "st_sc", phB_sc)
    ctx.wait("gpsimd", "st_ve", phB_ve)
    for smn in ("stoG0", "stoG1", "stoX0", "stoX1", "stoH"):
        ctx.wait("gpsimd", smn, ctx.val(smn))
    ctx.op("gpsimd", lambda e2: e2.dma_start(out=w_out_sb[:], in_=w_outT),
           inc=("ldgw", 16))
    wo_ready = ctx.val("ldgw")
    ctx.wait("tensor", "st_sc", phB_sc)
    ctx.wait("vector", "st_sc", phB_sc)
    ctx.wait("scalar", "st_ve", phB_ve)
    ctx.wait("scalar", "st_te", phB_te)
    for eng in ("tensor", "vector", "scalar", "gpsimd"):
        ctx.wait(eng, "stoB0", ctx.val("stoB0"))
        ctx.wait(eng, "stoB1", ctx.val("stoB1"))
        ctx.wait(eng, "stoD", ctx.val("stoD"))

    def emit_block_loads(i):
        W = bwid(i)
        tb0 = 40 * i
        lsem = "lds0" if i % 2 == 0 else "lds1"
        lw = []
        if i > 1:
            lw.append(("st_ve", mk[("wtt", SPB * (i - 1) - 1)]))
            lw.append(("st_ve", mk[("ytt", SPB * (i - 1) - 1)]))
            lw.append(mk[("blkload", i - 2)])
        if i > 2:
            lw.append(("st_ve", mk[("dxz", i - 2)]))
        first = [True]

        def w_():
            if first[0]:
                first[0] = False
                return lw
            return None
        dc = dtc[i % 2]
        xc = xcc[i % 2]
        for b in range(2):
            ctx.op("sync", lambda e2, b=b, tb0=tb0, W=W, dc=dc: e2.dma_start(
                out=ap_(dc, b * 40, [[1280, 128], [80, 16], [1, W]]),
                in_=ap_(dt_s, b * DI * T + tb0, [[T, 128], [128 * T, 16], [1, W]])),
                inc=(lsem, 16), wait=w_())
            ctx.op("sync", lambda e2, b=b, tb0=tb0, W=W, xc=xc: e2.dma_start(
                out=ap_(xc, b * 40, [[1280, 128], [80, 16], [1, W]]),
                in_=ap_(xc_s, b * DI * T + tb0, [[T, 128], [128 * T, 16], [1, W]])),
                inc=(lsem, 16))
        br = brepblk[i % 2]
        cr = crepblk[i % 2]
        for n in range(16):
            ctx.op("sync", lambda e2, n=n, tb0=tb0, W=W, br=br: e2.dma_start(
                out=ap_(br, n * 40, [[1280, 128], [640, 2], [1, W]]),
                in_=ap_(bcps_d, n * (B_CORE * T) + tb0, [[0, 128], [T, 2], [1, W]])),
                inc=(lsem, 16))
            ctx.op("sync", lambda e2, n=n, tb0=tb0, W=W, cr=cr: e2.dma_start(
                out=ap_(cr, n * 40, [[1280, 128], [640, 2], [1, W]]),
                in_=ap_(bcps_d, (16 + n) * (B_CORE * T) + tb0,
                        [[0, 128], [T, 2], [1, W]])),
                inc=(lsem, 16))
        mk[("blkload", i)] = (lsem, ctx.val(lsem))

    def emit_prep(s):
        i = s // SPB
        so = (s - SPB * i) * SEG
        L = slen(s)
        pb = s % 2
        lsem, lval = mk[("blkload", i)]
        da = dAb[pb]
        wv = wb[pb]
        qw = [(lsem, lval)]
        if s >= 2:
            qw.append(("st_sc", mk[("exp0", s - 2)]))
        for b in range(2):
            ctx.op("vector", lambda e2, b=b, so=so, L=L, da=da: e2.tensor_tensor(
                ap_(da, b * 2560, [[SL, 128], [160, 16], [10, 16], [1, L]]),
                ap_(dtc[i % 2], b * 40 + so,
                    [[1280, 128], [80, 16], [0, 16], [1, L]]),
                ap_(atan_sb, 0, [[256, 128], [16, 16], [1, 16], [0, L]]),
                OP.mult), wait=(qw if b == 0 else None),
                inc=(("st_ve", 1) if b == 1 else None))
        ctx.drain("vector")
        mk[("qtt", s)] = ctx.val("st_ve")
        ctx.op("vector", lambda e2, so=so, L=L, i=i: e2.tensor_tensor(
            ap_(dtc[i % 2], so, [[1280, 128], [80, 16], [40, 2], [1, L]]),
            ap_(dtc[i % 2], so, [[1280, 128], [80, 16], [40, 2], [1, L]]),
            ap_(xcc[i % 2], so, [[1280, 128], [80, 16], [40, 2], [1, L]]),
            OP.mult), inc=("st_ve", 1))
        ctx.drain("vector")
        for b in range(2):
            ctx.op("vector", lambda e2, b=b, so=so, L=L, wv=wv, i=i:
                   e2.tensor_tensor(
                ap_(wv, b * 2560, [[SL, 128], [160, 16], [10, 16], [1, L]]),
                ap_(dtc[i % 2], b * 40 + so,
                    [[1280, 128], [80, 16], [0, 16], [1, L]]),
                ap_(brepblk[i % 2], b * 640 + so,
                    [[1280, 128], [0, 16], [40, 16], [1, L]]),
                OP.mult), inc=(("st_ve", 1) if b == 1 else None))
        ctx.drain("vector")
        mk[("wtt", s)] = ctx.val("st_ve")
        ctx.op("scalar", lambda e2, da=da: e2.activation(da[:], da[:], AF.Tanh),
               wait=[("st_ve", mk[("qtt", s)])])
        ctx.drain("scalar")
        d0w = []
        if s >= 1:
            d0w.append(("st_ve", mk[("chain_tmp", s - 1)]))
        ctx.op("scalar", lambda e2, da=da, pb=pb: e2.activation(
            dA0[pb][:], ap_(da, 0, [[SL, 128], [10, 512]]), AF.Exp, scale=16.0),
            wait=d0w)
        ctx.drain("scalar")
        ctx.op("scalar", lambda e2, da=da, L=L: e2.activation(
            ap_(da, 1, [[SL, 128], [10, 512], [1, L - 1]]),
            ap_(da, 1, [[SL, 128], [10, 512], [1, L - 1]]), AF.Exp, scale=16.0),
            inc=("st_sc", 1))
        ctx.drain("scalar")
        mk[("exp0", s)] = ctx.val("st_sc")

    def emit_mset(s):
        da = dAb[s % 2]
        ctx.op("vector", lambda e2, da=da: e2.memset(
            ap_(da, 0, [[SL, 128], [10, 512]]), 0.0),
            wait=[("st_sc", mk[("exp0", s)])], inc=("st_ve", 1))
        ctx.drain("vector")
        mk[("mset", s)] = ctx.val("st_ve")

    def emit_dxz(i):
        W = bwid(i)
        lsem, lval = mk[("blkload", i)]
        ctx.op("vector", lambda e2, W=W, i=i: e2.tensor_tensor(
            ap_(yring, 0, [[1280, 128], [80, 16], [40, 2], [1, W]]),
            ap_(xcc[i % 2], 0, [[1280, 128], [80, 16], [40, 2], [1, W]]),
            ap_(dcol_sb, 0, [[16, 128], [1, 16], [0, 2], [0, W]]),
            OP.mult), wait=[(lsem, lval)], inc=("st_ve", 1))
        ctx.drain("vector")
        mk[("dxz", i)] = ctx.val("st_ve")

    def emit_y(s):
        i = s // SPB
        so = (s - SPB * i) * SEG
        L = slen(s)
        hv = hb
        lsem, lval = mk[("blkload", i)]
        for b in range(2):
            ctx.op("vector", lambda e2, b=b, so=so, L=L, hv=hv, i=i:
                   e2.tensor_tensor(
                ap_(hv, b * 2560, [[SL, 128], [160, 16], [10, 16], [1, L]]),
                ap_(hv, b * 2560, [[SL, 128], [160, 16], [10, 16], [1, L]]),
                ap_(crepblk[i % 2], b * 640 + so,
                    [[1280, 128], [0, 16], [40, 16], [1, L]]),
                OP.mult), wait=([(lsem, lval)] if b == 0 else None),
                inc=(("st_ve", 1) if b == 1 else None))
        ctx.drain("vector")
        mk[("ytt", s)] = ctx.val("st_ve")
        for b in range(2):
            ctx.op("vector", lambda e2, b=b, L=L, hv=hv: e2.reduce_sum(
                ap_(yscr, b * 160, [[320, 128], [10, 16], [1, L]]),
                ap_(hv, b * 2560, [[SL, 128], [160, 16], [1, L], [10, 16]]),
                AX.X))
        ctx.drain("vector")
        ctx.op("vector", lambda e2, so=so, L=L: e2.tensor_tensor(
            ap_(yring, so, [[1280, 128], [40, 2], [80, 16], [1, L]]),
            ap_(yring, so, [[1280, 128], [40, 2], [80, 16], [1, L]]),
            ap_(yscr, 0, [[320, 128], [160, 2], [10, 16], [1, L]]),
            OP.add), inc=("st_ve", 1))
        ctx.drain("vector")
        mk[("yadd", s)] = ctx.val("st_ve")

    def emit_phd(i):
        W = bwid(i)
        tb0 = 40 * i
        gw = [("st_ve", mk.get(("pd_gst",), 0)), mk.get(("pd_ldg",), ("ldg0", 0))]
        for kc in range(16):
            for b in range(2):
                ctx.op("gpsimd", lambda e2, b=b, kc=kc, tb0=tb0, W=W:
                       e2.dma_start(
                    out=ap_(mgb, kc * 80 + b * 40, [[1280, 128], [1, W]]),
                    in_=ap_(mg_s, b * DI * T + kc * 128 * T + tb0,
                            [[T, 128], [1, W]])),
                    inc=("ldg0", 16),
                    wait=(gw if kc == 0 and b == 0 else None))
        lvg = ctx.val("ldg0")
        mk[("pd_ldg",)] = ("ldg0", lvg)
        ctx.op("scalar", lambda e2: e2.activation(mgb[:], mgb[:], AF.Silu),
               wait=[("ldg0", lvg), ("st_ve", mk.get(("pd_gst",), 0))],
               inc=("st_sc", 1))
        ctx.drain("scalar")
        silv = ctx.val("st_sc")
        ctx.op("vector", lambda e2: e2.tensor_tensor(
            mgb[:], mgb[:], yring[:], OP.mult), wait=[("st_sc", silv)])
        ctx.drain("vector")
        ctx.op("vector", lambda e2: e2.tensor_copy(obuf[:], mgb[:]),
               inc=("st_ve", 1))
        ctx.drain("vector")
        mk[("pd_gst",)] = ctx.val("st_ve")
        mk[("phdy", i)] = ctx.val("st_ve")
        if i + 1 < NBLK:
            emit_dxz(i + 1)
        for mc in range(8):
            msem = "ldm0" if mc % 2 == 0 else "ldm1"
            x1t = x1d[mc % 2]
            h2t = h2d[mc % 2]
            mw = [("st_ve", mk.get(("pd_res", mc % 2), 0))]
            for b in range(2):
                ctx.op("gpsimd", lambda e2, b=b, mc=mc, tb0=tb0, W=W, x1t=x1t:
                       e2.dma_start(
                    out=ap_(x1t, b * 40, [[80, 128], [1, W]]),
                    in_=ap_(x1_s, b * D * T + mc * 128 * T + tb0,
                            [[T, 128], [1, W]])),
                    inc=(msem, 16), wait=(mw if b == 0 else None))
                ctx.op("gpsimd", lambda e2, b=b, mc=mc, tb0=tb0, W=W, h2t=h2t:
                       e2.dma_start(
                    out=ap_(h2t, b * 40, [[80, 128], [1, W]]),
                    in_=ap_(h2_s, b * D * T + mc * 128 * T + tb0,
                            [[T, 128], [1, W]])),
                    inc=(msem, 16))
            lvm = ctx.val(msem)
            psP = psD[mc % 2]
            mmw = [("st_ve", mk[("phdy", i)]), ("ldgw", wo_ready),
                   ("st_ve", mk.get(("pd_res", mc % 2), 0))]
            if W == 40:
                for kc in range(16):
                    ctx.op("tensor", lambda e2, kc=kc, mc=mc, psP=psP, W=W:
                           e2.matmul(
                        ap_(psP, 0, [[512, 128], [W, 2], [1, W]]),
                        w_out_sb[:, (kc * 8 + mc) * 128:(kc * 8 + mc) * 128 + 128],
                        ap_(obuf, kc * 80, [[1280, 128], [40, 2], [1, W]]),
                        start=(kc == 0), stop=(kc == 15)),
                        wait=(mmw if kc == 0 else None),
                        inc=(("st_te", 1) if kc == 15 else None))
            else:
                for kc in range(16):
                    for b in range(2):
                        ctx.op("tensor", lambda e2, kc=kc, mc=mc, psP=psP,
                               W=W, b=b: e2.matmul(
                            ap_(psP, b * W, [[512, 128], [1, W]]),
                            w_out_sb[:, (kc * 8 + mc) * 128:
                                     (kc * 8 + mc) * 128 + 128],
                            ap_(obuf, kc * 80 + b * 40, [[1280, 128], [1, W]]),
                            start=(kc == 0 and b == 0),
                            stop=(kc == 15 and b == 1)),
                            wait=(mmw if kc == 0 and b == 0 else None),
                            inc=(("st_te", 1) if kc == 15 and b == 1 else None))
            tev = ctx.val("st_te")
            ssem = "sto0" if mc % 2 == 0 else "sto1"
            stw = [("st_te", tev), (msem, lvm)]
            dm = mk.get(("pd_sto", mc % 2))
            if dm:
                stw.append((ssem, dm))
            ctx.op("vector", lambda e2, psP=psP, mc=mc, h2t=h2t, W=W:
                   e2.scalar_tensor_tensor(
                ap_(x2blk, mc * 80, [[640, 128], [40, 2], [1, W]]),
                ap_(psP, 0, [[512, 128], [W, 2], [1, W]]), rscol[:, 0:1],
                ap_(h2t, 0, [[80, 128], [40, 2], [1, W]]),
                OP.mult, OP.add),
                wait=stw)
            ctx.drain("vector")
            ctx.op("vector", lambda e2, mc=mc, x1t=x1t, W=W: e2.tensor_tensor(
                ap_(x2blk, mc * 80, [[640, 128], [40, 2], [1, W]]),
                ap_(x2blk, mc * 80, [[640, 128], [40, 2], [1, W]]),
                ap_(x1t, 0, [[80, 128], [40, 2], [1, W]]),
                OP.add), inc=("st_ve", 1))
            ctx.drain("vector")
            mk[("pd_res", mc % 2)] = ctx.val("st_ve")
            scv = ctx.val("st_ve")
            ssem = "sto0" if mc % 2 == 0 else "sto1"
            for b in range(2):
                sw2 = [("st_ve", scv)]
                dm = mk.get(("pd_sto", mc % 2))
                if dm and b == 0:
                    sw2.append((ssem, dm))
                ctx.op("sync", lambda e2, b=b, mc=mc, tb0=tb0, W=W: e2.dma_start(
                    out=ap_(x2_o, b * D * T + mc * 128 * T + tb0,
                            [[T, 128], [1, W]]),
                    in_=ap_(x2blk, mc * 80 + b * 40, [[640, 128], [1, W]])),
                    wait=sw2, inc=(ssem, 16))
            mk[("pd_sto", mc % 2)] = ctx.val(ssem)
        x2v = ctx.val("st_ve")
        sqw = [("st_ve", x2v), ("st_ve", mk.get(("pd_lnred",), 0)),
               ("st_sc", ctx.val("st_sc"))]
        ctx.op("scalar", lambda e2: e2.activation(sqb[:], x2blk[:], AF.Square),
               wait=sqw, inc=("st_sc", 1))
        ctx.drain("scalar")
        sqv = ctx.val("st_sc")
        smw = [("st_ve", x2v), ("st_sc", mk.get(("pd_lncp",), 0)),
               ("st_ve", mk.get(("pd_lnm2",), 0))]
        if W == 40:
            for mc in range(8):
                ctx.op("tensor", lambda e2, mc=mc, W=W: e2.matmul(
                    ap_(psS2, 0, [[1024, 128], [40, 2], [1, W]]), allones[:],
                    ap_(x2blk, mc * 80, [[640, 128], [40, 2], [1, W]]),
                    start=(mc == 0), stop=(mc == 7)),
                    wait=(smw if mc == 0 else None))
            for mc in range(8):
                ctx.op("tensor", lambda e2, mc=mc, W=W: e2.matmul(
                    ap_(psS2, 512, [[1024, 128], [40, 2], [1, W]]), allones[:],
                    ap_(sqb, mc * 80, [[640, 128], [40, 2], [1, W]]),
                    start=(mc == 0), stop=(mc == 7)),
                    wait=([("st_sc", sqv)] if mc == 0 else None),
                    inc=(("st_te", 1) if mc == 7 else None))
        else:
            for mc in range(8):
                for b in range(2):
                    ctx.op("tensor", lambda e2, mc=mc, W=W, b=b: e2.matmul(
                        ap_(psS2, b * 40, [[1024, 128], [1, W]]), allones[:],
                        ap_(x2blk, mc * 80 + b * 40, [[640, 128], [1, W]]),
                        start=(mc == 0 and b == 0),
                        stop=(mc == 7 and b == 1)),
                        wait=(smw if mc == 0 and b == 0 else None))
            for mc in range(8):
                for b in range(2):
                    ctx.op("tensor", lambda e2, mc=mc, W=W, b=b: e2.matmul(
                        ap_(psS2, 512 + b * 40, [[1024, 128], [1, W]]),
                        allones[:],
                        ap_(sqb, mc * 80 + b * 40, [[640, 128], [1, W]]),
                        start=(mc == 0 and b == 0),
                        stop=(mc == 7 and b == 1)),
                        wait=([("st_sc", sqv)] if mc == 0 and b == 0 else None),
                        inc=(("st_te", 1) if mc == 7 and b == 1 else None))
        tev = ctx.val("st_te")
        ctx.op("scalar", lambda e2, W=W: e2.activation(
            ap_(mrepD, 0, [[80, 128], [40, 2], [1, W]]),
            ap_(psS2, 0, [[1024, 128], [40, 2], [1, W]]), AF.Copy, scale=1.0 / D),
            wait=[("st_te", tev)])
        ctx.drain("scalar")
        ctx.op("scalar", lambda e2, W=W: e2.activation(
            ap_(vrepD, 0, [[80, 128], [40, 2], [1, W]]),
            ap_(psS2, 512, [[1024, 128], [40, 2], [1, W]]), AF.Copy,
            scale=1.0 / D), inc=("st_sc", 1))
        ctx.drain("scalar")
        mk[("pd_lncp",)] = ctx.val("st_sc")
        ctx.op("vector", lambda e2, W=W: e2.tensor_tensor(
            ap_(psS2, 0, [[1024, 128], [40, 2], [1, W]]),
            ap_(mrepD, 0, [[80, 128], [40, 2], [1, W]]),
            ap_(mrepD, 0, [[80, 128], [40, 2], [1, W]]), OP.mult),
            wait=[("st_sc", mk[("pd_lncp",)])])
        ctx.drain("vector")
        ctx.op("vector", lambda e2, W=W: e2.tensor_tensor(
            ap_(vrepD, 0, [[80, 128], [40, 2], [1, W]]),
            ap_(vrepD, 0, [[80, 128], [40, 2], [1, W]]),
            ap_(psS2, 0, [[1024, 128], [40, 2], [1, W]]), OP.subtract),
            inc=("st_ve", 1))
        ctx.drain("vector")
        mk[("pd_lnm2",)] = ctx.val("st_ve")
        ctx.op("scalar", lambda e2, W=W: e2.activation(
            ap_(vrepD, 0, [[80, 128], [40, 2], [1, W]]),
            ap_(vrepD, 0, [[80, 128], [40, 2], [1, W]]), AF.Ln,
            bias=epscol[:]), wait=[("st_ve", mk[("pd_lnm2",)])])
        ctx.drain("scalar")
        ctx.op("scalar", lambda e2, W=W: e2.activation(
            ap_(vrepD, 0, [[80, 128], [40, 2], [1, W]]),
            ap_(vrepD, 0, [[80, 128], [40, 2], [1, W]]), AF.Exp, scale=-0.5),
            inc=("st_sc", 1))
        ctx.drain("scalar")
        rsv = ctx.val("st_sc")
        ctx.op("vector", lambda e2, W=W: e2.tensor_tensor(
            ap_(sqb, 0, [[640, 128], [80, 8], [40, 2], [1, W]]),
            ap_(x2blk, 0, [[640, 128], [80, 8], [40, 2], [1, W]]),
            ap_(mrepD, 0, [[80, 128], [0, 8], [40, 2], [1, W]]), OP.subtract),
            wait=[("st_sc", rsv)])
        ctx.drain("vector")
        ctx.op("vector", lambda e2, W=W: e2.tensor_tensor(
            ap_(sqb, 0, [[640, 128], [80, 8], [40, 2], [1, W]]),
            ap_(sqb, 0, [[640, 128], [80, 8], [40, 2], [1, W]]),
            ap_(vrepD, 0, [[80, 128], [0, 8], [40, 2], [1, W]]), OP.mult))
        ctx.drain("vector")
        ctx.op("vector", lambda e2, W=W: e2.reduce_max(
            tmax16[:], ap_(sqb, 0, [[640, 128], [80, 8], [40, 2], [1, W]]),
            AX.X, apply_absolute_value=True))
        ctx.drain("vector")
        ctx.op("vector", lambda e2: e2.reduce_max(
            tmax[:], tmax16[:], AX.X, apply_absolute_value=True))
        ctx.drain("vector")
        ctx.op("vector", lambda e2: e2.tensor_tensor(
            pcol[:], pcol[:], tmax[:], OP.max), inc=("st_ve", 1))
        ctx.drain("vector")
        mk[("pd_lnred",)] = ctx.val("st_ve")

    emit_block_loads(0)
    emit_block_loads(1)
    emit_prep(0)
    emit_dxz(0)
    for s in range(NSEG):
        pb = s % 2
        if s == 0:
            ctx.op("vector", lambda e2: e2.tensor_copy(
                tmpc[:], ap_(wb[0], 0, [[SL, 128], [10, 512]])),
                inc=("st_ve", 1))
            ctx.drain("vector")
        else:
            cw = [("st_sc", mk.get(("sq", s - 1), 0)),
                  ("st_sc", mk[("exp0", s)])]
            ctx.op("vector", lambda e2, pb=pb: e2.tensor_tensor(
                tmpc[:], dA0[pb][:],
                ap_(hb, 9, [[SL, 128], [10, 512]]), OP.mult),
                wait=cw)
            ctx.drain("vector")
            ctx.op("vector", lambda e2, pb=pb: e2.tensor_tensor(
                tmpc[:], tmpc[:], ap_(wb[pb], 0, [[SL, 128], [10, 512]]),
                OP.add), inc=("st_ve", 1))
            ctx.drain("vector")
        mk[("chain_tmp", s)] = ctx.val("st_ve")
        sqw = [("st_ve", mk[("chain_tmp", s)]),
               ("st_te", mk.get(("clipmm", s - 1), 0))]
        for b in range(2):
            ctx.op("scalar", lambda e2, b=b: e2.activation(
                sqscr[:], tmpc[:, b * 256:(b + 1) * 256], AF.Square,
                accum_out=sq2[:, b:b + 1]),
                wait=(sqw if b == 0 else None),
                inc=(("st_sc", 1) if b == 1 else None))
            ctx.drain("scalar")
        mk[("sq", s)] = ctx.val("st_sc")
        mmw = [("st_sc", mk[("sq", s)]), ("st_sc", mk.get(("sqrt", s - 1), 0))]
        ctx.op("tensor", lambda e2: e2.matmul(
            ap_(psMisc, 128, [[130, 128], [1, 2]]), allones[:], sq2[:],
            start=True, stop=True), wait=mmw, inc=("st_te", 1))
        mk[("clipmm", s)] = ctx.val("st_te")
        ctx.op("scalar", lambda e2: e2.activation(
            nrmt[:], ap_(psMisc, 128, [[130, 128], [1, 2]]), AF.Ln),
            wait=[("st_te", mk[("clipmm", s)]),
                  ("st_ve", mk.get(("hclip", s - 1), 0))])
        ctx.drain("scalar")
        ctx.op("scalar", lambda e2: e2.activation(
            nrmt[:], nrmt[:], AF.Exp, scale=0.5), inc=("st_sc", 1))
        ctx.drain("scalar")
        mk[("sqrt", s)] = ctx.val("st_sc")
        if s + 1 < NSEG:
            emit_prep(s + 1)
        if s >= 1:
            emit_y(s - 1)
        if (s + 1) % SPB == 1 and s + 1 > 1 and (s + 1) // SPB + 1 < NBLK:
            emit_block_loads((s + 1) // SPB + 1)
        ctx.op("vector", lambda e2: e2.tensor_scalar(
            nrmt[:], nrmt[:], 1e-8, None, OP.add),
            wait=[("st_sc", mk[("sqrt", s)])])
        ctx.drain("vector")
        ctx.op("vector", lambda e2: e2.reciprocal(nrmt[:], nrmt[:]))
        ctx.drain("vector")
        ctx.op("vector", lambda e2: e2.tensor_scalar(
            nrmt[:], nrmt[:], 20.0, 1.0, OP.mult, OP.min))
        ctx.drain("vector")
        ctx.op("vector", lambda e2, pb=pb: e2.tensor_tensor(
            ap_(wb[pb], 0, [[SL, 128], [2560, 2], [10, 256]]),
            ap_(tmpc, 0, [[512, 128], [256, 2], [1, 256]]),
            ap_(nrmt, 0, [[2, 128], [1, 2], [0, 256]]), OP.mult),
            inc=("st_ve", 1))
        ctx.drain("vector")
        mk[("hclip", s)] = ctx.val("st_ve")
        emit_mset(s)
        ctx.op("vector", lambda e2, pb=pb: e2.tensor_tensor_scan(
            hb[:], dAb[pb][:], wb[pb][:], 0.0, OP.mult, OP.add),
            wait=[("st_sc", mk[("exp0", s)])], inc=("st_ve", 1))
        ctx.drain("vector")
        mk[("scan", s)] = ctx.val("st_ve")
        if s >= SPB and (s - 1) % SPB == SPB - 1:
            emit_phd((s - 1) // SPB)
    emit_y(NSEG - 1)
    emit_phd(NBLK - 1)


_IDENT = np.eye(128, dtype=np.float32)


def _selbc():
    out = np.zeros((32, 32 * 128), np.float32)
    for r in range(32):
        out[r, r * 128:(r + 1) * 128] = 1.0
    return out


_SELBC = _selbc()


def _conv_diag(conv_w):
    out = np.zeros((128, 16 * 4 * 128), np.float32)
    cw = np.asarray(conv_w)
    for j in range(16):
        for k in range(4):
            blk = np.diag(cw[j * 128:(j + 1) * 128, 0, k].astype(np.float32))
            out[:, (j * 4 + k) * 128:(j * 4 + k) * 128 + 128] = blk
    return out


def _tile_kmajor(wT, nkc, nmc, mwid=128):
    K, M = wT.shape
    assert K == nkc * 128 and M == nmc * mwid
    return np.ascontiguousarray(
        wT.reshape(nkc, 128, nmc, mwid).transpose(1, 0, 2, 3).reshape(128, -1))


ALL_SEMS = [
    "ldc", "ldsc", "ldcc", "cc", "ldx0", "ldx1", "ldu0", "ldu1", "ldw0",
    "ldw1", "ldf0", "ldf1", "ldr0", "ldr1", "lds0", "lds1", "ldm0", "ldm1",
    "ldg0", "ldg1", "ldgw", "ldsel", "st_te", "st_ve", "st_sc", "stoU0",
    "stoU1", "stoG0", "stoG1", "stoC0", "stoC1", "stoD", "stoB0", "stoB1",
    "stoX0", "stoX1", "stoH", "sto0", "sto1", "stoW",
    "dr_sync", "dr_tensor", "dr_vector", "dr_scalar", "dr_gpsimd"]


def full_barrier(ctx, exclude=("cc",)):
    for eng in Ctx.ENGS:
        for s, v in list(ctx.cnt.items()):
            if v and s not in exclude:
                ctx.wait(eng, s, v)


def build_MEGA():
    nc = bass.Bass("TRN2", target_bir_lowering=False,
                   detect_race_conditions=False)
    x = nc.dram_tensor("x", [B_CORE, D, T], F32, kind="ExternalInput").ap()
    ident = nc.dram_tensor("ident", [128, 128], F32, kind="ExternalInput").ap()
    identb = nc.dram_tensor("identb", [128, 128], BF16, kind="ExternalInput").ap()
    w_up = nc.dram_tensor("w_up", [FF, D], F32, kind="ExternalInput").ap()
    w_gate = nc.dram_tensor("w_gate", [FF, D], F32, kind="ExternalInput").ap()
    w_down = nc.dram_tensor("w_down", [D, FF], F32, kind="ExternalInput").ap()
    w_inT = nc.dram_tensor("w_inT", [128, 8 * 32 * 128], F32, kind="ExternalInput").ap()
    w_dtT = nc.dram_tensor("w_dtT", [128, 16 * 16 * 128], F32, kind="ExternalInput").ap()
    w_bcT = nc.dram_tensor("w_bcT", [128, 16 * 32], F32, kind="ExternalInput").ap()
    w_outT = nc.dram_tensor("w_outT", [128, 16 * 8 * 128], F32, kind="ExternalInput").ap()
    cdiag = nc.dram_tensor("cdiag", [128, 16 * 4 * 128], F32, kind="ExternalInput").ap()
    a_jln = nc.dram_tensor("a_jln", [128, 16 * DS], F32, kind="ExternalInput").ap()
    dcol = nc.dram_tensor("dcol", [128, 16], F32, kind="ExternalInput").ap()
    cbcol = nc.dram_tensor("cbcol", [128, 16], F32, kind="ExternalInput").ap()
    bdtcol = nc.dram_tensor("bdtcol", [128, 16], F32, kind="ExternalInput").ap()
    nwcol = nc.dram_tensor("nwcol", [128, 8], F32, kind="ExternalInput").ap()
    rscal = nc.dram_tensor("rscal", [1, 1], F32, kind="ExternalInput").ap()
    out = nc.dram_tensor("out", [B_CORE, D, T], F32, kind="ExternalOutput").ap()
    up_raw = nc.dram_tensor("up_raw", [B_CORE, FF, T], F32).ap()
    gate_raw = nc.dram_tensor("gate_raw", [B_CORE, FF, T], F32).ap()
    pre3_raw = nc.dram_tensor("pre3_raw", [B_CORE, FF, T], F32).ap()
    pre4_raw = nc.dram_tensor("pre4_raw", [B_CORE, FF, T], F32).ap()
    x2_s = nc.dram_tensor("x2_s", [B_CORE, D, T], F32).ap()
    x1_s = nc.dram_tensor("x1_s", [B_CORE, D, T], F32).ap()
    h2_s = nc.dram_tensor("h2_s", [B_CORE, D, T], F32).ap()
    mg_s = nc.dram_tensor("mg_s", [B_CORE, DI, T], F32).ap()
    xc_s = nc.dram_tensor("xc_s", [B_CORE, DI, T], F32).ap()
    dt_s = nc.dram_tensor("dt_s", [B_CORE, DI, T], F32).ap()
    bcps_d = nc.dram_tensor("bcps_d", [32, B_CORE * T], F32).ap()
    NCC = 8
    ccin = [nc.dram_tensor(f"ccin{k}", [1, 2], F32) for k in range(NCC)]
    ccout = [nc.dram_tensor(f"ccout{k}", [1, 2], F32) for k in range(NCC)]
    wqu_c = nc.dram_tensor("wqu_c", [128, 8 * 24 * 128], BF16).ap()
    wqg_c = nc.dram_tensor("wqg_c", [128, 8 * 24 * 128], BF16).ap()
    wdt_c = nc.dram_tensor("wdt_c", [128, 24 * 8 * 128], BF16).ap()

    with contextlib.ExitStack() as root:
        ctx = Ctx(nc)
        ctx.add_sems(ALL_SEMS, root)
        blk = root.enter_context(nc.Block())

        def ph(fn, *args):
            with contextlib.ExitStack() as st:
                r = fn(ctx, nc, st, *args)
            full_barrier(ctx)
            return r

        cc0 = ph(ph_P0, x, ident, ccin[0], ccout[0])
        cc1 = ph(ph_upgate, "f1", x, ident, identb, w_up, w_gate, up_raw,
                 gate_raw, (ccout[0].ap(), cc0, 0), ccin[1], ccout[1],
                 (wqu_c, wqg_c, "store"))
        cc2 = ph(ph_gate, "g1", gate_raw, ident, pre3_raw,
                 (ccout[1].ap(), cc1, 1), ccin[2], ccout[2])
        cc3 = ph(ph_prod, "o1", pre3_raw, up_raw, ident, pre4_raw,
                 (ccout[2].ap(), cc2, 0), (ccout[1].ap(), cc1, 0),
                 ccin[3], ccout[3])
        cc4 = ph(ph_P4, x, pre4_raw, (ccout[3].ap(), cc3, 0), w_down, ident,
                 identb, w_inT, w_dtT, w_bcT, w_outT, cdiag, a_jln, dcol,
                 cbcol, bdtcol, nwcol, rscal, x2_s, x1_s, h2_s, mg_s, xc_s,
                 dt_s, bcps_d, ccin[4], ccout[4], (wdt_c, "store"))
        cc5 = ph(ph_upgate, "f2", x2_s, ident, identb, w_up, w_gate, up_raw,
                 gate_raw, (ccout[4].ap(), cc4, 0), ccin[5], ccout[5],
                 (wqu_c, wqg_c, "load"))
        cc6 = ph(ph_gate, "g2", gate_raw, ident, pre3_raw,
                 (ccout[5].ap(), cc5, 1), ccin[6], ccout[6])
        cc7 = ph(ph_prod, "o2", pre3_raw, up_raw, ident, pre4_raw,
                 (ccout[6].ap(), cc6, 0), (ccout[5].ap(), cc5, 0),
                 ccin[7], ccout[7])
        ph(ph_tail, pre4_raw, x2_s, (ccout[7].ap(), cc7, 0), w_down, identb,
           out, (wdt_c, "load"))
        ctx.build(blk)
    return nc


_PROG = []


def _prog():
    if not _PROG:
        _PROG.append(build_MEGA())
    return _PROG[0]


def _prep_core_inputs(inputs):
    x = np.asarray(inputs["L0_attn_x"], dtype=np.float32)
    per_core = [np.ascontiguousarray(x[2 * c:2 * c + 2].transpose(0, 2, 1))
                for c in range(NCORES)]
    A_log = np.asarray(inputs["L0_attn_A_log"], dtype=np.float32)
    shared = {
        "ident": _IDENT,
        "identb": _IDENT.astype(ml_dtypes.bfloat16),
        "w_up": np.ascontiguousarray(inputs["L0_ffn_w_up"], dtype=np.float32),
        "w_gate": np.ascontiguousarray(inputs["L0_ffn_w_gate"], dtype=np.float32),
        "w_down": np.ascontiguousarray(inputs["L0_ffn_w_down"], dtype=np.float32),
        "w_inT": _tile_kmajor(np.ascontiguousarray(
            np.asarray(inputs["L0_attn_W_in"], dtype=np.float32).T), 8, 32),
        "w_dtT": _tile_kmajor(np.ascontiguousarray(
            np.asarray(inputs["L0_attn_W_dt"], dtype=np.float32).T), 16, 16),
        "w_bcT": _tile_kmajor(np.ascontiguousarray(np.concatenate(
            [np.asarray(inputs["L0_attn_W_B"], dtype=np.float32),
             np.asarray(inputs["L0_attn_W_C"], dtype=np.float32)], 0).T), 16, 1, 32),
        "w_outT": _tile_kmajor(np.ascontiguousarray(
            np.asarray(inputs["L0_attn_W_out"], dtype=np.float32).T), 16, 8),
        "cdiag": _conv_diag(np.asarray(inputs["L0_attn_conv_w"], dtype=np.float32)),
        "a_jln": np.ascontiguousarray(
            A_log.reshape(16, 128, 16).transpose(1, 0, 2).reshape(128, 256)),
        "dcol": np.ascontiguousarray(
            np.asarray(inputs["L0_attn_D_param"], np.float32).reshape(16, 128).T),
        "cbcol": np.ascontiguousarray(
            np.asarray(inputs["L0_attn_conv_b"], np.float32).reshape(16, 128).T),
        "bdtcol": np.ascontiguousarray(
            np.asarray(inputs["L0_attn_b_dt"], np.float32).reshape(16, 128).T),
        "nwcol": np.ascontiguousarray(
            np.asarray(inputs["L0_attn_norm_w"], np.float32).reshape(8, 128).T),
        "rscal": np.asarray(inputs["L0_attn_residual_scale"],
                            np.float32).reshape(1, 1),
    }
    return per_core, shared


def kernel(**inputs):
    xs, sh = _prep_core_inputs(inputs)
    nc = _prog()
    res = run_bass_kernel_spmd(
        nc, [{"x": xs[c], **sh} for c in range(NCORES)],
        core_ids=list(range(NCORES)))
    out = np.concatenate(
        [np.asarray(res.results[c]["out"]).transpose(0, 2, 1)
         for c in range(NCORES)], axis=0)
    return np.ascontiguousarray(out.astype(np.float32))


if __name__ == "__main__":
    pass
